# version 07
# baseline (speedup 1.0000x reference)
import numpy as np
import ml_dtypes
from contextlib import ExitStack

import concourse.bass as bass
import concourse.mybir as mybir
from concourse.bass_utils import run_bass_kernel_spmd

F32 = mybir.dt.float32
BF16 = mybir.dt.bfloat16
AF = mybir.ActivationFunctionType
ALU = mybir.AluOpType
AX = mybir.AxisListType
f32 = np.float32
bf16 = ml_dtypes.bfloat16

B, T, C = 2, 8192, 1024
HS = 64
E, FF = 8, 4096
NCORES = 8
TOK = 2048
NCH = 16
KCH = 64
NT = 1024
EPS = 1e-5

C_INV1024, C_NEG132, C_ONE, C_NEG1E30, C_ZERO, C_NEG1 = 0, 1, 2, 3, 4, 5

ENGINES = ("sync", "scalar", "vector", "tensor", "gpsimd")


class Sem:
    def __init__(self, nc, es, name):
        self.h = es.enter_context(nc.semaphore(name))
        self.n = 0

    def bump(self, k):
        self.n += k
        return self.n


class Prog:

    def __init__(self):
        self.streams = {e: [] for e in ENGINES}

    def op(self, eng, fn, inc=None, inc_n=1):
        if inc is not None:
            inc.bump(inc_n)
            self.streams[eng].append(
                lambda Eo, fn=fn, s=inc.h, n=inc_n: fn(Eo).then_inc(s, n))
        else:
            self.streams[eng].append(lambda Eo, fn=fn: fn(Eo))

    def eop(self, eng, fn, sem):
        self.op(eng, fn, inc=sem)
        self.wait(eng, sem)

    def wait(self, eng, sem, val=None):
        v = sem.n if val is None else val
        if v > 0:
            self.streams[eng].append(lambda Eo, s=sem.h, v=v: Eo.wait_ge(s, v))

    def run(self, block):
        for eng in ENGINES:
            fns = tuple(self.streams[eng])

            def body(Eo, fns=fns):
                for f in fns:
                    f(Eo)

            getattr(block, eng)(body)


class Rot:

    def __init__(self, n):
        self.n = n
        self.rel = [[] for _ in range(n)]

    def acquire(self, P, eng, i):
        for sem, cnt in self.rel[i % self.n]:
            P.wait(eng, sem, cnt)
        self.rel[i % self.n] = []

    def release(self, i, sem):
        self.rel[i % self.n].append((sem, sem.n))


class DmaSlots:

    def __init__(self, nc, es, name, n=2):
        self.sems = [Sem(nc, es, "%s%d" % (name, i)) for i in range(n)]
        self.n = n

    def load(self, P, eng, i, fn):
        P.op(eng, fn, inc=self.sems[i % self.n], inc_n=16)

    def wait(self, P, eng, i):
        P.wait(eng, self.sems[i % self.n])

    def sem_count(self, i):
        s = self.sems[i % self.n]
        return (s, s.n)


def build_nc(debug_taps=()):
    nc = bass.Bass("TRN2", target_bir_lowering=False)

    d_xown = nc.dram_tensor("x_own", [TOK, C], F32, kind="ExternalInput").ap()
    d_xbat = nc.dram_tensor("x_batch", [T, C], F32, kind="ExternalInput").ap()
    d_wqkv = nc.dram_tensor("wqkvT", [128, 8, 192], F32, kind="ExternalInput").ap()
    d_qkb = nc.dram_tensor("qkb", [128, 2], F32, kind="ExternalInput").ap()
    d_bvr = nc.dram_tensor("bvrow", [1, HS], F32, kind="ExternalInput").ap()
    d_b1t = nc.dram_tensor("b1t", [128, E, 32], F32, kind="ExternalInput").ap()
    d_b2w = nc.dram_tensor("b2w", [E, C], F32, kind="ExternalInput").ap()
    d_rw = nc.dram_tensor("rw", [128, 8, E], F32, kind="ExternalInput").ap()
    d_w1 = nc.dram_tensor("w1", [E, C, FF], F32, kind="ExternalInput").ap()
    d_w2 = nc.dram_tensor("w2", [E, FF, C], F32, kind="ExternalInput").ap()
    d_idb = nc.dram_tensor("identb", [128, 128], BF16, kind="ExternalInput").ap()
    d_idf = nc.dram_tensor("identf", [128, 128], F32, kind="ExternalInput").ap()
    d_qpi = nc.dram_tensor("qpiota", [128, 512], F32, kind="ExternalInput").ap()
    d_ones = nc.dram_tensor("onesb", [1, 128], BF16, kind="ExternalInput").ap()
    d_qoff = nc.dram_tensor("qoffb", [128, 1], F32, kind="ExternalInput").ap()
    d_cst = nc.dram_tensor("consts", [128, 8], F32, kind="ExternalInput").ap()
    d_sel = nc.dram_tensor("selb", [E, E * 128], BF16, kind="ExternalInput").ap()
    d_mth = nc.dram_tensor("mth", [128, 4], F32, kind="ExternalInput").ap()
    d_y1sp = nc.dram_tensor("y1sp", [TOK, C], F32, kind="Internal").ap()
    d_y2 = nc.dram_tensor("y2", [TOK, C], F32, kind="ExternalOutput").ap()
    taps = {}
    for name, shape in debug_taps:
        taps[name] = nc.dram_tensor("tap_" + name, shape, F32, kind="ExternalOutput").ap()

    esP = ExitStack()
    sb = lambda name, shape, dt: esP.enter_context(nc.sbuf_tensor(name, shape, dt))

    idb = sb("idb", [128, 128], BF16)
    idf = sb("idf", [128, 128], F32)
    onesb = sb("onesb_s", [1, 128], BF16)
    qoffb = sb("qoffb_s", [128, 1], F32)
    mth = sb("mth_s", [128, 4], F32)
    cst = sb("cst", [128, 8], F32)
    b1t = sb("b1t_s", [128, E * 32], F32)
    b2b = sb("b2b", [E, C], BF16)
    rw = sb("rw_s", [128, 8 * E], F32)
    y1T = sb("y1T", [128, 8 * TOK], BF16)
    gatesT = sb("gatesT", [E, TOK], BF16)
    selb = sb("selb_s", [E, E * 128], BF16)

    esA = ExitStack()
    sbA = lambda name, shape, dt: esA.enter_context(nc.sbuf_tensor(name, shape, dt))
    qpi = sbA("qpi", [128, 512], F32)
    maskadd = sbA("maskadd", [128, 512], F32)
    qT = sbA("qT", [65, TOK], BF16)
    kT = sbA("kT", [65, T], BF16)
    vtm = sbA("vtm", [128, KCH * 65], BF16)
    condt = sbA("condt", [128, 4 * 128], BF16)
    o_tm = sbA("o_tm", [128, NCH * HS], F32)
    mA = sbA("mA", [128, 1], F32)
    mB = sbA("mB", [128, 1], F32)
    dm = sbA("dm", [128, 1], F32)
    edm = sbA("edm", [128, 1], F32)
    denA = sbA("denA", [128, 1], F32)
    denB = sbA("denB", [128, 1], F32)
    negm32 = sbA("negm32", [128, 1], F32)
    lden = sbA("lden", [128, 1], F32)
    bias2 = sbA("bias2", [128, 1], F32)
    bsum = [sbA("bsum0", [128, 1], F32), sbA("bsum1", [128, 1], F32)]

    esS = ExitStack()
    mks = lambda name: Sem(nc, esS, name)
    sC1 = mks("sC1")
    sC2 = mks("sC2")
    sTAP = mks("sTAP")
    sPL = mks("sPL")
    sPE = mks("sPE")
    sAC = mks("sAC")
    sDV = mks("sDV")
    sBAR = mks("sBAR")
    dXB = DmaSlots(nc, esS, "dXB")
    dZC = DmaSlots(nc, esS, "dZC", n=3)
    dSPp = DmaSlots(nc, esS, "dSPp", n=3)
    dW1 = DmaSlots(nc, esS, "dW1")
    dW2 = DmaSlots(nc, esS, "dW2", n=4)
    dYC = DmaSlots(nc, esS, "dYC", n=3)
    dSTp = DmaSlots(nc, esS, "dSTp", n=3)
    dFD = DmaSlots(nc, esS, "dFD")

    c_inv1024 = cst[:, C_INV1024:C_INV1024 + 1]
    c_neg132 = cst[:, C_NEG132:C_NEG132 + 1]
    c_one = cst[:, C_ONE:C_ONE + 1]
    c_neg1e30 = cst[:, C_NEG1E30:C_NEG1E30 + 1]
    c_zero = cst[:, C_ZERO:C_ZERO + 1]
    c_neg1 = cst[:, C_NEG1:C_NEG1 + 1]

    barrier_count = [0]

    def barrier(P):
        barrier_count[0] += 1
        for eng in ENGINES:
            P.op(eng, lambda Eo: Eo.nop(), inc=sBAR)
        for eng in ENGINES:
            P.wait(eng, sBAR, 5 * barrier_count[0])

    es1 = ExitStack()
    sb1 = lambda name, shape, dt: es1.enter_context(nc.sbuf_tensor(name, shape, dt))
    ps1 = lambda name, shape, dt: es1.enter_context(nc.psum_tensor(name, shape, dt))
    xb = [sb1("xb0", [128, C], F32), sb1("xb1", [128, C], F32)]
    xT = [sb1("xT0", [128, C], BF16), sb1("xT1", [128, C], BF16)]
    wqkv = sb1("wqkv", [128, 8 * 192], BF16)
    bvr = sb1("bvr", [1, HS], BF16)
    qkb = sb1("qkb_s", [128, 2], F32)
    pXT = [ps1("pXT0", [128, C], F32), ps1("pXT1", [128, C], F32)]
    pQK = [ps1("pQK0", [64, 128], F32), ps1("pQK1", [64, 128], F32)]
    pV = [ps1("pV0", [128, HS], F32), ps1("pV1", [128, HS], F32)]

    P = Prog()
    for dst, src in [(idb, d_idb), (idf, d_idf), (qpi, d_qpi), (qoffb, d_qoff),
                     (mth, d_mth),
                     (cst, d_cst), (qkb[:], d_qkb),
                     (b1t, d_b1t.rearrange("p e k -> p (e k)")),
                     (rw, d_rw.rearrange("p c e -> p (c e)"))]:
        ap = dst[:] if hasattr(dst, "ap") else dst
        P.op("sync", lambda Eo, a=ap, s=src: Eo.dma_start(out=a, in_=s), inc=sC1, inc_n=16)
    for dst, src in [(onesb[:], d_ones), (selb[:], d_sel),
                     (wqkv[:], d_wqkv.rearrange("p c h -> p (c h)")),
                     (bvr[:], d_bvr), (b2b[:], d_b2w)]:
        P.op("gpsimd", lambda Eo, a=dst, s=src: Eo.dma_start(out=a, in_=s), inc=sC2, inc_n=16)
    ld_consts = sC1.n
    wl_consts = sC2.n
    for _eng in ("tensor", "scalar", "vector", "gpsimd"):
        P.wait(_eng, sC1, ld_consts)
        P.wait(_eng, sC2, wl_consts)
    P.op("gpsimd", lambda Eo: Eo.memset(kT[64:65, :], 1.0), inc=sPL)
    P.op("gpsimd", lambda Eo: Eo.memset(vtm[:], 1.0), inc=sPL)
    P.wait("vector", sC1, ld_consts)
    P.eop("vector", lambda Eo: Eo.tensor_scalar(maskadd[0:128, 0:128], qpi[:, 0:128], c_neg1, None, op0=ALU.mult), sDV)
    for _t in range(4):
        P.eop("vector", lambda Eo, t=_t: Eo.tensor_scalar(
            condt[:, t * 128:(t + 1) * 128], maskadd[0:128, 0:128],
            mth[:, t:t + 1], None, op0=ALU.is_ge), sDV)
    P.eop("vector", lambda Eo: Eo.tensor_scalar(maskadd[:], qpi[:], qoffb[:, 0:1], None, op0=ALU.add), sDV)
    P.eop("vector", lambda Eo: Eo.tensor_scalar(maskadd[:], maskadd[:], c_zero, None, op0=ALU.is_lt), sDV)
    P.op("vector", lambda Eo: Eo.tensor_scalar(maskadd[:], maskadd[:], c_neg1e30, None, op0=ALU.mult), inc=sDV)

    rXB = Rot(2)
    rXT = Rot(2)
    rQK = Rot(2)
    rV = Rot(2)

    rPXT = Rot(2)

    def chunk_setup(i, src_ap, is_q, kc):
        pi = i % 2
        rXB.acquire(P, "sync", i)
        dXB.load(P, "sync", i, lambda Eo, a=xb[pi][:], s=src_ap: Eo.dma_start(out=a, in_=s))
        dXB.wait(P, "tensor", i)
        rPXT.acquire(P, "tensor", i)
        rXT_psum = pXT[pi]
        for dc in range(8):
            P.op("tensor", lambda Eo, o=rXT_psum[:, dc * 128:(dc + 1) * 128],
                 a=xb[pi][:, dc * 128:(dc + 1) * 128]: Eo.transpose(o, a, idf[:]),
                 inc=(sPE if dc == 7 else None))
        transposes_done = sPE.n
        P.wait("vector", sPE)
        rXT.acquire(P, "vector", i)
        P.op("vector", lambda Eo, o=xT[pi][:], a=rXT_psum[:]: Eo.tensor_copy(o, a), inc=sDV)
        rPXT.release(i, sDV)
        evac_cnt = sDV.n
        P.wait("tensor", sDV, evac_cnt)
        if is_q:
            rQK.acquire(P, "tensor", i)
            for dc in range(8):
                P.op("tensor", lambda Eo, p=pQK[pi][:],
                     w=wqkv[:, dc * 192:dc * 192 + 64],
                     x=xT[pi][:, dc * 128:(dc + 1) * 128], st=(dc == 0), sp=(dc == 7):
                     Eo.matmul(p, w, x, start=st, stop=sp),
                     inc=(sPE if dc == 7 else None))
            P.wait("scalar", sPE)
            P.op("scalar", lambda Eo, o=qT[0:64, i * 128:(i + 1) * 128], p=pQK[pi][:]:
                 Eo.activation(o, p, AF.Identity, bias=qkb[0:64, 0:1], scale=f32(1.0)),
                 inc=sAC)
            rQK.release(i, sAC)
        else:
            rQK.acquire(P, "tensor", i)
            for dc in range(8):
                P.op("tensor", lambda Eo, p=pQK[pi][:],
                     w=wqkv[:, dc * 192 + 64:dc * 192 + 128],
                     x=xT[pi][:, dc * 128:(dc + 1) * 128], st=(dc == 0), sp=(dc == 7):
                     Eo.matmul(p, w, x, start=st, stop=sp),
                     inc=(sPE if dc == 7 else None))
            P.wait("scalar", sPE)
            P.op("scalar", lambda Eo, o=kT[0:64, kc * 128:(kc + 1) * 128], p=pQK[pi][:]:
                 Eo.activation(o, p, AF.Identity, bias=qkb[0:64, 1:2], scale=f32(1.0)),
                 inc=sAC)
            rQK.release(i, sAC)
            rV.acquire(P, "tensor", i)
            for dc in range(8):
                P.op("tensor", lambda Eo, p=pV[pi][:],
                     x=xT[pi][:, dc * 128:(dc + 1) * 128],
                     w=wqkv[:, dc * 192 + 128:dc * 192 + 192], st=(dc == 0):
                     Eo.matmul(p, x, w, start=st, stop=False))
            P.op("tensor", lambda Eo, p=pV[pi][:]:
                 Eo.matmul(p, onesb[:], bvr[:], start=False, stop=True), inc=sPE)
            P.wait("vector", sPE)
            P.wait("vector", sPL, 2)
            P.op("vector", lambda Eo, o=vtm[:, kc * 65:kc * 65 + HS], p=pV[pi][:]:
                 Eo.tensor_copy(o, p), inc=sDV)
            rV.release(i, sDV)
        rXB.rel[pi] = [(sPE, transposes_done)]
        rXT.rel[pi] = [(sPE, sPE.n)]

    for i in range(NCH):
        chunk_setup(i, d_xown[i * 128:(i + 1) * 128, :], True, None)
    for kc in range(KCH):
        chunk_setup(NCH + kc, d_xbat[kc * 128:(kc + 1) * 128, :], False, kc)

    P.wait("sync", sTAP)
    barrier(P)
    with nc.Block("b1") as blk:
        P.run(blk)
    es1.close()

    es2 = ExitStack()
    sb2 = lambda name, shape, dt: es2.enter_context(nc.sbuf_tensor(name, shape, dt))
    ps2 = lambda name, shape, dt: es2.enter_context(nc.psum_tensor(name, shape, dt))
    PT4 = [sb2("PT40", [128, 512], BF16), sb2("PT41", [128, 512], BF16)]
    ots = sb2("ots", [65, 128], F32)
    mtab = [sb2("mtab0", [128, NCH], F32), sb2("mtab1", [128, NCH], F32)]
    mfin = sb2("mfin", [128, 1], F32)
    foldv = [sb2("foldv0", [128, 1], F32), sb2("foldv1", [128, 1], F32)]
    dtmp = sb2("dtmp", [128, 1], F32)
    rcp = sb2("rcp", [128, 1], F32)
    pSA = [ps2("pSA0", [128, 512], F32), ps2("pSA1", [128, 512], F32)]
    pSk = [ps2("pSk0", [128, 512], F32), ps2("pSk1", [128, 512], F32)]
    pO = ps2("pO", [65, 128], F32)
    pOt = ps2("pOt", [128, 65], F32)

    P = Prog()
    for _eng in ENGINES:
        P.wait(_eng, sC1, ld_consts)
        P.wait(_eng, sC2, wl_consts)
    rSA = Rot(2)
    rSK = Rot(2)
    rPT4 = Rot(2)
    rPO = Rot(1)
    rMT = Rot(2)
    fold_done = {}

    def pass_a(j):
        par2 = j % 2
        rMT.acquire(P, "vector", j)
        for kb in range(j + 1):
            par = kb % 2
            rSA.acquire(P, "tensor", kb)
            P.op("tensor", lambda Eo, p=pSA[par][:],
                 q=qT[0:64, j * 128:(j + 1) * 128],
                 k=kT[0:64, kb * 512:(kb + 1) * 512]:
                 Eo.matmul(p, q, k, start=True, stop=True), inc=sPE)
            P.wait("vector", sPE)
            if kb == j:
                P.eop("vector", lambda Eo, p=pSA[par][:]:
                      Eo.tensor_add(p, p, maskadd[:]), sDV)
            P.op("vector", lambda Eo, o=mtab[par2][:, kb:kb + 1], p=pSA[par][:]:
                 Eo.reduce_max(o, p, axis=AX.X), inc=sDV)
            rSA.release(kb, sDV)
        P.wait("vector", sDV)
        P.eop("vector", lambda Eo, mt=mtab[par2][:, 0:j + 1]:
              Eo.reduce_max(mfin[:], mt, axis=AX.X), sDV)
        dFD.wait(P, "vector", j)
        P.eop("vector", lambda Eo, fv=foldv[j % 2][:]:
              Eo.tensor_scalar(fv, mfin[:], c_neg1, None, op0=ALU.mult), sDV)
        dFD.wait(P, "gpsimd", j)
        P.wait("gpsimd", sDV)
        dFD.load(P, "gpsimd", j, lambda Eo, o=qT[64:65, j * 128:(j + 1) * 128],
                 fv=foldv[j % 2][:]: Eo.dma_start(out=o, in_=fv))
        fold_done[j] = None

    def pass_b(j):
        par2 = j % 2
        ngrp = j + 1
        dFD.wait(P, "tensor", j)

        def b_mms(kg):
            par = kg % 2
            rSK.acquire(P, "tensor", kg)
            for t in range(4):
                kk = 4 * kg + t
                P.op("tensor", lambda Eo, p=pSk[par][:, t * 128:(t + 1) * 128],
                     k=kT[:, kk * 128:(kk + 1) * 128],
                     q=qT[:, j * 128:(j + 1) * 128]:
                     Eo.matmul(p, k, q, start=True, stop=True),
                     inc=(sPE if t == 3 else None))
            return sPE.n

        mm_done = {0: b_mms(0)}
        for kg in range(ngrp):
            par = kg % 2
            if kg + 1 < ngrp:
                mm_done[kg + 1] = b_mms(kg + 1)
            P.wait("scalar", sPE, mm_done[kg])
            rPT4.acquire(P, "scalar", kg)
            P.op("scalar", lambda Eo, o=PT4[par][:], p=pSk[par][:]:
                 Eo.activation(o, p, AF.Exp, scale=f32(1.0 / 32.0)), inc=sAC)
            rSK.release(kg, sAC)
            if kg == j:
                P.wait("vector", sAC)
                for t in range(4):
                    P.op("vector", lambda Eo, t=t, o=PT4[par][:, t * 128:(t + 1) * 128],
                         c=condt[:, t * 128:(t + 1) * 128]:
                         Eo.tensor_tensor(o, o, c, op=ALU.mult),
                         inc=(sDV if t == 3 else None))
                P.wait("tensor", sDV)
            else:
                P.wait("tensor", sAC)
            if kg == 0:
                rPO.acquire(P, "tensor", 0)
            for t in range(4):
                kk = 4 * kg + t
                P.op("tensor", lambda Eo, v=vtm[:, kk * 65:(kk + 1) * 65],
                     pp=PT4[par][:, t * 128:(t + 1) * 128],
                     st=(kg == 0 and t == 0), sp=(kg == j and t == 3):
                     Eo.matmul(pO[:], v, pp, start=st, stop=sp),
                     inc=(sPE if t == 3 else None))
            rPT4.release(kg, sPE)
        P.wait("scalar", sPE)
        P.op("scalar", lambda Eo: Eo.activation(ots[:], pO[:], AF.Copy), inc=sAC)
        rPO.rel[0] = [(sAC, sAC.n)]
        P.wait("tensor", sAC)
        P.op("tensor", lambda Eo: Eo.transpose(pOt[:], ots[:], idf[0:65, 0:65]), inc=sPE)
        P.wait("vector", sPE)
        P.eop("vector", lambda Eo: Eo.tensor_copy(dtmp[:], pOt[:, 64:65]), sDV)
        P.eop("vector", lambda Eo: Eo.reciprocal(rcp[:], dtmp[:]), sDV)
        P.op("vector", lambda Eo, o=o_tm[:, j * HS:(j + 1) * HS]:
             Eo.tensor_scalar(o, pOt[:, 0:HS], rcp[:, 0:1], None, op0=ALU.mult), inc=sDV)
        P.wait("tensor", sDV)
        P.wait("scalar", sDV)

    for j in range(NCH):
        pass_a(j)
        if j > 0:
            pass_b(j - 1)
    pass_b(NCH - 1)

    if "o_tm" in taps:
        P.wait("sync", sDV)
        P.op("sync", lambda Eo: Eo.dma_start(out=taps["o_tm"], in_=o_tm[:]),
             inc=sTAP, inc_n=16)
    if "condt" in taps:
        P.wait("gpsimd", sDV)
        P.op("gpsimd", lambda Eo: Eo.dma_start(out=taps["condt"], in_=condt[:]),
             inc=sTAP, inc_n=16)
    if "vtm" in taps:
        P.wait("gpsimd", sDV)
        P.op("gpsimd", lambda Eo: Eo.dma_start(out=taps["vtm"], in_=vtm[:]),
             inc=sTAP, inc_n=16)
    if "qTe" in taps:
        P.wait("gpsimd", sDV)
        P.op("gpsimd", lambda Eo: Eo.dma_start(out=taps["qTe"], in_=qT[:]),
             inc=sTAP, inc_n=16)
    if "qT" in taps:
        P.wait("gpsimd", sDV)
        P.op("gpsimd", lambda Eo: Eo.dma_start(out=taps["qT"], in_=qT[:]),
             inc=sTAP, inc_n=16)
    if "kT" in taps:
        P.wait("gpsimd", sDV)
        P.op("gpsimd", lambda Eo: Eo.dma_start(out=taps["kT"], in_=kT[:]),
             inc=sTAP, inc_n=16)
    if "vtm" in taps:
        P.wait("gpsimd", sDV)
        P.op("gpsimd", lambda Eo: Eo.dma_start(out=taps["vtm"], in_=vtm[:]),
             inc=sTAP, inc_n=16)
    P.wait("sync", sTAP)
    P.wait("gpsimd", sTAP)
    barrier(P)
    with nc.Block("b2") as blk:
        P.run(blk)
    es2.close()

    es3 = ExitStack()
    sb3 = lambda name, shape, dt: es3.enter_context(nc.sbuf_tensor(name, shape, dt))
    ps3 = lambda name, shape, dt: es3.enter_context(nc.psum_tensor(name, shape, dt))
    zch = [sb3("zch%d" % i, [128, C], F32) for i in range(3)]
    sqb = [sb3("sqb%d" % i, [128, C], F32) for i in range(3)]
    tmp4 = [sb3("tmp4%d" % i, [128, 512], F32) for i in range(3)]
    logits = sb3("logits", [E, TOK], F32)
    gtm = sb3("gtm", [128, 8], F32)
    gates_tm = sb3("gates_tm", [128, 128], F32)
    muv = [sb3("mu%d" % i, [128, 1], F32) for i in range(3)]
    varv_ = [sb3("varv%d" % i, [128, 1], F32) for i in range(3)]
    lnvv_ = [sb3("lnvv%d" % i, [128, 1], F32) for i in range(3)]
    rstdv = [sb3("rstd%d" % i, [128, 1], F32) for i in range(3)]
    eq1 = sb3("eq1", [128, 8], F32)
    eq2 = sb3("eq2", [128, 8], F32)
    mskl = sb3("mskl", [128, 8], F32)
    m1v = sb3("m1v", [128, 1], F32)
    m2v = sb3("m2v", [128, 1], F32)
    edv = sb3("edv", [128, 1], F32)
    edv2 = sb3("edv2", [128, 1], F32)
    s1v = sb3("s1v", [128, 1], F32)
    rv = sb3("rv", [128, 1], F32)
    w2v = sb3("w2v", [128, 1], F32)
    g1v = sb3("g1v", [128, 8], F32)
    pTf = [ps3("pTf%d" % i, [128, 512], F32) for i in range(3)]
    pLr = [ps3("pLr0", [E, 128], F32), ps3("pLr1", [E, 128], F32)]
    pG8 = ps3("pG8", [128, 16], F32)
    pTg = ps3("pTg", [E, 256], F32)

    P = Prog()
    for _eng in ENGINES:
        P.wait(_eng, sC1, ld_consts)
        P.wait(_eng, sC2, wl_consts)
    rZ = Rot(3)
    rSQ = Rot(3)
    rMU = Rot(3)
    rRS = Rot(3)
    rTF = Rot(3)
    rT4 = Rot(3)
    rPL = Rot(2)

    for tc in range(NCH):
        par = tc % 3
        z = zch[par]
        rZ.acquire(P, "sync", tc)
        dZC.load(P, "sync", tc, lambda Eo, o=z[:], s=d_xown[tc * 128:(tc + 1) * 128, :]:
                 Eo.dma_start(out=o, in_=s))
        dZC.wait(P, "vector", tc)
        orep = o_tm[:, tc * HS:(tc + 1) * HS].unsqueeze(1).broadcast_to((128, 16, HS))
        P.eop("vector", lambda Eo, z_=z[:], r=orep:
              Eo.tensor_add(z_.rearrange("p (r h) -> p r h", r=16), z_.rearrange("p (r h) -> p r h", r=16), r), sDV)
        rMU.acquire(P, "vector", tc)
        mu = muv[par]
        P.eop("vector", lambda Eo, z_=z[:], m=mu[:]: Eo.reduce_sum(m, z_, axis=AX.X), sDV)
        P.eop("vector", lambda Eo, m=mu[:]:
              Eo.tensor_scalar(m, m, c_inv1024, None, op0=ALU.mult), sDV)
        P.wait("scalar", sDV)
        rSQ.acquire(P, "scalar", tc)
        P.op("scalar", lambda Eo, z_=z[:], m=mu[:], q=sqb[par][:]:
             Eo.activation(q, z_, AF.Square, bias=m, scale=f32(-1.0)), inc=sAC)
        P.wait("vector", sAC)
        varv = varv_[par]
        lnvv = lnvv_[par]
        P.eop("vector", lambda Eo, vv=varv[:], q=sqb[par][:]: Eo.reduce_sum(vv, q, axis=AX.X), sDV)
        P.op("vector", lambda Eo, vv=varv[:]:
             Eo.tensor_scalar(vv, vv, c_inv1024, None, op0=ALU.mult), inc=sDV)
        rSQ.rel[par] = [(sDV, sDV.n)]
        P.wait("scalar", sDV)
        rRS.acquire(P, "scalar", tc)
        rstd = rstdv[par]
        P.eop("scalar", lambda Eo, lv=lnvv[:], vv=varv[:]: Eo.activation(lv, vv, AF.Ln, bias=f32(EPS)), sAC)
        P.op("scalar", lambda Eo, r=rstd[:], lv=lnvv[:]:
             Eo.activation(r, lv, AF.Exp, scale=f32(-0.5)), inc=sAC)
        P.wait("vector", sAC)
        P.op("vector", lambda Eo, z_=z[:], m=mu[:], r=rstd[:]:
             Eo.tensor_scalar(z_, z_, m, r, op0=ALU.subtract, op1=ALU.mult), inc=sDV)
        rMU.release(tc, sDV)
        rRS.release(tc, sDV)
        y1_done = sDV.n
        P.wait("sync", sDV, y1_done)
        dSPp.load(P, "sync", tc, lambda Eo, o=d_y1sp[tc * 128:(tc + 1) * 128, :], z_=z[:]:
                  Eo.dma_start(out=o, in_=z_))
        P.wait("tensor", sDV, y1_done)
        for g in range(2):
            tfp = pTf[(tc * 2 + g) % 3]
            rTF.acquire(P, "tensor", tc * 2 + g)
            for t in range(4):
                dc = g * 4 + t
                P.op("tensor", lambda Eo, o=tfp[:, t * 128:(t + 1) * 128],
                     a=z[:, dc * 128:(dc + 1) * 128]: Eo.transpose(o, a, idf[:]),
                     inc=(sPE if t == 3 else None))
            P.wait("vector", sPE)
            rT4.acquire(P, "vector", tc * 2 + g)
            tm4 = tmp4[(tc * 2 + g) % 3]
            P.op("vector", lambda Eo, o=tm4[:], a=tfp[:]: Eo.tensor_copy(o, a), inc=sDV)
            P.wait("scalar", sDV)
            y1T_view = y1T[:].rearrange("p (d t) -> p d t", d=8)[:, g * 4:(g + 1) * 4,
                                                               tc * 128:(tc + 1) * 128]
            P.op("scalar", lambda Eo, o=y1T_view, a=tfp[:].rearrange("p (d t) -> p d t", d=4):
                 Eo.activation(o, a, AF.Copy), inc=sAC)
            rTF.release(tc * 2 + g, sAC)
            rTF.rel[(tc * 2 + g) % 3].append((sDV, sDV.n))
            P.wait("tensor", sDV)
            if g == 0:
                rPL.acquire(P, "tensor", tc)
            for t in range(4):
                dc = g * 4 + t
                P.op("tensor", lambda Eo, p=pLr[tc % 2][:],
                     w=rw[:, dc * 8:(dc + 1) * 8],
                     x=tm4[:, t * 128:(t + 1) * 128], st=(dc == 0), sp=(dc == 7):
                     Eo.matmul(p, w, x, start=st, stop=sp),
                     inc=(sPE if t == 3 else None))
            rT4.release(tc * 2 + g, sPE)
        P.wait("scalar", sPE)
        P.op("scalar", lambda Eo, o=logits[:, tc * 128:(tc + 1) * 128],
             p=pLr[tc % 2][:]: Eo.activation(o, p, AF.Copy), inc=sAC)
        rPL.release(tc, sAC)
        rZ.release(tc, sPE)
        rZ.rel[par].append(dSPp.sem_count(tc))
    logits_done = sAC.n
    gate_dve_done = {}
    for tc in range(NCH):
        sl = tc % 2
        P.wait("tensor", sAC)
        if tc >= 2:
            P.wait("tensor", sDV, gate_dve_done[sl])
        P.op("tensor", lambda Eo, o=pG8[:, sl * 8:(sl + 1) * 8],
             a=logits[:, tc * 128:(tc + 1) * 128]:
             Eo.transpose(o, a, idf[0:8, 0:8]), inc=sPE)
        P.wait("vector", sPE)
        P.eop("vector", lambda Eo, p=pG8[:, sl * 8:(sl + 1) * 8]: Eo.tensor_copy(gtm[:], p), sDV)
        P.eop("vector", lambda Eo: Eo.reduce_max(m1v[:], gtm[:], axis=AX.X), sDV)
        P.eop("vector", lambda Eo: Eo.tensor_scalar(eq1[:], gtm[:], m1v[:, 0:1], None, op0=ALU.is_equal), sDV)
        P.eop("vector", lambda Eo: Eo.scalar_tensor_tensor(mskl[:], eq1[:], c_neg1e30, gtm[:],
                                                           op0=ALU.mult, op1=ALU.add), sDV)
        P.eop("vector", lambda Eo: Eo.reduce_max(m2v[:], mskl[:], axis=AX.X), sDV)
        P.eop("vector", lambda Eo: Eo.tensor_scalar(eq2[:], mskl[:], m2v[:, 0:1], None, op0=ALU.is_equal), sDV)
        P.op("vector", lambda Eo: Eo.tensor_tensor(edv2[:], m2v[:], m1v[:], op=ALU.subtract), inc=sDV)
        P.wait("scalar", sDV)
        P.op("scalar", lambda Eo: Eo.activation(edv[:], edv2[:], AF.Exp), inc=sAC)
        P.wait("vector", sAC)
        P.eop("vector", lambda Eo: Eo.tensor_scalar(s1v[:], edv[:], c_one, None, op0=ALU.add), sDV)
        P.eop("vector", lambda Eo: Eo.reciprocal(rv[:], s1v[:]), sDV)
        P.eop("vector", lambda Eo: Eo.tensor_tensor(w2v[:], edv[:], rv[:], op=ALU.mult), sDV)
        P.eop("vector", lambda Eo: Eo.tensor_scalar(g1v[:], eq1[:], rv[:, 0:1], None, op0=ALU.mult), sDV)
        P.op("vector", lambda Eo, o=gates_tm[:, tc * 8:(tc + 1) * 8]:
             Eo.scalar_tensor_tensor(o, eq2[:], w2v[:, 0:1], g1v[:], op0=ALU.mult, op1=ALU.add),
             inc=sDV)
        gate_dve_done[sl] = sDV.n
    P.wait("tensor", sDV)
    for tc in range(NCH):
        sl = tc % 2
        P.op("tensor", lambda Eo, o=pTg[:, sl * 128:(sl + 1) * 128],
             a=gates_tm[:, tc * 8:(tc + 1) * 8]:
             Eo.transpose(o, a, idf[:]), inc=sPE)
        P.wait("scalar", sPE)
        P.op("scalar", lambda Eo, o=gatesT[:, tc * 128:(tc + 1) * 128],
             p=pTg[:, sl * 128:(sl + 1) * 128]: Eo.activation(o, p, AF.Copy), inc=sAC)
        P.wait("tensor", sAC)

    if "logits" in taps:
        P.wait("sync", sAC, logits_done)
        P.op("sync", lambda Eo: Eo.dma_start(out=taps["logits"], in_=logits[:]),
             inc=sTAP, inc_n=16)
    if "gates" in taps:
        P.wait("sync", sDV)
        P.op("sync", lambda Eo: Eo.dma_start(out=taps["gates"], in_=gates_tm[:]),
             inc=sTAP, inc_n=16)
    if "y1T" in taps:
        P.wait("gpsimd", sAC)
        P.op("gpsimd", lambda Eo: Eo.dma_start(out=taps["y1T"], in_=y1T[:]),
             inc=sTAP, inc_n=16)
    if "gatesT" in taps:
        P.wait("gpsimd", sAC)
        P.op("gpsimd", lambda Eo: Eo.dma_start(out=taps["gatesT"], in_=gatesT[:]),
             inc=sTAP, inc_n=16)
    if "y1" in taps:
        for s_ in dSPp.sems:
            P.wait("sync", s_)
        P.op("sync", lambda Eo: Eo.dma_start(out=taps["y1"], in_=d_y1sp), inc=sTAP, inc_n=16)
    for s_ in dSPp.sems:
        P.wait("sync", s_)
    P.wait("sync", sTAP)
    barrier(P)
    with nc.Block("b3") as blk:
        P.run(blk)
    es3.close()
    esA.close()

    esO = ExitStack()
    O_acc = esO.enter_context(nc.sbuf_tensor("O_acc", [128, 8 * TOK], F32))
    es4 = ExitStack()
    sb4 = lambda name, shape, dt: es4.enter_context(nc.sbuf_tensor(name, shape, dt))
    ps4 = lambda name, shape, dt: es4.enter_context(nc.psum_tensor(name, shape, dt))
    H = sb4("H", [128, 32 * NT], BF16)
    w1b = [sb4("w1b0", [128, 8 * 256], BF16), sb4("w1b1", [128, 8 * 256], BF16)]
    w2b = [sb4("w2b%d" % i, [128, 16 * 128], BF16) for i in range(4)]
    gbb = [sb4("gbb0", [128, NT], BF16), sb4("gbb1", [128, NT], BF16)]
    pH = [ps4("pH0", [128, 1024], F32), ps4("pH1", [128, 1024], F32)]
    pO2 = [ps4("pO20", [128, 512], F32), ps4("pO21", [128, 512], F32)]
    pGB = ps4("pGB", [128, NT], F32)

    P = Prog()
    for _eng in ENGINES:
        P.wait(_eng, sC1, ld_consts)
        P.wait(_eng, sC2, wl_consts)
    rW1 = Rot(2)
    rW2 = Rot(4)
    rGB = Rot(2)
    rPH = Rot(2)
    rPO2 = Rot(2)
    rGBpsum = Rot(1)

    w1_view = d_w1
    w2_view = d_w2
    ei = 0
    for nt in range(2):
        for e in range(E):
            gpar = ei % 2
            rGBpsum.acquire(P, "tensor", 0)
            for s in range(2):
                P.op("tensor", lambda Eo, p=pGB[:, s * 512:(s + 1) * 512],
                     sel=selb[:, e * 128:(e + 1) * 128],
                     g=gatesT[:, nt * NT + s * 512:nt * NT + (s + 1) * 512]:
                     Eo.matmul(p, sel, g, start=True, stop=True),
                     inc=(sPE if s == 1 else None))
            P.wait("vector", sPE)
            rGB.acquire(P, "vector", ei)
            P.op("vector", lambda Eo, o=gbb[gpar][:]: Eo.tensor_copy(o, pGB[:]), inc=sDV)
            rGBpsum.rel[0] = [(sDV, sDV.n)]
            gb_ready = sDV.n
            def issue_w2(k, e=e, ei=ei):
                gk = ei * 16 + k
                dc, fh = k // 2, k % 2
                w2i = gk % 4
                rW2.acquire(P, "gpsimd", gk)
                srcw = w2_view[e, fh * 2048:(fh + 1) * 2048,
                               dc * 128:(dc + 1) * 128].rearrange(
                    "(fc p) d -> p fc d", p=128)
                dW2.load(P, "gpsimd", gk,
                         lambda Eo, o=w2b[w2i][:].rearrange("p (fc d) -> p fc d", fc=16),
                         s=srcw: Eo.dma_start(out=o, in_=s))

            gate_done = {}
            for fp in range(16):
                wpar = (ei * 16 + fp) % 2
                rW1.acquire(P, "gpsimd", ei * 16 + fp)
                src = w1_view[e, :, fp * 256:(fp + 1) * 256].rearrange(
                    "(kc p) f -> p kc f", p=128)
                dW1.load(P, "gpsimd", ei * 16 + fp,
                         lambda Eo, o=w1b[wpar][:].rearrange("p (kc f) -> p kc f", kc=8),
                         s=src: Eo.dma_start(out=o, in_=s))
                dW1.wait(P, "tensor", ei * 16 + fp)
                for fc in range(2):
                    fcg = fp * 2 + fc
                    hpar = fcg % 2
                    rPH.acquire(P, "tensor", fcg)
                    for nb in range(2):
                        for kc in range(8):
                            P.op("tensor", lambda Eo,
                                 p=pH[hpar][:, nb * 512:(nb + 1) * 512],
                                 w=w1b[wpar][:, kc * 256 + fc * 128:kc * 256 + (fc + 1) * 128],
                                 x=y1T[:, kc * TOK + nt * NT + nb * 512:
                                       kc * TOK + nt * NT + (nb + 1) * 512],
                                 st=(kc == 0), sp=(kc == 7):
                                 Eo.matmul(p, w, x, start=st, stop=sp),
                                 inc=(sPE if (nb == 1 and kc == 7) else None))
                    P.wait("scalar", sPE)
                    hsl = H[:, fcg * NT:(fcg + 1) * NT]
                    P.op("scalar", lambda Eo, o=hsl, p=pH[hpar][:],
                         b=b1t[:, e * 32 + fcg:e * 32 + fcg + 1]:
                         Eo.activation(o, p, AF.Gelu_apprx_tanh, bias=b, scale=f32(1.0)),
                         inc=sAC)
                    rPH.release(fcg, sAC)
                    P.wait("vector", sAC)
                    P.wait("vector", sDV, gb_ready)
                    P.op("vector", lambda Eo, h=hsl, g=gbb[gpar][:]:
                         Eo.tensor_tensor(h, h, g, op=ALU.mult), inc=sDV)
                    gate_done[fcg] = sDV.n
                rW1.release(ei * 16 + fp, sPE)
                if fp >= 12:
                    issue_w2(fp - 12)
            h_gated = sDV.n
            rGB.release(ei, sDV)
            for dc in range(8):
                if dc == 1:
                    P.wait("tensor", sDV, h_gated)
                for fh in range(2):
                    k = dc * 2 + fh
                    w2i = (ei * 16 + k) % 4
                    dW2.wait(P, "tensor", ei * 16 + k)
                    for nb in range(2):
                        opar = nb
                        if fh == 0:
                            rPO2.acquire(P, "tensor", ei * 16 + dc * 2 + nb)
                        for fcL in range(16):
                            fcg = fh * 16 + fcL
                            if e == 0 and fh == 0 and fcL == 0:
                                P.op("tensor", lambda Eo, p=pO2[opar][:],
                                     b=b2b[:, dc * 128:(dc + 1) * 128],
                                     g=gatesT[:, nt * NT + nb * 512:nt * NT + (nb + 1) * 512]:
                                     Eo.matmul(p, b, g, start=True, stop=False))
                            if dc == 0:
                                P.wait("tensor", sDV, gate_done[fcg])
                            P.op("tensor", lambda Eo, p=pO2[opar][:],
                                 w=w2b[w2i][:, fcL * 128:(fcL + 1) * 128],
                                 x=H[:, fcg * NT + nb * 512:fcg * NT + (nb + 1) * 512],
                                 st=(fh == 0 and fcL == 0 and e != 0),
                                 sp=(fh == 1 and fcL == 15):
                                 Eo.matmul(p, w, x, start=st, stop=sp),
                                 inc=(sPE if fcL == 15 else None))
                        if fh == 1:
                            P.wait("vector", sPE)
                            osl = O_acc[:, dc * TOK + nt * NT + nb * 512:
                                        dc * TOK + nt * NT + (nb + 1) * 512]
                            if e == 0:
                                P.op("vector", lambda Eo, o=osl, p=pO2[opar][:]:
                                     Eo.tensor_copy(o, p), inc=sDV)
                            else:
                                P.op("vector", lambda Eo, o=osl, p=pO2[opar][:]:
                                     Eo.tensor_add(o, o, p), inc=sDV)
                            rPO2.release(ei * 16 + dc * 2 + nb, sDV)
                    rW2.release(ei * 16 + k, sPE)
                    if k + 4 < 16:
                        issue_w2(k + 4)
            ei += 1

    if "O_acc" in taps:
        P.wait("sync", sDV)
        P.op("sync", lambda Eo: Eo.dma_start(out=taps["O_acc"], in_=O_acc[:]),
             inc=sTAP, inc_n=16)
    P.wait("sync", sTAP)
    barrier(P)
    with nc.Block("b4") as blk:
        P.run(blk)
    es4.close()

    es5 = ExitStack()
    sb5 = lambda name, shape, dt: es5.enter_context(nc.sbuf_tensor(name, shape, dt))
    ps5 = lambda name, shape, dt: es5.enter_context(nc.psum_tensor(name, shape, dt))
    y1ch = [sb5("y1ch%d" % i, [128, C], F32) for i in range(3)]
    z2ch = [sb5("z2ch%d" % i, [128, C], F32) for i in range(3)]
    sq2 = [sb5("sq2%d" % i, [128, C], F32) for i in range(3)]
    mu2 = [sb5("mu2%d" % i, [128, 1], F32) for i in range(3)]
    var2_ = [sb5("var2%d" % i, [128, 1], F32) for i in range(3)]
    lnv2_ = [sb5("lnv2%d" % i, [128, 1], F32) for i in range(3)]
    rstd2 = [sb5("rstd2%d" % i, [128, 1], F32) for i in range(3)]
    pTf2 = [ps5("pTf2%d" % i, [128, 512], F32) for i in range(3)]

    P = Prog()
    for _eng in ENGINES:
        P.wait(_eng, sC1, ld_consts)
        P.wait(_eng, sC2, wl_consts)
    rY1 = Rot(3)
    rZ2 = Rot(3)
    rSQ2 = Rot(3)
    rMU2 = Rot(3)
    rRS2 = Rot(3)
    rTF2 = Rot(3)
    for s_ in dSPp.sems:
        P.wait("sync", s_)
    for tc in range(NCH):
        par = tc % 3
        rY1.acquire(P, "sync", tc)
        dYC.load(P, "sync", tc, lambda Eo, o=y1ch[par][:], s=d_y1sp[tc * 128:(tc + 1) * 128, :]:
                 Eo.dma_start(out=o, in_=s))
        for g in range(2):
            tfp = pTf2[(tc * 2 + g) % 3]
            rTF2.acquire(P, "tensor", tc * 2 + g)
            for t in range(4):
                dc = g * 4 + t
                P.op("tensor", lambda Eo, o=tfp[:, t * 128:(t + 1) * 128],
                     a=O_acc[:, dc * TOK + tc * 128:dc * TOK + (tc + 1) * 128]:
                     Eo.transpose(o, a, idf[:]), inc=(sPE if t == 3 else None))
            P.wait("vector", sPE)
            dYC.wait(P, "vector", tc)
            rZ2.acquire(P, "vector", tc) if g == 0 else None
            P.op("vector", lambda Eo, o=z2ch[par][:, g * 512:(g + 1) * 512],
                 a=tfp[:], b=y1ch[par][:, g * 512:(g + 1) * 512]:
                 Eo.tensor_add(o, a, b), inc=sDV)
            rTF2.release(tc * 2 + g, sDV)
        rY1.release(tc, sDV)
        z2 = z2ch[par]
        rMU2.acquire(P, "vector", tc)
        mu = mu2[par]
        P.wait("vector", sDV)
        P.eop("vector", lambda Eo, m=mu[:], z_=z2[:]: Eo.reduce_sum(m, z_, axis=AX.X), sDV)
        P.eop("vector", lambda Eo, m=mu[:]:
              Eo.tensor_scalar(m, m, c_inv1024, None, op0=ALU.mult), sDV)
        P.wait("scalar", sDV)
        rSQ2.acquire(P, "scalar", tc)
        P.op("scalar", lambda Eo, q=sq2[par][:], z_=z2[:], m=mu[:]:
             Eo.activation(q, z_, AF.Square, bias=m, scale=f32(-1.0)), inc=sAC)
        P.wait("vector", sAC)
        var2 = var2_[par]
        lnv2 = lnv2_[par]
        P.eop("vector", lambda Eo, vv=var2[:], q=sq2[par][:]: Eo.reduce_sum(vv, q, axis=AX.X), sDV)
        P.op("vector", lambda Eo, vv=var2[:]:
             Eo.tensor_scalar(vv, vv, c_inv1024, None, op0=ALU.mult), inc=sDV)
        rSQ2.rel[par] = [(sDV, sDV.n)]
        P.wait("scalar", sDV)
        rRS2.acquire(P, "scalar", tc)
        P.eop("scalar", lambda Eo, lv=lnv2[:], vv=var2[:]: Eo.activation(lv, vv, AF.Ln, bias=f32(EPS)), sAC)
        P.op("scalar", lambda Eo, r=rstd2[par][:], lv=lnv2[:]:
             Eo.activation(r, lv, AF.Exp, scale=f32(-0.5)), inc=sAC)
        P.wait("vector", sAC)
        P.op("vector", lambda Eo, z_=z2[:], m=mu[:], r=rstd2[par][:]:
             Eo.tensor_scalar(z_, z_, m, r, op0=ALU.subtract, op1=ALU.mult), inc=sDV)
        rMU2.release(tc, sDV)
        rRS2.release(tc, sDV)
        P.wait("sync", sDV)
        dSTp.load(P, "sync", tc, lambda Eo, o=d_y2[tc * 128:(tc + 1) * 128, :], z_=z2[:]:
                  Eo.dma_start(out=o, in_=z_))
        rZ2.rel[par] = [dSTp.sem_count(tc)]
    for s_ in dSTp.sems:
        P.wait("sync", s_)
    P.wait("sync", sTAP)
    with nc.Block("b5") as blk:
        P.run(blk)
    es5.close()
    esO.close()
    esP.close()
    esS.close()
    return nc


_CACHE = {}


def kernel(L0_attn_x, L0_attn_Wq, L0_attn_bq, L0_attn_Wk, L0_attn_bk,
           L0_attn_Wv, L0_attn_bv, L0_ffn_router_w, L0_ffn_w1, L0_ffn_b1,
           L0_ffn_w2, L0_ffn_b2):
    x = np.ascontiguousarray(np.asarray(L0_attn_x, dtype=np.float32))
    Wq = np.asarray(L0_attn_Wq, np.float32)
    Wk = np.asarray(L0_attn_Wk, np.float32)
    Wv = np.asarray(L0_attn_Wv, np.float32)
    bq = np.asarray(L0_attn_bq, np.float32)
    bk = np.asarray(L0_attn_bk, np.float32)
    bv = np.asarray(L0_attn_bv, np.float32)
    rw_in = np.asarray(L0_ffn_router_w, np.float32)
    w1 = np.ascontiguousarray(np.asarray(L0_ffn_w1, np.float32))
    b1 = np.asarray(L0_ffn_b1, np.float32)
    w2 = np.ascontiguousarray(np.asarray(L0_ffn_w2, np.float32))
    b2 = np.asarray(L0_ffn_b2, np.float32)

    wqkvT = np.concatenate([Wq.T, Wk.T, Wv.T], axis=1)
    wqkvT = np.ascontiguousarray(wqkvT.reshape(8, 128, 192).transpose(1, 0, 2))
    qkb = np.zeros((128, 2), np.float32)
    qkb[0:64, 0] = bq
    qkb[0:64, 1] = bk
    bvrow = bv.reshape(1, HS)
    b1t = np.ascontiguousarray(
        b1.reshape(E, 32, 128).transpose(2, 0, 1))
    rw_pre = np.ascontiguousarray(rw_in.reshape(8, 128, E).transpose(1, 0, 2))
    identb = np.eye(128, dtype=bf16)
    identf = np.eye(128, dtype=np.float32)
    qpiota = (np.arange(128, dtype=np.float32)[:, None]
              - np.arange(512, dtype=np.float32)[None, :])
    onesb = np.ones((1, 128), dtype=bf16)
    consts = np.zeros((128, 8), np.float32)
    consts[:, C_INV1024] = 1.0 / 1024.0
    consts[:, C_NEG132] = -1.0 / 32.0
    consts[:, C_ONE] = 1.0
    consts[:, C_NEG1E30] = -1e30
    consts[:, C_NEG1] = -1.0
    selb = np.zeros((E, E * 128), dtype=bf16)
    for e in range(E):
        selb[e, e * 128:(e + 1) * 128] = 1

    nc = build_nc()
    in_maps = []
    for c in range(NCORES):
        b = c // 4
        ph = c % 4
        chunks = [4 * j + ph for j in range(NCH)]
        x_b = x[b]
        x_own = np.ascontiguousarray(
            x_b.reshape(64, 128, C)[chunks].reshape(TOK, C))
        qoffb = np.full((128, 1), 128.0 * ph, np.float32)
        mth = np.zeros((128, 4), np.float32)
        for t in range(4):
            mth[:, t] = (t - ph) * 128.0
        in_maps.append(dict(
            x_own=x_own, x_batch=x_b, wqkvT=wqkvT, qkb=qkb, bvrow=bvrow,
            b1t=b1t, b2w=b2, rw=rw_pre, w1=w1, w2=w2, identb=identb,
            identf=identf, qpiota=qpiota, onesb=onesb, qoffb=qoffb,
            consts=consts, selb=selb, mth=mth))
    res = run_bass_kernel_spmd(nc, in_maps, core_ids=list(range(NCORES)))
    out = np.zeros((B, T, C), np.float32)
    for c in range(NCORES):
        b = c // 4
        ph = c % 4
        y2 = np.asarray(res.results[c]["y2"]).reshape(NCH, 128, C)
        for j in range(NCH):
            out[b, (4 * j + ph) * 128:(4 * j + ph + 1) * 128, :] = y2[j]
    return out


# version 08
# speedup vs baseline: 1.0867x; 1.0867x over previous
import numpy as np
import ml_dtypes
from contextlib import ExitStack

import concourse.bass as bass
import concourse.mybir as mybir
from concourse.bass_utils import run_bass_kernel_spmd

F32 = mybir.dt.float32
BF16 = mybir.dt.bfloat16
AF = mybir.ActivationFunctionType
ALU = mybir.AluOpType
AX = mybir.AxisListType
f32 = np.float32
bf16 = ml_dtypes.bfloat16

B, T, C = 2, 8192, 1024
HS = 64
E, FF = 8, 4096
NCORES = 8
TOK = 2048
NCH = 16
KCH = 64
NT = 1024
EPS = 1e-5

C_INV1024, C_NEG132, C_ONE, C_NEG1E30, C_ZERO, C_NEG1 = 0, 1, 2, 3, 4, 5

ENGINES = ("sync", "scalar", "vector", "tensor", "gpsimd")


class Sem:
    def __init__(self, nc, es, name):
        self.h = es.enter_context(nc.semaphore(name))
        self.n = 0

    def bump(self, k):
        self.n += k
        return self.n


class Prog:

    def __init__(self):
        self.streams = {e: [] for e in ENGINES}

    def op(self, eng, fn, inc=None, inc_n=1):
        if inc is not None:
            inc.bump(inc_n)
            self.streams[eng].append(
                lambda Eo, fn=fn, s=inc.h, n=inc_n: fn(Eo).then_inc(s, n))
        else:
            self.streams[eng].append(lambda Eo, fn=fn: fn(Eo))

    def eop(self, eng, fn, sem):
        self.op(eng, fn, inc=sem)
        self.wait(eng, sem)

    def wait(self, eng, sem, val=None):
        v = sem.n if val is None else val
        if v > 0:
            self.streams[eng].append(lambda Eo, s=sem.h, v=v: Eo.wait_ge(s, v))

    def run(self, block):
        for eng in ENGINES:
            fns = tuple(self.streams[eng])

            def body(Eo, fns=fns):
                for f in fns:
                    f(Eo)

            getattr(block, eng)(body)


class Rot:

    def __init__(self, n):
        self.n = n
        self.rel = [[] for _ in range(n)]

    def acquire(self, P, eng, i):
        for sem, cnt in self.rel[i % self.n]:
            P.wait(eng, sem, cnt)
        self.rel[i % self.n] = []

    def release(self, i, sem):
        self.rel[i % self.n].append((sem, sem.n))


class DmaSlots:

    def __init__(self, nc, es, name, n=2):
        self.sems = [Sem(nc, es, "%s%d" % (name, i)) for i in range(n)]
        self.n = n

    def load(self, P, eng, i, fn):
        P.op(eng, fn, inc=self.sems[i % self.n], inc_n=16)

    def wait(self, P, eng, i):
        P.wait(eng, self.sems[i % self.n])

    def sem_count(self, i):
        s = self.sems[i % self.n]
        return (s, s.n)


def build_nc(debug_taps=()):
    nc = bass.Bass("TRN2", target_bir_lowering=False)

    d_xown = nc.dram_tensor("x_own", [TOK, C], F32, kind="ExternalInput").ap()
    d_xbat = nc.dram_tensor("x_batch", [T, C], F32, kind="ExternalInput").ap()
    d_wqkv = nc.dram_tensor("wqkvT", [128, 8, 192], F32, kind="ExternalInput").ap()
    d_qkb = nc.dram_tensor("qkb", [128, 2], F32, kind="ExternalInput").ap()
    d_bvr = nc.dram_tensor("bvrow", [1, HS], F32, kind="ExternalInput").ap()
    d_b1t = nc.dram_tensor("b1t", [128, E, 32], F32, kind="ExternalInput").ap()
    d_b2w = nc.dram_tensor("b2w", [E, C], F32, kind="ExternalInput").ap()
    d_rw = nc.dram_tensor("rw", [128, 8, E], F32, kind="ExternalInput").ap()
    d_w1 = nc.dram_tensor("w1", [E, C, FF], F32, kind="ExternalInput").ap()
    d_w2 = nc.dram_tensor("w2", [E, FF, C], F32, kind="ExternalInput").ap()
    d_idb = nc.dram_tensor("identb", [128, 128], BF16, kind="ExternalInput").ap()
    d_idf = nc.dram_tensor("identf", [128, 128], F32, kind="ExternalInput").ap()
    d_qpi = nc.dram_tensor("qpiota", [128, 512], F32, kind="ExternalInput").ap()
    d_ones = nc.dram_tensor("onesb", [1, 128], BF16, kind="ExternalInput").ap()
    d_qoff = nc.dram_tensor("qoffb", [128, 1], F32, kind="ExternalInput").ap()
    d_cst = nc.dram_tensor("consts", [128, 8], F32, kind="ExternalInput").ap()
    d_sel = nc.dram_tensor("selb", [E, E * 128], BF16, kind="ExternalInput").ap()
    d_mth = nc.dram_tensor("mth", [128, 4], F32, kind="ExternalInput").ap()
    d_y1sp = nc.dram_tensor("y1sp", [TOK, C], F32, kind="Internal").ap()
    d_y2 = nc.dram_tensor("y2", [TOK, C], F32, kind="ExternalOutput").ap()
    taps = {}
    for name, shape in debug_taps:
        taps[name] = nc.dram_tensor("tap_" + name, shape, F32, kind="ExternalOutput").ap()

    esP = ExitStack()
    sb = lambda name, shape, dt: esP.enter_context(nc.sbuf_tensor(name, shape, dt))

    idb = sb("idb", [128, 128], BF16)
    idf = sb("idf", [128, 128], F32)
    onesb = sb("onesb_s", [1, 128], BF16)
    qoffb = sb("qoffb_s", [128, 1], F32)
    mth = sb("mth_s", [128, 4], F32)
    cst = sb("cst", [128, 8], F32)
    b1t = sb("b1t_s", [128, E * 32], F32)
    b2b = sb("b2b", [E, C], BF16)
    rw = sb("rw_s", [128, 8 * E], F32)
    y1T = sb("y1T", [128, 8 * TOK], BF16)
    gatesT = sb("gatesT", [E, TOK], BF16)
    selb = sb("selb_s", [E, E * 128], BF16)

    esA = ExitStack()
    sbA = lambda name, shape, dt: esA.enter_context(nc.sbuf_tensor(name, shape, dt))
    qpi = sbA("qpi", [128, 512], F32)
    maskadd = sbA("maskadd", [128, 512], F32)
    qT = sbA("qT", [65, TOK], BF16)
    kT = sbA("kT", [65, T], BF16)
    vtm = sbA("vtm", [128, KCH * 65], BF16)
    condt = sbA("condt", [128, 4 * 128], BF16)
    o_tm = sbA("o_tm", [128, NCH * HS], F32)
    mA = sbA("mA", [128, 1], F32)
    mB = sbA("mB", [128, 1], F32)
    dm = sbA("dm", [128, 1], F32)
    edm = sbA("edm", [128, 1], F32)
    denA = sbA("denA", [128, 1], F32)
    denB = sbA("denB", [128, 1], F32)
    negm32 = sbA("negm32", [128, 1], F32)
    lden = sbA("lden", [128, 1], F32)
    bias2 = sbA("bias2", [128, 1], F32)
    bsum = [sbA("bsum0", [128, 1], F32), sbA("bsum1", [128, 1], F32)]

    esS = ExitStack()
    mks = lambda name: Sem(nc, esS, name)
    sC1 = mks("sC1")
    sC2 = mks("sC2")
    sTAP = mks("sTAP")
    sPL = mks("sPL")
    sPE = mks("sPE")
    sAC = mks("sAC")
    sDV = mks("sDV")
    sBAR = mks("sBAR")
    dXB = DmaSlots(nc, esS, "dXB")
    dZC = DmaSlots(nc, esS, "dZC", n=3)
    dSPp = DmaSlots(nc, esS, "dSPp", n=3)
    dW1 = DmaSlots(nc, esS, "dW1")
    dW2 = DmaSlots(nc, esS, "dW2", n=4)
    dYC = DmaSlots(nc, esS, "dYC", n=3)
    dSTp = DmaSlots(nc, esS, "dSTp", n=3)
    dFD = DmaSlots(nc, esS, "dFD")

    c_inv1024 = cst[:, C_INV1024:C_INV1024 + 1]
    c_neg132 = cst[:, C_NEG132:C_NEG132 + 1]
    c_one = cst[:, C_ONE:C_ONE + 1]
    c_neg1e30 = cst[:, C_NEG1E30:C_NEG1E30 + 1]
    c_zero = cst[:, C_ZERO:C_ZERO + 1]
    c_neg1 = cst[:, C_NEG1:C_NEG1 + 1]

    barrier_count = [0]

    def barrier(P):
        barrier_count[0] += 1
        for eng in ENGINES:
            P.op(eng, lambda Eo: Eo.nop(), inc=sBAR)
        for eng in ENGINES:
            P.wait(eng, sBAR, 5 * barrier_count[0])

    es1 = ExitStack()
    sb1 = lambda name, shape, dt: es1.enter_context(nc.sbuf_tensor(name, shape, dt))
    ps1 = lambda name, shape, dt: es1.enter_context(nc.psum_tensor(name, shape, dt))
    xb = [sb1("xb0", [128, C], F32), sb1("xb1", [128, C], F32)]
    xT = [sb1("xT0", [128, C], BF16), sb1("xT1", [128, C], BF16)]
    wqkv = sb1("wqkv", [128, 8 * 192], BF16)
    bvr = sb1("bvr", [1, HS], BF16)
    qkb = sb1("qkb_s", [128, 2], F32)
    pXT = [ps1("pXT0", [128, C], F32), ps1("pXT1", [128, C], F32)]
    pQK = [ps1("pQK0", [64, 128], F32), ps1("pQK1", [64, 128], F32)]
    pV = [ps1("pV0", [128, HS], F32), ps1("pV1", [128, HS], F32)]

    P = Prog()
    for dst, src in [(idb, d_idb), (idf, d_idf), (qpi, d_qpi), (qoffb, d_qoff),
                     (mth, d_mth),
                     (cst, d_cst), (qkb[:], d_qkb),
                     (b1t, d_b1t.rearrange("p e k -> p (e k)")),
                     (rw, d_rw.rearrange("p c e -> p (c e)"))]:
        ap = dst[:] if hasattr(dst, "ap") else dst
        P.op("sync", lambda Eo, a=ap, s=src: Eo.dma_start(out=a, in_=s), inc=sC1, inc_n=16)
    for dst, src in [(onesb[:], d_ones), (selb[:], d_sel),
                     (wqkv[:], d_wqkv.rearrange("p c h -> p (c h)")),
                     (bvr[:], d_bvr), (b2b[:], d_b2w)]:
        P.op("gpsimd", lambda Eo, a=dst, s=src: Eo.dma_start(out=a, in_=s), inc=sC2, inc_n=16)
    ld_consts = sC1.n
    wl_consts = sC2.n
    for _eng in ("tensor", "scalar", "vector", "gpsimd"):
        P.wait(_eng, sC1, ld_consts)
        P.wait(_eng, sC2, wl_consts)
    P.op("gpsimd", lambda Eo: Eo.memset(kT[64:65, :], 1.0), inc=sPL)
    P.op("gpsimd", lambda Eo: Eo.memset(vtm[:], 1.0), inc=sPL)
    P.wait("vector", sC1, ld_consts)
    P.eop("vector", lambda Eo: Eo.tensor_scalar(maskadd[0:128, 0:128], qpi[:, 0:128], c_neg1, None, op0=ALU.mult), sDV)
    for _t in range(4):
        P.eop("vector", lambda Eo, t=_t: Eo.tensor_scalar(
            condt[:, t * 128:(t + 1) * 128], maskadd[0:128, 0:128],
            mth[:, t:t + 1], None, op0=ALU.is_ge), sDV)
    P.eop("vector", lambda Eo: Eo.tensor_scalar(maskadd[:], qpi[:], qoffb[:, 0:1], None, op0=ALU.add), sDV)
    P.eop("vector", lambda Eo: Eo.tensor_scalar(maskadd[:], maskadd[:], c_zero, None, op0=ALU.is_lt), sDV)
    P.op("vector", lambda Eo: Eo.tensor_scalar(maskadd[:], maskadd[:], c_neg1e30, None, op0=ALU.mult), inc=sDV)

    rXB = Rot(2)
    rXT = Rot(2)
    rQK = Rot(2)
    rV = Rot(2)

    rPXT = Rot(2)

    def chunk_setup(i, src_ap, is_q, kc):
        pi = i % 2
        rXB.acquire(P, "sync", i)
        dXB.load(P, "sync", i, lambda Eo, a=xb[pi][:], s=src_ap: Eo.dma_start(out=a, in_=s))
        dXB.wait(P, "tensor", i)
        rPXT.acquire(P, "tensor", i)
        rXT_psum = pXT[pi]
        for dc in range(8):
            P.op("tensor", lambda Eo, o=rXT_psum[:, dc * 128:(dc + 1) * 128],
                 a=xb[pi][:, dc * 128:(dc + 1) * 128]: Eo.transpose(o, a, idf[:]),
                 inc=(sPE if dc == 7 else None))
        transposes_done = sPE.n
        P.wait("vector", sPE)
        rXT.acquire(P, "vector", i)
        P.op("vector", lambda Eo, o=xT[pi][:], a=rXT_psum[:]: Eo.tensor_copy(o, a), inc=sDV)
        rPXT.release(i, sDV)
        evac_cnt = sDV.n
        P.wait("tensor", sDV, evac_cnt)
        if is_q:
            rQK.acquire(P, "tensor", i)
            for dc in range(8):
                P.op("tensor", lambda Eo, p=pQK[pi][:],
                     w=wqkv[:, dc * 192:dc * 192 + 64],
                     x=xT[pi][:, dc * 128:(dc + 1) * 128], st=(dc == 0), sp=(dc == 7):
                     Eo.matmul(p, w, x, start=st, stop=sp),
                     inc=(sPE if dc == 7 else None))
            P.wait("scalar", sPE)
            P.op("scalar", lambda Eo, o=qT[0:64, i * 128:(i + 1) * 128], p=pQK[pi][:]:
                 Eo.activation(o, p, AF.Identity, bias=qkb[0:64, 0:1], scale=f32(1.0)),
                 inc=sAC)
            rQK.release(i, sAC)
        else:
            rQK.acquire(P, "tensor", i)
            for dc in range(8):
                P.op("tensor", lambda Eo, p=pQK[pi][:],
                     w=wqkv[:, dc * 192 + 64:dc * 192 + 128],
                     x=xT[pi][:, dc * 128:(dc + 1) * 128], st=(dc == 0), sp=(dc == 7):
                     Eo.matmul(p, w, x, start=st, stop=sp),
                     inc=(sPE if dc == 7 else None))
            P.wait("scalar", sPE)
            P.op("scalar", lambda Eo, o=kT[0:64, kc * 128:(kc + 1) * 128], p=pQK[pi][:]:
                 Eo.activation(o, p, AF.Identity, bias=qkb[0:64, 1:2], scale=f32(1.0)),
                 inc=sAC)
            rQK.release(i, sAC)
            rV.acquire(P, "tensor", i)
            for dc in range(8):
                P.op("tensor", lambda Eo, p=pV[pi][:],
                     x=xT[pi][:, dc * 128:(dc + 1) * 128],
                     w=wqkv[:, dc * 192 + 128:dc * 192 + 192], st=(dc == 0):
                     Eo.matmul(p, x, w, start=st, stop=False))
            P.op("tensor", lambda Eo, p=pV[pi][:]:
                 Eo.matmul(p, onesb[:], bvr[:], start=False, stop=True), inc=sPE)
            P.wait("vector", sPE)
            P.wait("vector", sPL, 2)
            P.op("vector", lambda Eo, o=vtm[:, kc * 65:kc * 65 + HS], p=pV[pi][:]:
                 Eo.tensor_copy(o, p), inc=sDV)
            rV.release(i, sDV)
        rXB.rel[pi] = [(sPE, transposes_done)]
        rXT.rel[pi] = [(sPE, sPE.n)]

    for i in range(NCH):
        chunk_setup(i, d_xown[i * 128:(i + 1) * 128, :], True, None)
    for kc in range(KCH):
        chunk_setup(NCH + kc, d_xbat[kc * 128:(kc + 1) * 128, :], False, kc)

    P.wait("sync", sTAP)
    barrier(P)
    with nc.Block("b1") as blk:
        P.run(blk)
    es1.close()

    es2 = ExitStack()
    sb2 = lambda name, shape, dt: es2.enter_context(nc.sbuf_tensor(name, shape, dt))
    ps2 = lambda name, shape, dt: es2.enter_context(nc.psum_tensor(name, shape, dt))
    PT4 = [sb2("PT40", [128, 512], BF16), sb2("PT41", [128, 512], BF16)]
    ots = sb2("ots", [65, 128], F32)
    mtab = [sb2("mtab0", [128, NCH], F32), sb2("mtab1", [128, NCH], F32)]
    mfin = sb2("mfin", [128, 1], F32)
    foldv = [sb2("foldv0", [128, 1], F32), sb2("foldv1", [128, 1], F32)]
    dtmp = sb2("dtmp", [128, 1], F32)
    rcp = sb2("rcp", [128, 1], F32)
    pSA = [ps2("pSA0", [128, 512], F32), ps2("pSA1", [128, 512], F32)]
    pSk = [ps2("pSk0", [128, 512], F32), ps2("pSk1", [128, 512], F32)]
    pO = ps2("pO", [65, 128], F32)
    pOt = ps2("pOt", [128, 65], F32)

    P = Prog()
    for _eng in ENGINES:
        P.wait(_eng, sC1, ld_consts)
        P.wait(_eng, sC2, wl_consts)
    rSA = Rot(2)
    rSK = Rot(2)
    rPT4 = Rot(2)
    rPO = Rot(1)
    rMT = Rot(2)
    fold_done = {}

    def pass_a(j):
        par2 = j % 2
        rMT.acquire(P, "vector", j)
        for kb in range(j + 1):
            par = kb % 2
            rSA.acquire(P, "tensor", kb)
            P.op("tensor", lambda Eo, p=pSA[par][:],
                 q=qT[0:64, j * 128:(j + 1) * 128],
                 k=kT[0:64, kb * 512:(kb + 1) * 512]:
                 Eo.matmul(p, q, k, start=True, stop=True), inc=sPE)
            P.wait("vector", sPE)
            if kb == j:
                P.eop("vector", lambda Eo, p=pSA[par][:]:
                      Eo.tensor_add(p, p, maskadd[:]), sDV)
            P.op("vector", lambda Eo, o=mtab[par2][:, kb:kb + 1], p=pSA[par][:]:
                 Eo.reduce_max(o, p, axis=AX.X), inc=sDV)
            rSA.release(kb, sDV)
        P.wait("vector", sDV)
        P.eop("vector", lambda Eo, mt=mtab[par2][:, 0:j + 1]:
              Eo.reduce_max(mfin[:], mt, axis=AX.X), sDV)
        dFD.wait(P, "vector", j)
        P.eop("vector", lambda Eo, fv=foldv[j % 2][:]:
              Eo.tensor_scalar(fv, mfin[:], c_neg1, None, op0=ALU.mult), sDV)
        dFD.wait(P, "gpsimd", j)
        P.wait("gpsimd", sDV)
        dFD.load(P, "gpsimd", j, lambda Eo, o=qT[64:65, j * 128:(j + 1) * 128],
                 fv=foldv[j % 2][:]: Eo.dma_start(out=o, in_=fv))
        fold_done[j] = None

    def pass_b(j):
        par2 = j % 2
        ngrp = j + 1
        dFD.wait(P, "tensor", j)

        def b_mms(kg):
            par = kg % 2
            rSK.acquire(P, "tensor", kg)
            for t in range(4):
                kk = 4 * kg + t
                P.op("tensor", lambda Eo, p=pSk[par][:, t * 128:(t + 1) * 128],
                     k=kT[:, kk * 128:(kk + 1) * 128],
                     q=qT[:, j * 128:(j + 1) * 128]:
                     Eo.matmul(p, k, q, start=True, stop=True),
                     inc=(sPE if t == 3 else None))
            return sPE.n

        mm_done = {0: b_mms(0)}
        for kg in range(ngrp):
            par = kg % 2
            if kg + 1 < ngrp:
                mm_done[kg + 1] = b_mms(kg + 1)
            P.wait("scalar", sPE, mm_done[kg])
            rPT4.acquire(P, "scalar", kg)
            P.op("scalar", lambda Eo, o=PT4[par][:], p=pSk[par][:]:
                 Eo.activation(o, p, AF.Exp, scale=f32(1.0 / 32.0)), inc=sAC)
            rSK.release(kg, sAC)
            if kg == j:
                P.wait("vector", sAC)
                for t in range(4):
                    P.op("vector", lambda Eo, t=t, o=PT4[par][:, t * 128:(t + 1) * 128],
                         c=condt[:, t * 128:(t + 1) * 128]:
                         Eo.tensor_tensor(o, o, c, op=ALU.mult),
                         inc=(sDV if t == 3 else None))
                P.wait("tensor", sDV)
            else:
                P.wait("tensor", sAC)
            if kg == 0:
                rPO.acquire(P, "tensor", 0)
            for t in range(4):
                kk = 4 * kg + t
                P.op("tensor", lambda Eo, v=vtm[:, kk * 65:(kk + 1) * 65],
                     pp=PT4[par][:, t * 128:(t + 1) * 128],
                     st=(kg == 0 and t == 0), sp=(kg == j and t == 3):
                     Eo.matmul(pO[:], v, pp, start=st, stop=sp),
                     inc=(sPE if t == 3 else None))
            rPT4.release(kg, sPE)
        P.wait("scalar", sPE)
        P.op("scalar", lambda Eo: Eo.activation(ots[:], pO[:], AF.Copy), inc=sAC)
        rPO.rel[0] = [(sAC, sAC.n)]
        P.wait("tensor", sAC)
        P.op("tensor", lambda Eo: Eo.transpose(pOt[:], ots[:], idf[0:65, 0:65]), inc=sPE)
        P.wait("vector", sPE)
        P.eop("vector", lambda Eo: Eo.tensor_copy(dtmp[:], pOt[:, 64:65]), sDV)
        P.eop("vector", lambda Eo: Eo.reciprocal(rcp[:], dtmp[:]), sDV)
        P.op("vector", lambda Eo, o=o_tm[:, j * HS:(j + 1) * HS]:
             Eo.tensor_scalar(o, pOt[:, 0:HS], rcp[:, 0:1], None, op0=ALU.mult), inc=sDV)
        P.wait("tensor", sDV)
        P.wait("scalar", sDV)

    for j in range(NCH):
        pass_a(j)
        if j > 0:
            pass_b(j - 1)
    pass_b(NCH - 1)

    if "o_tm" in taps:
        P.wait("sync", sDV)
        P.op("sync", lambda Eo: Eo.dma_start(out=taps["o_tm"], in_=o_tm[:]),
             inc=sTAP, inc_n=16)
    if "condt" in taps:
        P.wait("gpsimd", sDV)
        P.op("gpsimd", lambda Eo: Eo.dma_start(out=taps["condt"], in_=condt[:]),
             inc=sTAP, inc_n=16)
    if "vtm" in taps:
        P.wait("gpsimd", sDV)
        P.op("gpsimd", lambda Eo: Eo.dma_start(out=taps["vtm"], in_=vtm[:]),
             inc=sTAP, inc_n=16)
    if "qTe" in taps:
        P.wait("gpsimd", sDV)
        P.op("gpsimd", lambda Eo: Eo.dma_start(out=taps["qTe"], in_=qT[:]),
             inc=sTAP, inc_n=16)
    if "qT" in taps:
        P.wait("gpsimd", sDV)
        P.op("gpsimd", lambda Eo: Eo.dma_start(out=taps["qT"], in_=qT[:]),
             inc=sTAP, inc_n=16)
    if "kT" in taps:
        P.wait("gpsimd", sDV)
        P.op("gpsimd", lambda Eo: Eo.dma_start(out=taps["kT"], in_=kT[:]),
             inc=sTAP, inc_n=16)
    if "vtm" in taps:
        P.wait("gpsimd", sDV)
        P.op("gpsimd", lambda Eo: Eo.dma_start(out=taps["vtm"], in_=vtm[:]),
             inc=sTAP, inc_n=16)
    P.wait("sync", sTAP)
    P.wait("gpsimd", sTAP)
    barrier(P)
    with nc.Block("b2") as blk:
        P.run(blk)
    es2.close()

    es3 = ExitStack()
    sb3 = lambda name, shape, dt: es3.enter_context(nc.sbuf_tensor(name, shape, dt))
    ps3 = lambda name, shape, dt: es3.enter_context(nc.psum_tensor(name, shape, dt))
    zch = [sb3("zch%d" % i, [128, C], F32) for i in range(3)]
    sqb = [sb3("sqb%d" % i, [128, C], F32) for i in range(3)]
    tmp4 = [sb3("tmp4%d" % i, [128, 512], F32) for i in range(3)]
    logits = sb3("logits", [E, TOK], F32)
    gtm = sb3("gtm", [128, 8], F32)
    gates_tm = sb3("gates_tm", [128, 128], F32)
    muv = [sb3("mu%d" % i, [128, 1], F32) for i in range(3)]
    varv_ = [sb3("varv%d" % i, [128, 1], F32) for i in range(3)]
    lnvv_ = [sb3("lnvv%d" % i, [128, 1], F32) for i in range(3)]
    rstdv = [sb3("rstd%d" % i, [128, 1], F32) for i in range(3)]
    eq1 = sb3("eq1", [128, 8], F32)
    eq2 = sb3("eq2", [128, 8], F32)
    mskl = sb3("mskl", [128, 8], F32)
    m1v = sb3("m1v", [128, 1], F32)
    m2v = sb3("m2v", [128, 1], F32)
    edv = sb3("edv", [128, 1], F32)
    edv2 = sb3("edv2", [128, 1], F32)
    s1v = sb3("s1v", [128, 1], F32)
    rv = sb3("rv", [128, 1], F32)
    w2v = sb3("w2v", [128, 1], F32)
    g1v = sb3("g1v", [128, 8], F32)
    pTf = [ps3("pTf%d" % i, [128, 512], F32) for i in range(3)]
    pLr = [ps3("pLr0", [E, 128], F32), ps3("pLr1", [E, 128], F32)]
    pG8 = ps3("pG8", [128, 16], F32)
    pTg = ps3("pTg", [E, 256], F32)

    P = Prog()
    for _eng in ENGINES:
        P.wait(_eng, sC1, ld_consts)
        P.wait(_eng, sC2, wl_consts)
    rZ = Rot(3)
    rSQ = Rot(3)
    rMU = Rot(3)
    rRS = Rot(3)
    rTF = Rot(3)
    rT4 = Rot(3)
    rPL = Rot(2)

    for tc in range(NCH):
        par = tc % 3
        z = zch[par]
        rZ.acquire(P, "sync", tc)
        dZC.load(P, "sync", tc, lambda Eo, o=z[:], s=d_xown[tc * 128:(tc + 1) * 128, :]:
                 Eo.dma_start(out=o, in_=s))
        dZC.wait(P, "gpsimd", tc)
        orep = o_tm[:, tc * HS:(tc + 1) * HS].unsqueeze(1).broadcast_to((128, 16, HS))
        P.op("gpsimd", lambda Eo, z_=z[:], r=orep:
             Eo.tensor_add(z_.rearrange("p (r h) -> p r h", r=16), z_.rearrange("p (r h) -> p r h", r=16), r), inc=sPL)
        rMU.acquire(P, "vector", tc)
        mu = muv[par]
        P.wait("vector", sPL)
        P.eop("vector", lambda Eo, z_=z[:], m=mu[:]: Eo.reduce_sum(m, z_, axis=AX.X), sDV)
        P.eop("vector", lambda Eo, m=mu[:]:
              Eo.tensor_scalar(m, m, c_inv1024, None, op0=ALU.mult), sDV)
        P.wait("scalar", sDV)
        rSQ.acquire(P, "scalar", tc)
        P.op("scalar", lambda Eo, z_=z[:], m=mu[:], q=sqb[par][:]:
             Eo.activation(q, z_, AF.Square, bias=m, scale=f32(-1.0)), inc=sAC)
        P.wait("vector", sAC)
        varv = varv_[par]
        lnvv = lnvv_[par]
        P.eop("vector", lambda Eo, vv=varv[:], q=sqb[par][:]: Eo.reduce_sum(vv, q, axis=AX.X), sDV)
        P.op("vector", lambda Eo, vv=varv[:]:
             Eo.tensor_scalar(vv, vv, c_inv1024, None, op0=ALU.mult), inc=sDV)
        rSQ.rel[par] = [(sDV, sDV.n)]
        P.wait("scalar", sDV)
        rRS.acquire(P, "scalar", tc)
        rstd = rstdv[par]
        P.eop("scalar", lambda Eo, lv=lnvv[:], vv=varv[:]: Eo.activation(lv, vv, AF.Ln, bias=f32(EPS)), sAC)
        P.op("scalar", lambda Eo, r=rstd[:], lv=lnvv[:]:
             Eo.activation(r, lv, AF.Exp, scale=f32(-0.5)), inc=sAC)
        P.wait("vector", sAC)
        P.op("vector", lambda Eo, z_=z[:], m=mu[:], r=rstd[:]:
             Eo.tensor_scalar(z_, z_, m, r, op0=ALU.subtract, op1=ALU.mult), inc=sDV)
        rMU.release(tc, sDV)
        rRS.release(tc, sDV)
        y1_done = sDV.n
        P.wait("sync", sDV, y1_done)
        dSPp.load(P, "sync", tc, lambda Eo, o=d_y1sp[tc * 128:(tc + 1) * 128, :], z_=z[:]:
                  Eo.dma_start(out=o, in_=z_))
        P.wait("tensor", sDV, y1_done)
        for g in range(2):
            tfp = pTf[(tc * 2 + g) % 3]
            rTF.acquire(P, "tensor", tc * 2 + g)
            for t in range(4):
                dc = g * 4 + t
                P.op("tensor", lambda Eo, o=tfp[:, t * 128:(t + 1) * 128],
                     a=z[:, dc * 128:(dc + 1) * 128]: Eo.transpose(o, a, idf[:]),
                     inc=(sPE if t == 3 else None))
            P.wait("scalar", sPE)
            rT4.acquire(P, "scalar", tc * 2 + g)
            tm4 = tmp4[(tc * 2 + g) % 3]
            P.op("scalar", lambda Eo, o=tm4[:], a=tfp[:]: Eo.activation(o, a, AF.Copy), inc=sAC)
            y1T_view = y1T[:].rearrange("p (d t) -> p d t", d=8)[:, g * 4:(g + 1) * 4,
                                                               tc * 128:(tc + 1) * 128]
            P.op("scalar", lambda Eo, o=y1T_view, a=tfp[:].rearrange("p (d t) -> p d t", d=4):
                 Eo.activation(o, a, AF.Copy), inc=sAC)
            rTF.release(tc * 2 + g, sAC)
            P.wait("tensor", sAC)
            if g == 0:
                rPL.acquire(P, "tensor", tc)
            for t in range(4):
                dc = g * 4 + t
                P.op("tensor", lambda Eo, p=pLr[tc % 2][:],
                     w=rw[:, dc * 8:(dc + 1) * 8],
                     x=tm4[:, t * 128:(t + 1) * 128], st=(dc == 0), sp=(dc == 7):
                     Eo.matmul(p, w, x, start=st, stop=sp),
                     inc=(sPE if t == 3 else None))
            rT4.release(tc * 2 + g, sPE)
        P.wait("scalar", sPE)
        P.op("scalar", lambda Eo, o=logits[:, tc * 128:(tc + 1) * 128],
             p=pLr[tc % 2][:]: Eo.activation(o, p, AF.Copy), inc=sAC)
        rPL.release(tc, sAC)
        rZ.release(tc, sPE)
        rZ.rel[par].append(dSPp.sem_count(tc))
    logits_done = sAC.n
    gate_dve_done = {}
    for tc in range(NCH):
        sl = tc % 2
        P.wait("tensor", sAC)
        if tc >= 2:
            P.wait("tensor", sDV, gate_dve_done[sl])
        P.op("tensor", lambda Eo, o=pG8[:, sl * 8:(sl + 1) * 8],
             a=logits[:, tc * 128:(tc + 1) * 128]:
             Eo.transpose(o, a, idf[0:8, 0:8]), inc=sPE)
        P.wait("vector", sPE)
        P.eop("vector", lambda Eo, p=pG8[:, sl * 8:(sl + 1) * 8]: Eo.tensor_copy(gtm[:], p), sDV)
        P.eop("vector", lambda Eo: Eo.reduce_max(m1v[:], gtm[:], axis=AX.X), sDV)
        P.eop("vector", lambda Eo: Eo.tensor_scalar(eq1[:], gtm[:], m1v[:, 0:1], None, op0=ALU.is_equal), sDV)
        P.eop("vector", lambda Eo: Eo.scalar_tensor_tensor(mskl[:], eq1[:], c_neg1e30, gtm[:],
                                                           op0=ALU.mult, op1=ALU.add), sDV)
        P.eop("vector", lambda Eo: Eo.reduce_max(m2v[:], mskl[:], axis=AX.X), sDV)
        P.eop("vector", lambda Eo: Eo.tensor_scalar(eq2[:], mskl[:], m2v[:, 0:1], None, op0=ALU.is_equal), sDV)
        P.op("vector", lambda Eo: Eo.tensor_tensor(edv2[:], m2v[:], m1v[:], op=ALU.subtract), inc=sDV)
        P.wait("scalar", sDV)
        P.op("scalar", lambda Eo: Eo.activation(edv[:], edv2[:], AF.Exp), inc=sAC)
        P.wait("vector", sAC)
        P.eop("vector", lambda Eo: Eo.tensor_scalar(s1v[:], edv[:], c_one, None, op0=ALU.add), sDV)
        P.eop("vector", lambda Eo: Eo.reciprocal(rv[:], s1v[:]), sDV)
        P.eop("vector", lambda Eo: Eo.tensor_tensor(w2v[:], edv[:], rv[:], op=ALU.mult), sDV)
        P.eop("vector", lambda Eo: Eo.tensor_scalar(g1v[:], eq1[:], rv[:, 0:1], None, op0=ALU.mult), sDV)
        P.op("vector", lambda Eo, o=gates_tm[:, tc * 8:(tc + 1) * 8]:
             Eo.scalar_tensor_tensor(o, eq2[:], w2v[:, 0:1], g1v[:], op0=ALU.mult, op1=ALU.add),
             inc=sDV)
        gate_dve_done[sl] = sDV.n
    P.wait("tensor", sDV)
    for tc in range(NCH):
        sl = tc % 2
        P.op("tensor", lambda Eo, o=pTg[:, sl * 128:(sl + 1) * 128],
             a=gates_tm[:, tc * 8:(tc + 1) * 8]:
             Eo.transpose(o, a, idf[:]), inc=sPE)
        P.wait("scalar", sPE)
        P.op("scalar", lambda Eo, o=gatesT[:, tc * 128:(tc + 1) * 128],
             p=pTg[:, sl * 128:(sl + 1) * 128]: Eo.activation(o, p, AF.Copy), inc=sAC)
        P.wait("tensor", sAC)

    if "logits" in taps:
        P.wait("sync", sAC, logits_done)
        P.op("sync", lambda Eo: Eo.dma_start(out=taps["logits"], in_=logits[:]),
             inc=sTAP, inc_n=16)
    if "gates" in taps:
        P.wait("sync", sDV)
        P.op("sync", lambda Eo: Eo.dma_start(out=taps["gates"], in_=gates_tm[:]),
             inc=sTAP, inc_n=16)
    if "y1T" in taps:
        P.wait("gpsimd", sAC)
        P.op("gpsimd", lambda Eo: Eo.dma_start(out=taps["y1T"], in_=y1T[:]),
             inc=sTAP, inc_n=16)
    if "gatesT" in taps:
        P.wait("gpsimd", sAC)
        P.op("gpsimd", lambda Eo: Eo.dma_start(out=taps["gatesT"], in_=gatesT[:]),
             inc=sTAP, inc_n=16)
    if "y1" in taps:
        for s_ in dSPp.sems:
            P.wait("sync", s_)
        P.op("sync", lambda Eo: Eo.dma_start(out=taps["y1"], in_=d_y1sp), inc=sTAP, inc_n=16)
    for s_ in dSPp.sems:
        P.wait("sync", s_)
    P.wait("sync", sTAP)
    barrier(P)
    with nc.Block("b3") as blk:
        P.run(blk)
    es3.close()
    esA.close()

    esO = ExitStack()
    O_acc = esO.enter_context(nc.sbuf_tensor("O_acc", [128, 8 * TOK], F32))
    es4 = ExitStack()
    sb4 = lambda name, shape, dt: es4.enter_context(nc.sbuf_tensor(name, shape, dt))
    ps4 = lambda name, shape, dt: es4.enter_context(nc.psum_tensor(name, shape, dt))
    H = sb4("H", [128, 32 * NT], BF16)
    w1b = [sb4("w1b0", [128, 8 * 256], BF16), sb4("w1b1", [128, 8 * 256], BF16)]
    w2b = [sb4("w2b%d" % i, [128, 16 * 128], BF16) for i in range(4)]
    gbb = [sb4("gbb0", [128, NT], BF16), sb4("gbb1", [128, NT], BF16)]
    pH = [ps4("pH0", [128, 1024], F32), ps4("pH1", [128, 1024], F32)]
    pO2 = [ps4("pO20", [128, 512], F32), ps4("pO21", [128, 512], F32)]
    pGB = ps4("pGB", [128, NT], F32)

    P = Prog()
    for _eng in ENGINES:
        P.wait(_eng, sC1, ld_consts)
        P.wait(_eng, sC2, wl_consts)
    rW1 = Rot(2)
    rW2 = Rot(4)
    rGB = Rot(2)
    rPH = Rot(2)
    rPO2 = Rot(2)
    rGBpsum = Rot(1)

    w1_view = d_w1
    w2_view = d_w2
    ei = 0
    for nt in range(2):
        for e in range(E):
            gpar = ei % 2
            rGBpsum.acquire(P, "tensor", 0)
            for s in range(2):
                P.op("tensor", lambda Eo, p=pGB[:, s * 512:(s + 1) * 512],
                     sel=selb[:, e * 128:(e + 1) * 128],
                     g=gatesT[:, nt * NT + s * 512:nt * NT + (s + 1) * 512]:
                     Eo.matmul(p, sel, g, start=True, stop=True),
                     inc=(sPE if s == 1 else None))
            P.wait("vector", sPE)
            rGB.acquire(P, "vector", ei)
            P.op("vector", lambda Eo, o=gbb[gpar][:]: Eo.tensor_copy(o, pGB[:]), inc=sDV)
            rGBpsum.rel[0] = [(sDV, sDV.n)]
            gb_ready = sDV.n
            def issue_w2(k, e=e, ei=ei):
                gk = ei * 16 + k
                dc, fh = k // 2, k % 2
                w2i = gk % 4
                rW2.acquire(P, "gpsimd", gk)
                srcw = w2_view[e, fh * 2048:(fh + 1) * 2048,
                               dc * 128:(dc + 1) * 128].rearrange(
                    "(fc p) d -> p fc d", p=128)
                dW2.load(P, "gpsimd", gk,
                         lambda Eo, o=w2b[w2i][:].rearrange("p (fc d) -> p fc d", fc=16),
                         s=srcw: Eo.dma_start(out=o, in_=s))

            gate_done = {}
            for fp in range(16):
                wpar = (ei * 16 + fp) % 2
                rW1.acquire(P, "gpsimd", ei * 16 + fp)
                src = w1_view[e, :, fp * 256:(fp + 1) * 256].rearrange(
                    "(kc p) f -> p kc f", p=128)
                dW1.load(P, "gpsimd", ei * 16 + fp,
                         lambda Eo, o=w1b[wpar][:].rearrange("p (kc f) -> p kc f", kc=8),
                         s=src: Eo.dma_start(out=o, in_=s))
                dW1.wait(P, "tensor", ei * 16 + fp)
                for fc in range(2):
                    fcg = fp * 2 + fc
                    hpar = fcg % 2
                    rPH.acquire(P, "tensor", fcg)
                    for kc in range(8):
                        for nb in range(2):
                            P.op("tensor", lambda Eo,
                                 p=pH[hpar][:, nb * 512:(nb + 1) * 512],
                                 w=w1b[wpar][:, kc * 256 + fc * 128:kc * 256 + (fc + 1) * 128],
                                 x=y1T[:, kc * TOK + nt * NT + nb * 512:
                                       kc * TOK + nt * NT + (nb + 1) * 512],
                                 st=(kc == 0), sp=(kc == 7):
                                 Eo.matmul(p, w, x, start=st, stop=sp),
                                 inc=(sPE if (nb == 1 and kc == 7) else None))
                    P.wait("scalar", sPE)
                    hsl = H[:, fcg * NT:(fcg + 1) * NT]
                    P.op("scalar", lambda Eo, o=hsl, p=pH[hpar][:],
                         b=b1t[:, e * 32 + fcg:e * 32 + fcg + 1]:
                         Eo.activation(o, p, AF.Gelu_apprx_tanh, bias=b, scale=f32(1.0)),
                         inc=sAC)
                    rPH.release(fcg, sAC)
                    P.wait("vector", sAC)
                    P.wait("vector", sDV, gb_ready)
                    P.op("vector", lambda Eo, h=hsl, g=gbb[gpar][:]:
                         Eo.tensor_tensor(h, h, g, op=ALU.mult), inc=sDV)
                    gate_done[fcg] = sDV.n
                rW1.release(ei * 16 + fp, sPE)
                if fp >= 12:
                    issue_w2(fp - 12)
            h_gated = sDV.n
            rGB.release(ei, sDV)
            for dc in range(8):
                if dc == 1:
                    P.wait("tensor", sDV, h_gated)
                for fh in range(2):
                    k = dc * 2 + fh
                    w2i = (ei * 16 + k) % 4
                    dW2.wait(P, "tensor", ei * 16 + k)
                    if fh == 0:
                        rPO2.acquire(P, "tensor", ei * 16 + dc * 2)
                        rPO2.acquire(P, "tensor", ei * 16 + dc * 2 + 1)
                        if e == 0:
                            for nb in range(2):
                                P.op("tensor", lambda Eo, p=pO2[nb][:],
                                     b=b2b[:, dc * 128:(dc + 1) * 128],
                                     g=gatesT[:, nt * NT + nb * 512:nt * NT + (nb + 1) * 512]:
                                     Eo.matmul(p, b, g, start=True, stop=False))
                    for fcL in range(16):
                        fcg = fh * 16 + fcL
                        if dc == 0:
                            P.wait("tensor", sDV, gate_done[fcg])
                        for nb in range(2):
                            P.op("tensor", lambda Eo, p=pO2[nb][:],
                                 w=w2b[w2i][:, fcL * 128:(fcL + 1) * 128],
                                 x=H[:, fcg * NT + nb * 512:fcg * NT + (nb + 1) * 512],
                                 st=(fh == 0 and fcL == 0 and e != 0),
                                 sp=(fh == 1 and fcL == 15):
                                 Eo.matmul(p, w, x, start=st, stop=sp),
                                 inc=(sPE if (fcL == 15 and nb == 1) else None))
                    if fh == 1:
                        P.wait("vector", sPE)
                        for nb in range(2):
                            osl = O_acc[:, dc * TOK + nt * NT + nb * 512:
                                        dc * TOK + nt * NT + (nb + 1) * 512]
                            if e == 0:
                                P.op("vector", lambda Eo, o=osl, p=pO2[nb][:]:
                                     Eo.tensor_copy(o, p), inc=sDV)
                            else:
                                P.op("vector", lambda Eo, o=osl, p=pO2[nb][:]:
                                     Eo.tensor_add(o, o, p), inc=sDV)
                            rPO2.release(ei * 16 + dc * 2 + nb, sDV)
                    rW2.release(ei * 16 + k, sPE)
                    if k + 4 < 16:
                        issue_w2(k + 4)
            ei += 1

    if "O_acc" in taps:
        P.wait("sync", sDV)
        P.op("sync", lambda Eo: Eo.dma_start(out=taps["O_acc"], in_=O_acc[:]),
             inc=sTAP, inc_n=16)
    P.wait("sync", sTAP)
    barrier(P)
    with nc.Block("b4") as blk:
        P.run(blk)
    es4.close()

    es5 = ExitStack()
    sb5 = lambda name, shape, dt: es5.enter_context(nc.sbuf_tensor(name, shape, dt))
    ps5 = lambda name, shape, dt: es5.enter_context(nc.psum_tensor(name, shape, dt))
    y1ch = [sb5("y1ch%d" % i, [128, C], F32) for i in range(3)]
    z2ch = [sb5("z2ch%d" % i, [128, C], F32) for i in range(3)]
    sq2 = [sb5("sq2%d" % i, [128, C], F32) for i in range(3)]
    mu2 = [sb5("mu2%d" % i, [128, 1], F32) for i in range(3)]
    var2_ = [sb5("var2%d" % i, [128, 1], F32) for i in range(3)]
    lnv2_ = [sb5("lnv2%d" % i, [128, 1], F32) for i in range(3)]
    rstd2 = [sb5("rstd2%d" % i, [128, 1], F32) for i in range(3)]
    pTf2 = [ps5("pTf2%d" % i, [128, 512], F32) for i in range(3)]

    P = Prog()
    for _eng in ENGINES:
        P.wait(_eng, sC1, ld_consts)
        P.wait(_eng, sC2, wl_consts)
    rY1 = Rot(3)
    rZ2 = Rot(3)
    rSQ2 = Rot(3)
    rMU2 = Rot(3)
    rRS2 = Rot(3)
    rTF2 = Rot(3)
    for s_ in dSPp.sems:
        P.wait("sync", s_)
    for tc in range(NCH):
        par = tc % 3
        rY1.acquire(P, "sync", tc)
        dYC.load(P, "sync", tc, lambda Eo, o=y1ch[par][:], s=d_y1sp[tc * 128:(tc + 1) * 128, :]:
                 Eo.dma_start(out=o, in_=s))
        for g in range(2):
            tfp = pTf2[(tc * 2 + g) % 3]
            rTF2.acquire(P, "tensor", tc * 2 + g)
            for t in range(4):
                dc = g * 4 + t
                P.op("tensor", lambda Eo, o=tfp[:, t * 128:(t + 1) * 128],
                     a=O_acc[:, dc * TOK + tc * 128:dc * TOK + (tc + 1) * 128]:
                     Eo.transpose(o, a, idf[:]), inc=(sPE if t == 3 else None))
            P.wait("vector", sPE)
            dYC.wait(P, "vector", tc)
            rZ2.acquire(P, "vector", tc) if g == 0 else None
            P.op("vector", lambda Eo, o=z2ch[par][:, g * 512:(g + 1) * 512],
                 a=tfp[:], b=y1ch[par][:, g * 512:(g + 1) * 512]:
                 Eo.tensor_add(o, a, b), inc=sDV)
            rTF2.release(tc * 2 + g, sDV)
        rY1.release(tc, sDV)
        z2 = z2ch[par]
        rMU2.acquire(P, "vector", tc)
        mu = mu2[par]
        P.wait("vector", sDV)
        P.eop("vector", lambda Eo, m=mu[:], z_=z2[:]: Eo.reduce_sum(m, z_, axis=AX.X), sDV)
        P.eop("vector", lambda Eo, m=mu[:]:
              Eo.tensor_scalar(m, m, c_inv1024, None, op0=ALU.mult), sDV)
        P.wait("scalar", sDV)
        rSQ2.acquire(P, "scalar", tc)
        P.op("scalar", lambda Eo, q=sq2[par][:], z_=z2[:], m=mu[:]:
             Eo.activation(q, z_, AF.Square, bias=m, scale=f32(-1.0)), inc=sAC)
        P.wait("vector", sAC)
        var2 = var2_[par]
        lnv2 = lnv2_[par]
        P.eop("vector", lambda Eo, vv=var2[:], q=sq2[par][:]: Eo.reduce_sum(vv, q, axis=AX.X), sDV)
        P.op("vector", lambda Eo, vv=var2[:]:
             Eo.tensor_scalar(vv, vv, c_inv1024, None, op0=ALU.mult), inc=sDV)
        rSQ2.rel[par] = [(sDV, sDV.n)]
        P.wait("scalar", sDV)
        rRS2.acquire(P, "scalar", tc)
        P.eop("scalar", lambda Eo, lv=lnv2[:], vv=var2[:]: Eo.activation(lv, vv, AF.Ln, bias=f32(EPS)), sAC)
        P.op("scalar", lambda Eo, r=rstd2[par][:], lv=lnv2[:]:
             Eo.activation(r, lv, AF.Exp, scale=f32(-0.5)), inc=sAC)
        P.wait("vector", sAC)
        P.op("vector", lambda Eo, z_=z2[:], m=mu[:], r=rstd2[par][:]:
             Eo.tensor_scalar(z_, z_, m, r, op0=ALU.subtract, op1=ALU.mult), inc=sDV)
        rMU2.release(tc, sDV)
        rRS2.release(tc, sDV)
        P.wait("sync", sDV)
        dSTp.load(P, "sync", tc, lambda Eo, o=d_y2[tc * 128:(tc + 1) * 128, :], z_=z2[:]:
                  Eo.dma_start(out=o, in_=z_))
        rZ2.rel[par] = [dSTp.sem_count(tc)]
    for s_ in dSTp.sems:
        P.wait("sync", s_)
    P.wait("sync", sTAP)
    with nc.Block("b5") as blk:
        P.run(blk)
    es5.close()
    esO.close()
    esP.close()
    esS.close()
    return nc


_CACHE = {}


def kernel(L0_attn_x, L0_attn_Wq, L0_attn_bq, L0_attn_Wk, L0_attn_bk,
           L0_attn_Wv, L0_attn_bv, L0_ffn_router_w, L0_ffn_w1, L0_ffn_b1,
           L0_ffn_w2, L0_ffn_b2):
    x = np.ascontiguousarray(np.asarray(L0_attn_x, dtype=np.float32))
    Wq = np.asarray(L0_attn_Wq, np.float32)
    Wk = np.asarray(L0_attn_Wk, np.float32)
    Wv = np.asarray(L0_attn_Wv, np.float32)
    bq = np.asarray(L0_attn_bq, np.float32)
    bk = np.asarray(L0_attn_bk, np.float32)
    bv = np.asarray(L0_attn_bv, np.float32)
    rw_in = np.asarray(L0_ffn_router_w, np.float32)
    w1 = np.ascontiguousarray(np.asarray(L0_ffn_w1, np.float32))
    b1 = np.asarray(L0_ffn_b1, np.float32)
    w2 = np.ascontiguousarray(np.asarray(L0_ffn_w2, np.float32))
    b2 = np.asarray(L0_ffn_b2, np.float32)

    wqkvT = np.concatenate([Wq.T, Wk.T, Wv.T], axis=1)
    wqkvT = np.ascontiguousarray(wqkvT.reshape(8, 128, 192).transpose(1, 0, 2))
    qkb = np.zeros((128, 2), np.float32)
    qkb[0:64, 0] = bq
    qkb[0:64, 1] = bk
    bvrow = bv.reshape(1, HS)
    b1t = np.ascontiguousarray(
        b1.reshape(E, 32, 128).transpose(2, 0, 1))
    rw_pre = np.ascontiguousarray(rw_in.reshape(8, 128, E).transpose(1, 0, 2))
    identb = np.eye(128, dtype=bf16)
    identf = np.eye(128, dtype=np.float32)
    qpiota = (np.arange(128, dtype=np.float32)[:, None]
              - np.arange(512, dtype=np.float32)[None, :])
    onesb = np.ones((1, 128), dtype=bf16)
    consts = np.zeros((128, 8), np.float32)
    consts[:, C_INV1024] = 1.0 / 1024.0
    consts[:, C_NEG132] = -1.0 / 32.0
    consts[:, C_ONE] = 1.0
    consts[:, C_NEG1E30] = -1e30
    consts[:, C_NEG1] = -1.0
    selb = np.zeros((E, E * 128), dtype=bf16)
    for e in range(E):
        selb[e, e * 128:(e + 1) * 128] = 1

    nc = build_nc()
    in_maps = []
    for c in range(NCORES):
        b = c // 4
        ph = c % 4
        chunks = [4 * j + ph for j in range(NCH)]
        x_b = x[b]
        x_own = np.ascontiguousarray(
            x_b.reshape(64, 128, C)[chunks].reshape(TOK, C))
        qoffb = np.full((128, 1), 128.0 * ph, np.float32)
        mth = np.zeros((128, 4), np.float32)
        for t in range(4):
            mth[:, t] = (t - ph) * 128.0
        in_maps.append(dict(
            x_own=x_own, x_batch=x_b, wqkvT=wqkvT, qkb=qkb, bvrow=bvrow,
            b1t=b1t, b2w=b2, rw=rw_pre, w1=w1, w2=w2, identb=identb,
            identf=identf, qpiota=qpiota, onesb=onesb, qoffb=qoffb,
            consts=consts, selb=selb, mth=mth))
    res = run_bass_kernel_spmd(nc, in_maps, core_ids=list(range(NCORES)))
    out = np.zeros((B, T, C), np.float32)
    for c in range(NCORES):
        b = c // 4
        ph = c % 4
        y2 = np.asarray(res.results[c]["y2"]).reshape(NCH, 128, C)
        for j in range(NCH):
            out[b, (4 * j + ph) * 128:(4 * j + ph + 1) * 128, :] = y2[j]
    return out


# version 11
# speedup vs baseline: 1.2811x; 1.1788x over previous
import numpy as np
import ml_dtypes
from contextlib import ExitStack

import concourse.bass as bass
import concourse.mybir as mybir
from concourse.bass_utils import run_bass_kernel_spmd
from concourse.bass_core_rust import IndirectOffsetOnAxis

F32 = mybir.dt.float32
BF16 = mybir.dt.bfloat16
AF = mybir.ActivationFunctionType
ALU = mybir.AluOpType
AX = mybir.AxisListType
f32 = np.float32
bf16 = ml_dtypes.bfloat16

B, T, C = 2, 8192, 1024
HS = 64
E, FF = 8, 4096
NCORES = 8
TOK = 2048
NCH = 16
KCH = 64
NT = 1024
EPS = 1e-5

C_INV1024, C_NEG132, C_ONE, C_NEG1E30, C_ZERO, C_NEG1, C_BIG, C_EIGHT = 0, 1, 2, 3, 4, 5, 6, 7
CAP, NG = 768, 6

ENGINES = ("sync", "scalar", "vector", "tensor", "gpsimd")


class Sem:
    def __init__(self, nc, es, name):
        self.h = es.enter_context(nc.semaphore(name))
        self.n = 0

    def bump(self, k):
        self.n += k
        return self.n


class Prog:

    def __init__(self):
        self.streams = {e: [] for e in ENGINES}

    def op(self, eng, fn, inc=None, inc_n=1):
        if inc is not None:
            inc.bump(inc_n)
            self.streams[eng].append(
                lambda Eo, fn=fn, s=inc.h, n=inc_n: fn(Eo).then_inc(s, n))
            if (PARANOID[0] is True and eng == "gpsimd" and inc_n == 16) or \
               (PARANOID[0] == "ind" and eng == "gpsimd" and inc_n == 16 and getattr(self, "_mark_ind", False)):
                self.wait(eng, inc)
        else:
            self.streams[eng].append(lambda Eo, fn=fn: fn(Eo))

    def eop(self, eng, fn, sem):
        self.op(eng, fn, inc=sem)
        self.wait(eng, sem)

    def wait(self, eng, sem, val=None):
        v = sem.n if val is None else val
        if v > 0:
            self.streams[eng].append(lambda Eo, s=sem.h, v=v: Eo.wait_ge(s, v))

    def run(self, block):
        for eng in ENGINES:
            fns = tuple(self.streams[eng])

            def body(Eo, fns=fns):
                for f in fns:
                    f(Eo)

            getattr(block, eng)(body)


class Rot:

    def __init__(self, n):
        self.n = n
        self.rel = [[] for _ in range(n)]

    def acquire(self, P, eng, i):
        for sem, cnt in self.rel[i % self.n]:
            P.wait(eng, sem, cnt)
        self.rel[i % self.n] = []

    def release(self, i, sem):
        self.rel[i % self.n].append((sem, sem.n))


class DmaSlots:

    def __init__(self, nc, es, name, n=2):
        self.sems = [Sem(nc, es, "%s%d" % (name, i)) for i in range(n)]
        self.n = n

    def load(self, P, eng, i, fn):
        P.op(eng, fn, inc=self.sems[i % self.n], inc_n=16)

    def wait(self, P, eng, i):
        P.wait(eng, self.sems[i % self.n])

    def sem_count(self, i):
        s = self.sems[i % self.n]
        return (s, s.n)


PARANOID = ["ind"]


def build_nc(debug_taps=(), n_experts=E):
    nc = bass.Bass("TRN2", target_bir_lowering=False)

    d_xown = nc.dram_tensor("x_own", [TOK, C], F32, kind="ExternalInput").ap()
    d_xbat = nc.dram_tensor("x_batch", [T, C], F32, kind="ExternalInput").ap()
    d_wqkv = nc.dram_tensor("wqkvT", [128, 8, 192], F32, kind="ExternalInput").ap()
    d_qkb = nc.dram_tensor("qkb", [128, 2], F32, kind="ExternalInput").ap()
    d_bvr = nc.dram_tensor("bvrow", [1, HS], F32, kind="ExternalInput").ap()
    d_b1t = nc.dram_tensor("b1t", [128, E, 32], F32, kind="ExternalInput").ap()
    d_b2w = nc.dram_tensor("b2w", [E, C], F32, kind="ExternalInput").ap()
    d_rw = nc.dram_tensor("rw", [128, 8, E], F32, kind="ExternalInput").ap()
    d_w1 = nc.dram_tensor("w1", [E, C, FF], F32, kind="ExternalInput").ap()
    d_w2 = nc.dram_tensor("w2", [E, FF, C], F32, kind="ExternalInput").ap()
    d_idb = nc.dram_tensor("identb", [128, 128], BF16, kind="ExternalInput").ap()
    d_idf = nc.dram_tensor("identf", [128, 128], F32, kind="ExternalInput").ap()
    d_qpi = nc.dram_tensor("qpiota", [128, 512], F32, kind="ExternalInput").ap()
    d_ones = nc.dram_tensor("onesb", [1, 128], BF16, kind="ExternalInput").ap()
    d_qoff = nc.dram_tensor("qoffb", [128, 1], F32, kind="ExternalInput").ap()
    d_cst = nc.dram_tensor("consts", [128, 8], F32, kind="ExternalInput").ap()
    d_sel = nc.dram_tensor("selb", [E, E * 128], BF16, kind="ExternalInput").ap()
    d_mth = nc.dram_tensor("mth", [128, 4], F32, kind="ExternalInput").ap()
    d_tokid = nc.dram_tensor("tokid", [128, NCH], F32, kind="ExternalInput").ap()
    d_evals = nc.dram_tensor("evals", [128, E], F32, kind="ExternalInput").ap()
    d_b2t = nc.dram_tensor("b2t", [128, 8, E], F32, kind="ExternalInput").ap()
    d_y1bf = nc.dram_tensor("y1bf", [TOK, C], BF16, kind="Internal").ap()
    d_idxf = nc.dram_tensor("idxf", [E, CAP], F32, kind="Internal").ap()
    d_gc = nc.dram_tensor("gc", [TOK, E], F32, kind="Internal").ap()
    d_y1sp = nc.dram_tensor("y1sp", [TOK, C], F32, kind="Internal").ap()
    d_y2 = nc.dram_tensor("y2", [TOK, C], F32, kind="ExternalOutput").ap()
    taps = {}
    for name, shape in debug_taps:
        taps[name] = nc.dram_tensor("tap_" + name, shape, F32, kind="ExternalOutput").ap()

    esP = ExitStack()
    sb = lambda name, shape, dt: esP.enter_context(nc.sbuf_tensor(name, shape, dt))

    idb = sb("idb", [128, 128], BF16)
    idf = sb("idf", [128, 128], F32)
    onesb = sb("onesb_s", [1, 128], BF16)
    qoffb = sb("qoffb_s", [128, 1], F32)
    mth = sb("mth_s", [128, 4], F32)
    tokid = sb("tokid_s", [128, NCH], F32)
    evals = sb("evals_s", [128, E], F32)
    b2ts = sb("b2t_s", [128, 8 * E], F32)
    cst = sb("cst", [128, 8], F32)
    b1t = sb("b1t_s", [128, E * 32], F32)
    b2b = sb("b2b", [E, C], BF16)
    rw = sb("rw_s", [128, 8 * E], F32)
    y1T = sb("y1T", [128, 8 * TOK], BF16)
    gatesT = sb("gatesT", [E, TOK], BF16)
    selb = sb("selb_s", [E, E * 128], BF16)

    esA = ExitStack()
    sbA = lambda name, shape, dt: esA.enter_context(nc.sbuf_tensor(name, shape, dt))
    qpi = sbA("qpi", [128, 512], F32)
    maskadd = sbA("maskadd", [128, 512], F32)
    qT = sbA("qT", [65, TOK], BF16)
    kT = sbA("kT", [65, T], BF16)
    vtm = sbA("vtm", [128, KCH * 65], BF16)
    condt = sbA("condt", [128, 4 * 128], BF16)
    o_tm = sbA("o_tm", [128, NCH * HS], F32)
    mA = sbA("mA", [128, 1], F32)
    mB = sbA("mB", [128, 1], F32)
    dm = sbA("dm", [128, 1], F32)
    edm = sbA("edm", [128, 1], F32)
    denA = sbA("denA", [128, 1], F32)
    denB = sbA("denB", [128, 1], F32)
    negm32 = sbA("negm32", [128, 1], F32)
    lden = sbA("lden", [128, 1], F32)
    bias2 = sbA("bias2", [128, 1], F32)
    bsum = [sbA("bsum0", [128, 1], F32), sbA("bsum1", [128, 1], F32)]

    esS = ExitStack()
    mks = lambda name: Sem(nc, esS, name)
    sC1 = mks("sC1")
    sC2 = mks("sC2")
    sTAP = mks("sTAP")
    sPL = mks("sPL")
    sPE = mks("sPE")
    sAC = mks("sAC")
    sDV = mks("sDV")
    sBAR = mks("sBAR")
    dXB = DmaSlots(nc, esS, "dXB")
    dZC = DmaSlots(nc, esS, "dZC", n=3)
    dSPp = DmaSlots(nc, esS, "dSPp", n=3)
    dW1 = DmaSlots(nc, esS, "dW1")
    dW2 = DmaSlots(nc, esS, "dW2", n=4)
    dYC = DmaSlots(nc, esS, "dYC", n=3)
    dSTp = DmaSlots(nc, esS, "dSTp", n=3)
    dFD = DmaSlots(nc, esS, "dFD")
    dSPb = DmaSlots(nc, esS, "dSPb", n=3)
    sGC = mks("sGC")
    sSCT = mks("sSCT")
    sSCTp = [mks("sSCTp0"), mks("sSCTp1")]
    sSCA = mks("sSCA")
    dPF = DmaSlots(nc, esS, "dPF", n=1)
    dIX = DmaSlots(nc, esS, "dIX", n=2)
    dGA = DmaSlots(nc, esS, "dGA", n=2)
    dGV = DmaSlots(nc, esS, "dGV", n=2)

    c_inv1024 = cst[:, C_INV1024:C_INV1024 + 1]
    c_neg132 = cst[:, C_NEG132:C_NEG132 + 1]
    c_one = cst[:, C_ONE:C_ONE + 1]
    c_neg1e30 = cst[:, C_NEG1E30:C_NEG1E30 + 1]
    c_zero = cst[:, C_ZERO:C_ZERO + 1]
    c_neg1 = cst[:, C_NEG1:C_NEG1 + 1]
    c_big = cst[:, C_BIG:C_BIG + 1]
    c_eight = cst[:, C_EIGHT:C_EIGHT + 1]

    regB_cap = nc.gpsimd.alloc_register("bnd_cap")
    regB_tok = nc.gpsimd.alloc_register("bnd_tok")
    regB_gc = nc.gpsimd.alloc_register("bnd_gc")

    barrier_count = [0]

    def barrier(P):
        barrier_count[0] += 1
        for eng in ENGINES:
            P.op(eng, lambda Eo: Eo.nop(), inc=sBAR)
        for eng in ENGINES:
            P.wait(eng, sBAR, 5 * barrier_count[0])

    es1 = ExitStack()
    sb1 = lambda name, shape, dt: es1.enter_context(nc.sbuf_tensor(name, shape, dt))
    ps1 = lambda name, shape, dt: es1.enter_context(nc.psum_tensor(name, shape, dt))
    xb = [sb1("xb0", [128, C], F32), sb1("xb1", [128, C], F32)]
    xT = [sb1("xT0", [128, C], BF16), sb1("xT1", [128, C], BF16)]
    wqkv = sb1("wqkv", [128, 8 * 192], BF16)
    bvr = sb1("bvr", [1, HS], BF16)
    qkb = sb1("qkb_s", [128, 2], F32)
    pXT = [ps1("pXT0", [128, C], F32), ps1("pXT1", [128, C], F32)]
    pQK = [ps1("pQK0", [64, 128], F32), ps1("pQK1", [64, 128], F32)]
    pV = [ps1("pV0", [128, HS], F32), ps1("pV1", [128, HS], F32)]

    P = Prog()
    for dst, src in [(idb, d_idb), (idf, d_idf), (qpi, d_qpi), (qoffb, d_qoff),
                     (mth, d_mth), (tokid, d_tokid), (evals, d_evals),
                     (b2ts, d_b2t.rearrange("p a b -> p (a b)")),
                     (cst, d_cst), (qkb[:], d_qkb),
                     (b1t, d_b1t.rearrange("p e k -> p (e k)")),
                     (rw, d_rw.rearrange("p c e -> p (c e)"))]:
        ap = dst[:] if hasattr(dst, "ap") else dst
        P.op("sync", lambda Eo, a=ap, s=src: Eo.dma_start(out=a, in_=s), inc=sC1, inc_n=16)
    for dst, src in [(onesb[:], d_ones), (selb[:], d_sel),
                     (wqkv[:], d_wqkv.rearrange("p c h -> p (c h)")),
                     (bvr[:], d_bvr), (b2b[:], d_b2w)]:
        P.op("gpsimd", lambda Eo, a=dst, s=src: Eo.dma_start(out=a, in_=s), inc=sC2, inc_n=16)
    ld_consts = sC1.n
    wl_consts = sC2.n
    for _eng in ("tensor", "scalar", "vector", "gpsimd"):
        P.wait(_eng, sC1, ld_consts)
        P.wait(_eng, sC2, wl_consts)
    P.op("gpsimd", lambda Eo: Eo.reg_mov(regB_cap, CAP - 1))
    P.op("gpsimd", lambda Eo: Eo.reg_mov(regB_tok, TOK - 1))
    P.op("gpsimd", lambda Eo: Eo.reg_mov(regB_gc, TOK * E - 1))
    P.op("gpsimd", lambda Eo: Eo.memset(kT[64:65, :], 1.0), inc=sPL)
    P.op("gpsimd", lambda Eo: Eo.memset(vtm[:], 1.0), inc=sPL)
    P.wait("vector", sC1, ld_consts)
    P.eop("vector", lambda Eo: Eo.tensor_scalar(maskadd[0:128, 0:128], qpi[:, 0:128], c_neg1, None, op0=ALU.mult), sDV)
    for _t in range(4):
        P.eop("vector", lambda Eo, t=_t: Eo.tensor_scalar(
            condt[:, t * 128:(t + 1) * 128], maskadd[0:128, 0:128],
            mth[:, t:t + 1], None, op0=ALU.is_ge), sDV)
    P.eop("vector", lambda Eo: Eo.tensor_scalar(maskadd[:], qpi[:], qoffb[:, 0:1], None, op0=ALU.add), sDV)
    P.eop("vector", lambda Eo: Eo.tensor_scalar(maskadd[:], maskadd[:], c_zero, None, op0=ALU.is_lt), sDV)
    P.op("vector", lambda Eo: Eo.tensor_scalar(maskadd[:], maskadd[:], c_neg1e30, None, op0=ALU.mult), inc=sDV)

    rXB = Rot(2)
    rXT = Rot(2)
    rQK = Rot(2)
    rV = Rot(2)

    rPXT = Rot(2)

    def chunk_setup(i, src_ap, is_q, kc):
        pi = i % 2
        rXB.acquire(P, "sync", i)
        dXB.load(P, "sync", i, lambda Eo, a=xb[pi][:], s=src_ap: Eo.dma_start(out=a, in_=s))
        dXB.wait(P, "tensor", i)
        rPXT.acquire(P, "tensor", i)
        rXT_psum = pXT[pi]
        for dc in range(8):
            P.op("tensor", lambda Eo, o=rXT_psum[:, dc * 128:(dc + 1) * 128],
                 a=xb[pi][:, dc * 128:(dc + 1) * 128]: Eo.transpose(o, a, idf[:]),
                 inc=(sPE if dc == 7 else None))
        transposes_done = sPE.n
        P.wait("vector", sPE)
        rXT.acquire(P, "vector", i)
        P.op("vector", lambda Eo, o=xT[pi][:], a=rXT_psum[:]: Eo.tensor_copy(o, a), inc=sDV)
        rPXT.release(i, sDV)
        evac_cnt = sDV.n
        P.wait("tensor", sDV, evac_cnt)
        if is_q:
            rQK.acquire(P, "tensor", i)
            for dc in range(8):
                P.op("tensor", lambda Eo, p=pQK[pi][:],
                     w=wqkv[:, dc * 192:dc * 192 + 64],
                     x=xT[pi][:, dc * 128:(dc + 1) * 128], st=(dc == 0), sp=(dc == 7):
                     Eo.matmul(p, w, x, start=st, stop=sp),
                     inc=(sPE if dc == 7 else None))
            P.wait("scalar", sPE)
            P.op("scalar", lambda Eo, o=qT[0:64, i * 128:(i + 1) * 128], p=pQK[pi][:]:
                 Eo.activation(o, p, AF.Identity, bias=qkb[0:64, 0:1], scale=f32(1.0)),
                 inc=sAC)
            rQK.release(i, sAC)
        else:
            rQK.acquire(P, "tensor", i)
            for dc in range(8):
                P.op("tensor", lambda Eo, p=pQK[pi][:],
                     w=wqkv[:, dc * 192 + 64:dc * 192 + 128],
                     x=xT[pi][:, dc * 128:(dc + 1) * 128], st=(dc == 0), sp=(dc == 7):
                     Eo.matmul(p, w, x, start=st, stop=sp),
                     inc=(sPE if dc == 7 else None))
            P.wait("scalar", sPE)
            P.op("scalar", lambda Eo, o=kT[0:64, kc * 128:(kc + 1) * 128], p=pQK[pi][:]:
                 Eo.activation(o, p, AF.Identity, bias=qkb[0:64, 1:2], scale=f32(1.0)),
                 inc=sAC)
            rQK.release(i, sAC)
            rV.acquire(P, "tensor", i)
            for dc in range(8):
                P.op("tensor", lambda Eo, p=pV[pi][:],
                     x=xT[pi][:, dc * 128:(dc + 1) * 128],
                     w=wqkv[:, dc * 192 + 128:dc * 192 + 192], st=(dc == 0):
                     Eo.matmul(p, x, w, start=st, stop=False))
            P.op("tensor", lambda Eo, p=pV[pi][:]:
                 Eo.matmul(p, onesb[:], bvr[:], start=False, stop=True), inc=sPE)
            P.wait("vector", sPE)
            P.wait("vector", sPL, 2)
            P.op("vector", lambda Eo, o=vtm[:, kc * 65:kc * 65 + HS], p=pV[pi][:]:
                 Eo.tensor_copy(o, p), inc=sDV)
            rV.release(i, sDV)
        rXB.rel[pi] = [(sPE, transposes_done)]
        rXT.rel[pi] = [(sPE, sPE.n)]

    for i in range(NCH):
        chunk_setup(i, d_xown[i * 128:(i + 1) * 128, :], True, None)
    for kc in range(KCH):
        chunk_setup(NCH + kc, d_xbat[kc * 128:(kc + 1) * 128, :], False, kc)

    P.wait("sync", sTAP)
    barrier(P)
    with nc.Block("b1") as blk:
        P.run(blk)
    es1.close()

    es2 = ExitStack()
    sb2 = lambda name, shape, dt: es2.enter_context(nc.sbuf_tensor(name, shape, dt))
    ps2 = lambda name, shape, dt: es2.enter_context(nc.psum_tensor(name, shape, dt))
    PT4 = [sb2("PT40", [128, 512], BF16), sb2("PT41", [128, 512], BF16)]
    ots = sb2("ots", [65, 128], F32)
    mtab = [sb2("mtab0", [128, NCH], F32), sb2("mtab1", [128, NCH], F32)]
    mfin = sb2("mfin", [128, 1], F32)
    foldv = [sb2("foldv0", [128, 1], F32), sb2("foldv1", [128, 1], F32)]
    dtmp = sb2("dtmp", [128, 1], F32)
    rcp = sb2("rcp", [128, 1], F32)
    pSA = [ps2("pSA0", [128, 512], F32), ps2("pSA1", [128, 512], F32)]
    pSk = [ps2("pSk0", [128, 512], F32), ps2("pSk1", [128, 512], F32)]
    pO = ps2("pO", [65, 128], F32)
    pOt = ps2("pOt", [128, 65], F32)

    P = Prog()
    for _eng in ENGINES:
        P.wait(_eng, sC1, ld_consts)
        P.wait(_eng, sC2, wl_consts)
    rSA = Rot(2)
    rSK = Rot(2)
    rPT4 = Rot(2)
    rPO = Rot(1)
    rMT = Rot(2)
    fold_done = {}

    def pass_a(j):
        par2 = j % 2
        rMT.acquire(P, "vector", j)
        for kb in range(j + 1):
            par = kb % 2
            rSA.acquire(P, "tensor", kb)
            P.op("tensor", lambda Eo, p=pSA[par][:],
                 q=qT[0:64, j * 128:(j + 1) * 128],
                 k=kT[0:64, kb * 512:(kb + 1) * 512]:
                 Eo.matmul(p, q, k, start=True, stop=True), inc=sPE)
            P.wait("vector", sPE)
            if kb == j:
                P.eop("vector", lambda Eo, p=pSA[par][:]:
                      Eo.tensor_add(p, p, maskadd[:]), sDV)
            P.op("vector", lambda Eo, o=mtab[par2][:, kb:kb + 1], p=pSA[par][:]:
                 Eo.reduce_max(o, p, axis=AX.X), inc=sDV)
            rSA.release(kb, sDV)
        P.wait("vector", sDV)
        P.eop("vector", lambda Eo, mt=mtab[par2][:, 0:j + 1]:
              Eo.reduce_max(mfin[:], mt, axis=AX.X), sDV)
        dFD.wait(P, "vector", j)
        P.eop("vector", lambda Eo, fv=foldv[j % 2][:]:
              Eo.tensor_scalar(fv, mfin[:], c_neg1, None, op0=ALU.mult), sDV)
        dFD.wait(P, "gpsimd", j)
        P.wait("gpsimd", sDV)
        dFD.load(P, "gpsimd", j, lambda Eo, o=qT[64:65, j * 128:(j + 1) * 128],
                 fv=foldv[j % 2][:]: Eo.dma_start(out=o, in_=fv))
        fold_done[j] = None

    def pass_b(j):
        par2 = j % 2
        ngrp = j + 1
        dFD.wait(P, "tensor", j)

        def b_mms(kg):
            par = kg % 2
            rSK.acquire(P, "tensor", kg)
            for t in range(4):
                kk = 4 * kg + t
                P.op("tensor", lambda Eo, p=pSk[par][:, t * 128:(t + 1) * 128],
                     k=kT[:, kk * 128:(kk + 1) * 128],
                     q=qT[:, j * 128:(j + 1) * 128]:
                     Eo.matmul(p, k, q, start=True, stop=True),
                     inc=(sPE if t == 3 else None))
            return sPE.n

        mm_done = {0: b_mms(0)}
        for kg in range(ngrp):
            par = kg % 2
            if kg + 1 < ngrp:
                mm_done[kg + 1] = b_mms(kg + 1)
            P.wait("scalar", sPE, mm_done[kg])
            rPT4.acquire(P, "scalar", kg)
            P.op("scalar", lambda Eo, o=PT4[par][:], p=pSk[par][:]:
                 Eo.activation(o, p, AF.Exp, scale=f32(1.0 / 32.0)), inc=sAC)
            rSK.release(kg, sAC)
            if kg == j:
                P.wait("vector", sAC)
                for t in range(4):
                    P.op("vector", lambda Eo, t=t, o=PT4[par][:, t * 128:(t + 1) * 128],
                         c=condt[:, t * 128:(t + 1) * 128]:
                         Eo.tensor_tensor(o, o, c, op=ALU.mult),
                         inc=(sDV if t == 3 else None))
                P.wait("tensor", sDV)
            else:
                P.wait("tensor", sAC)
            if kg == 0:
                rPO.acquire(P, "tensor", 0)
            for t in range(4):
                kk = 4 * kg + t
                P.op("tensor", lambda Eo, v=vtm[:, kk * 65:(kk + 1) * 65],
                     pp=PT4[par][:, t * 128:(t + 1) * 128],
                     st=(kg == 0 and t == 0), sp=(kg == j and t == 3):
                     Eo.matmul(pO[:], v, pp, start=st, stop=sp),
                     inc=(sPE if t == 3 else None))
            rPT4.release(kg, sPE)
        P.wait("scalar", sPE)
        P.op("scalar", lambda Eo: Eo.activation(ots[:], pO[:], AF.Copy), inc=sAC)
        rPO.rel[0] = [(sAC, sAC.n)]
        P.wait("tensor", sAC)
        P.op("tensor", lambda Eo: Eo.transpose(pOt[:], ots[:], idf[0:65, 0:65]), inc=sPE)
        P.wait("vector", sPE)
        P.eop("vector", lambda Eo: Eo.tensor_copy(dtmp[:], pOt[:, 64:65]), sDV)
        P.eop("vector", lambda Eo: Eo.reciprocal(rcp[:], dtmp[:]), sDV)
        P.op("vector", lambda Eo, o=o_tm[:, j * HS:(j + 1) * HS]:
             Eo.tensor_scalar(o, pOt[:, 0:HS], rcp[:, 0:1], None, op0=ALU.mult), inc=sDV)
        P.wait("tensor", sDV)
        P.wait("scalar", sDV)

    for j in range(NCH):
        pass_a(j)
        if j > 0:
            pass_b(j - 1)
    pass_b(NCH - 1)

    if "o_tm" in taps:
        P.wait("sync", sDV)
        P.op("sync", lambda Eo: Eo.dma_start(out=taps["o_tm"], in_=o_tm[:]),
             inc=sTAP, inc_n=16)
    if "condt" in taps:
        P.wait("gpsimd", sDV)
        P.op("gpsimd", lambda Eo: Eo.dma_start(out=taps["condt"], in_=condt[:]),
             inc=sTAP, inc_n=16)
    if "vtm" in taps:
        P.wait("gpsimd", sDV)
        P.op("gpsimd", lambda Eo: Eo.dma_start(out=taps["vtm"], in_=vtm[:]),
             inc=sTAP, inc_n=16)
    if "qTe" in taps:
        P.wait("gpsimd", sDV)
        P.op("gpsimd", lambda Eo: Eo.dma_start(out=taps["qTe"], in_=qT[:]),
             inc=sTAP, inc_n=16)
    if "qT" in taps:
        P.wait("gpsimd", sDV)
        P.op("gpsimd", lambda Eo: Eo.dma_start(out=taps["qT"], in_=qT[:]),
             inc=sTAP, inc_n=16)
    if "kT" in taps:
        P.wait("gpsimd", sDV)
        P.op("gpsimd", lambda Eo: Eo.dma_start(out=taps["kT"], in_=kT[:]),
             inc=sTAP, inc_n=16)
    if "vtm" in taps:
        P.wait("gpsimd", sDV)
        P.op("gpsimd", lambda Eo: Eo.dma_start(out=taps["vtm"], in_=vtm[:]),
             inc=sTAP, inc_n=16)
    P.wait("sync", sTAP)
    P.wait("gpsimd", sTAP)
    barrier(P)
    with nc.Block("b2") as blk:
        P.run(blk)
    es2.close()

    es3 = ExitStack()
    sb3 = lambda name, shape, dt: es3.enter_context(nc.sbuf_tensor(name, shape, dt))
    ps3 = lambda name, shape, dt: es3.enter_context(nc.psum_tensor(name, shape, dt))
    zch = [sb3("zch%d" % i, [128, C], F32) for i in range(3)]
    sqb = [sb3("sqb%d" % i, [128, C], F32) for i in range(3)]
    tmp4 = [sb3("tmp4%d" % i, [128, 512], F32) for i in range(3)]
    logits = sb3("logits", [E, TOK], F32)
    gtm = sb3("gtm", [128, 8], F32)
    gates_tm = sb3("gates_tm", [128, 128], F32)
    muv = [sb3("mu%d" % i, [128, 1], F32) for i in range(3)]
    varv_ = [sb3("varv%d" % i, [128, 1], F32) for i in range(3)]
    lnvv_ = [sb3("lnvv%d" % i, [128, 1], F32) for i in range(3)]
    rstdv = [sb3("rstd%d" % i, [128, 1], F32) for i in range(3)]
    eq1 = sb3("eq1", [128, 8], F32)
    eq2 = sb3("eq2", [128, 8], F32)
    mskl = sb3("mskl", [128, 8], F32)
    m1v = sb3("m1v", [128, 1], F32)
    m2v = sb3("m2v", [128, 1], F32)
    edv = sb3("edv", [128, 1], F32)
    edv2 = sb3("edv2", [128, 1], F32)
    s1v = sb3("s1v", [128, 1], F32)
    rv = sb3("rv", [128, 1], F32)
    w2v = sb3("w2v", [128, 1], F32)
    g1v = sb3("g1v", [128, 8], F32)
    y1bch = [sb3("y1bch%d" % i, [128, C], BF16) for i in range(3)]
    ones8 = sb3("ones8", [E, TOK], F32)
    mask_fm = sb3("mask_fm", [E, TOK], F32)
    pos_fm = sb3("pos_fm", [E, TOK], F32)
    w1m_fm = sb3("w1m_fm", [E, TOK], F32)
    slot_fm = sb3("slot_fm", [E, TOK], F32)
    slots_tm = [sb3("slots_tm%d" % i, [128, 8], F32) for i in range(2)]
    sidx_i = [sb3("sidx_i%d" % i, [128, 8], mybir.dt.int32) for i in range(2)]
    tokcol = [sb3("tokcol%d" % i, [128, 1], F32) for i in range(2)]
    pft = sb3("pft", [E, CAP], F32)
    pTf = [ps3("pTf%d" % i, [128, 512], F32) for i in range(3)]
    pLr = [ps3("pLr0", [E, 128], F32), ps3("pLr1", [E, 128], F32)]
    pG8 = ps3("pG8", [128, 16], F32)
    pTg = ps3("pTg", [E, 256], F32)

    P = Prog()
    for _eng in ENGINES:
        P.wait(_eng, sC1, ld_consts)
        P.wait(_eng, sC2, wl_consts)
    rZ = Rot(3)
    rSQ = Rot(3)
    rMU = Rot(3)
    rRS = Rot(3)
    rTF = Rot(3)
    rT4 = Rot(3)
    rPL = Rot(2)

    for tc in range(NCH):
        par = tc % 3
        z = zch[par]
        rZ.acquire(P, "sync", tc)
        dZC.load(P, "sync", tc, lambda Eo, o=z[:], s=d_xown[tc * 128:(tc + 1) * 128, :]:
                 Eo.dma_start(out=o, in_=s))
        dZC.wait(P, "gpsimd", tc)
        orep = o_tm[:, tc * HS:(tc + 1) * HS].unsqueeze(1).broadcast_to((128, 16, HS))
        P.op("gpsimd", lambda Eo, z_=z[:], r=orep:
             Eo.tensor_add(z_.rearrange("p (r h) -> p r h", r=16), z_.rearrange("p (r h) -> p r h", r=16), r), inc=sPL)
        rMU.acquire(P, "vector", tc)
        mu = muv[par]
        P.wait("vector", sPL)
        P.eop("vector", lambda Eo, z_=z[:], m=mu[:]: Eo.reduce_sum(m, z_, axis=AX.X), sDV)
        P.eop("vector", lambda Eo, m=mu[:]:
              Eo.tensor_scalar(m, m, c_inv1024, None, op0=ALU.mult), sDV)
        P.wait("scalar", sDV)
        rSQ.acquire(P, "scalar", tc)
        P.op("scalar", lambda Eo, z_=z[:], m=mu[:], q=sqb[par][:]:
             Eo.activation(q, z_, AF.Square, bias=m, scale=f32(-1.0)), inc=sAC)
        P.wait("vector", sAC)
        varv = varv_[par]
        lnvv = lnvv_[par]
        P.eop("vector", lambda Eo, vv=varv[:], q=sqb[par][:]: Eo.reduce_sum(vv, q, axis=AX.X), sDV)
        P.op("vector", lambda Eo, vv=varv[:]:
             Eo.tensor_scalar(vv, vv, c_inv1024, None, op0=ALU.mult), inc=sDV)
        rSQ.rel[par] = [(sDV, sDV.n)]
        P.wait("scalar", sDV)
        rRS.acquire(P, "scalar", tc)
        rstd = rstdv[par]
        P.eop("scalar", lambda Eo, lv=lnvv[:], vv=varv[:]: Eo.activation(lv, vv, AF.Ln, bias=f32(EPS)), sAC)
        P.op("scalar", lambda Eo, r=rstd[:], lv=lnvv[:]:
             Eo.activation(r, lv, AF.Exp, scale=f32(-0.5)), inc=sAC)
        P.wait("vector", sAC)
        P.op("vector", lambda Eo, z_=z[:], m=mu[:], r=rstd[:]:
             Eo.tensor_scalar(z_, z_, m, r, op0=ALU.subtract, op1=ALU.mult), inc=sDV)
        rMU.release(tc, sDV)
        rRS.release(tc, sDV)
        y1_done = sDV.n
        P.wait("sync", sDV, y1_done)
        dSPp.load(P, "sync", tc, lambda Eo, o=d_y1sp[tc * 128:(tc + 1) * 128, :], z_=z[:]:
                  Eo.dma_start(out=o, in_=z_))
        P.wait("gpsimd", sDV, y1_done)
        dSPb.wait(P, "gpsimd", tc)
        P.op("gpsimd", lambda Eo, o=y1bch[par][:], z_=z[:]: Eo.tensor_copy(o, z_), inc=sPL)
        P.wait("sync", sPL)
        dSPb.load(P, "sync", tc, lambda Eo, o=d_y1bf[tc * 128:(tc + 1) * 128, :],
                  a=y1bch[par][:]: Eo.dma_start(out=o, in_=a))
        P.wait("tensor", sDV, y1_done)
        for g in range(2):
            tfp = pTf[(tc * 2 + g) % 3]
            rTF.acquire(P, "tensor", tc * 2 + g)
            for t in range(4):
                dc = g * 4 + t
                P.op("tensor", lambda Eo, o=tfp[:, t * 128:(t + 1) * 128],
                     a=z[:, dc * 128:(dc + 1) * 128]: Eo.transpose(o, a, idf[:]),
                     inc=(sPE if t == 3 else None))
            P.wait("scalar", sPE)
            rT4.acquire(P, "scalar", tc * 2 + g)
            tm4 = tmp4[(tc * 2 + g) % 3]
            P.op("scalar", lambda Eo, o=tm4[:], a=tfp[:]: Eo.activation(o, a, AF.Copy), inc=sAC)
            y1T_view = y1T[:].rearrange("p (d t) -> p d t", d=8)[:, g * 4:(g + 1) * 4,
                                                               tc * 128:(tc + 1) * 128]
            P.op("scalar", lambda Eo, o=y1T_view, a=tfp[:].rearrange("p (d t) -> p d t", d=4):
                 Eo.activation(o, a, AF.Copy), inc=sAC)
            rTF.release(tc * 2 + g, sAC)
            P.wait("tensor", sAC)
            if g == 0:
                rPL.acquire(P, "tensor", tc)
            for t in range(4):
                dc = g * 4 + t
                P.op("tensor", lambda Eo, p=pLr[tc % 2][:],
                     w=rw[:, dc * 8:(dc + 1) * 8],
                     x=tm4[:, t * 128:(t + 1) * 128], st=(dc == 0), sp=(dc == 7):
                     Eo.matmul(p, w, x, start=st, stop=sp),
                     inc=(sPE if t == 3 else None))
            rT4.release(tc * 2 + g, sPE)
        P.wait("scalar", sPE)
        P.op("scalar", lambda Eo, o=logits[:, tc * 128:(tc + 1) * 128],
             p=pLr[tc % 2][:]: Eo.activation(o, p, AF.Copy), inc=sAC)
        rPL.release(tc, sAC)
        rZ.release(tc, sPE)
        rZ.rel[par].append(dSPp.sem_count(tc))
    logits_done = sAC.n
    gate_dve_done = {}
    for tc in range(NCH):
        sl = tc % 2
        P.wait("tensor", sAC)
        if tc >= 2:
            P.wait("tensor", sDV, gate_dve_done[sl])
        P.op("tensor", lambda Eo, o=pG8[:, sl * 8:(sl + 1) * 8],
             a=logits[:, tc * 128:(tc + 1) * 128]:
             Eo.transpose(o, a, idf[0:8, 0:8]), inc=sPE)
        P.wait("vector", sPE)
        P.eop("vector", lambda Eo, p=pG8[:, sl * 8:(sl + 1) * 8]: Eo.tensor_copy(gtm[:], p), sDV)
        P.eop("vector", lambda Eo: Eo.reduce_max(m1v[:], gtm[:], axis=AX.X), sDV)
        P.eop("vector", lambda Eo: Eo.tensor_scalar(eq1[:], gtm[:], m1v[:, 0:1], None, op0=ALU.is_equal), sDV)
        P.eop("vector", lambda Eo: Eo.scalar_tensor_tensor(mskl[:], eq1[:], c_neg1e30, gtm[:],
                                                           op0=ALU.mult, op1=ALU.add), sDV)
        P.eop("vector", lambda Eo: Eo.reduce_max(m2v[:], mskl[:], axis=AX.X), sDV)
        P.eop("vector", lambda Eo: Eo.tensor_scalar(eq2[:], mskl[:], m2v[:, 0:1], None, op0=ALU.is_equal), sDV)
        P.op("vector", lambda Eo: Eo.tensor_tensor(edv2[:], m2v[:], m1v[:], op=ALU.subtract), inc=sDV)
        P.wait("scalar", sDV)
        P.op("scalar", lambda Eo: Eo.activation(edv[:], edv2[:], AF.Exp), inc=sAC)
        P.wait("vector", sAC)
        P.eop("vector", lambda Eo: Eo.tensor_scalar(s1v[:], edv[:], c_one, None, op0=ALU.add), sDV)
        P.eop("vector", lambda Eo: Eo.reciprocal(rv[:], s1v[:]), sDV)
        P.eop("vector", lambda Eo: Eo.tensor_tensor(w2v[:], edv[:], rv[:], op=ALU.mult), sDV)
        P.eop("vector", lambda Eo: Eo.tensor_scalar(g1v[:], eq1[:], rv[:, 0:1], None, op0=ALU.mult), sDV)
        P.op("vector", lambda Eo, o=gates_tm[:, tc * 8:(tc + 1) * 8]:
             Eo.scalar_tensor_tensor(o, eq2[:], w2v[:, 0:1], g1v[:], op0=ALU.mult, op1=ALU.add),
             inc=sDV)
        gate_dve_done[sl] = sDV.n
        P.wait("sync", sDV)
        P.op("sync", lambda Eo, o=d_gc[tc * 128:(tc + 1) * 128, :],
             a=gates_tm[:, tc * 8:(tc + 1) * 8]: Eo.dma_start(out=o, in_=a),
             inc=sGC, inc_n=16)
    P.wait("tensor", sDV)
    for tc in range(NCH):
        sl = tc % 2
        P.op("tensor", lambda Eo, o=pTg[:, sl * 128:(sl + 1) * 128],
             a=gates_tm[:, tc * 8:(tc + 1) * 8]:
             Eo.transpose(o, a, idf[:]), inc=sPE)
        P.wait("scalar", sPE)
        P.op("scalar", lambda Eo, o=gatesT[:, tc * 128:(tc + 1) * 128],
             p=pTg[:, sl * 128:(sl + 1) * 128]: Eo.activation(o, p, AF.Copy), inc=sAC)
        P.wait("tensor", sAC)

    P.op("gpsimd", lambda Eo: Eo.memset(ones8[:], 1.0), inc=sPL)
    P.op("gpsimd", lambda Eo: Eo.memset(pft[:], 2.0e6), inc=sPL)
    P.wait("sync", sPL)
    dPF.load(P, "sync", 0, lambda Eo: Eo.dma_start(out=d_idxf, in_=pft[:]))
    P.wait("vector", sAC)
    P.wait("vector", sPL)
    P.eop("vector", lambda Eo: Eo.tensor_scalar(mask_fm[:], gatesT[:], c_zero[0:E, :], None,
                                                op0=ALU.is_gt), sDV)
    P.eop("vector", lambda Eo: Eo.tensor_tensor_scan(pos_fm[:], ones8[:], mask_fm[:], 0.0,
                                                     op0=ALU.mult, op1=ALU.add), sDV)
    P.eop("vector", lambda Eo: Eo.tensor_scalar(pos_fm[:], pos_fm[:], c_one[0:E, :], None,
                                                op0=ALU.subtract), sDV)
    P.eop("vector", lambda Eo: Eo.tensor_scalar(w1m_fm[:], mask_fm[:], c_neg1[0:E, :],
                                                c_one[0:E, :], op0=ALU.mult, op1=ALU.add), sDV)
    P.eop("vector", lambda Eo: Eo.scalar_tensor_tensor(slot_fm[:], w1m_fm[:], c_big[0:E, :],
                                                       pos_fm[:], op0=ALU.mult, op1=ALU.add), sDV)
    slot_ready = sDV.n
    P.wait("tensor", sDV, slot_ready)
    dPF.wait(P, "gpsimd", 0)
    sct_done = {}
    for tc in range(NCH):
        sl = tc % 2
        P.op("tensor", lambda Eo, o=pG8[:, sl * 8:(sl + 1) * 8],
             a=slot_fm[:, tc * 128:(tc + 1) * 128]:
             Eo.transpose(o, a, idf[0:8, 0:8]), inc=sPE)
        P.wait("vector", sPE)
        P.eop("vector", lambda Eo, o=slots_tm[sl][:], p=pG8[:, sl * 8:(sl + 1) * 8]:
              Eo.tensor_copy(o, p), sDV)
        if tc >= 2:
            P.wait("vector", sSCTp[sl].h and sSCTp[sl], sct_done[sl]) if False else None
            P.wait("vector", sSCTp[sl], sct_done[sl])
        P.eop("vector", lambda Eo, o=sidx_i[sl][:], s_=slots_tm[sl][:]:
              Eo.tensor_copy(o, s_), sDV)
        P.eop("vector", lambda Eo, o=tokcol[sl][:], s_=tokid[:, tc:tc + 1]:
              Eo.tensor_copy(o, s_), sDV)
        P.wait("gpsimd", sDV)
        P._mark_ind = True
        for e in range(E):
            P.op("gpsimd", lambda Eo, e=e, sl=sl:
                 Eo.indirect_dma_start(
                     out=d_idxf.rearrange("e (s o) -> (e s) o", o=1),
                     out_offset=IndirectOffsetOnAxis(ap=sidx_i[sl][:, e:e + 1], axis=0),
                     in_=tokcol[sl][:], in_offset=None,
                     element_offset=e * CAP,
                     bounds_check=regB_cap, oob_is_err=False),
                 inc=sSCTp[sl], inc_n=16)
        P._mark_ind = False
        sct_done[sl] = sSCTp[sl].n
        P.wait("tensor", sDV)

    if "logits" in taps:
        P.wait("sync", sAC, logits_done)
        P.op("sync", lambda Eo: Eo.dma_start(out=taps["logits"], in_=logits[:]),
             inc=sTAP, inc_n=16)
    if "gates" in taps:
        P.wait("sync", sDV)
        P.op("sync", lambda Eo: Eo.dma_start(out=taps["gates"], in_=gates_tm[:]),
             inc=sTAP, inc_n=16)
    if "y1T" in taps:
        P.wait("gpsimd", sAC)
        P.op("gpsimd", lambda Eo: Eo.dma_start(out=taps["y1T"], in_=y1T[:]),
             inc=sTAP, inc_n=16)
    if "gatesT" in taps:
        P.wait("gpsimd", sAC)
        P.op("gpsimd", lambda Eo: Eo.dma_start(out=taps["gatesT"], in_=gatesT[:]),
             inc=sTAP, inc_n=16)
    if "y1" in taps:
        for s_ in dSPp.sems:
            P.wait("sync", s_)
        P.op("sync", lambda Eo: Eo.dma_start(out=taps["y1"], in_=d_y1sp), inc=sTAP, inc_n=16)
    for s_ in dSPp.sems:
        P.wait("sync", s_)
    P.wait("sync", sTAP)
    barrier(P)
    with nc.Block("b3") as blk:
        P.run(blk)
    es3.close()
    esA.close()

    es4 = ExitStack()
    sb4 = lambda name, shape, dt: es4.enter_context(nc.sbuf_tensor(name, shape, dt))
    ps4 = lambda name, shape, dt: es4.enter_context(nc.psum_tensor(name, shape, dt))
    H = sb4("H", [128, 32 * CAP], BF16)
    y1gT = sb4("y1gT", [128, 8 * CAP], BF16)
    o_sb = sb4("o_sb", [128, 8 * CAP], F32)
    w1b = [sb4("w1b%d" % i, [128, 8 * 256], BF16) for i in range(2)]
    w2b = [sb4("w2b%d" % i, [128, 16 * 128], BF16) for i in range(2)]
    gbuf = [sb4("gbuf%d" % i, [128, C], BF16) for i in range(2)]
    scatb = [sb4("scatb%d" % i, [128, C], F32) for i in range(2)]
    idxf_s = [sb4("idxf_s%d" % i, [128, NG], F32) for i in range(2)]
    idxi_s = [sb4("idxi_s%d" % i, [128, NG], mybir.dt.int32) for i in range(2)]
    gvif = sb4("gvif", [128, NG], F32)
    gvii = [sb4("gvii%d" % i, [128, NG], mybir.dt.int32) for i in range(2)]
    gv = [sb4("gv%d" % i, [128, NG], F32) for i in range(2)]
    pH = [ps4("pH%d" % i, [128, 1024], F32) for i in range(2)]
    pTin = ps4("pTin", [128, 512], BF16)
    pTout = ps4("pTout", [128, 512], F32)

    P = Prog()
    w1_view = d_w1
    w2_view = d_w2
    for _eng in ENGINES:
        P.wait(_eng, sC1, ld_consts)
        P.wait(_eng, sC2, wl_consts)
    for _eng in ("gpsimd", "sync"):
        P.wait(_eng, sSCTp[0])
        P.wait(_eng, sSCTp[1])
        P.wait(_eng, sGC)
        for s_ in dSPb.sems:
            P.wait(_eng, s_)
        for s_ in dSPp.sems:
            P.wait(_eng, s_)
        P.wait(_eng, dPF.sems[0])
    P.wait("gpsimd", sGC)
    for s_ in dSPb.sems:
        P.wait("gpsimd", s_)
    P.op("gpsimd", lambda Eo: Eo.memset(gbuf[0][:], 0.0))
    P.op("gpsimd", lambda Eo: Eo.memset(gbuf[1][:], 0.0))
    P.op("gpsimd", lambda Eo: Eo.memset(gv[0][:], 0.0))
    P.op("gpsimd", lambda Eo: Eo.memset(gv[1][:], 0.0), inc=sPL)
    P.wait("gpsimd", sPL)
    P.wait("tensor", sPL)

    rW1 = Rot(2)
    rW2 = Rot(2)
    rGB = Rot(2)
    rIX = Rot(2)
    rGV = Rot(2)
    rSCB = Rot(2)
    rPH = Rot(2)
    rTIN = Rot(1)
    rTOUT = Rot(1)
    gc_flat = d_gc.rearrange("t (e o) -> (t e) o", o=1)
    NB = ((0, 512), (512, CAP))

    for e in range(n_experts):
        ep = e % 2
        rIX.acquire(P, "sync", e)
        dIX.load(P, "sync", e, lambda Eo, o=idxf_s[ep][:],
                 s=d_idxf[e, :].rearrange("(p g) -> p g", g=NG): Eo.dma_start(out=o, in_=s))
        dIX.wait(P, "vector", e)
        P.eop("vector", lambda Eo, o=idxi_s[ep][:], a=idxf_s[ep][:]: Eo.tensor_copy(o, a), sDV)
        P.eop("vector", lambda Eo, a=idxf_s[ep][:], ev=evals[:, e:e + 1]:
              Eo.tensor_scalar(gvif[:], a, c_eight, ev, op0=ALU.mult, op1=ALU.add), sDV)
        rGV.acquire(P, "vector", e)
        P.eop("vector", lambda Eo, o=gvii[ep][:]: Eo.tensor_copy(o, gvif[:]), sDV)
        idx_ready = sDV.n
        P.wait("gpsimd", sDV, idx_ready)
        for g in range(NG):
            rGB.acquire(P, "gpsimd", e * NG + g)
            P._mark_ind = True
            dGA.load(P, "gpsimd", e * NG + g, lambda Eo, g=g, ep=ep:
                     Eo.indirect_dma_start(
                         out=gbuf[g % 2][:], out_offset=None,
                         in_=d_y1bf, in_offset=IndirectOffsetOnAxis(ap=idxi_s[ep][:, g:g + 1], axis=0),
                         bounds_check=regB_tok, oob_is_err=False))
            dGV.load(P, "gpsimd", e * NG + g, lambda Eo, g=g, ep=ep:
                     Eo.indirect_dma_start(
                         out=gv[ep][:, g:g + 1], out_offset=None,
                         in_=gc_flat, in_offset=IndirectOffsetOnAxis(ap=gvii[ep][:, g:g + 1], axis=0),
                         bounds_check=regB_gc, oob_is_err=False))
            P._mark_ind = False
            dGA.wait(P, "tensor", e * NG + g)
            rTIN.acquire(P, "tensor", 0)
            for kc in range(8):
                P.op("tensor", lambda Eo, kc=kc, g=g,
                     o=pTin[:, (kc % 4) * 128:(kc % 4 + 1) * 128],
                     a=gbuf[g % 2][:, kc * 128:(kc + 1) * 128]:
                     Eo.transpose(o, a, idb[:]),
                     inc=(sPE if kc % 4 == 3 else None))
                if kc % 4 == 3:
                    P.wait("vector", sPE)
                    kc0 = kc - 3
                    P.op("vector", lambda Eo, kc0=kc0, g=g, o4=[
                         y1gT[:, (kc0 + i) * CAP + g * 128:(kc0 + i) * CAP + (g + 1) * 128]
                         for i in range(4)], pt=pTin[:]:
                         [Eo.tensor_copy(o4[i], pt[:, i * 128:(i + 1) * 128]) for i in range(4)][-1],
                         inc=sDV)
                    if kc == 3:
                        rTIN.rel[0] = [(sDV, sDV.n)]
                        rTIN.acquire(P, "tensor", 0)
                    else:
                        rTIN.rel[0] = [(sDV, sDV.n)]
            rGB.release(e * NG + g, sPE)
        in_done = sDV.n
        P.wait("tensor", sDV, in_done)
        for fp in range(16):
            wpar = (e * 16 + fp) % 2
            rW1.acquire(P, "gpsimd", e * 16 + fp)
            srcw = w1_view[e, :, fp * 256:(fp + 1) * 256].rearrange("(kc p) f -> p kc f", p=128)
            dW1.load(P, "gpsimd", e * 16 + fp,
                     lambda Eo, o=w1b[wpar][:].rearrange("p (kc f) -> p kc f", kc=8),
                     s=srcw: Eo.dma_start(out=o, in_=s))
            dW1.wait(P, "tensor", e * 16 + fp)
            for fc in range(2):
                fcg = fp * 2 + fc
                hpar = fcg % 2
                rPH.acquire(P, "tensor", fcg)
                for kc in range(8):
                    for nbi, (lo, hi) in enumerate(NB):
                        P.op("tensor", lambda Eo, p=pH[hpar][:, lo:hi],
                             w=w1b[wpar][:, kc * 256 + fc * 128:kc * 256 + (fc + 1) * 128],
                             x=y1gT[:, kc * CAP + lo:kc * CAP + hi],
                             st=(kc == 0), sp=(kc == 7):
                             Eo.matmul(p, w, x, start=st, stop=sp),
                             inc=(sPE if (nbi == 1 and kc == 7) else None))
                P.wait("scalar", sPE)
                hsl = H[:, fcg * CAP:(fcg + 1) * CAP]
                P.op("scalar", lambda Eo, o=hsl, p=pH[hpar][:, 0:CAP],
                     b=b1t[:, e * 32 + fcg:e * 32 + fcg + 1]:
                     Eo.activation(o, p, AF.Gelu_apprx_tanh, bias=b, scale=f32(1.0)),
                     inc=sAC)
                rPH.release(fcg, sAC)
            rW1.release(e * 16 + fp, sPE)
        h_done = sAC.n
        P.wait("tensor", sAC, h_done)
        for dc in range(8):
            opar = dc % 2
            for fh in range(2):
                w2i = (e * 16 + dc * 2 + fh) % 2
                rW2.acquire(P, "gpsimd", e * 16 + dc * 2 + fh)
                srcw = w2_view[e, fh * 2048:(fh + 1) * 2048,
                               dc * 128:(dc + 1) * 128].rearrange("(fc p) d -> p fc d", p=128)
                dW2.load(P, "gpsimd", e * 16 + dc * 2 + fh,
                         lambda Eo, o=w2b[w2i][:].rearrange("p (fc d) -> p fc d", fc=16),
                         s=srcw: Eo.dma_start(out=o, in_=s))
                dW2.wait(P, "tensor", e * 16 + dc * 2 + fh)
                if fh == 0:
                    rPH.acquire(P, "tensor", dc)
                for fcL in range(16):
                    fcg = fh * 16 + fcL
                    for nbi, (lo, hi) in enumerate(NB):
                        P.op("tensor", lambda Eo, p=pH[opar][:, lo:hi],
                             w=w2b[w2i][:, fcL * 128:(fcL + 1) * 128],
                             x=H[:, fcg * CAP + lo:fcg * CAP + hi],
                             st=(fh == 0 and fcL == 0), sp=(fh == 1 and fcL == 15):
                             Eo.matmul(p, w, x, start=st, stop=sp),
                             inc=(sPE if (nbi == 1 and fcL == 15) else None))
                rW2.release(e * 16 + dc * 2 + fh, sPE)
            P.wait("vector", sPE)
            P.op("vector", lambda Eo, o=o_sb[:, dc * CAP:(dc + 1) * CAP],
                 p=pH[opar][:, 0:CAP], b=b2ts[:, dc * 8 + e:dc * 8 + e + 1]:
                 Eo.tensor_scalar(o, p, b, None, op0=ALU.add), inc=sDV)
            rPH.release(dc, sDV)
        o_done = sDV.n
        P.wait("tensor", sDV, o_done)
        dGV.wait(P, "vector", e * NG)
        for g in range(NG):
            dGV.wait(P, "vector", e * NG + g)
            rSCB.acquire(P, "tensor", e * NG + g)
            rTOUT.acquire(P, "tensor", 0)
            for dc in range(8):
                P.op("tensor", lambda Eo, dc=dc, g=g,
                     o=pTout[:, (dc % 4) * 128:(dc % 4 + 1) * 128],
                     a=o_sb[:, dc * CAP + g * 128:dc * CAP + (g + 1) * 128]:
                     Eo.transpose(o, a, idf[:]),
                     inc=(sPE if dc % 4 == 3 else None))
                if dc % 4 == 3:
                    P.wait("vector", sPE)
                    dc0 = dc - 3
                    for i in range(4):
                        P.op("vector", lambda Eo, i=i, dc0=dc0, g=g,
                             o=scatb[g % 2][:, (dc0 + i) * 128:(dc0 + i + 1) * 128],
                             p=pTout[:, i * 128:(i + 1) * 128],
                             gvc=gv[ep][:, g:g + 1]:
                             Eo.tensor_scalar(o, p, gvc, None, op0=ALU.mult),
                             inc=(sDV if i == 3 else None))
                    rTOUT.rel[0] = [(sDV, sDV.n)]
                    if dc == 3:
                        rTOUT.acquire(P, "tensor", 0)
            P.wait("gpsimd", sDV)
            P.wait("gpsimd", sSCA)
            P.op("gpsimd", lambda Eo, g=g, ep=ep:
                 Eo.indirect_dma_start(
                     out=d_y1sp, out_offset=IndirectOffsetOnAxis(ap=idxi_s[ep][:, g:g + 1], axis=0),
                     in_=scatb[g % 2][:], in_offset=None,
                     bounds_check=regB_tok, oob_is_err=False,
                     compute_op=ALU.add),
                 inc=sSCA, inc_n=16)
            rSCB.release(e * NG + g, sSCA)
        rIX.release(e, sSCA)
        rGV.release(e, sSCA)

    if "O_acc" in taps:
        P.wait("sync", sSCA)
        P.op("sync", lambda Eo: Eo.dma_start(out=taps["O_acc"], in_=o_sb[:]),
             inc=sTAP, inc_n=16)
    P.wait("sync", sTAP)
    barrier(P)
    with nc.Block("b4") as blk:
        P.run(blk)
    es4.close()

    es5 = ExitStack()
    sb5 = lambda name, shape, dt: es5.enter_context(nc.sbuf_tensor(name, shape, dt))
    z2ch = [sb5("z2ch%d" % i, [128, C], F32) for i in range(3)]
    sq2 = [sb5("sq2%d" % i, [128, C], F32) for i in range(3)]
    mu2 = [sb5("mu2%d" % i, [128, 1], F32) for i in range(3)]
    var2_ = [sb5("var2%d" % i, [128, 1], F32) for i in range(3)]
    lnv2_ = [sb5("lnv2%d" % i, [128, 1], F32) for i in range(3)]
    rstd2 = [sb5("rstd2%d" % i, [128, 1], F32) for i in range(3)]

    P = Prog()
    for _eng in ENGINES:
        P.wait(_eng, sC1, ld_consts)
        P.wait(_eng, sC2, wl_consts)
    P.wait("sync", sSCA)
    rZ2 = Rot(3)
    rSQ2 = Rot(3)
    rMU2 = Rot(3)
    rRS2 = Rot(3)
    for tc in range(NCH):
        par = tc % 3
        z2 = z2ch[par]
        rZ2.acquire(P, "sync", tc)
        dYC.load(P, "sync", tc, lambda Eo, o=z2[:], s=d_y1sp[tc * 128:(tc + 1) * 128, :]:
                 Eo.dma_start(out=o, in_=s))
        dYC.wait(P, "vector", tc)
        rMU2.acquire(P, "vector", tc)
        mu = mu2[par]
        P.eop("vector", lambda Eo, m=mu[:], z_=z2[:]: Eo.reduce_sum(m, z_, axis=AX.X), sDV)
        P.eop("vector", lambda Eo, m=mu[:]:
              Eo.tensor_scalar(m, m, c_inv1024, None, op0=ALU.mult), sDV)
        P.wait("scalar", sDV)
        rSQ2.acquire(P, "scalar", tc)
        P.op("scalar", lambda Eo, q=sq2[par][:], z_=z2[:], m=mu[:]:
             Eo.activation(q, z_, AF.Square, bias=m, scale=f32(-1.0)), inc=sAC)
        P.wait("vector", sAC)
        var2 = var2_[par]
        lnv2 = lnv2_[par]
        P.eop("vector", lambda Eo, vv=var2[:], q=sq2[par][:]: Eo.reduce_sum(vv, q, axis=AX.X), sDV)
        P.op("vector", lambda Eo, vv=var2[:]:
             Eo.tensor_scalar(vv, vv, c_inv1024, None, op0=ALU.mult), inc=sDV)
        rSQ2.rel[par] = [(sDV, sDV.n)]
        P.wait("scalar", sDV)
        rRS2.acquire(P, "scalar", tc)
        P.eop("scalar", lambda Eo, lv=lnv2[:], vv=var2[:]: Eo.activation(lv, vv, AF.Ln, bias=f32(EPS)), sAC)
        P.op("scalar", lambda Eo, r=rstd2[par][:], lv=lnv2[:]:
             Eo.activation(r, lv, AF.Exp, scale=f32(-0.5)), inc=sAC)
        P.wait("vector", sAC)
        P.op("vector", lambda Eo, z_=z2[:], m=mu[:], r=rstd2[par][:]:
             Eo.tensor_scalar(z_, z_, m, r, op0=ALU.subtract, op1=ALU.mult), inc=sDV)
        rMU2.release(tc, sDV)
        rRS2.release(tc, sDV)
        P.wait("sync", sDV)
        dSTp.load(P, "sync", tc, lambda Eo, o=d_y2[tc * 128:(tc + 1) * 128, :], z_=z2[:]:
                  Eo.dma_start(out=o, in_=z_))
        rZ2.rel[par] = [dSTp.sem_count(tc)]
    for s_ in dSTp.sems:
        P.wait("sync", s_)
    P.wait("sync", sTAP)
    with nc.Block("b5") as blk:
        P.run(blk)
    es5.close()
    esP.close()
    esS.close()
    return nc


_CACHE = {}


def kernel(L0_attn_x, L0_attn_Wq, L0_attn_bq, L0_attn_Wk, L0_attn_bk,
           L0_attn_Wv, L0_attn_bv, L0_ffn_router_w, L0_ffn_w1, L0_ffn_b1,
           L0_ffn_w2, L0_ffn_b2):
    x = np.ascontiguousarray(np.asarray(L0_attn_x, dtype=np.float32))
    Wq = np.asarray(L0_attn_Wq, np.float32)
    Wk = np.asarray(L0_attn_Wk, np.float32)
    Wv = np.asarray(L0_attn_Wv, np.float32)
    bq = np.asarray(L0_attn_bq, np.float32)
    bk = np.asarray(L0_attn_bk, np.float32)
    bv = np.asarray(L0_attn_bv, np.float32)
    rw_in = np.asarray(L0_ffn_router_w, np.float32)
    w1 = np.ascontiguousarray(np.asarray(L0_ffn_w1, np.float32))
    b1 = np.asarray(L0_ffn_b1, np.float32)
    w2 = np.ascontiguousarray(np.asarray(L0_ffn_w2, np.float32))
    b2 = np.asarray(L0_ffn_b2, np.float32)

    wqkvT = np.concatenate([Wq.T, Wk.T, Wv.T], axis=1)
    wqkvT = np.ascontiguousarray(wqkvT.reshape(8, 128, 192).transpose(1, 0, 2))
    qkb = np.zeros((128, 2), np.float32)
    qkb[0:64, 0] = bq
    qkb[0:64, 1] = bk
    bvrow = bv.reshape(1, HS)
    b1t = np.ascontiguousarray(
        b1.reshape(E, 32, 128).transpose(2, 0, 1))
    rw_pre = np.ascontiguousarray(rw_in.reshape(8, 128, E).transpose(1, 0, 2))
    identb = np.eye(128, dtype=bf16)
    identf = np.eye(128, dtype=np.float32)
    qpiota = (np.arange(128, dtype=np.float32)[:, None]
              - np.arange(512, dtype=np.float32)[None, :])
    onesb = np.ones((1, 128), dtype=bf16)
    consts = np.zeros((128, 8), np.float32)
    consts[:, C_INV1024] = 1.0 / 1024.0
    consts[:, C_NEG132] = -1.0 / 32.0
    consts[:, C_ONE] = 1.0
    consts[:, C_NEG1E30] = -1e30
    consts[:, C_NEG1] = -1.0
    selb = np.zeros((E, E * 128), dtype=bf16)
    for e in range(E):
        selb[e, e * 128:(e + 1) * 128] = 1
    tokid = (np.arange(NCH, dtype=np.float32)[None, :] * 128.0
             + np.arange(128, dtype=np.float32)[:, None])
    evals = np.tile(np.arange(E, dtype=np.float32)[None, :], (128, 1))
    consts[:, C_BIG] = 1.0e6
    consts[:, C_EIGHT] = 8.0

    nc = build_nc()
    in_maps = []
    for c in range(NCORES):
        b = c // 4
        ph = c % 4
        chunks = [4 * j + ph for j in range(NCH)]
        x_b = x[b]
        x_own = np.ascontiguousarray(
            x_b.reshape(64, 128, C)[chunks].reshape(TOK, C))
        qoffb = np.full((128, 1), 128.0 * ph, np.float32)
        mth = np.zeros((128, 4), np.float32)
        for t in range(4):
            mth[:, t] = (t - ph) * 128.0
        in_maps.append(dict(
            x_own=x_own, x_batch=x_b, wqkvT=wqkvT, qkb=qkb, bvrow=bvrow,
            b1t=b1t, b2w=b2, rw=rw_pre, w1=w1, w2=w2, identb=identb,
            identf=identf, qpiota=qpiota, onesb=onesb, qoffb=qoffb,
            consts=consts, selb=selb, mth=mth, tokid=tokid, evals=evals,
            b2t=np.ascontiguousarray(b2.T.reshape(8, 128, E).transpose(1, 0, 2))))
    res = run_bass_kernel_spmd(nc, in_maps, core_ids=list(range(NCORES)))
    out = np.zeros((B, T, C), np.float32)
    for c in range(NCORES):
        b = c // 4
        ph = c % 4
        y2 = np.asarray(res.results[c]["y2"]).reshape(NCH, 128, C)
        for j in range(NCH):
            out[b, (4 * j + ph) * 128:(4 * j + ph + 1) * 128, :] = y2[j]
    return out


# version 12
# speedup vs baseline: 1.3236x; 1.0332x over previous
import numpy as np
import ml_dtypes
from contextlib import ExitStack

import concourse.bass as bass
import concourse.mybir as mybir
from concourse.bass_utils import run_bass_kernel_spmd
from concourse.bass_core_rust import IndirectOffsetOnAxis

F32 = mybir.dt.float32
BF16 = mybir.dt.bfloat16
AF = mybir.ActivationFunctionType
ALU = mybir.AluOpType
AX = mybir.AxisListType
f32 = np.float32
bf16 = ml_dtypes.bfloat16

B, T, C = 2, 8192, 1024
HS = 64
E, FF = 8, 4096
NCORES = 8
TOK = 2048
NCH = 16
KCH = 64
NT = 1024
EPS = 1e-5

C_INV1024, C_NEG132, C_ONE, C_NEG1E30, C_ZERO, C_NEG1, C_BIG, C_EIGHT, C_CAP = 0, 1, 2, 3, 4, 5, 6, 7, 8
CAP, NG = 768, 6

ENGINES = ("sync", "scalar", "vector", "tensor", "gpsimd")


class Sem:
    def __init__(self, nc, es, name):
        self.h = es.enter_context(nc.semaphore(name))
        self.n = 0

    def bump(self, k):
        self.n += k
        return self.n


class Prog:

    def __init__(self):
        self.streams = {e: [] for e in ENGINES}

    def op(self, eng, fn, inc=None, inc_n=1):
        if inc is not None:
            inc.bump(inc_n)
            self.streams[eng].append(
                lambda Eo, fn=fn, s=inc.h, n=inc_n: fn(Eo).then_inc(s, n))
            if (PARANOID[0] is True and eng == "gpsimd" and inc_n == 16) or \
               (PARANOID[0] == "ind" and eng == "gpsimd" and inc_n == 16 and getattr(self, "_mark_ind", False)):
                self.wait(eng, inc)
        else:
            self.streams[eng].append(lambda Eo, fn=fn: fn(Eo))

    def eop(self, eng, fn, sem):
        self.op(eng, fn, inc=sem)
        self.wait(eng, sem)

    def wait(self, eng, sem, val=None):
        v = sem.n if val is None else val
        if v > 0:
            self.streams[eng].append(lambda Eo, s=sem.h, v=v: Eo.wait_ge(s, v))

    def run(self, block):
        for eng in ENGINES:
            fns = tuple(self.streams[eng])

            def body(Eo, fns=fns):
                for f in fns:
                    f(Eo)

            getattr(block, eng)(body)


class Rot:

    def __init__(self, n):
        self.n = n
        self.rel = [[] for _ in range(n)]

    def acquire(self, P, eng, i):
        for sem, cnt in self.rel[i % self.n]:
            P.wait(eng, sem, cnt)
        self.rel[i % self.n] = []

    def release(self, i, sem):
        self.rel[i % self.n].append((sem, sem.n))


class DmaSlots:

    def __init__(self, nc, es, name, n=2):
        self.sems = [Sem(nc, es, "%s%d" % (name, i)) for i in range(n)]
        self.n = n

    def load(self, P, eng, i, fn):
        P.op(eng, fn, inc=self.sems[i % self.n], inc_n=16)

    def wait(self, P, eng, i):
        P.wait(eng, self.sems[i % self.n])

    def sem_count(self, i):
        s = self.sems[i % self.n]
        return (s, s.n)


PARANOID = ["ind"]


def build_nc(debug_taps=(), n_experts=E):
    nc = bass.Bass("TRN2", target_bir_lowering=False)

    d_xown = nc.dram_tensor("x_own", [TOK, C], F32, kind="ExternalInput").ap()
    d_xbat = nc.dram_tensor("x_batch", [T, C], F32, kind="ExternalInput").ap()
    d_wqkv = nc.dram_tensor("wqkvT", [128, 8, 192], F32, kind="ExternalInput").ap()
    d_qkb = nc.dram_tensor("qkb", [128, 2], F32, kind="ExternalInput").ap()
    d_bvr = nc.dram_tensor("bvrow", [1, HS], F32, kind="ExternalInput").ap()
    d_b1t = nc.dram_tensor("b1t", [128, E, 32], F32, kind="ExternalInput").ap()
    d_b2w = nc.dram_tensor("b2w", [E, C], F32, kind="ExternalInput").ap()
    d_rw = nc.dram_tensor("rw", [128, 8, E], F32, kind="ExternalInput").ap()
    d_w1 = nc.dram_tensor("w1", [E, C, FF], F32, kind="ExternalInput").ap()
    d_w2 = nc.dram_tensor("w2", [E, FF, C], F32, kind="ExternalInput").ap()
    d_idb = nc.dram_tensor("identb", [128, 128], BF16, kind="ExternalInput").ap()
    d_idf = nc.dram_tensor("identf", [128, 128], F32, kind="ExternalInput").ap()
    d_qpi = nc.dram_tensor("qpiota", [128, 512], F32, kind="ExternalInput").ap()
    d_ones = nc.dram_tensor("onesb", [1, 128], BF16, kind="ExternalInput").ap()
    d_qoff = nc.dram_tensor("qoffb", [128, 1], F32, kind="ExternalInput").ap()
    d_cst = nc.dram_tensor("consts", [128, 12], F32, kind="ExternalInput").ap()
    d_sel = nc.dram_tensor("selb", [E, E * 128], BF16, kind="ExternalInput").ap()
    d_mth = nc.dram_tensor("mth", [128, 4], F32, kind="ExternalInput").ap()
    d_tokid = nc.dram_tensor("tokid", [128, NCH], F32, kind="ExternalInput").ap()
    d_evals = nc.dram_tensor("evals", [128, E], F32, kind="ExternalInput").ap()
    d_b2t = nc.dram_tensor("b2t", [128, 8, E], F32, kind="ExternalInput").ap()
    d_y1bf = nc.dram_tensor("y1bf", [TOK, C + 8], BF16, kind="Internal").ap()
    d_idxf = nc.dram_tensor("idxf", [E, CAP], F32, kind="Internal").ap()
    d_gc = nc.dram_tensor("gc", [TOK, E], F32, kind="Internal").ap()
    d_y1sp = nc.dram_tensor("y1sp", [TOK, C], F32, kind="Internal").ap()
    d_y2 = nc.dram_tensor("y2", [TOK, C], F32, kind="ExternalOutput").ap()
    taps = {}
    for name, shape in debug_taps:
        taps[name] = nc.dram_tensor("tap_" + name, shape, F32, kind="ExternalOutput").ap()

    esP = ExitStack()
    sb = lambda name, shape, dt: esP.enter_context(nc.sbuf_tensor(name, shape, dt))

    idb = sb("idb", [128, 128], BF16)
    idf = sb("idf", [128, 128], F32)
    onesb = sb("onesb_s", [1, 128], BF16)
    qoffb = sb("qoffb_s", [128, 1], F32)
    mth = sb("mth_s", [128, 4], F32)
    tokid = sb("tokid_s", [128, NCH], F32)
    evals = sb("evals_s", [128, E], F32)
    b2ts = sb("b2t_s", [128, 8 * E], F32)
    cst = sb("cst", [128, 12], F32)
    b1t = sb("b1t_s", [128, E * 32], F32)
    b2b = sb("b2b", [E, C], BF16)
    rw = sb("rw_s", [128, 8 * E], F32)
    y1T = sb("y1T", [128, 8 * TOK], BF16)
    gatesT = sb("gatesT", [E, TOK], BF16)
    selb = sb("selb_s", [E, E * 128], BF16)

    esA = ExitStack()
    sbA = lambda name, shape, dt: esA.enter_context(nc.sbuf_tensor(name, shape, dt))
    qpi = sbA("qpi", [128, 512], F32)
    maskadd = sbA("maskadd", [128, 512], F32)
    qT = sbA("qT", [65, TOK], BF16)
    kT = sbA("kT", [65, T], BF16)
    vtm = sbA("vtm", [128, KCH * 65], BF16)
    condt = sbA("condt", [128, 4 * 128], BF16)
    o_tm = sbA("o_tm", [128, NCH * HS], F32)
    mA = sbA("mA", [128, 1], F32)
    mB = sbA("mB", [128, 1], F32)
    dm = sbA("dm", [128, 1], F32)
    edm = sbA("edm", [128, 1], F32)
    denA = sbA("denA", [128, 1], F32)
    denB = sbA("denB", [128, 1], F32)
    negm32 = sbA("negm32", [128, 1], F32)
    lden = sbA("lden", [128, 1], F32)
    bias2 = sbA("bias2", [128, 1], F32)
    bsum = [sbA("bsum0", [128, 1], F32), sbA("bsum1", [128, 1], F32)]

    esS = ExitStack()
    mks = lambda name: Sem(nc, esS, name)
    sC1 = mks("sC1")
    sC2 = mks("sC2")
    sTAP = mks("sTAP")
    sPL = mks("sPL")
    sPE = mks("sPE")
    sAC = mks("sAC")
    sDV = mks("sDV")
    sBAR = mks("sBAR")
    dXB = DmaSlots(nc, esS, "dXB")
    dZC = DmaSlots(nc, esS, "dZC", n=3)
    dSPp = DmaSlots(nc, esS, "dSPp", n=3)
    dW1 = DmaSlots(nc, esS, "dW1")
    dW2 = DmaSlots(nc, esS, "dW2", n=4)
    dYC = DmaSlots(nc, esS, "dYC", n=3)
    dSTp = DmaSlots(nc, esS, "dSTp", n=3)
    dFD = DmaSlots(nc, esS, "dFD")
    dSPb = DmaSlots(nc, esS, "dSPb", n=3)
    sGC = mks("sGC")
    sSCT = mks("sSCT")
    sSCTp = [mks("sSCTp0"), mks("sSCTp1")]
    sSCA = mks("sSCA")
    dPF = DmaSlots(nc, esS, "dPF", n=1)
    dIX = DmaSlots(nc, esS, "dIX", n=2)
    dGA = DmaSlots(nc, esS, "dGA", n=2)
    dGV = DmaSlots(nc, esS, "dGV", n=2)

    c_inv1024 = cst[:, C_INV1024:C_INV1024 + 1]
    c_neg132 = cst[:, C_NEG132:C_NEG132 + 1]
    c_one = cst[:, C_ONE:C_ONE + 1]
    c_neg1e30 = cst[:, C_NEG1E30:C_NEG1E30 + 1]
    c_zero = cst[:, C_ZERO:C_ZERO + 1]
    c_neg1 = cst[:, C_NEG1:C_NEG1 + 1]
    c_big = cst[:, C_BIG:C_BIG + 1]
    c_eight = cst[:, C_EIGHT:C_EIGHT + 1]
    c_cap = cst[:, C_CAP:C_CAP + 1]

    regB_cap = nc.gpsimd.alloc_register("bnd_cap")
    regB_tok = nc.gpsimd.alloc_register("bnd_tok")
    regB_gc = nc.gpsimd.alloc_register("bnd_gc")
    regB_all = nc.gpsimd.alloc_register("bnd_all")

    barrier_count = [0]

    def barrier(P):
        barrier_count[0] += 1
        for eng in ENGINES:
            P.op(eng, lambda Eo: Eo.nop(), inc=sBAR)
        for eng in ENGINES:
            P.wait(eng, sBAR, 5 * barrier_count[0])

    es1 = ExitStack()
    sb1 = lambda name, shape, dt: es1.enter_context(nc.sbuf_tensor(name, shape, dt))
    ps1 = lambda name, shape, dt: es1.enter_context(nc.psum_tensor(name, shape, dt))
    xb = [sb1("xb0", [128, C], F32), sb1("xb1", [128, C], F32)]
    xT = [sb1("xT0", [128, C], BF16), sb1("xT1", [128, C], BF16)]
    wqkv = sb1("wqkv", [128, 8 * 192], BF16)
    bvr = sb1("bvr", [1, HS], BF16)
    qkb = sb1("qkb_s", [128, 2], F32)
    pXT = [ps1("pXT0", [128, C], F32), ps1("pXT1", [128, C], F32)]
    pQK = [ps1("pQK0", [64, 128], F32), ps1("pQK1", [64, 128], F32)]
    pV = [ps1("pV0", [128, HS], F32), ps1("pV1", [128, HS], F32)]

    P = Prog()
    for dst, src in [(idb, d_idb), (idf, d_idf), (qpi, d_qpi), (qoffb, d_qoff),
                     (mth, d_mth), (tokid, d_tokid), (evals, d_evals),
                     (b2ts, d_b2t.rearrange("p a b -> p (a b)")),
                     (cst, d_cst), (qkb[:], d_qkb),
                     (b1t, d_b1t.rearrange("p e k -> p (e k)")),
                     (rw, d_rw.rearrange("p c e -> p (c e)"))]:
        ap = dst[:] if hasattr(dst, "ap") else dst
        P.op("sync", lambda Eo, a=ap, s=src: Eo.dma_start(out=a, in_=s), inc=sC1, inc_n=16)
    for dst, src in [(onesb[:], d_ones), (selb[:], d_sel),
                     (wqkv[:], d_wqkv.rearrange("p c h -> p (c h)")),
                     (bvr[:], d_bvr), (b2b[:], d_b2w)]:
        P.op("gpsimd", lambda Eo, a=dst, s=src: Eo.dma_start(out=a, in_=s), inc=sC2, inc_n=16)
    ld_consts = sC1.n
    wl_consts = sC2.n
    for _eng in ("tensor", "scalar", "vector", "gpsimd"):
        P.wait(_eng, sC1, ld_consts)
        P.wait(_eng, sC2, wl_consts)
    P.op("gpsimd", lambda Eo: Eo.reg_mov(regB_cap, CAP - 1))
    P.op("gpsimd", lambda Eo: Eo.reg_mov(regB_tok, TOK - 1))
    P.op("gpsimd", lambda Eo: Eo.reg_mov(regB_gc, TOK * E - 1))
    P.op("gpsimd", lambda Eo: Eo.reg_mov(regB_all, E * CAP - 1))
    P.op("gpsimd", lambda Eo: Eo.memset(kT[64:65, :], 1.0), inc=sPL)
    P.op("gpsimd", lambda Eo: Eo.memset(vtm[:], 1.0), inc=sPL)
    P.wait("vector", sC1, ld_consts)
    P.eop("vector", lambda Eo: Eo.tensor_scalar(maskadd[0:128, 0:128], qpi[:, 0:128], c_neg1, None, op0=ALU.mult), sDV)
    for _t in range(4):
        P.eop("vector", lambda Eo, t=_t: Eo.tensor_scalar(
            condt[:, t * 128:(t + 1) * 128], maskadd[0:128, 0:128],
            mth[:, t:t + 1], None, op0=ALU.is_ge), sDV)
    P.eop("vector", lambda Eo: Eo.tensor_scalar(maskadd[:], qpi[:], qoffb[:, 0:1], None, op0=ALU.add), sDV)
    P.eop("vector", lambda Eo: Eo.tensor_scalar(maskadd[:], maskadd[:], c_zero, None, op0=ALU.is_lt), sDV)
    P.op("vector", lambda Eo: Eo.tensor_scalar(maskadd[:], maskadd[:], c_neg1e30, None, op0=ALU.mult), inc=sDV)

    rXB = Rot(2)
    rXT = Rot(2)
    rQK = Rot(2)
    rV = Rot(2)

    rPXT = Rot(2)

    def chunk_setup(i, src_ap, is_q, kc):
        pi = i % 2
        rXB.acquire(P, "sync", i)
        dXB.load(P, "sync", i, lambda Eo, a=xb[pi][:], s=src_ap: Eo.dma_start(out=a, in_=s))
        dXB.wait(P, "tensor", i)
        rPXT.acquire(P, "tensor", i)
        rXT_psum = pXT[pi]
        for dc in range(8):
            P.op("tensor", lambda Eo, o=rXT_psum[:, dc * 128:(dc + 1) * 128],
                 a=xb[pi][:, dc * 128:(dc + 1) * 128]: Eo.transpose(o, a, idf[:]),
                 inc=(sPE if dc == 7 else None))
        transposes_done = sPE.n
        P.wait("vector", sPE)
        rXT.acquire(P, "vector", i)
        P.op("vector", lambda Eo, o=xT[pi][:], a=rXT_psum[:]: Eo.tensor_copy(o, a), inc=sDV)
        rPXT.release(i, sDV)
        evac_cnt = sDV.n
        P.wait("tensor", sDV, evac_cnt)
        if is_q:
            rQK.acquire(P, "tensor", i)
            for dc in range(8):
                P.op("tensor", lambda Eo, p=pQK[pi][:],
                     w=wqkv[:, dc * 192:dc * 192 + 64],
                     x=xT[pi][:, dc * 128:(dc + 1) * 128], st=(dc == 0), sp=(dc == 7):
                     Eo.matmul(p, w, x, start=st, stop=sp),
                     inc=(sPE if dc == 7 else None))
            P.wait("scalar", sPE)
            P.op("scalar", lambda Eo, o=qT[0:64, i * 128:(i + 1) * 128], p=pQK[pi][:]:
                 Eo.activation(o, p, AF.Identity, bias=qkb[0:64, 0:1], scale=f32(1.0)),
                 inc=sAC)
            rQK.release(i, sAC)
        else:
            rQK.acquire(P, "tensor", i)
            for dc in range(8):
                P.op("tensor", lambda Eo, p=pQK[pi][:],
                     w=wqkv[:, dc * 192 + 64:dc * 192 + 128],
                     x=xT[pi][:, dc * 128:(dc + 1) * 128], st=(dc == 0), sp=(dc == 7):
                     Eo.matmul(p, w, x, start=st, stop=sp),
                     inc=(sPE if dc == 7 else None))
            P.wait("scalar", sPE)
            P.op("scalar", lambda Eo, o=kT[0:64, kc * 128:(kc + 1) * 128], p=pQK[pi][:]:
                 Eo.activation(o, p, AF.Identity, bias=qkb[0:64, 1:2], scale=f32(1.0)),
                 inc=sAC)
            rQK.release(i, sAC)
            rV.acquire(P, "tensor", i)
            for dc in range(8):
                P.op("tensor", lambda Eo, p=pV[pi][:],
                     x=xT[pi][:, dc * 128:(dc + 1) * 128],
                     w=wqkv[:, dc * 192 + 128:dc * 192 + 192], st=(dc == 0):
                     Eo.matmul(p, x, w, start=st, stop=False))
            P.op("tensor", lambda Eo, p=pV[pi][:]:
                 Eo.matmul(p, onesb[:], bvr[:], start=False, stop=True), inc=sPE)
            P.wait("vector", sPE)
            P.wait("vector", sPL, 2)
            P.op("vector", lambda Eo, o=vtm[:, kc * 65:kc * 65 + HS], p=pV[pi][:]:
                 Eo.tensor_copy(o, p), inc=sDV)
            rV.release(i, sDV)
        rXB.rel[pi] = [(sPE, transposes_done)]
        rXT.rel[pi] = [(sPE, sPE.n)]

    for i in range(NCH):
        chunk_setup(i, d_xown[i * 128:(i + 1) * 128, :], True, None)
    for kc in range(KCH):
        chunk_setup(NCH + kc, d_xbat[kc * 128:(kc + 1) * 128, :], False, kc)

    P.wait("sync", sTAP)
    barrier(P)
    with nc.Block("b1") as blk:
        P.run(blk)
    es1.close()

    es2 = ExitStack()
    sb2 = lambda name, shape, dt: es2.enter_context(nc.sbuf_tensor(name, shape, dt))
    ps2 = lambda name, shape, dt: es2.enter_context(nc.psum_tensor(name, shape, dt))
    PT4 = [sb2("PT40", [128, 512], BF16), sb2("PT41", [128, 512], BF16)]
    ots = sb2("ots", [65, 128], F32)
    mtab = [sb2("mtab0", [128, NCH], F32), sb2("mtab1", [128, NCH], F32)]
    mfin = sb2("mfin", [128, 1], F32)
    foldv = [sb2("foldv0", [128, 1], F32), sb2("foldv1", [128, 1], F32)]
    dtmp = sb2("dtmp", [128, 1], F32)
    rcp = sb2("rcp", [128, 1], F32)
    pSA = [ps2("pSA0", [128, 512], F32), ps2("pSA1", [128, 512], F32)]
    pSk = [ps2("pSk0", [128, 512], F32), ps2("pSk1", [128, 512], F32)]
    pO = ps2("pO", [65, 128], F32)
    pOt = ps2("pOt", [128, 65], F32)

    P = Prog()
    for _eng in ENGINES:
        P.wait(_eng, sC1, ld_consts)
        P.wait(_eng, sC2, wl_consts)
    rSA = Rot(2)
    rSK = Rot(2)
    rPT4 = Rot(2)
    rPO = Rot(1)
    rMT = Rot(2)
    fold_done = {}

    def pass_a(j):
        par2 = j % 2
        rMT.acquire(P, "vector", j)
        for kb in range(j + 1):
            par = kb % 2
            rSA.acquire(P, "tensor", kb)
            P.op("tensor", lambda Eo, p=pSA[par][:],
                 q=qT[0:64, j * 128:(j + 1) * 128],
                 k=kT[0:64, kb * 512:(kb + 1) * 512]:
                 Eo.matmul(p, q, k, start=True, stop=True), inc=sPE)
            P.wait("vector", sPE)
            if kb == j:
                P.eop("vector", lambda Eo, p=pSA[par][:]:
                      Eo.tensor_add(p, p, maskadd[:]), sDV)
            P.op("vector", lambda Eo, o=mtab[par2][:, kb:kb + 1], p=pSA[par][:]:
                 Eo.reduce_max(o, p, axis=AX.X), inc=sDV)
            rSA.release(kb, sDV)
        P.wait("vector", sDV)
        P.eop("vector", lambda Eo, mt=mtab[par2][:, 0:j + 1]:
              Eo.reduce_max(mfin[:], mt, axis=AX.X), sDV)
        dFD.wait(P, "vector", j)
        P.eop("vector", lambda Eo, fv=foldv[j % 2][:]:
              Eo.tensor_scalar(fv, mfin[:], c_neg1, None, op0=ALU.mult), sDV)
        dFD.wait(P, "gpsimd", j)
        P.wait("gpsimd", sDV)
        dFD.load(P, "gpsimd", j, lambda Eo, o=qT[64:65, j * 128:(j + 1) * 128],
                 fv=foldv[j % 2][:]: Eo.dma_start(out=o, in_=fv))
        fold_done[j] = None

    def pass_b(j):
        par2 = j % 2
        ngrp = j + 1
        dFD.wait(P, "tensor", j)

        def b_mms(kg):
            par = kg % 2
            rSK.acquire(P, "tensor", kg)
            for t in range(4):
                kk = 4 * kg + t
                P.op("tensor", lambda Eo, p=pSk[par][:, t * 128:(t + 1) * 128],
                     k=kT[:, kk * 128:(kk + 1) * 128],
                     q=qT[:, j * 128:(j + 1) * 128]:
                     Eo.matmul(p, k, q, start=True, stop=True),
                     inc=(sPE if t == 3 else None))
            return sPE.n

        mm_done = {0: b_mms(0)}
        for kg in range(ngrp):
            par = kg % 2
            if kg + 1 < ngrp:
                mm_done[kg + 1] = b_mms(kg + 1)
            P.wait("scalar", sPE, mm_done[kg])
            rPT4.acquire(P, "scalar", kg)
            P.op("scalar", lambda Eo, o=PT4[par][:], p=pSk[par][:]:
                 Eo.activation(o, p, AF.Exp, scale=f32(1.0 / 32.0)), inc=sAC)
            rSK.release(kg, sAC)
            if kg == j:
                P.wait("vector", sAC)
                for t in range(4):
                    P.op("vector", lambda Eo, t=t, o=PT4[par][:, t * 128:(t + 1) * 128],
                         c=condt[:, t * 128:(t + 1) * 128]:
                         Eo.tensor_tensor(o, o, c, op=ALU.mult),
                         inc=(sDV if t == 3 else None))
                P.wait("tensor", sDV)
            else:
                P.wait("tensor", sAC)
            if kg == 0:
                rPO.acquire(P, "tensor", 0)
            for t in range(4):
                kk = 4 * kg + t
                P.op("tensor", lambda Eo, v=vtm[:, kk * 65:(kk + 1) * 65],
                     pp=PT4[par][:, t * 128:(t + 1) * 128],
                     st=(kg == 0 and t == 0), sp=(kg == j and t == 3):
                     Eo.matmul(pO[:], v, pp, start=st, stop=sp),
                     inc=(sPE if t == 3 else None))
            rPT4.release(kg, sPE)
        P.wait("scalar", sPE)
        P.op("scalar", lambda Eo: Eo.activation(ots[:], pO[:], AF.Copy), inc=sAC)
        rPO.rel[0] = [(sAC, sAC.n)]
        P.wait("tensor", sAC)
        P.op("tensor", lambda Eo: Eo.transpose(pOt[:], ots[:], idf[0:65, 0:65]), inc=sPE)
        P.wait("vector", sPE)
        P.eop("vector", lambda Eo: Eo.tensor_copy(dtmp[:], pOt[:, 64:65]), sDV)
        P.eop("vector", lambda Eo: Eo.reciprocal(rcp[:], dtmp[:]), sDV)
        P.op("vector", lambda Eo, o=o_tm[:, j * HS:(j + 1) * HS]:
             Eo.tensor_scalar(o, pOt[:, 0:HS], rcp[:, 0:1], None, op0=ALU.mult), inc=sDV)
        P.wait("tensor", sDV)
        P.wait("scalar", sDV)

    for j in range(NCH):
        pass_a(j)
        if j > 0:
            pass_b(j - 1)
    pass_b(NCH - 1)

    if "o_tm" in taps:
        P.wait("sync", sDV)
        P.op("sync", lambda Eo: Eo.dma_start(out=taps["o_tm"], in_=o_tm[:]),
             inc=sTAP, inc_n=16)
    if "condt" in taps:
        P.wait("gpsimd", sDV)
        P.op("gpsimd", lambda Eo: Eo.dma_start(out=taps["condt"], in_=condt[:]),
             inc=sTAP, inc_n=16)
    if "vtm" in taps:
        P.wait("gpsimd", sDV)
        P.op("gpsimd", lambda Eo: Eo.dma_start(out=taps["vtm"], in_=vtm[:]),
             inc=sTAP, inc_n=16)
    if "qTe" in taps:
        P.wait("gpsimd", sDV)
        P.op("gpsimd", lambda Eo: Eo.dma_start(out=taps["qTe"], in_=qT[:]),
             inc=sTAP, inc_n=16)
    if "qT" in taps:
        P.wait("gpsimd", sDV)
        P.op("gpsimd", lambda Eo: Eo.dma_start(out=taps["qT"], in_=qT[:]),
             inc=sTAP, inc_n=16)
    if "kT" in taps:
        P.wait("gpsimd", sDV)
        P.op("gpsimd", lambda Eo: Eo.dma_start(out=taps["kT"], in_=kT[:]),
             inc=sTAP, inc_n=16)
    if "vtm" in taps:
        P.wait("gpsimd", sDV)
        P.op("gpsimd", lambda Eo: Eo.dma_start(out=taps["vtm"], in_=vtm[:]),
             inc=sTAP, inc_n=16)
    P.wait("sync", sTAP)
    P.wait("gpsimd", sTAP)
    barrier(P)
    with nc.Block("b2") as blk:
        P.run(blk)
    es2.close()

    es3 = ExitStack()
    sb3 = lambda name, shape, dt: es3.enter_context(nc.sbuf_tensor(name, shape, dt))
    ps3 = lambda name, shape, dt: es3.enter_context(nc.psum_tensor(name, shape, dt))
    zch = [sb3("zch%d" % i, [128, C], F32) for i in range(3)]
    sqb = [sb3("sqb%d" % i, [128, C], F32) for i in range(3)]
    tmp4 = [sb3("tmp4%d" % i, [128, 512], F32) for i in range(3)]
    logits = sb3("logits", [E, TOK], F32)
    gtm = sb3("gtm", [128, 8], F32)
    gates_tm = sb3("gates_tm", [128, 128], F32)
    muv = [sb3("mu%d" % i, [128, 1], F32) for i in range(3)]
    varv_ = [sb3("varv%d" % i, [128, 1], F32) for i in range(3)]
    lnvv_ = [sb3("lnvv%d" % i, [128, 1], F32) for i in range(3)]
    rstdv = [sb3("rstd%d" % i, [128, 1], F32) for i in range(3)]
    eq1 = sb3("eq1", [128, 8], F32)
    eq2 = sb3("eq2", [128, 8], F32)
    mskl = sb3("mskl", [128, 8], F32)
    m1v = sb3("m1v", [128, 1], F32)
    m2v = sb3("m2v", [128, 1], F32)
    edv = sb3("edv", [128, 1], F32)
    edv2 = sb3("edv2", [128, 1], F32)
    s1v = sb3("s1v", [128, 1], F32)
    rv = sb3("rv", [128, 1], F32)
    w2v = sb3("w2v", [128, 1], F32)
    g1v = sb3("g1v", [128, 8], F32)
    y1bch = [sb3("y1bch%d" % i, [128, C], BF16) for i in range(3)]
    ones8 = sb3("ones8", [E, TOK], F32)
    mask_fm = sb3("mask_fm", [E, TOK], F32)
    pos_fm = sb3("pos_fm", [E, TOK], F32)
    w1m_fm = sb3("w1m_fm", [E, TOK], F32)
    slot_fm = sb3("slot_fm", [E, TOK], F32)
    slots_tm = [sb3("slots_tm%d" % i, [128, 8], F32) for i in range(2)]
    sidx_i = [sb3("sidx_i%d" % i, [128, 8], mybir.dt.int32) for i in range(2)]
    tokcol = [sb3("tokcol%d" % i, [128, 1], F32) for i in range(2)]
    selv = sb3("selv", [128, 8], F32)
    selw = sb3("selw", [128, 8], F32)
    eqA = sb3("eqA", [128, 8], F32)
    eqB = sb3("eqB", [128, 8], F32)
    eqS = sb3("eqS", [128, 8], F32)
    selt = sb3("selt", [128, 8], F32)
    tmp1 = sb3("tmp1", [128, 1], F32)
    ecap = sb3("ecap", [128, 8], F32)
    pft = sb3("pft", [E, CAP], F32)
    pTf = [ps3("pTf%d" % i, [128, 512], F32) for i in range(3)]
    pLr = [ps3("pLr0", [E, 128], F32), ps3("pLr1", [E, 128], F32)]
    pG8 = ps3("pG8", [128, 16], F32)
    pTg = ps3("pTg", [E, 256], F32)

    P = Prog()
    for _eng in ENGINES:
        P.wait(_eng, sC1, ld_consts)
        P.wait(_eng, sC2, wl_consts)
    rZ = Rot(3)
    rSQ = Rot(3)
    rMU = Rot(3)
    rRS = Rot(3)
    rTF = Rot(3)
    rT4 = Rot(3)
    rPL = Rot(2)

    for tc in range(NCH):
        par = tc % 3
        z = zch[par]
        rZ.acquire(P, "sync", tc)
        dZC.load(P, "sync", tc, lambda Eo, o=z[:], s=d_xown[tc * 128:(tc + 1) * 128, :]:
                 Eo.dma_start(out=o, in_=s))
        dZC.wait(P, "gpsimd", tc)
        orep = o_tm[:, tc * HS:(tc + 1) * HS].unsqueeze(1).broadcast_to((128, 16, HS))
        P.op("gpsimd", lambda Eo, z_=z[:], r=orep:
             Eo.tensor_add(z_.rearrange("p (r h) -> p r h", r=16), z_.rearrange("p (r h) -> p r h", r=16), r), inc=sPL)
        rMU.acquire(P, "vector", tc)
        mu = muv[par]
        P.wait("vector", sPL)
        P.eop("vector", lambda Eo, z_=z[:], m=mu[:]: Eo.reduce_sum(m, z_, axis=AX.X), sDV)
        P.eop("vector", lambda Eo, m=mu[:]:
              Eo.tensor_scalar(m, m, c_inv1024, None, op0=ALU.mult), sDV)
        P.wait("scalar", sDV)
        rSQ.acquire(P, "scalar", tc)
        P.op("scalar", lambda Eo, z_=z[:], m=mu[:], q=sqb[par][:]:
             Eo.activation(q, z_, AF.Square, bias=m, scale=f32(-1.0)), inc=sAC)
        P.wait("vector", sAC)
        varv = varv_[par]
        lnvv = lnvv_[par]
        P.eop("vector", lambda Eo, vv=varv[:], q=sqb[par][:]: Eo.reduce_sum(vv, q, axis=AX.X), sDV)
        P.op("vector", lambda Eo, vv=varv[:]:
             Eo.tensor_scalar(vv, vv, c_inv1024, None, op0=ALU.mult), inc=sDV)
        rSQ.rel[par] = [(sDV, sDV.n)]
        P.wait("scalar", sDV)
        rRS.acquire(P, "scalar", tc)
        rstd = rstdv[par]
        P.eop("scalar", lambda Eo, lv=lnvv[:], vv=varv[:]: Eo.activation(lv, vv, AF.Ln, bias=f32(EPS)), sAC)
        P.op("scalar", lambda Eo, r=rstd[:], lv=lnvv[:]:
             Eo.activation(r, lv, AF.Exp, scale=f32(-0.5)), inc=sAC)
        P.wait("vector", sAC)
        P.op("vector", lambda Eo, z_=z[:], m=mu[:], r=rstd[:]:
             Eo.tensor_scalar(z_, z_, m, r, op0=ALU.subtract, op1=ALU.mult), inc=sDV)
        rMU.release(tc, sDV)
        rRS.release(tc, sDV)
        y1_done = sDV.n
        P.wait("sync", sDV, y1_done)
        dSPp.load(P, "sync", tc, lambda Eo, o=d_y1sp[tc * 128:(tc + 1) * 128, :], z_=z[:]:
                  Eo.dma_start(out=o, in_=z_))
        P.wait("gpsimd", sDV, y1_done)
        dSPb.wait(P, "gpsimd", tc)
        P.op("gpsimd", lambda Eo, o=y1bch[par][:], z_=z[:]: Eo.tensor_copy(o, z_), inc=sPL)
        P.wait("sync", sPL)
        dSPb.load(P, "sync", tc, lambda Eo, o=d_y1bf[tc * 128:(tc + 1) * 128, 0:C],
                  a=y1bch[par][:]: Eo.dma_start(out=o, in_=a))
        P.wait("tensor", sDV, y1_done)
        for g in range(2):
            tfp = pTf[(tc * 2 + g) % 3]
            rTF.acquire(P, "tensor", tc * 2 + g)
            for t in range(4):
                dc = g * 4 + t
                P.op("tensor", lambda Eo, o=tfp[:, t * 128:(t + 1) * 128],
                     a=z[:, dc * 128:(dc + 1) * 128]: Eo.transpose(o, a, idf[:]),
                     inc=(sPE if t == 3 else None))
            P.wait("scalar", sPE)
            rT4.acquire(P, "scalar", tc * 2 + g)
            tm4 = tmp4[(tc * 2 + g) % 3]
            P.op("scalar", lambda Eo, o=tm4[:], a=tfp[:]: Eo.activation(o, a, AF.Copy), inc=sAC)
            y1T_view = y1T[:].rearrange("p (d t) -> p d t", d=8)[:, g * 4:(g + 1) * 4,
                                                               tc * 128:(tc + 1) * 128]
            P.op("scalar", lambda Eo, o=y1T_view, a=tfp[:].rearrange("p (d t) -> p d t", d=4):
                 Eo.activation(o, a, AF.Copy), inc=sAC)
            rTF.release(tc * 2 + g, sAC)
            P.wait("tensor", sAC)
            if g == 0:
                rPL.acquire(P, "tensor", tc)
            for t in range(4):
                dc = g * 4 + t
                P.op("tensor", lambda Eo, p=pLr[tc % 2][:],
                     w=rw[:, dc * 8:(dc + 1) * 8],
                     x=tm4[:, t * 128:(t + 1) * 128], st=(dc == 0), sp=(dc == 7):
                     Eo.matmul(p, w, x, start=st, stop=sp),
                     inc=(sPE if t == 3 else None))
            rT4.release(tc * 2 + g, sPE)
        P.wait("scalar", sPE)
        P.op("scalar", lambda Eo, o=logits[:, tc * 128:(tc + 1) * 128],
             p=pLr[tc % 2][:]: Eo.activation(o, p, AF.Copy), inc=sAC)
        rPL.release(tc, sAC)
        rZ.release(tc, sPE)
        rZ.rel[par].append(dSPp.sem_count(tc))
    logits_done = sAC.n
    gate_dve_done = {}
    for tc in range(NCH):
        sl = tc % 2
        P.wait("tensor", sAC)
        if tc >= 2:
            P.wait("tensor", sDV, gate_dve_done[sl])
        P.op("tensor", lambda Eo, o=pG8[:, sl * 8:(sl + 1) * 8],
             a=logits[:, tc * 128:(tc + 1) * 128]:
             Eo.transpose(o, a, idf[0:8, 0:8]), inc=sPE)
        P.wait("vector", sPE)
        P.eop("vector", lambda Eo, p=pG8[:, sl * 8:(sl + 1) * 8]: Eo.tensor_copy(gtm[:], p), sDV)
        P.eop("vector", lambda Eo: Eo.reduce_max(m1v[:], gtm[:], axis=AX.X), sDV)
        P.eop("vector", lambda Eo: Eo.tensor_scalar(eq1[:], gtm[:], m1v[:, 0:1], None, op0=ALU.is_equal), sDV)
        P.eop("vector", lambda Eo: Eo.scalar_tensor_tensor(mskl[:], eq1[:], c_neg1e30, gtm[:],
                                                           op0=ALU.mult, op1=ALU.add), sDV)
        P.eop("vector", lambda Eo: Eo.reduce_max(m2v[:], mskl[:], axis=AX.X), sDV)
        P.eop("vector", lambda Eo: Eo.tensor_scalar(eq2[:], mskl[:], m2v[:, 0:1], None, op0=ALU.is_equal), sDV)
        P.op("vector", lambda Eo: Eo.tensor_tensor(edv2[:], m2v[:], m1v[:], op=ALU.subtract), inc=sDV)
        P.wait("scalar", sDV)
        P.op("scalar", lambda Eo: Eo.activation(edv[:], edv2[:], AF.Exp), inc=sAC)
        P.wait("vector", sAC)
        P.eop("vector", lambda Eo: Eo.tensor_scalar(s1v[:], edv[:], c_one, None, op0=ALU.add), sDV)
        P.eop("vector", lambda Eo: Eo.reciprocal(rv[:], s1v[:]), sDV)
        P.eop("vector", lambda Eo: Eo.tensor_tensor(w2v[:], edv[:], rv[:], op=ALU.mult), sDV)
        P.eop("vector", lambda Eo: Eo.tensor_scalar(g1v[:], eq1[:], rv[:, 0:1], None, op0=ALU.mult), sDV)
        P.op("vector", lambda Eo, o=gates_tm[:, tc * 8:(tc + 1) * 8]:
             Eo.scalar_tensor_tensor(o, eq2[:], w2v[:, 0:1], g1v[:], op0=ALU.mult, op1=ALU.add),
             inc=sDV)
        gate_dve_done[sl] = sDV.n
        P.wait("gpsimd", sDV)
        P.op("gpsimd", lambda Eo, tc=tc, o=d_y1bf[tc * 128:(tc + 1) * 128, C:C + 8],
             a=gates_tm[:, tc * 8:(tc + 1) * 8]: Eo.dma_start(out=o, in_=a),
             inc=sGC, inc_n=16)
    P.wait("tensor", sDV)
    for tc in range(NCH):
        sl = tc % 2
        P.op("tensor", lambda Eo, o=pTg[:, sl * 128:(sl + 1) * 128],
             a=gates_tm[:, tc * 8:(tc + 1) * 8]:
             Eo.transpose(o, a, idf[:]), inc=sPE)
        P.wait("scalar", sPE)
        P.op("scalar", lambda Eo, o=gatesT[:, tc * 128:(tc + 1) * 128],
             p=pTg[:, sl * 128:(sl + 1) * 128]: Eo.activation(o, p, AF.Copy), inc=sAC)
        P.wait("tensor", sAC)

    P.eop("vector", lambda Eo: Eo.tensor_scalar(ecap[:], evals[:], c_cap, None, op0=ALU.mult), sDV)
    P.op("gpsimd", lambda Eo: Eo.memset(ones8[:], 1.0), inc=sPL)
    P.op("gpsimd", lambda Eo: Eo.memset(pft[:], 2.0e6), inc=sPL)
    P.wait("sync", sPL)
    dPF.load(P, "sync", 0, lambda Eo: Eo.dma_start(out=d_idxf, in_=pft[:]))
    P.wait("vector", sAC)
    P.wait("vector", sPL)
    P.eop("vector", lambda Eo: Eo.tensor_scalar(mask_fm[:], gatesT[:], c_zero[0:E, :], None,
                                                op0=ALU.is_gt), sDV)
    P.eop("vector", lambda Eo: Eo.tensor_tensor_scan(pos_fm[:], ones8[:], mask_fm[:], 0.0,
                                                     op0=ALU.mult, op1=ALU.add), sDV)
    P.eop("vector", lambda Eo: Eo.tensor_scalar(pos_fm[:], pos_fm[:], c_one[0:E, :], None,
                                                op0=ALU.subtract), sDV)
    P.eop("vector", lambda Eo: Eo.tensor_scalar(w1m_fm[:], mask_fm[:], c_neg1[0:E, :],
                                                c_one[0:E, :], op0=ALU.mult, op1=ALU.add), sDV)
    P.eop("vector", lambda Eo: Eo.scalar_tensor_tensor(slot_fm[:], w1m_fm[:], c_big[0:E, :],
                                                       pos_fm[:], op0=ALU.mult, op1=ALU.add), sDV)
    slot_ready = sDV.n
    P.wait("tensor", sDV, slot_ready)
    dPF.wait(P, "gpsimd", 0)
    sct_done = {}
    for tc in range(NCH):
        sl = tc % 2
        P.op("tensor", lambda Eo, o=pG8[:, sl * 8:(sl + 1) * 8],
             a=slot_fm[:, tc * 128:(tc + 1) * 128]:
             Eo.transpose(o, a, idf[0:8, 0:8]), inc=sPE)
        P.wait("vector", sPE)
        P.eop("vector", lambda Eo, o=slots_tm[sl][:], p=pG8[:, sl * 8:(sl + 1) * 8]:
              Eo.tensor_copy(o, p), sDV)
        if tc >= 2:
            P.wait("vector", sSCTp[sl], sct_done[sl])
        P.eop("vector", lambda Eo, s_=slots_tm[sl][:]:
              Eo.tensor_scalar(selv[:], s_, c_cap, None, op0=ALU.is_ge), sDV)
        P.eop("vector", lambda Eo, s_=slots_tm[sl][:]:
              Eo.scalar_tensor_tensor(selw[:], selv[:], c_big, s_, op0=ALU.mult, op1=ALU.add), sDV)
        P.eop("vector", lambda Eo: Eo.tensor_add(selw[:], selw[:], ecap[:]), sDV)
        gsl = gates_tm[:, tc * 8:(tc + 1) * 8]
        P.eop("vector", lambda Eo, g=gsl: Eo.reduce_max(m1v[:], g, axis=AX.X), sDV)
        P.eop("vector", lambda Eo, g=gsl: Eo.tensor_scalar(eqA[:], g, m1v[:, 0:1], None, op0=ALU.is_equal), sDV)
        P.eop("vector", lambda Eo, g=gsl: Eo.tensor_scalar(eqS[:], g, c_zero, None, op0=ALU.is_gt), sDV)
        P.eop("vector", lambda Eo: Eo.tensor_sub(eqB[:], eqS[:], eqA[:]), sDV)
        P.eop("vector", lambda Eo: Eo.tensor_mul(selt[:], eqA[:], selw[:]), sDV)
        P.eop("vector", lambda Eo, o=sidx_i[sl][:, 0:1]:
              Eo.reduce_sum(tmp1[:], selt[:], axis=AX.X), sDV)
        P.eop("vector", lambda Eo, o=sidx_i[sl][:, 0:1]: Eo.tensor_copy(o, tmp1[:]), sDV)
        P.eop("vector", lambda Eo: Eo.tensor_mul(selt[:], eqB[:], selw[:]), sDV)
        P.eop("vector", lambda Eo: Eo.reduce_sum(tmp1[:], selt[:], axis=AX.X), sDV)
        P.eop("vector", lambda Eo, o=sidx_i[sl][:, 1:2]: Eo.tensor_copy(o, tmp1[:]), sDV)
        P.eop("vector", lambda Eo, o=tokcol[sl][:], s_=tokid[:, tc:tc + 1]:
              Eo.tensor_copy(o, s_), sDV)
        P.wait("gpsimd", sDV)
        P._mark_ind = True
        for kk in range(2):
            P.op("gpsimd", lambda Eo, kk=kk, sl=sl:
                 Eo.indirect_dma_start(
                     out=d_idxf.rearrange("e (s o) -> (e s) o", o=1),
                     out_offset=IndirectOffsetOnAxis(ap=sidx_i[sl][:, kk:kk + 1], axis=0),
                     in_=tokcol[sl][:], in_offset=None,
                     bounds_check=regB_all, oob_is_err=False),
                 inc=sSCTp[sl], inc_n=16)
        P._mark_ind = False
        sct_done[sl] = sSCTp[sl].n
        P.wait("tensor", sDV)

    if "logits" in taps:
        P.wait("sync", sAC, logits_done)
        P.op("sync", lambda Eo: Eo.dma_start(out=taps["logits"], in_=logits[:]),
             inc=sTAP, inc_n=16)
    if "gates" in taps:
        P.wait("sync", sDV)
        P.op("sync", lambda Eo: Eo.dma_start(out=taps["gates"], in_=gates_tm[:]),
             inc=sTAP, inc_n=16)
    if "y1T" in taps:
        P.wait("gpsimd", sAC)
        P.op("gpsimd", lambda Eo: Eo.dma_start(out=taps["y1T"], in_=y1T[:]),
             inc=sTAP, inc_n=16)
    if "gatesT" in taps:
        P.wait("gpsimd", sAC)
        P.op("gpsimd", lambda Eo: Eo.dma_start(out=taps["gatesT"], in_=gatesT[:]),
             inc=sTAP, inc_n=16)
    if "y1" in taps:
        for s_ in dSPp.sems:
            P.wait("sync", s_)
        P.op("sync", lambda Eo: Eo.dma_start(out=taps["y1"], in_=d_y1sp), inc=sTAP, inc_n=16)
    for s_ in dSPp.sems:
        P.wait("sync", s_)
    P.wait("sync", sTAP)
    barrier(P)
    with nc.Block("b3") as blk:
        P.run(blk)
    es3.close()
    esA.close()

    es4 = ExitStack()
    sb4 = lambda name, shape, dt: es4.enter_context(nc.sbuf_tensor(name, shape, dt))
    ps4 = lambda name, shape, dt: es4.enter_context(nc.psum_tensor(name, shape, dt))
    H = sb4("H", [128, 32 * CAP], BF16)
    y1gT = sb4("y1gT", [128, 8 * CAP], BF16)
    o_sb = sb4("o_sb", [128, 8 * CAP], F32)
    w1b = [sb4("w1b%d" % i, [128, 8 * 256], BF16) for i in range(2)]
    w2b = [sb4("w2b%d" % i, [128, 16 * 128], BF16) for i in range(2)]
    gbuf = [sb4("gbuf%d" % i, [128, C + 8], BF16) for i in range(2)]
    scatb = [sb4("scatb%d" % i, [128, C], F32) for i in range(2)]
    idxf_s = [sb4("idxf_s%d" % i, [128, NG], F32) for i in range(2)]
    idxi_s = [sb4("idxi_s%d" % i, [128, NG], mybir.dt.int32) for i in range(2)]
    gv = [sb4("gv%d" % i, [128, NG], F32) for i in range(2)]
    pH = [ps4("pH%d" % i, [128, 1024], F32) for i in range(2)]
    pTin = ps4("pTin", [128, 512], BF16)
    pTout = ps4("pTout", [128, 512], F32)

    P = Prog()
    w1_view = d_w1
    w2_view = d_w2
    for _eng in ENGINES:
        P.wait(_eng, sC1, ld_consts)
        P.wait(_eng, sC2, wl_consts)
    for _eng in ("gpsimd", "sync"):
        P.wait(_eng, sSCTp[0])
        P.wait(_eng, sSCTp[1])
        P.wait(_eng, sGC)
        for s_ in dSPb.sems:
            P.wait(_eng, s_)
        for s_ in dSPp.sems:
            P.wait(_eng, s_)
        P.wait(_eng, dPF.sems[0])
    P.wait("gpsimd", sGC)
    for s_ in dSPb.sems:
        P.wait("gpsimd", s_)
    P.op("gpsimd", lambda Eo: Eo.memset(gbuf[0][:], 0.0))
    P.op("gpsimd", lambda Eo: Eo.memset(gbuf[1][:], 0.0))
    P.op("gpsimd", lambda Eo: Eo.memset(gv[0][:], 0.0))
    P.op("gpsimd", lambda Eo: Eo.memset(gv[1][:], 0.0), inc=sPL)
    P.wait("gpsimd", sPL)
    P.wait("tensor", sPL)

    rW1 = Rot(2)
    rW2 = Rot(2)
    rGB = Rot(2)
    rIX = Rot(2)
    rGV = Rot(2)
    rSCB = Rot(2)
    rPH = Rot(2)
    rTIN = Rot(1)
    rTOUT = Rot(1)
    NB = ((0, 512), (512, CAP))

    for e in range(n_experts):
        ep = e % 2
        rIX.acquire(P, "sync", e)
        dIX.load(P, "sync", e, lambda Eo, o=idxf_s[ep][:],
                 s=d_idxf[e, :].rearrange("(p g) -> p g", g=NG): Eo.dma_start(out=o, in_=s))
        dIX.wait(P, "vector", e)
        P.eop("vector", lambda Eo, o=idxi_s[ep][:], a=idxf_s[ep][:]: Eo.tensor_copy(o, a), sDV)
        rGV.acquire(P, "vector", e)
        idx_ready = sDV.n
        P.wait("gpsimd", sDV, idx_ready)
        for g in range(NG):
            rGB.acquire(P, "gpsimd", e * NG + g)
            P._mark_ind = True
            dGA.load(P, "gpsimd", e * NG + g, lambda Eo, g=g, ep=ep:
                     Eo.indirect_dma_start(
                         out=gbuf[g % 2][:], out_offset=None,
                         in_=d_y1bf, in_offset=IndirectOffsetOnAxis(ap=idxi_s[ep][:, g:g + 1], axis=0),
                         bounds_check=regB_tok, oob_is_err=False))
            P._mark_ind = False
            dGA.wait(P, "vector", e * NG + g)
            P.eop("vector", lambda Eo, g=g, ep=ep, e=e, o=gv[ep][:, g:g + 1]:
                  Eo.tensor_copy(o, gbuf[g % 2][:, C + e:C + e + 1]), sDV)
            dGA.wait(P, "tensor", e * NG + g)
            rTIN.acquire(P, "tensor", 0)
            for kc in range(8):
                P.op("tensor", lambda Eo, kc=kc, g=g,
                     o=pTin[:, (kc % 4) * 128:(kc % 4 + 1) * 128],
                     a=gbuf[g % 2][:, kc * 128:(kc + 1) * 128]:
                     Eo.transpose(o, a, idb[:]),
                     inc=(sPE if kc % 4 == 3 else None))
                if kc % 4 == 3:
                    P.wait("vector", sPE)
                    kc0 = kc - 3
                    P.op("vector", lambda Eo, kc0=kc0, g=g, o4=[
                         y1gT[:, (kc0 + i) * CAP + g * 128:(kc0 + i) * CAP + (g + 1) * 128]
                         for i in range(4)], pt=pTin[:]:
                         [Eo.tensor_copy(o4[i], pt[:, i * 128:(i + 1) * 128]) for i in range(4)][-1],
                         inc=sDV)
                    if kc == 3:
                        rTIN.rel[0] = [(sDV, sDV.n)]
                        rTIN.acquire(P, "tensor", 0)
                    else:
                        rTIN.rel[0] = [(sDV, sDV.n)]
            rGB.release(e * NG + g, sPE)
            rGB.release(e * NG + g, sDV)
        in_done = sDV.n
        P.wait("tensor", sDV, in_done)
        for fp in range(16):
            wpar = (e * 16 + fp) % 2
            rW1.acquire(P, "gpsimd", e * 16 + fp)
            srcw = w1_view[e, :, fp * 256:(fp + 1) * 256].rearrange("(kc p) f -> p kc f", p=128)
            dW1.load(P, "gpsimd", e * 16 + fp,
                     lambda Eo, o=w1b[wpar][:].rearrange("p (kc f) -> p kc f", kc=8),
                     s=srcw: Eo.dma_start(out=o, in_=s))
            dW1.wait(P, "tensor", e * 16 + fp)
            for fc in range(2):
                fcg = fp * 2 + fc
                hpar = fcg % 2
                rPH.acquire(P, "tensor", fcg)
                for kc in range(8):
                    for nbi, (lo, hi) in enumerate(NB):
                        P.op("tensor", lambda Eo, p=pH[hpar][:, lo:hi],
                             w=w1b[wpar][:, kc * 256 + fc * 128:kc * 256 + (fc + 1) * 128],
                             x=y1gT[:, kc * CAP + lo:kc * CAP + hi],
                             st=(kc == 0), sp=(kc == 7):
                             Eo.matmul(p, w, x, start=st, stop=sp),
                             inc=(sPE if (nbi == 1 and kc == 7) else None))
                P.wait("scalar", sPE)
                hsl = H[:, fcg * CAP:(fcg + 1) * CAP]
                P.op("scalar", lambda Eo, o=hsl, p=pH[hpar][:, 0:CAP],
                     b=b1t[:, e * 32 + fcg:e * 32 + fcg + 1]:
                     Eo.activation(o, p, AF.Gelu_apprx_tanh, bias=b, scale=f32(1.0)),
                     inc=sAC)
                rPH.release(fcg, sAC)
            rW1.release(e * 16 + fp, sPE)
        h_done = sAC.n
        P.wait("tensor", sAC, h_done)
        for dc in range(8):
            opar = dc % 2
            for fh in range(2):
                w2i = (e * 16 + dc * 2 + fh) % 2
                rW2.acquire(P, "gpsimd", e * 16 + dc * 2 + fh)
                srcw = w2_view[e, fh * 2048:(fh + 1) * 2048,
                               dc * 128:(dc + 1) * 128].rearrange("(fc p) d -> p fc d", p=128)
                dW2.load(P, "gpsimd", e * 16 + dc * 2 + fh,
                         lambda Eo, o=w2b[w2i][:].rearrange("p (fc d) -> p fc d", fc=16),
                         s=srcw: Eo.dma_start(out=o, in_=s))
                dW2.wait(P, "tensor", e * 16 + dc * 2 + fh)
                if fh == 0:
                    rPH.acquire(P, "tensor", dc)
                for fcL in range(16):
                    fcg = fh * 16 + fcL
                    for nbi, (lo, hi) in enumerate(NB):
                        P.op("tensor", lambda Eo, p=pH[opar][:, lo:hi],
                             w=w2b[w2i][:, fcL * 128:(fcL + 1) * 128],
                             x=H[:, fcg * CAP + lo:fcg * CAP + hi],
                             st=(fh == 0 and fcL == 0), sp=(fh == 1 and fcL == 15):
                             Eo.matmul(p, w, x, start=st, stop=sp),
                             inc=(sPE if (nbi == 1 and fcL == 15) else None))
                rW2.release(e * 16 + dc * 2 + fh, sPE)
            P.wait("vector", sPE)
            P.op("vector", lambda Eo, o=o_sb[:, dc * CAP:(dc + 1) * CAP],
                 p=pH[opar][:, 0:CAP], b=b2ts[:, dc * 8 + e:dc * 8 + e + 1]:
                 Eo.tensor_scalar(o, p, b, None, op0=ALU.add), inc=sDV)
            rPH.release(dc, sDV)
        o_done = sDV.n
        P.wait("tensor", sDV, o_done)
        for g in range(NG):
            rSCB.acquire(P, "tensor", e * NG + g)
            rTOUT.acquire(P, "tensor", 0)
            for dc in range(8):
                P.op("tensor", lambda Eo, dc=dc, g=g,
                     o=pTout[:, (dc % 4) * 128:(dc % 4 + 1) * 128],
                     a=o_sb[:, dc * CAP + g * 128:dc * CAP + (g + 1) * 128]:
                     Eo.transpose(o, a, idf[:]),
                     inc=(sPE if dc % 4 == 3 else None))
                if dc % 4 == 3:
                    P.wait("vector", sPE)
                    dc0 = dc - 3
                    for i in range(4):
                        P.op("vector", lambda Eo, i=i, dc0=dc0, g=g,
                             o=scatb[g % 2][:, (dc0 + i) * 128:(dc0 + i + 1) * 128],
                             p=pTout[:, i * 128:(i + 1) * 128],
                             gvc=gv[ep][:, g:g + 1]:
                             Eo.tensor_scalar(o, p, gvc, None, op0=ALU.mult),
                             inc=(sDV if i == 3 else None))
                    rTOUT.rel[0] = [(sDV, sDV.n)]
                    if dc == 3:
                        rTOUT.acquire(P, "tensor", 0)
            P.wait("gpsimd", sDV)
            P.wait("gpsimd", sSCA)
            P.op("gpsimd", lambda Eo, g=g, ep=ep:
                 Eo.indirect_dma_start(
                     out=d_y1sp, out_offset=IndirectOffsetOnAxis(ap=idxi_s[ep][:, g:g + 1], axis=0),
                     in_=scatb[g % 2][:], in_offset=None,
                     bounds_check=regB_tok, oob_is_err=False,
                     compute_op=ALU.add),
                 inc=sSCA, inc_n=16)
            rSCB.release(e * NG + g, sSCA)
        rIX.release(e, sSCA)
        rGV.release(e, sSCA)

    if "O_acc" in taps:
        P.wait("sync", sSCA)
        P.op("sync", lambda Eo: Eo.dma_start(out=taps["O_acc"], in_=o_sb[:]),
             inc=sTAP, inc_n=16)
    P.wait("sync", sTAP)
    barrier(P)
    with nc.Block("b4") as blk:
        P.run(blk)
    es4.close()

    es5 = ExitStack()
    sb5 = lambda name, shape, dt: es5.enter_context(nc.sbuf_tensor(name, shape, dt))
    z2ch = [sb5("z2ch%d" % i, [128, C], F32) for i in range(3)]
    sq2 = [sb5("sq2%d" % i, [128, C], F32) for i in range(3)]
    mu2 = [sb5("mu2%d" % i, [128, 1], F32) for i in range(3)]
    var2_ = [sb5("var2%d" % i, [128, 1], F32) for i in range(3)]
    lnv2_ = [sb5("lnv2%d" % i, [128, 1], F32) for i in range(3)]
    rstd2 = [sb5("rstd2%d" % i, [128, 1], F32) for i in range(3)]

    P = Prog()
    for _eng in ENGINES:
        P.wait(_eng, sC1, ld_consts)
        P.wait(_eng, sC2, wl_consts)
    P.wait("sync", sSCA)
    rZ2 = Rot(3)
    rSQ2 = Rot(3)
    rMU2 = Rot(3)
    rRS2 = Rot(3)
    for tc in range(NCH):
        par = tc % 3
        z2 = z2ch[par]
        rZ2.acquire(P, "sync", tc)
        dYC.load(P, "sync", tc, lambda Eo, o=z2[:], s=d_y1sp[tc * 128:(tc + 1) * 128, :]:
                 Eo.dma_start(out=o, in_=s))
        dYC.wait(P, "vector", tc)
        rMU2.acquire(P, "vector", tc)
        mu = mu2[par]
        P.eop("vector", lambda Eo, m=mu[:], z_=z2[:]: Eo.reduce_sum(m, z_, axis=AX.X), sDV)
        P.eop("vector", lambda Eo, m=mu[:]:
              Eo.tensor_scalar(m, m, c_inv1024, None, op0=ALU.mult), sDV)
        P.wait("scalar", sDV)
        rSQ2.acquire(P, "scalar", tc)
        P.op("scalar", lambda Eo, q=sq2[par][:], z_=z2[:], m=mu[:]:
             Eo.activation(q, z_, AF.Square, bias=m, scale=f32(-1.0)), inc=sAC)
        P.wait("vector", sAC)
        var2 = var2_[par]
        lnv2 = lnv2_[par]
        P.eop("vector", lambda Eo, vv=var2[:], q=sq2[par][:]: Eo.reduce_sum(vv, q, axis=AX.X), sDV)
        P.op("vector", lambda Eo, vv=var2[:]:
             Eo.tensor_scalar(vv, vv, c_inv1024, None, op0=ALU.mult), inc=sDV)
        rSQ2.rel[par] = [(sDV, sDV.n)]
        P.wait("scalar", sDV)
        rRS2.acquire(P, "scalar", tc)
        P.eop("scalar", lambda Eo, lv=lnv2[:], vv=var2[:]: Eo.activation(lv, vv, AF.Ln, bias=f32(EPS)), sAC)
        P.op("scalar", lambda Eo, r=rstd2[par][:], lv=lnv2[:]:
             Eo.activation(r, lv, AF.Exp, scale=f32(-0.5)), inc=sAC)
        P.wait("vector", sAC)
        P.op("vector", lambda Eo, z_=z2[:], m=mu[:], r=rstd2[par][:]:
             Eo.tensor_scalar(z_, z_, m, r, op0=ALU.subtract, op1=ALU.mult), inc=sDV)
        rMU2.release(tc, sDV)
        rRS2.release(tc, sDV)
        P.wait("sync", sDV)
        dSTp.load(P, "sync", tc, lambda Eo, o=d_y2[tc * 128:(tc + 1) * 128, :], z_=z2[:]:
                  Eo.dma_start(out=o, in_=z_))
        rZ2.rel[par] = [dSTp.sem_count(tc)]
    for s_ in dSTp.sems:
        P.wait("sync", s_)
    P.wait("sync", sTAP)
    with nc.Block("b5") as blk:
        P.run(blk)
    es5.close()
    esP.close()
    esS.close()
    return nc


_CACHE = {}


def kernel(L0_attn_x, L0_attn_Wq, L0_attn_bq, L0_attn_Wk, L0_attn_bk,
           L0_attn_Wv, L0_attn_bv, L0_ffn_router_w, L0_ffn_w1, L0_ffn_b1,
           L0_ffn_w2, L0_ffn_b2):
    x = np.ascontiguousarray(np.asarray(L0_attn_x, dtype=np.float32))
    Wq = np.asarray(L0_attn_Wq, np.float32)
    Wk = np.asarray(L0_attn_Wk, np.float32)
    Wv = np.asarray(L0_attn_Wv, np.float32)
    bq = np.asarray(L0_attn_bq, np.float32)
    bk = np.asarray(L0_attn_bk, np.float32)
    bv = np.asarray(L0_attn_bv, np.float32)
    rw_in = np.asarray(L0_ffn_router_w, np.float32)
    w1 = np.ascontiguousarray(np.asarray(L0_ffn_w1, np.float32))
    b1 = np.asarray(L0_ffn_b1, np.float32)
    w2 = np.ascontiguousarray(np.asarray(L0_ffn_w2, np.float32))
    b2 = np.asarray(L0_ffn_b2, np.float32)

    wqkvT = np.concatenate([Wq.T, Wk.T, Wv.T], axis=1)
    wqkvT = np.ascontiguousarray(wqkvT.reshape(8, 128, 192).transpose(1, 0, 2))
    qkb = np.zeros((128, 2), np.float32)
    qkb[0:64, 0] = bq
    qkb[0:64, 1] = bk
    bvrow = bv.reshape(1, HS)
    b1t = np.ascontiguousarray(
        b1.reshape(E, 32, 128).transpose(2, 0, 1))
    rw_pre = np.ascontiguousarray(rw_in.reshape(8, 128, E).transpose(1, 0, 2))
    identb = np.eye(128, dtype=bf16)
    identf = np.eye(128, dtype=np.float32)
    qpiota = (np.arange(128, dtype=np.float32)[:, None]
              - np.arange(512, dtype=np.float32)[None, :])
    onesb = np.ones((1, 128), dtype=bf16)
    consts = np.zeros((128, 12), np.float32)
    consts[:, C_INV1024] = 1.0 / 1024.0
    consts[:, C_NEG132] = -1.0 / 32.0
    consts[:, C_ONE] = 1.0
    consts[:, C_NEG1E30] = -1e30
    consts[:, C_NEG1] = -1.0
    selb = np.zeros((E, E * 128), dtype=bf16)
    for e in range(E):
        selb[e, e * 128:(e + 1) * 128] = 1
    tokid = (np.arange(NCH, dtype=np.float32)[None, :] * 128.0
             + np.arange(128, dtype=np.float32)[:, None])
    evals = np.tile(np.arange(E, dtype=np.float32)[None, :], (128, 1))
    consts[:, C_BIG] = 1.0e6
    consts[:, C_EIGHT] = 8.0
    consts[:, C_CAP] = float(CAP)

    nc = build_nc()
    in_maps = []
    for c in range(NCORES):
        b = c // 4
        ph = c % 4
        chunks = [4 * j + ph for j in range(NCH)]
        x_b = x[b]
        x_own = np.ascontiguousarray(
            x_b.reshape(64, 128, C)[chunks].reshape(TOK, C))
        qoffb = np.full((128, 1), 128.0 * ph, np.float32)
        mth = np.zeros((128, 4), np.float32)
        for t in range(4):
            mth[:, t] = (t - ph) * 128.0
        in_maps.append(dict(
            x_own=x_own, x_batch=x_b, wqkvT=wqkvT, qkb=qkb, bvrow=bvrow,
            b1t=b1t, b2w=b2, rw=rw_pre, w1=w1, w2=w2, identb=identb,
            identf=identf, qpiota=qpiota, onesb=onesb, qoffb=qoffb,
            consts=consts, selb=selb, mth=mth, tokid=tokid, evals=evals,
            b2t=np.ascontiguousarray(b2.T.reshape(8, 128, E).transpose(1, 0, 2))))
    res = run_bass_kernel_spmd(nc, in_maps, core_ids=list(range(NCORES)))
    out = np.zeros((B, T, C), np.float32)
    for c in range(NCORES):
        b = c // 4
        ph = c % 4
        y2 = np.asarray(res.results[c]["y2"]).reshape(NCH, 128, C)
        for j in range(NCH):
            out[b, (4 * j + ph) * 128:(4 * j + ph + 1) * 128, :] = y2[j]
    return out


# version 16
# speedup vs baseline: 1.8473x; 1.3957x over previous
import numpy as np
import ml_dtypes
from contextlib import ExitStack

import concourse.bass as bass
import concourse.mybir as mybir
from concourse.bass_utils import run_bass_kernel_spmd
from concourse.bass_core_rust import IndirectOffsetOnAxis

F32 = mybir.dt.float32
BF16 = mybir.dt.bfloat16
AF = mybir.ActivationFunctionType
ALU = mybir.AluOpType
AX = mybir.AxisListType
f32 = np.float32
bf16 = ml_dtypes.bfloat16

B, T, C = 2, 8192, 1024
HS = 64
E, FF = 8, 4096
NCORES = 8
TOK = 2048
NCH = 16
KCH = 64
NT = 1024
EPS = 1e-5

C_INV1024, C_NEG132, C_ONE, C_NEG1E30, C_ZERO, C_NEG1, C_BIG, C_EIGHT, C_CAP = 0, 1, 2, 3, 4, 5, 6, 7, 8
CAP, NG = 640, 5

ENGINES = ("sync", "scalar", "vector", "tensor", "gpsimd")


class Sem:
    def __init__(self, nc, es, name):
        self.h = es.enter_context(nc.semaphore(name))
        self.n = 0

    def bump(self, k):
        self.n += k
        return self.n


class Prog:

    def __init__(self):
        self.streams = {e: [] for e in ENGINES}

    def op(self, eng, fn, inc=None, inc_n=1):
        if inc is not None:
            inc.bump(inc_n)
            self.streams[eng].append(
                lambda Eo, fn=fn, s=inc.h, n=inc_n: fn(Eo).then_inc(s, n))
            if (PARANOID[0] is True and eng == "gpsimd" and inc_n == 16) or \
               (PARANOID[0] == "ind" and eng == "gpsimd" and inc_n == 16 and getattr(self, "_mark_ind", False)):
                self.wait(eng, inc)
        else:
            self.streams[eng].append(lambda Eo, fn=fn: fn(Eo))

    def eop(self, eng, fn, sem):
        self.op(eng, fn, inc=sem)
        self.wait(eng, sem)

    def wait(self, eng, sem, val=None):
        v = sem.n if val is None else val
        if v > 0:
            self.streams[eng].append(lambda Eo, s=sem.h, v=v: Eo.wait_ge(s, v))

    def run(self, block):
        for eng in ENGINES:
            fns = tuple(self.streams[eng])

            def body(Eo, fns=fns):
                for f in fns:
                    f(Eo)

            getattr(block, eng)(body)


class Rot:

    def __init__(self, n):
        self.n = n
        self.rel = [[] for _ in range(n)]

    def acquire(self, P, eng, i):
        for sem, cnt in self.rel[i % self.n]:
            P.wait(eng, sem, cnt)
        self.rel[i % self.n] = []

    def release(self, i, sem):
        self.rel[i % self.n].append((sem, sem.n))


class DmaSlots:

    def __init__(self, nc, es, name, n=2):
        self.sems = [Sem(nc, es, "%s%d" % (name, i)) for i in range(n)]
        self.n = n

    def load(self, P, eng, i, fn):
        P.op(eng, fn, inc=self.sems[i % self.n], inc_n=16)

    def wait(self, P, eng, i):
        P.wait(eng, self.sems[i % self.n])

    def sem_count(self, i):
        s = self.sems[i % self.n]
        return (s, s.n)


PARANOID = ["ind"]


def build_nc(debug_taps=(), n_experts=E):
    nc = bass.Bass("TRN2", target_bir_lowering=False)

    d_xown = nc.dram_tensor("x_own", [TOK, C], F32, kind="ExternalInput").ap()
    d_xbat = nc.dram_tensor("x_batch", [T, C], F32, kind="ExternalInput").ap()
    d_wqkv = nc.dram_tensor("wqkvT", [128, 8, 192], F32, kind="ExternalInput").ap()
    d_qkb = nc.dram_tensor("qkb", [128, 2], F32, kind="ExternalInput").ap()
    d_bvr = nc.dram_tensor("bvrow", [1, HS], F32, kind="ExternalInput").ap()
    d_b1t = nc.dram_tensor("b1t", [128, E, 32], F32, kind="ExternalInput").ap()
    d_b2w = nc.dram_tensor("b2w", [E, C], F32, kind="ExternalInput").ap()
    d_rw = nc.dram_tensor("rw", [128, 8, E], F32, kind="ExternalInput").ap()
    d_w1 = nc.dram_tensor("w1", [E, C, FF], F32, kind="ExternalInput").ap()
    d_w2 = nc.dram_tensor("w2", [E, FF, C], F32, kind="ExternalInput").ap()
    d_idb = nc.dram_tensor("identb", [128, 128], BF16, kind="ExternalInput").ap()
    d_idf = nc.dram_tensor("identf", [128, 128], F32, kind="ExternalInput").ap()
    d_qpi = nc.dram_tensor("qpiota", [128, 512], F32, kind="ExternalInput").ap()
    d_ones = nc.dram_tensor("onesb", [1, 128], BF16, kind="ExternalInput").ap()
    d_qoff = nc.dram_tensor("qoffb", [128, 1], F32, kind="ExternalInput").ap()
    d_cst = nc.dram_tensor("consts", [128, 12], F32, kind="ExternalInput").ap()
    d_sel = nc.dram_tensor("selb", [E, E * 128], BF16, kind="ExternalInput").ap()
    d_mth = nc.dram_tensor("mth", [128, 4], F32, kind="ExternalInput").ap()
    d_tokid = nc.dram_tensor("tokid", [128, NCH], F32, kind="ExternalInput").ap()
    d_evals = nc.dram_tensor("evals", [128, E], F32, kind="ExternalInput").ap()
    d_b2t = nc.dram_tensor("b2t", [128, 8, E], F32, kind="ExternalInput").ap()
    d_y1bf = nc.dram_tensor("y1bf", [TOK, C + 8], BF16, kind="Internal").ap()
    d_idxf = nc.dram_tensor("idxf", [E, CAP], F32, kind="Internal").ap()
    d_gc = nc.dram_tensor("gc", [TOK, E], F32, kind="Internal").ap()
    d_y1sp = nc.dram_tensor("y1sp", [TOK, C], F32, kind="Internal").ap()
    d_y2 = nc.dram_tensor("y2", [TOK, C], F32, kind="ExternalOutput").ap()
    taps = {}
    for name, shape in debug_taps:
        taps[name] = nc.dram_tensor("tap_" + name, shape, F32, kind="ExternalOutput").ap()

    esP = ExitStack()
    sb = lambda name, shape, dt: esP.enter_context(nc.sbuf_tensor(name, shape, dt))

    idb = sb("idb", [128, 128], BF16)
    idf = sb("idf", [128, 128], F32)
    onesb = sb("onesb_s", [1, 128], BF16)
    qoffb = sb("qoffb_s", [128, 1], F32)
    mth = sb("mth_s", [128, 4], F32)
    tokid = sb("tokid_s", [128, NCH], F32)
    evals = sb("evals_s", [128, E], F32)
    b2ts = sb("b2t_s", [128, 8 * E], F32)
    cst = sb("cst", [128, 12], F32)
    b1t = sb("b1t_s", [128, E * 32], F32)
    b2b = sb("b2b", [E, C], BF16)
    rw = sb("rw_s", [128, 8 * E], F32)
    gatesT = sb("gatesT", [E, TOK], BF16)
    selb = sb("selb_s", [E, E * 128], BF16)

    esA = ExitStack()
    sbA = lambda name, shape, dt: esA.enter_context(nc.sbuf_tensor(name, shape, dt))
    qpi = sbA("qpi", [128, 512], F32)
    maskadd = sbA("maskadd", [128, 512], F32)
    qT = sbA("qT", [65, TOK], BF16)
    kT = sbA("kT", [65, T], BF16)
    vtm = sbA("vtm", [128, KCH * 65], BF16)
    condt = sbA("condt", [128, 4 * 128], BF16)
    o_tm = sbA("o_tm", [128, NCH * HS], F32)
    mA = sbA("mA", [128, 1], F32)
    mB = sbA("mB", [128, 1], F32)
    dm = sbA("dm", [128, 1], F32)
    edm = sbA("edm", [128, 1], F32)
    denA = sbA("denA", [128, 1], F32)
    denB = sbA("denB", [128, 1], F32)
    negm32 = sbA("negm32", [128, 1], F32)
    lden = sbA("lden", [128, 1], F32)
    bias2 = sbA("bias2", [128, 1], F32)
    bsum = [sbA("bsum0", [128, 1], F32), sbA("bsum1", [128, 1], F32)]

    esS = ExitStack()
    mks = lambda name: Sem(nc, esS, name)
    sC1 = mks("sC1")
    sC2 = mks("sC2")
    sTAP = mks("sTAP")
    sPL = mks("sPL")
    sPE = mks("sPE")
    sAC = mks("sAC")
    sDV = mks("sDV")
    sBAR = mks("sBAR")
    dXB = DmaSlots(nc, esS, "dXB")
    dZC = DmaSlots(nc, esS, "dZC", n=3)
    dSPp = DmaSlots(nc, esS, "dSPp", n=3)
    dW1 = DmaSlots(nc, esS, "dW1", n=4)
    dW2 = DmaSlots(nc, esS, "dW2", n=4)
    dYC = DmaSlots(nc, esS, "dYC", n=3)
    dSTp = DmaSlots(nc, esS, "dSTp", n=3)
    dFD = DmaSlots(nc, esS, "dFD")
    dSPb = DmaSlots(nc, esS, "dSPb", n=3)
    sGC = mks("sGC")
    sSCT = mks("sSCT")
    sSCTp = [mks("sSCTp0"), mks("sSCTp1")]
    sSCA = mks("sSCA")
    dPF = DmaSlots(nc, esS, "dPF", n=1)
    dIX = DmaSlots(nc, esS, "dIX", n=2)
    dGA = DmaSlots(nc, esS, "dGA", n=2)
    dGV = DmaSlots(nc, esS, "dGV", n=2)

    c_inv1024 = cst[:, C_INV1024:C_INV1024 + 1]
    c_neg132 = cst[:, C_NEG132:C_NEG132 + 1]
    c_one = cst[:, C_ONE:C_ONE + 1]
    c_neg1e30 = cst[:, C_NEG1E30:C_NEG1E30 + 1]
    c_zero = cst[:, C_ZERO:C_ZERO + 1]
    c_neg1 = cst[:, C_NEG1:C_NEG1 + 1]
    c_big = cst[:, C_BIG:C_BIG + 1]
    c_eight = cst[:, C_EIGHT:C_EIGHT + 1]
    c_cap = cst[:, C_CAP:C_CAP + 1]

    regB_cap = nc.gpsimd.alloc_register("bnd_cap")
    regB_tok = nc.gpsimd.alloc_register("bnd_tok")
    regB_gc = nc.gpsimd.alloc_register("bnd_gc")
    regB_all = nc.gpsimd.alloc_register("bnd_all")

    barrier_count = [0]

    def barrier(P):
        barrier_count[0] += 1
        for eng in ENGINES:
            P.op(eng, lambda Eo: Eo.nop(), inc=sBAR)
        for eng in ENGINES:
            P.wait(eng, sBAR, 5 * barrier_count[0])

    es1 = ExitStack()
    sb1 = lambda name, shape, dt: es1.enter_context(nc.sbuf_tensor(name, shape, dt))
    ps1 = lambda name, shape, dt: es1.enter_context(nc.psum_tensor(name, shape, dt))
    xb = [sb1("xb0", [128, C], BF16), sb1("xb1", [128, C], BF16)]
    xT = [sb1("xT0", [128, C], BF16), sb1("xT1", [128, C], BF16)]
    wqkv = sb1("wqkv", [128, 8 * 192], BF16)
    bvr = sb1("bvr", [1, HS], BF16)
    qkb = sb1("qkb_s", [128, 2], F32)
    pXT = [ps1("pXT0", [128, C], BF16), ps1("pXT1", [128, C], BF16)]
    pQK = [ps1("pQK0", [64, 128], F32), ps1("pQK1", [64, 128], F32)]
    pV = [ps1("pV0", [128, HS], F32), ps1("pV1", [128, HS], F32)]

    P = Prog()
    for dst, src in [(idb, d_idb), (idf, d_idf), (qpi, d_qpi), (qoffb, d_qoff),
                     (mth, d_mth), (tokid, d_tokid), (evals, d_evals),
                     (b2ts, d_b2t.rearrange("p a b -> p (a b)")),
                     (cst, d_cst), (qkb[:], d_qkb),
                     (b1t, d_b1t.rearrange("p e k -> p (e k)")),
                     (rw, d_rw.rearrange("p c e -> p (c e)"))]:
        ap = dst[:] if hasattr(dst, "ap") else dst
        P.op("sync", lambda Eo, a=ap, s=src: Eo.dma_start(out=a, in_=s), inc=sC1, inc_n=16)
    for dst, src in [(onesb[:], d_ones), (selb[:], d_sel),
                     (wqkv[:], d_wqkv.rearrange("p c h -> p (c h)")),
                     (bvr[:], d_bvr), (b2b[:], d_b2w)]:
        P.op("gpsimd", lambda Eo, a=dst, s=src: Eo.dma_start(out=a, in_=s), inc=sC2, inc_n=16)
    ld_consts = sC1.n
    wl_consts = sC2.n
    for _eng in ("tensor", "scalar", "vector", "gpsimd"):
        P.wait(_eng, sC1, ld_consts)
        P.wait(_eng, sC2, wl_consts)
    P.op("gpsimd", lambda Eo: Eo.reg_mov(regB_cap, CAP - 1))
    P.op("gpsimd", lambda Eo: Eo.reg_mov(regB_tok, TOK - 1))
    P.op("gpsimd", lambda Eo: Eo.reg_mov(regB_gc, TOK * E - 1))
    P.op("gpsimd", lambda Eo: Eo.reg_mov(regB_all, E * CAP - 1))
    P.op("gpsimd", lambda Eo: Eo.memset(kT[64:65, :], 1.0), inc=sPL)
    P.op("gpsimd", lambda Eo: Eo.memset(vtm[:], 1.0), inc=sPL)
    P.wait("vector", sC1, ld_consts)
    P.eop("vector", lambda Eo: Eo.tensor_scalar(maskadd[0:128, 0:128], qpi[:, 0:128], c_neg1, None, op0=ALU.mult), sDV)
    for _t in range(4):
        P.eop("vector", lambda Eo, t=_t: Eo.tensor_scalar(
            condt[:, t * 128:(t + 1) * 128], maskadd[0:128, 0:128],
            mth[:, t:t + 1], None, op0=ALU.is_ge), sDV)
    P.eop("vector", lambda Eo: Eo.tensor_scalar(maskadd[:], qpi[:], qoffb[:, 0:1], None, op0=ALU.add), sDV)
    P.eop("vector", lambda Eo: Eo.tensor_scalar(maskadd[:], maskadd[:], c_zero, None, op0=ALU.is_lt), sDV)
    P.op("vector", lambda Eo: Eo.tensor_scalar(maskadd[:], maskadd[:], c_neg1e30, None, op0=ALU.mult), inc=sDV)

    rXB = Rot(2)
    rXT = Rot(2)
    rQK = Rot(2)
    rV = Rot(2)

    rPXT = Rot(2)

    def chunk_setup(i, src_ap, is_q, kc):
        pi = i % 2
        rXB.acquire(P, "gpsimd", i)
        dXB.load(P, "gpsimd", i, lambda Eo, a=xb[pi][:], s=src_ap: Eo.dma_start(out=a, in_=s))
        dXB.wait(P, "tensor", i)
        rPXT.acquire(P, "tensor", i)
        rXT_psum = pXT[pi]
        for dc in range(8):
            P.op("tensor", lambda Eo, o=rXT_psum[:, dc * 128:(dc + 1) * 128],
                 a=xb[pi][:, dc * 128:(dc + 1) * 128]: Eo.transpose(o, a, idb[:]),
                 inc=(sPE if dc == 7 else None))
        transposes_done = sPE.n
        P.wait("vector", sPE)
        rXT.acquire(P, "vector", i)
        P.op("vector", lambda Eo, o=xT[pi][:], a=rXT_psum[:]: Eo.tensor_copy(o, a), inc=sDV)
        rPXT.release(i, sDV)
        evac_cnt = sDV.n
        P.wait("tensor", sDV, evac_cnt)
        if is_q:
            rQK.acquire(P, "tensor", i)
            for dc in range(8):
                P.op("tensor", lambda Eo, p=pQK[pi][:],
                     w=wqkv[:, dc * 192:dc * 192 + 64],
                     x=xT[pi][:, dc * 128:(dc + 1) * 128], st=(dc == 0), sp=(dc == 7):
                     Eo.matmul(p, w, x, start=st, stop=sp),
                     inc=(sPE if dc == 7 else None))
            P.wait("scalar", sPE)
            P.op("scalar", lambda Eo, o=qT[0:64, i * 128:(i + 1) * 128], p=pQK[pi][:]:
                 Eo.activation(o, p, AF.Identity, bias=qkb[0:64, 0:1], scale=f32(1.0)),
                 inc=sAC)
            rQK.release(i, sAC)
        else:
            rQK.acquire(P, "tensor", i)
            for dc in range(8):
                P.op("tensor", lambda Eo, p=pQK[pi][:],
                     w=wqkv[:, dc * 192 + 64:dc * 192 + 128],
                     x=xT[pi][:, dc * 128:(dc + 1) * 128], st=(dc == 0), sp=(dc == 7):
                     Eo.matmul(p, w, x, start=st, stop=sp),
                     inc=(sPE if dc == 7 else None))
            P.wait("scalar", sPE)
            P.op("scalar", lambda Eo, o=kT[0:64, kc * 128:(kc + 1) * 128], p=pQK[pi][:]:
                 Eo.activation(o, p, AF.Identity, bias=qkb[0:64, 1:2], scale=f32(1.0)),
                 inc=sAC)
            rQK.release(i, sAC)
            rV.acquire(P, "tensor", i)
            for dc in range(8):
                P.op("tensor", lambda Eo, p=pV[pi][:],
                     x=xT[pi][:, dc * 128:(dc + 1) * 128],
                     w=wqkv[:, dc * 192 + 128:dc * 192 + 192], st=(dc == 0):
                     Eo.matmul(p, x, w, start=st, stop=False))
            P.op("tensor", lambda Eo, p=pV[pi][:]:
                 Eo.matmul(p, onesb[:], bvr[:], start=False, stop=True), inc=sPE)
            P.wait("vector", sPE)
            P.wait("vector", sPL, 2)
            P.op("vector", lambda Eo, o=vtm[:, kc * 65:kc * 65 + HS], p=pV[pi][:]:
                 Eo.tensor_copy(o, p), inc=sDV)
            rV.release(i, sDV)
        rXB.rel[pi] = [(sPE, transposes_done)]
        rXT.rel[pi] = [(sPE, sPE.n)]

    for i in range(NCH):
        chunk_setup(i, d_xown[i * 128:(i + 1) * 128, :], True, None)
    for kc in range(KCH):
        chunk_setup(NCH + kc, d_xbat[kc * 128:(kc + 1) * 128, :], False, kc)

    P.wait("sync", sTAP)
    barrier(P)
    with nc.Block("b1") as blk:
        P.run(blk)
    es1.close()

    es2 = ExitStack()
    sb2 = lambda name, shape, dt: es2.enter_context(nc.sbuf_tensor(name, shape, dt))
    ps2 = lambda name, shape, dt: es2.enter_context(nc.psum_tensor(name, shape, dt))
    PT4 = [sb2("PT40", [128, 512], BF16), sb2("PT41", [128, 512], BF16)]
    ots = sb2("ots", [65, 128], F32)
    mtab = [sb2("mtab0", [128, NCH], F32), sb2("mtab1", [128, NCH], F32)]
    mfin = sb2("mfin", [128, 1], F32)
    foldv = [sb2("foldv0", [128, 1], F32), sb2("foldv1", [128, 1], F32)]
    dtmp = sb2("dtmp", [128, 1], F32)
    rcp = sb2("rcp", [128, 1], F32)
    pSA = [ps2("pSA0", [128, 512], F32), ps2("pSA1", [128, 512], F32)]
    pSk = [ps2("pSk0", [128, 512], F32), ps2("pSk1", [128, 512], F32)]
    pO = ps2("pO", [65, 128], F32)
    pOt = ps2("pOt", [128, 65], F32)

    P = Prog()
    for _eng in ENGINES:
        P.wait(_eng, sC1, ld_consts)
        P.wait(_eng, sC2, wl_consts)
    rSA = Rot(2)
    rSK = Rot(2)
    rPT4 = Rot(2)
    rPO = Rot(1)
    rMT = Rot(2)
    fold_done = {}

    def pass_a(j):
        par2 = j % 2
        rMT.acquire(P, "vector", j)
        for kb in range(j + 1):
            par = kb % 2
            rSA.acquire(P, "tensor", kb)
            P.op("tensor", lambda Eo, p=pSA[par][:],
                 q=qT[0:64, j * 128:(j + 1) * 128],
                 k=kT[0:64, kb * 512:(kb + 1) * 512]:
                 Eo.matmul(p, q, k, start=True, stop=True), inc=sPE)
            P.wait("vector", sPE)
            if kb == j:
                P.eop("vector", lambda Eo, p=pSA[par][:]:
                      Eo.tensor_add(p, p, maskadd[:]), sDV)
            P.op("vector", lambda Eo, o=mtab[par2][:, kb:kb + 1], p=pSA[par][:]:
                 Eo.reduce_max(o, p, axis=AX.X), inc=sDV)
            rSA.release(kb, sDV)
        P.wait("vector", sDV)
        P.eop("vector", lambda Eo, mt=mtab[par2][:, 0:j + 1]:
              Eo.reduce_max(mfin[:], mt, axis=AX.X), sDV)
        dFD.wait(P, "vector", j)
        P.eop("vector", lambda Eo, fv=foldv[j % 2][:]:
              Eo.tensor_scalar(fv, mfin[:], c_neg1, None, op0=ALU.mult), sDV)
        dFD.wait(P, "gpsimd", j)
        P.wait("gpsimd", sDV)
        dFD.load(P, "gpsimd", j, lambda Eo, o=qT[64:65, j * 128:(j + 1) * 128],
                 fv=foldv[j % 2][:]: Eo.dma_start(out=o, in_=fv))
        fold_done[j] = None

    def pass_b(j):
        par2 = j % 2
        ngrp = j + 1
        dFD.wait(P, "tensor", j)

        def b_mms(kg):
            par = kg % 2
            rSK.acquire(P, "tensor", kg)
            for t in range(4):
                kk = 4 * kg + t
                P.op("tensor", lambda Eo, p=pSk[par][:, t * 128:(t + 1) * 128],
                     k=kT[:, kk * 128:(kk + 1) * 128],
                     q=qT[:, j * 128:(j + 1) * 128]:
                     Eo.matmul(p, k, q, start=True, stop=True),
                     inc=(sPE if t == 3 else None))
            return sPE.n

        mm_done = {0: b_mms(0)}
        for kg in range(ngrp):
            par = kg % 2
            if kg + 1 < ngrp:
                mm_done[kg + 1] = b_mms(kg + 1)
            P.wait("scalar", sPE, mm_done[kg])
            rPT4.acquire(P, "scalar", kg)
            P.op("scalar", lambda Eo, o=PT4[par][:], p=pSk[par][:]:
                 Eo.activation(o, p, AF.Exp, scale=f32(1.0 / 32.0)), inc=sAC)
            rSK.release(kg, sAC)
            if kg == j:
                P.wait("vector", sAC)
                for t in range(4):
                    P.op("vector", lambda Eo, t=t, o=PT4[par][:, t * 128:(t + 1) * 128],
                         c=condt[:, t * 128:(t + 1) * 128]:
                         Eo.tensor_tensor(o, o, c, op=ALU.mult),
                         inc=(sDV if t == 3 else None))
                P.wait("tensor", sDV)
            else:
                P.wait("tensor", sAC)
            if kg == 0:
                rPO.acquire(P, "tensor", 0)
            for t in range(4):
                kk = 4 * kg + t
                P.op("tensor", lambda Eo, v=vtm[:, kk * 65:(kk + 1) * 65],
                     pp=PT4[par][:, t * 128:(t + 1) * 128],
                     st=(kg == 0 and t == 0), sp=(kg == j and t == 3):
                     Eo.matmul(pO[:], v, pp, start=st, stop=sp),
                     inc=(sPE if t == 3 else None))
            rPT4.release(kg, sPE)
        P.wait("scalar", sPE)
        P.op("scalar", lambda Eo: Eo.activation(ots[:], pO[:], AF.Copy), inc=sAC)
        rPO.rel[0] = [(sAC, sAC.n)]
        P.wait("tensor", sAC)
        P.op("tensor", lambda Eo: Eo.transpose(pOt[:], ots[:], idf[0:65, 0:65]), inc=sPE)
        P.wait("vector", sPE)
        P.eop("vector", lambda Eo: Eo.tensor_copy(dtmp[:], pOt[:, 64:65]), sDV)
        P.eop("vector", lambda Eo: Eo.reciprocal(rcp[:], dtmp[:]), sDV)
        P.op("vector", lambda Eo, o=o_tm[:, j * HS:(j + 1) * HS]:
             Eo.tensor_scalar(o, pOt[:, 0:HS], rcp[:, 0:1], None, op0=ALU.mult), inc=sDV)
        P.wait("tensor", sDV)
        P.wait("scalar", sDV)

    for j in range(NCH):
        pass_a(j)
        if j > 0:
            pass_b(j - 1)
    pass_b(NCH - 1)

    if "o_tm" in taps:
        P.wait("sync", sDV)
        P.op("sync", lambda Eo: Eo.dma_start(out=taps["o_tm"], in_=o_tm[:]),
             inc=sTAP, inc_n=16)
    if "condt" in taps:
        P.wait("gpsimd", sDV)
        P.op("gpsimd", lambda Eo: Eo.dma_start(out=taps["condt"], in_=condt[:]),
             inc=sTAP, inc_n=16)
    if "vtm" in taps:
        P.wait("gpsimd", sDV)
        P.op("gpsimd", lambda Eo: Eo.dma_start(out=taps["vtm"], in_=vtm[:]),
             inc=sTAP, inc_n=16)
    if "qTe" in taps:
        P.wait("gpsimd", sDV)
        P.op("gpsimd", lambda Eo: Eo.dma_start(out=taps["qTe"], in_=qT[:]),
             inc=sTAP, inc_n=16)
    if "qT" in taps:
        P.wait("gpsimd", sDV)
        P.op("gpsimd", lambda Eo: Eo.dma_start(out=taps["qT"], in_=qT[:]),
             inc=sTAP, inc_n=16)
    if "kT" in taps:
        P.wait("gpsimd", sDV)
        P.op("gpsimd", lambda Eo: Eo.dma_start(out=taps["kT"], in_=kT[:]),
             inc=sTAP, inc_n=16)
    if "vtm" in taps:
        P.wait("gpsimd", sDV)
        P.op("gpsimd", lambda Eo: Eo.dma_start(out=taps["vtm"], in_=vtm[:]),
             inc=sTAP, inc_n=16)
    P.wait("sync", sTAP)
    P.wait("gpsimd", sTAP)
    barrier(P)
    with nc.Block("b2") as blk:
        P.run(blk)
    es2.close()

    es3 = ExitStack()
    sb3 = lambda name, shape, dt: es3.enter_context(nc.sbuf_tensor(name, shape, dt))
    ps3 = lambda name, shape, dt: es3.enter_context(nc.psum_tensor(name, shape, dt))
    zch = [sb3("zch%d" % i, [128, C], F32) for i in range(3)]
    sqb = [sb3("sqb%d" % i, [128, C], F32) for i in range(3)]
    tmp4 = [sb3("tmp4%d" % i, [128, 512], F32) for i in range(3)]
    logits = sb3("logits", [E, TOK], F32)
    gates_tm = sb3("gates_tm", [128, 128], F32)
    muv = [sb3("mu%d" % i, [128, 1], F32) for i in range(3)]
    varv_ = [sb3("varv%d" % i, [128, 1], F32) for i in range(3)]
    lnvv_ = [sb3("lnvv%d" % i, [128, 1], F32) for i in range(3)]
    rstdv = [sb3("rstd%d" % i, [128, 1], F32) for i in range(3)]
    gtm_all = sb3("gtm_all", [128, NCH * 8], F32)
    eq1_all = sb3("eq1_all", [128, NCH * 8], F32)
    eq2_all = sb3("eq2_all", [128, NCH * 8], F32)
    mskl_all = sb3("mskl_all", [128, NCH * 8], F32)
    tA_all = sb3("tA_all", [128, NCH * 8], F32)
    tB_all = sb3("tB_all", [128, NCH * 8], F32)
    slots_all = sb3("slots_all", [128, NCH * 8], F32)
    selw_all = sb3("selw_all", [128, NCH * 8], F32)
    m1_all = sb3("m1_all", [128, NCH], F32)
    m2_all = sb3("m2_all", [128, NCH], F32)
    edv_all = sb3("edv_all", [128, NCH], F32)
    edv2_all = sb3("edv2_all", [128, NCH], F32)
    s1_all = sb3("s1_all", [128, NCH], F32)
    rv_all = sb3("rv_all", [128, NCH], F32)
    w2g_all = sb3("w2g_all", [128, NCH], F32)
    idx1_all = sb3("idx1_all", [128, NCH], F32)
    idx2_all = sb3("idx2_all", [128, NCH], F32)
    idxi1 = sb3("idxi1", [128, NCH], mybir.dt.int32)
    idxi2 = sb3("idxi2", [128, NCH], mybir.dt.int32)
    y1bch = [sb3("y1bch%d" % i, [128, C], BF16) for i in range(3)]
    ones8 = sb3("ones8", [E, TOK], F32)
    mask_fm = sb3("mask_fm", [E, TOK], F32)
    pos_fm = sb3("pos_fm", [E, TOK], F32)
    w1m_fm = sb3("w1m_fm", [E, TOK], F32)
    slot_fm = sb3("slot_fm", [E, TOK], F32)
    ecap = sb3("ecap", [128, 8], F32)
    pft = sb3("pft", [E, CAP], F32)
    pTf = [ps3("pTf%d" % i, [128, 512], F32) for i in range(3)]
    pLr = [ps3("pLr0", [E, 128], F32), ps3("pLr1", [E, 128], F32)]
    pG8 = ps3("pG8", [128, 16], F32)
    pTg = ps3("pTg", [E, 256], F32)

    P = Prog()
    for _eng in ENGINES:
        P.wait(_eng, sC1, ld_consts)
        P.wait(_eng, sC2, wl_consts)
    rZ = Rot(3)
    rSQ = Rot(3)
    rMU = Rot(3)
    rRS = Rot(3)
    rTF = Rot(3)
    rT4 = Rot(3)
    rPL = Rot(2)

    for tc in range(NCH):
        par = tc % 3
        z = zch[par]
        rZ.acquire(P, "sync", tc)
        dZC.load(P, "sync", tc, lambda Eo, o=z[:], s=d_xown[tc * 128:(tc + 1) * 128, :]:
                 Eo.dma_start(out=o, in_=s))
        dZC.wait(P, "gpsimd", tc)
        orep = o_tm[:, tc * HS:(tc + 1) * HS].unsqueeze(1).broadcast_to((128, 16, HS))
        P.op("gpsimd", lambda Eo, z_=z[:], r=orep:
             Eo.tensor_add(z_.rearrange("p (r h) -> p r h", r=16), z_.rearrange("p (r h) -> p r h", r=16), r), inc=sPL)
        rMU.acquire(P, "vector", tc)
        mu = muv[par]
        P.wait("vector", sPL)
        P.eop("vector", lambda Eo, z_=z[:], m=mu[:]: Eo.reduce_sum(m, z_, axis=AX.X), sDV)
        P.eop("vector", lambda Eo, m=mu[:]:
              Eo.tensor_scalar(m, m, c_inv1024, None, op0=ALU.mult), sDV)
        P.wait("scalar", sDV)
        rSQ.acquire(P, "scalar", tc)
        P.op("scalar", lambda Eo, z_=z[:], m=mu[:], q=sqb[par][:]:
             Eo.activation(q, z_, AF.Square, bias=m, scale=f32(-1.0)), inc=sAC)
        P.wait("vector", sAC)
        varv = varv_[par]
        lnvv = lnvv_[par]
        P.eop("vector", lambda Eo, vv=varv[:], q=sqb[par][:]: Eo.reduce_sum(vv, q, axis=AX.X), sDV)
        P.op("vector", lambda Eo, vv=varv[:]:
             Eo.tensor_scalar(vv, vv, c_inv1024, None, op0=ALU.mult), inc=sDV)
        rSQ.rel[par] = [(sDV, sDV.n)]
        P.wait("scalar", sDV)
        rRS.acquire(P, "scalar", tc)
        rstd = rstdv[par]
        P.eop("scalar", lambda Eo, lv=lnvv[:], vv=varv[:]: Eo.activation(lv, vv, AF.Ln, bias=f32(EPS)), sAC)
        P.op("scalar", lambda Eo, r=rstd[:], lv=lnvv[:]:
             Eo.activation(r, lv, AF.Exp, scale=f32(-0.5)), inc=sAC)
        P.wait("vector", sAC)
        P.op("vector", lambda Eo, z_=z[:], m=mu[:], r=rstd[:]:
             Eo.tensor_scalar(z_, z_, m, r, op0=ALU.subtract, op1=ALU.mult), inc=sDV)
        rMU.release(tc, sDV)
        rRS.release(tc, sDV)
        y1_done = sDV.n
        P.wait("sync", sDV, y1_done)
        dSPp.load(P, "sync", tc, lambda Eo, o=d_y1sp[tc * 128:(tc + 1) * 128, :], z_=z[:]:
                  Eo.dma_start(out=o, in_=z_))
        P.wait("gpsimd", sDV, y1_done)
        dSPb.wait(P, "gpsimd", tc)
        P.op("gpsimd", lambda Eo, o=y1bch[par][:], z_=z[:]: Eo.tensor_copy(o, z_), inc=sPL)
        P.wait("sync", sPL)
        dSPb.load(P, "sync", tc, lambda Eo, o=d_y1bf[tc * 128:(tc + 1) * 128, 0:C],
                  a=y1bch[par][:]: Eo.dma_start(out=o, in_=a))
        P.wait("tensor", sDV, y1_done)
        for g in range(2):
            tfp = pTf[(tc * 2 + g) % 3]
            rTF.acquire(P, "tensor", tc * 2 + g)
            for t in range(4):
                dc = g * 4 + t
                P.op("tensor", lambda Eo, o=tfp[:, t * 128:(t + 1) * 128],
                     a=z[:, dc * 128:(dc + 1) * 128]: Eo.transpose(o, a, idf[:]),
                     inc=(sPE if t == 3 else None))
            P.wait("scalar", sPE)
            rT4.acquire(P, "scalar", tc * 2 + g)
            tm4 = tmp4[(tc * 2 + g) % 3]
            P.op("scalar", lambda Eo, o=tm4[:], a=tfp[:]: Eo.activation(o, a, AF.Copy), inc=sAC)
            rTF.release(tc * 2 + g, sAC)
            P.wait("tensor", sAC)
            if g == 0:
                rPL.acquire(P, "tensor", tc)
            for t in range(4):
                dc = g * 4 + t
                P.op("tensor", lambda Eo, p=pLr[tc % 2][:],
                     w=rw[:, dc * 8:(dc + 1) * 8],
                     x=tm4[:, t * 128:(t + 1) * 128], st=(dc == 0), sp=(dc == 7):
                     Eo.matmul(p, w, x, start=st, stop=sp),
                     inc=(sPE if t == 3 else None))
            rT4.release(tc * 2 + g, sPE)
        P.wait("scalar", sPE)
        P.op("scalar", lambda Eo, o=logits[:, tc * 128:(tc + 1) * 128],
             p=pLr[tc % 2][:]: Eo.activation(o, p, AF.Copy), inc=sAC)
        rPL.release(tc, sAC)
        rZ.release(tc, sPE)
        rZ.rel[par].append(dSPp.sem_count(tc))
    logits_done = sAC.n
    r3 = lambda ap: ap.rearrange("p (t e) -> p t e", e=8)
    rm = lambda ap: ap.rearrange("p (t o) -> p t o", o=1)
    bc = lambda ap: ap.rearrange("p (t o) -> p t o", o=1).broadcast_to((128, NCH, 8))
    cp_done = {}
    for tc in range(NCH):
        sl = tc % 2
        P.wait("tensor", sAC)
        if tc >= 2:
            P.wait("tensor", sDV, cp_done[sl])
        P.op("tensor", lambda Eo, o=pG8[:, sl * 8:(sl + 1) * 8],
             a=logits[:, tc * 128:(tc + 1) * 128]:
             Eo.transpose(o, a, idf[0:8, 0:8]), inc=sPE)
        P.wait("vector", sPE)
        P.op("vector", lambda Eo, o=gtm_all[:, tc * 8:(tc + 1) * 8],
             p=pG8[:, sl * 8:(sl + 1) * 8]: Eo.tensor_copy(o, p), inc=sDV)
        cp_done[sl] = sDV.n
    P.wait("vector", sDV)
    P.eop("vector", lambda Eo: Eo.reduce_max(rm(m1_all[:]), r3(gtm_all[:]), axis=AX.X), sDV)
    P.eop("vector", lambda Eo: Eo.tensor_tensor(r3(eq1_all[:]), r3(gtm_all[:]), bc(m1_all[:]),
                                                op=ALU.is_equal), sDV)
    P.eop("vector", lambda Eo: Eo.scalar_tensor_tensor(mskl_all[:], eq1_all[:], c_neg1e30,
                                                       gtm_all[:], op0=ALU.mult, op1=ALU.add), sDV)
    P.eop("vector", lambda Eo: Eo.reduce_max(rm(m2_all[:]), r3(mskl_all[:]), axis=AX.X), sDV)
    P.eop("vector", lambda Eo: Eo.tensor_tensor(r3(eq2_all[:]), r3(mskl_all[:]), bc(m2_all[:]),
                                                op=ALU.is_equal), sDV)
    P.eop("vector", lambda Eo: Eo.tensor_sub(edv2_all[:], m2_all[:], m1_all[:]), sDV)
    P.wait("scalar", sDV)
    P.eop("scalar", lambda Eo: Eo.activation(edv_all[:], edv2_all[:], AF.Exp), sAC)
    P.wait("vector", sAC)
    P.eop("vector", lambda Eo: Eo.tensor_scalar(s1_all[:], edv_all[:], c_one, None, op0=ALU.add), sDV)
    P.eop("vector", lambda Eo: Eo.reciprocal(rv_all[:], s1_all[:]), sDV)
    P.eop("vector", lambda Eo: Eo.tensor_mul(w2g_all[:], edv_all[:], rv_all[:]), sDV)
    P.eop("vector", lambda Eo: Eo.tensor_tensor(r3(tA_all[:]), r3(eq1_all[:]), bc(rv_all[:]),
                                                op=ALU.mult), sDV)
    P.eop("vector", lambda Eo: Eo.tensor_tensor(r3(tB_all[:]), r3(eq2_all[:]), bc(w2g_all[:]),
                                                op=ALU.mult), sDV)
    P.eop("vector", lambda Eo: Eo.tensor_add(gates_tm[:], tA_all[:], tB_all[:]), sDV)
    P.wait("gpsimd", sDV)
    P.op("gpsimd", lambda Eo, o=d_y1bf.rearrange("(t p) c -> p t c", p=128)[:, :, C:C + 8],
         a=r3(gates_tm[:]): Eo.dma_start(out=o, in_=a), inc=sGC, inc_n=16)
    P.wait("tensor", sDV)
    for tc in range(NCH):
        sl = tc % 2
        P.op("tensor", lambda Eo, o=pTg[:, sl * 128:(sl + 1) * 128],
             a=gates_tm[:, tc * 8:(tc + 1) * 8]:
             Eo.transpose(o, a, idf[:]), inc=sPE)
        P.wait("scalar", sPE)
        P.op("scalar", lambda Eo, o=gatesT[:, tc * 128:(tc + 1) * 128],
             p=pTg[:, sl * 128:(sl + 1) * 128]: Eo.activation(o, p, AF.Copy), inc=sAC)
        P.wait("tensor", sAC)

    P.eop("vector", lambda Eo: Eo.tensor_scalar(ecap[:], evals[:], c_cap, None, op0=ALU.mult), sDV)
    P.op("gpsimd", lambda Eo: Eo.memset(ones8[:], 1.0), inc=sPL)
    P.op("gpsimd", lambda Eo: Eo.memset(pft[:], 2.0e6), inc=sPL)
    P.wait("sync", sPL)
    dPF.load(P, "sync", 0, lambda Eo: Eo.dma_start(out=d_idxf, in_=pft[:]))
    P.wait("vector", sAC)
    P.wait("vector", sPL)
    P.eop("vector", lambda Eo: Eo.tensor_scalar(mask_fm[:], gatesT[:], c_zero[0:E, :], None,
                                                op0=ALU.is_gt), sDV)
    P.eop("vector", lambda Eo: Eo.tensor_tensor_scan(pos_fm[:], ones8[:], mask_fm[:], 0.0,
                                                     op0=ALU.mult, op1=ALU.add), sDV)
    P.eop("vector", lambda Eo: Eo.tensor_scalar(pos_fm[:], pos_fm[:], c_one[0:E, :], None,
                                                op0=ALU.subtract), sDV)
    P.eop("vector", lambda Eo: Eo.tensor_scalar(w1m_fm[:], mask_fm[:], c_neg1[0:E, :],
                                                c_one[0:E, :], op0=ALU.mult, op1=ALU.add), sDV)
    P.eop("vector", lambda Eo: Eo.scalar_tensor_tensor(slot_fm[:], w1m_fm[:], c_big[0:E, :],
                                                       pos_fm[:], op0=ALU.mult, op1=ALU.add), sDV)
    slot_ready = sDV.n
    P.wait("tensor", sDV, slot_ready)
    dPF.wait(P, "gpsimd", 0)
    for tc in range(NCH):
        sl = tc % 2
        if tc >= 2:
            P.wait("tensor", sDV, cp_done[sl])
        P.op("tensor", lambda Eo, o=pG8[:, sl * 8:(sl + 1) * 8],
             a=slot_fm[:, tc * 128:(tc + 1) * 128]:
             Eo.transpose(o, a, idf[0:8, 0:8]), inc=sPE)
        P.wait("vector", sPE)
        P.op("vector", lambda Eo, o=slots_all[:, tc * 8:(tc + 1) * 8],
             p=pG8[:, sl * 8:(sl + 1) * 8]: Eo.tensor_copy(o, p), inc=sDV)
        cp_done[sl] = sDV.n
    P.wait("vector", sDV)
    P.eop("vector", lambda Eo: Eo.tensor_scalar(tA_all[:], slots_all[:], c_cap, None,
                                                op0=ALU.is_ge), sDV)
    P.eop("vector", lambda Eo: Eo.scalar_tensor_tensor(selw_all[:], tA_all[:], c_big,
                                                       slots_all[:], op0=ALU.mult, op1=ALU.add), sDV)
    ecap_b = ecap[:].rearrange("p (o e) -> p o e", o=1).broadcast_to((128, NCH, 8))
    P.eop("vector", lambda Eo: Eo.tensor_tensor(r3(selw_all[:]), r3(selw_all[:]), ecap_b,
                                                op=ALU.add), sDV)
    P.eop("vector", lambda Eo: Eo.reduce_max(rm(m1_all[:]), r3(gates_tm[:]), axis=AX.X), sDV)
    P.eop("vector", lambda Eo: Eo.tensor_tensor(r3(tA_all[:]), r3(gates_tm[:]), bc(m1_all[:]),
                                                op=ALU.is_equal), sDV)
    P.eop("vector", lambda Eo: Eo.tensor_scalar(tB_all[:], gates_tm[:], c_zero, None,
                                                op0=ALU.is_gt), sDV)
    P.eop("vector", lambda Eo: Eo.tensor_sub(tB_all[:], tB_all[:], tA_all[:]), sDV)
    P.eop("vector", lambda Eo: Eo.tensor_mul(tA_all[:], tA_all[:], selw_all[:]), sDV)
    P.eop("vector", lambda Eo: Eo.reduce_sum(rm(idx1_all[:]), r3(tA_all[:]), axis=AX.X), sDV)
    P.eop("vector", lambda Eo: Eo.tensor_mul(tB_all[:], tB_all[:], selw_all[:]), sDV)
    P.eop("vector", lambda Eo: Eo.reduce_sum(rm(idx2_all[:]), r3(tB_all[:]), axis=AX.X), sDV)
    P.eop("vector", lambda Eo: Eo.tensor_copy(idxi1[:], idx1_all[:]), sDV)
    P.eop("vector", lambda Eo: Eo.tensor_copy(idxi2[:], idx2_all[:]), sDV)
    P.wait("gpsimd", sDV)
    P.wait("gpsimd", sGC)
    P._mark_ind = True
    for tc in range(NCH):
        for ib in (idxi1, idxi2):
            P.op("gpsimd", lambda Eo, tc=tc, ib=ib:
                 Eo.indirect_dma_start(
                     out=d_idxf.rearrange("e (s o) -> (e s) o", o=1),
                     out_offset=IndirectOffsetOnAxis(ap=ib[:, tc:tc + 1], axis=0),
                     in_=tokid[:, tc:tc + 1], in_offset=None,
                     bounds_check=regB_all, oob_is_err=False),
                 inc=sSCT, inc_n=16)
    P._mark_ind = False

    if "logits" in taps:
        P.wait("sync", sAC, logits_done)
        P.op("sync", lambda Eo: Eo.dma_start(out=taps["logits"], in_=logits[:]),
             inc=sTAP, inc_n=16)
    if "gates" in taps:
        P.wait("sync", sDV)
        P.op("sync", lambda Eo: Eo.dma_start(out=taps["gates"], in_=gates_tm[:]),
             inc=sTAP, inc_n=16)
    if "gatesT" in taps:
        P.wait("gpsimd", sAC)
        P.op("gpsimd", lambda Eo: Eo.dma_start(out=taps["gatesT"], in_=gatesT[:]),
             inc=sTAP, inc_n=16)
    if "y1" in taps:
        for s_ in dSPp.sems:
            P.wait("sync", s_)
        P.op("sync", lambda Eo: Eo.dma_start(out=taps["y1"], in_=d_y1sp), inc=sTAP, inc_n=16)
    for s_ in dSPp.sems:
        P.wait("sync", s_)
    P.wait("sync", sTAP)
    barrier(P)
    with nc.Block("b3") as blk:
        P.run(blk)
    es3.close()
    esA.close()

    es4 = ExitStack()
    sb4 = lambda name, shape, dt: es4.enter_context(nc.sbuf_tensor(name, shape, dt))
    ps4 = lambda name, shape, dt: es4.enter_context(nc.psum_tensor(name, shape, dt))
    H = sb4("H", [128, 32 * CAP], BF16)
    y1gT = [sb4("y1gT%d" % i, [128, 8 * CAP], BF16) for i in range(2)]
    o_sb = sb4("o_sb", [128, 8 * CAP], F32)
    w1b = [sb4("w1b%d" % i, [128, 8 * 256], BF16) for i in range(4)]
    w2b = [sb4("w2b%d" % i, [128, 16 * 128], BF16) for i in range(4)]
    gbuf = [sb4("gbuf%d" % i, [128, C + 8], BF16) for i in range(NG)]
    scatb = [sb4("scatb%d" % i, [128, C], F32) for i in range(4)]
    idxf_s = [sb4("idxf_s%d" % i, [128, NG], F32) for i in range(2)]
    idxi_s = [sb4("idxi_s%d" % i, [128, NG], mybir.dt.int32) for i in range(2)]
    gv = [sb4("gv%d" % i, [128, NG], F32) for i in range(2)]
    pH = [ps4("pH%d" % i, [128, 1024], F32) for i in range(2)]
    pTin = ps4("pTin", [128, 512], BF16)
    pTout = ps4("pTout", [128, 512], F32)

    P = Prog()
    w1_view = d_w1
    w2_view = d_w2
    for _eng in ENGINES:
        P.wait(_eng, sC1, ld_consts)
        P.wait(_eng, sC2, wl_consts)
    for _eng in ("gpsimd", "sync"):
        P.wait(_eng, sSCT)
        P.wait(_eng, sGC)
        for s_ in dSPb.sems:
            P.wait(_eng, s_)
        for s_ in dSPp.sems:
            P.wait(_eng, s_)
        P.wait(_eng, dPF.sems[0])
    P.wait("gpsimd", sGC)
    for s_ in dSPb.sems:
        P.wait("gpsimd", s_)
    for _gb in gbuf:
        P.op("gpsimd", lambda Eo, b=_gb: Eo.memset(b[:], 0.0))
    P.op("gpsimd", lambda Eo: Eo.memset(gv[0][:], 0.0))
    P.op("gpsimd", lambda Eo: Eo.memset(gv[1][:], 0.0), inc=sPL)
    P.wait("gpsimd", sPL)
    P.wait("tensor", sPL)

    rW1 = Rot(4)
    rW2 = Rot(4)
    rGB = Rot(NG)
    rIX = Rot(2)
    rGV = Rot(2)
    rSCB = Rot(4)
    rPH = Rot(2)
    rTIN = Rot(1)
    rTOUT = Rot(1)
    NB = ((0, 512), (512, CAP))

    in_done = {}

    idx_ready = {}

    def emit_meta(e):
        ep = e % 2
        rIX.acquire(P, "sync", e)
        dIX.load(P, "sync", e, lambda Eo, o=idxf_s[ep][:],
                 s=d_idxf[e, :].rearrange("(p g) -> p g", g=NG): Eo.dma_start(out=o, in_=s))
        dIX.wait(P, "vector", e)
        P.eop("vector", lambda Eo, o=idxi_s[ep][:], a=idxf_s[ep][:]: Eo.tensor_copy(o, a), sDV)
        rGV.acquire(P, "vector", e)
        idx_ready[e] = sDV.n

    def emit_gath1(e, g):
        ep = e % 2
        if g == 0:
            P.wait("gpsimd", sDV, idx_ready[e])
            for s_ in dW1.sems:
                P.wait("gpsimd", s_)
            for s_ in dW2.sems:
                P.wait("gpsimd", s_)
        rGB.acquire(P, "gpsimd", g)
        P._mark_ind = True
        dGA.load(P, "gpsimd", e * NG + g, lambda Eo, g=g, ep=ep:
                 Eo.indirect_dma_start(
                     out=gbuf[g][:], out_offset=None,
                     in_=d_y1bf, in_offset=IndirectOffsetOnAxis(ap=idxi_s[ep][:, g:g + 1], axis=0),
                     bounds_check=regB_tok, oob_is_err=False))
        P._mark_ind = False

    def emit_gvx(e):
        ep = e % 2
        for g in range(NG):
            dGA.wait(P, "vector", e * NG + g)
            P.eop("vector", lambda Eo, g=g, ep=ep, e=e, o=gv[ep][:, g:g + 1]:
                  Eo.tensor_copy(o, gbuf[g][:, C + e:C + e + 1]), sDV)

    def emit_int(e):
        ep = e % 2
        for g in range(NG):
            dGA.wait(P, "tensor", e * NG + g)
            rTIN.acquire(P, "tensor", 0)
            for kc in range(8):
                P.op("tensor", lambda Eo, kc=kc, g=g,
                     o=pTin[:, (kc % 4) * 128:(kc % 4 + 1) * 128],
                     a=gbuf[g][:, kc * 128:(kc + 1) * 128]:
                     Eo.transpose(o, a, idb[:]),
                     inc=(sPE if kc % 4 == 3 else None))
                if kc % 4 == 3:
                    P.wait("vector", sPE)
                    kc0 = kc - 3
                    P.op("vector", lambda Eo, kc0=kc0, g=g, o4=[
                         y1gT[ep][:, (kc0 + i) * CAP + g * 128:(kc0 + i) * CAP + (g + 1) * 128]
                         for i in range(4)], pt=pTin[:]:
                         [Eo.tensor_copy(o4[i], pt[:, i * 128:(i + 1) * 128]) for i in range(4)][-1],
                         inc=sDV)
                    if kc == 3:
                        rTIN.rel[0] = [(sDV, sDV.n)]
                        rTIN.acquire(P, "tensor", 0)
                    else:
                        rTIN.rel[0] = [(sDV, sDV.n)]
            rGB.release(g, sPE)
            rGB.release(g, sDV)
        in_done[e] = sDV.n

    PIPE_META = True
    prefetched = set()

    def load_w1(e, fp):
        wpar = (e * 16 + fp) % 4
        rW1.acquire(P, "gpsimd", e * 16 + fp)
        srcw = w1_view[e, :, fp * 256:(fp + 1) * 256].rearrange("(kc p) f -> p kc f", p=128)
        dW1.load(P, "gpsimd", e * 16 + fp,
                 lambda Eo, o=w1b[wpar][:].rearrange("p (kc f) -> p kc f", kc=8),
                 s=srcw: Eo.dma_start(out=o, in_=s))

    emit_meta(0)
    for _g in range(NG):
        emit_gath1(0, _g)
    emit_gvx(0)
    emit_int(0)
    for e in range(n_experts):
        ep = e % 2
        if e > 0:
            emit_int(e)
        P.wait("tensor", sDV, in_done[e])
        for fp in range(16):
            wpar = (e * 16 + fp) % 4
            if (e, fp) not in prefetched:
                load_w1(e, fp)
            dW1.wait(P, "tensor", e * 16 + fp)
            for fc in range(2):
                fcg = fp * 2 + fc
                hpar = fcg % 2
                rPH.acquire(P, "tensor", fcg)
                for kc in range(8):
                    for nbi, (lo, hi) in enumerate(NB):
                        P.op("tensor", lambda Eo, p=pH[hpar][:, lo:hi],
                             w=w1b[wpar][:, kc * 256 + fc * 128:kc * 256 + (fc + 1) * 128],
                             x=y1gT[ep][:, kc * CAP + lo:kc * CAP + hi],
                             st=(kc == 0), sp=(kc == 7):
                             Eo.matmul(p, w, x, start=st, stop=sp),
                             inc=(sPE if (nbi == 1 and kc == 7) else None))
                P.wait("scalar", sPE)
                hsl = H[:, fcg * CAP:(fcg + 1) * CAP]
                P.op("scalar", lambda Eo, o=hsl, p=pH[hpar][:, 0:CAP],
                     b=b1t[:, e * 32 + fcg:e * 32 + fcg + 1]:
                     Eo.activation(o, p, AF.Gelu_apprx_tanh, bias=b, scale=f32(1.0)),
                     inc=sAC)
                rPH.release(fcg, sAC)
            rW1.release(e * 16 + fp, sPE)
        h_done = sAC.n
        if PIPE_META and e + 1 < n_experts:
            emit_meta(e + 1)
        P.wait("tensor", sAC, h_done)
        for dc in range(8):
            opar = dc % 2
            if dc == 4 and e + 1 < n_experts:
                for g in range(NG):
                    emit_gath1(e + 1, g)
                emit_gvx(e + 1)
                for fpp in range(4):
                    load_w1(e + 1, fpp)
                    prefetched.add((e + 1, fpp))
            for fh in range(2):
                w2i = (e * 16 + dc * 2 + fh) % 4
                rW2.acquire(P, "gpsimd", e * 16 + dc * 2 + fh)
                srcw = w2_view[e, fh * 2048:(fh + 1) * 2048,
                               dc * 128:(dc + 1) * 128].rearrange("(fc p) d -> p fc d", p=128)
                dW2.load(P, "gpsimd", e * 16 + dc * 2 + fh,
                         lambda Eo, o=w2b[w2i][:].rearrange("p (fc d) -> p fc d", fc=16),
                         s=srcw: Eo.dma_start(out=o, in_=s))
                dW2.wait(P, "tensor", e * 16 + dc * 2 + fh)
                if fh == 0:
                    rPH.acquire(P, "tensor", dc)
                for fcL in range(16):
                    fcg = fh * 16 + fcL
                    for nbi, (lo, hi) in enumerate(NB):
                        P.op("tensor", lambda Eo, p=pH[opar][:, lo:hi],
                             w=w2b[w2i][:, fcL * 128:(fcL + 1) * 128],
                             x=H[:, fcg * CAP + lo:fcg * CAP + hi],
                             st=(fh == 0 and fcL == 0), sp=(fh == 1 and fcL == 15):
                             Eo.matmul(p, w, x, start=st, stop=sp),
                             inc=(sPE if (nbi == 1 and fcL == 15) else None))
                rW2.release(e * 16 + dc * 2 + fh, sPE)
            P.wait("vector", sPE)
            P.op("vector", lambda Eo, o=o_sb[:, dc * CAP:(dc + 1) * CAP],
                 p=pH[opar][:, 0:CAP], b=b2ts[:, dc * 8 + e:dc * 8 + e + 1]:
                 Eo.tensor_scalar(o, p, b, None, op0=ALU.add), inc=sDV)
            rPH.release(dc, sDV)
        o_done = sDV.n
        P.wait("tensor", sDV, o_done)
        for g in range(NG):
            sbi = (e * NG + g) % 4
            rSCB.acquire(P, "tensor", e * NG + g)
            rTOUT.acquire(P, "tensor", 0)
            for dc in range(8):
                P.op("tensor", lambda Eo, dc=dc, g=g,
                     o=pTout[:, (dc % 4) * 128:(dc % 4 + 1) * 128],
                     a=o_sb[:, dc * CAP + g * 128:dc * CAP + (g + 1) * 128]:
                     Eo.transpose(o, a, idf[:]),
                     inc=(sPE if dc % 4 == 3 else None))
                if dc % 4 == 3:
                    P.wait("vector", sPE)
                    dc0 = dc - 3
                    for i in range(4):
                        P.op("vector", lambda Eo, i=i, dc0=dc0, g=g, sbi=sbi,
                             o=scatb[sbi][:, (dc0 + i) * 128:(dc0 + i + 1) * 128],
                             p=pTout[:, i * 128:(i + 1) * 128],
                             gvc=gv[ep][:, g:g + 1]:
                             Eo.tensor_scalar(o, p, gvc, None, op0=ALU.mult),
                             inc=(sDV if i == 3 else None))
                    rTOUT.rel[0] = [(sDV, sDV.n)]
                    if dc == 3:
                        rTOUT.acquire(P, "tensor", 0)
            P.wait("gpsimd", sDV)
            P.wait("gpsimd", sSCA)
            P.op("gpsimd", lambda Eo, g=g, ep=ep, sbi=sbi:
                 Eo.indirect_dma_start(
                     out=d_y1sp, out_offset=IndirectOffsetOnAxis(ap=idxi_s[ep][:, g:g + 1], axis=0),
                     in_=scatb[sbi][:], in_offset=None,
                     bounds_check=regB_tok, oob_is_err=False,
                     compute_op=ALU.add),
                 inc=sSCA, inc_n=16)
            rSCB.release(e * NG + g, sSCA)
        rIX.release(e, sSCA)
        rGV.release(e, sSCA)

    if "O_acc" in taps:
        P.wait("sync", sSCA)
        P.op("sync", lambda Eo: Eo.dma_start(out=taps["O_acc"], in_=o_sb[:]),
             inc=sTAP, inc_n=16)
    P.wait("sync", sTAP)
    barrier(P)
    with nc.Block("b4") as blk:
        P.run(blk)
    es4.close()

    es5 = ExitStack()
    sb5 = lambda name, shape, dt: es5.enter_context(nc.sbuf_tensor(name, shape, dt))
    z2ch = [sb5("z2ch%d" % i, [128, C], F32) for i in range(3)]
    sq2 = [sb5("sq2%d" % i, [128, C], F32) for i in range(3)]
    mu2 = [sb5("mu2%d" % i, [128, 1], F32) for i in range(3)]
    var2_ = [sb5("var2%d" % i, [128, 1], F32) for i in range(3)]
    lnv2_ = [sb5("lnv2%d" % i, [128, 1], F32) for i in range(3)]
    rstd2 = [sb5("rstd2%d" % i, [128, 1], F32) for i in range(3)]

    P = Prog()
    for _eng in ENGINES:
        P.wait(_eng, sC1, ld_consts)
        P.wait(_eng, sC2, wl_consts)
    P.wait("sync", sSCA)
    rZ2 = Rot(3)
    rSQ2 = Rot(3)
    rMU2 = Rot(3)
    rRS2 = Rot(3)
    for tc in range(NCH):
        par = tc % 3
        z2 = z2ch[par]
        rZ2.acquire(P, "sync", tc)
        dYC.load(P, "sync", tc, lambda Eo, o=z2[:], s=d_y1sp[tc * 128:(tc + 1) * 128, :]:
                 Eo.dma_start(out=o, in_=s))
        dYC.wait(P, "vector", tc)
        rMU2.acquire(P, "vector", tc)
        mu = mu2[par]
        P.eop("vector", lambda Eo, m=mu[:], z_=z2[:]: Eo.reduce_sum(m, z_, axis=AX.X), sDV)
        P.eop("vector", lambda Eo, m=mu[:]:
              Eo.tensor_scalar(m, m, c_inv1024, None, op0=ALU.mult), sDV)
        P.wait("scalar", sDV)
        rSQ2.acquire(P, "scalar", tc)
        P.op("scalar", lambda Eo, q=sq2[par][:], z_=z2[:], m=mu[:]:
             Eo.activation(q, z_, AF.Square, bias=m, scale=f32(-1.0)), inc=sAC)
        P.wait("vector", sAC)
        var2 = var2_[par]
        lnv2 = lnv2_[par]
        P.eop("vector", lambda Eo, vv=var2[:], q=sq2[par][:]: Eo.reduce_sum(vv, q, axis=AX.X), sDV)
        P.op("vector", lambda Eo, vv=var2[:]:
             Eo.tensor_scalar(vv, vv, c_inv1024, None, op0=ALU.mult), inc=sDV)
        rSQ2.rel[par] = [(sDV, sDV.n)]
        P.wait("scalar", sDV)
        rRS2.acquire(P, "scalar", tc)
        P.eop("scalar", lambda Eo, lv=lnv2[:], vv=var2[:]: Eo.activation(lv, vv, AF.Ln, bias=f32(EPS)), sAC)
        P.op("scalar", lambda Eo, r=rstd2[par][:], lv=lnv2[:]:
             Eo.activation(r, lv, AF.Exp, scale=f32(-0.5)), inc=sAC)
        P.wait("vector", sAC)
        P.op("vector", lambda Eo, z_=z2[:], m=mu[:], r=rstd2[par][:]:
             Eo.tensor_scalar(z_, z_, m, r, op0=ALU.subtract, op1=ALU.mult), inc=sDV)
        rMU2.release(tc, sDV)
        rRS2.release(tc, sDV)
        P.wait("sync", sDV)
        dSTp.load(P, "sync", tc, lambda Eo, o=d_y2[tc * 128:(tc + 1) * 128, :], z_=z2[:]:
                  Eo.dma_start(out=o, in_=z_))
        rZ2.rel[par] = [dSTp.sem_count(tc)]
    for s_ in dSTp.sems:
        P.wait("sync", s_)
    P.wait("sync", sTAP)
    with nc.Block("b5") as blk:
        P.run(blk)
    es5.close()
    esP.close()
    esS.close()
    return nc


_CACHE = {}


def kernel(L0_attn_x, L0_attn_Wq, L0_attn_bq, L0_attn_Wk, L0_attn_bk,
           L0_attn_Wv, L0_attn_bv, L0_ffn_router_w, L0_ffn_w1, L0_ffn_b1,
           L0_ffn_w2, L0_ffn_b2):
    x = np.ascontiguousarray(np.asarray(L0_attn_x, dtype=np.float32))
    Wq = np.asarray(L0_attn_Wq, np.float32)
    Wk = np.asarray(L0_attn_Wk, np.float32)
    Wv = np.asarray(L0_attn_Wv, np.float32)
    bq = np.asarray(L0_attn_bq, np.float32)
    bk = np.asarray(L0_attn_bk, np.float32)
    bv = np.asarray(L0_attn_bv, np.float32)
    rw_in = np.asarray(L0_ffn_router_w, np.float32)
    w1 = np.ascontiguousarray(np.asarray(L0_ffn_w1, np.float32))
    b1 = np.asarray(L0_ffn_b1, np.float32)
    w2 = np.ascontiguousarray(np.asarray(L0_ffn_w2, np.float32))
    b2 = np.asarray(L0_ffn_b2, np.float32)

    wqkvT = np.concatenate([Wq.T, Wk.T, Wv.T], axis=1)
    wqkvT = np.ascontiguousarray(wqkvT.reshape(8, 128, 192).transpose(1, 0, 2))
    qkb = np.zeros((128, 2), np.float32)
    qkb[0:64, 0] = bq
    qkb[0:64, 1] = bk
    bvrow = bv.reshape(1, HS)
    b1t = np.ascontiguousarray(
        b1.reshape(E, 32, 128).transpose(2, 0, 1))
    rw_pre = np.ascontiguousarray(rw_in.reshape(8, 128, E).transpose(1, 0, 2))
    identb = np.eye(128, dtype=bf16)
    identf = np.eye(128, dtype=np.float32)
    qpiota = (np.arange(128, dtype=np.float32)[:, None]
              - np.arange(512, dtype=np.float32)[None, :])
    onesb = np.ones((1, 128), dtype=bf16)
    consts = np.zeros((128, 12), np.float32)
    consts[:, C_INV1024] = 1.0 / 1024.0
    consts[:, C_NEG132] = -1.0 / 32.0
    consts[:, C_ONE] = 1.0
    consts[:, C_NEG1E30] = -1e30
    consts[:, C_NEG1] = -1.0
    selb = np.zeros((E, E * 128), dtype=bf16)
    for e in range(E):
        selb[e, e * 128:(e + 1) * 128] = 1
    tokid = (np.arange(NCH, dtype=np.float32)[None, :] * 128.0
             + np.arange(128, dtype=np.float32)[:, None])
    evals = np.tile(np.arange(E, dtype=np.float32)[None, :], (128, 1))
    consts[:, C_BIG] = 1.0e6
    consts[:, C_EIGHT] = 8.0
    consts[:, C_CAP] = float(CAP)

    nc = build_nc()
    in_maps = []
    for c in range(NCORES):
        b = c // 4
        ph = c % 4
        chunks = [4 * j + ph for j in range(NCH)]
        x_b = x[b]
        x_own = np.ascontiguousarray(
            x_b.reshape(64, 128, C)[chunks].reshape(TOK, C))
        qoffb = np.full((128, 1), 128.0 * ph, np.float32)
        mth = np.zeros((128, 4), np.float32)
        for t in range(4):
            mth[:, t] = (t - ph) * 128.0
        in_maps.append(dict(
            x_own=x_own, x_batch=x_b, wqkvT=wqkvT, qkb=qkb, bvrow=bvrow,
            b1t=b1t, b2w=b2, rw=rw_pre, w1=w1, w2=w2, identb=identb,
            identf=identf, qpiota=qpiota, onesb=onesb, qoffb=qoffb,
            consts=consts, selb=selb, mth=mth, tokid=tokid, evals=evals,
            b2t=np.ascontiguousarray(b2.T.reshape(8, 128, E).transpose(1, 0, 2))))
    res = run_bass_kernel_spmd(nc, in_maps, core_ids=list(range(NCORES)))
    out = np.zeros((B, T, C), np.float32)
    for c in range(NCORES):
        b = c // 4
        ph = c % 4
        y2 = np.asarray(res.results[c]["y2"]).reshape(NCH, 128, C)
        for j in range(NCH):
            out[b, (4 * j + ph) * 128:(4 * j + ph + 1) * 128, :] = y2[j]
    return out


# version 19
# speedup vs baseline: 2.0512x; 1.1103x over previous
import numpy as np
import ml_dtypes
from contextlib import ExitStack

import concourse.bass as bass
import concourse.mybir as mybir
from concourse.bass_utils import run_bass_kernel_spmd
from concourse.bass_core_rust import IndirectOffsetOnAxis

F32 = mybir.dt.float32
BF16 = mybir.dt.bfloat16
AF = mybir.ActivationFunctionType
ALU = mybir.AluOpType
AX = mybir.AxisListType
f32 = np.float32
bf16 = ml_dtypes.bfloat16

B, T, C = 2, 8192, 1024
HS = 64
E, FF = 8, 4096
NCORES = 8
TOK = 2048
NCH = 16
KCH = 64
NT = 1024
EPS = 1e-5

C_INV1024, C_NEG132, C_ONE, C_NEG1E30, C_ZERO, C_NEG1, C_BIG, C_EIGHT, C_CAP = 0, 1, 2, 3, 4, 5, 6, 7, 8
CAP, NG = 640, 5

ENGINES = ("sync", "scalar", "vector", "tensor", "gpsimd")


class Sem:
    def __init__(self, nc, es, name):
        self.h = es.enter_context(nc.semaphore(name))
        self.n = 0

    def bump(self, k):
        self.n += k
        return self.n


class Prog:

    def __init__(self):
        self.streams = {e: [] for e in ENGINES}

    def op(self, eng, fn, inc=None, inc_n=1):
        if inc is not None:
            inc.bump(inc_n)
            self.streams[eng].append(
                lambda Eo, fn=fn, s=inc.h, n=inc_n: fn(Eo).then_inc(s, n))
            if (PARANOID[0] is True and eng == "gpsimd" and inc_n == 16) or \
               (PARANOID[0] == "ind" and eng == "gpsimd" and inc_n == 16 and getattr(self, "_mark_ind", False)):
                self.wait(eng, inc)
        else:
            self.streams[eng].append(lambda Eo, fn=fn: fn(Eo))

    def eop(self, eng, fn, sem):
        self.op(eng, fn, inc=sem)
        self.wait(eng, sem)

    def wait(self, eng, sem, val=None):
        v = sem.n if val is None else val
        if v > 0:
            self.streams[eng].append(lambda Eo, s=sem.h, v=v: Eo.wait_ge(s, v))

    def run(self, block):
        for eng in ENGINES:
            fns = tuple(self.streams[eng])

            def body(Eo, fns=fns):
                for f in fns:
                    f(Eo)

            getattr(block, eng)(body)


class Rot:

    def __init__(self, n):
        self.n = n
        self.rel = [[] for _ in range(n)]

    def acquire(self, P, eng, i):
        for sem, cnt in self.rel[i % self.n]:
            P.wait(eng, sem, cnt)
        self.rel[i % self.n] = []

    def release(self, i, sem):
        self.rel[i % self.n].append((sem, sem.n))


class DmaSlots:

    def __init__(self, nc, es, name, n=2):
        self.sems = [Sem(nc, es, "%s%d" % (name, i)) for i in range(n)]
        self.n = n

    def load(self, P, eng, i, fn):
        P.op(eng, fn, inc=self.sems[i % self.n], inc_n=16)

    def wait(self, P, eng, i):
        P.wait(eng, self.sems[i % self.n])

    def sem_count(self, i):
        s = self.sems[i % self.n]
        return (s, s.n)


PARANOID = [False]


def build_nc(debug_taps=(), n_experts=E):
    nc = bass.Bass("TRN2", target_bir_lowering=False)

    d_xown = nc.dram_tensor("x_own", [TOK, C], F32, kind="ExternalInput").ap()
    d_xbat = nc.dram_tensor("x_batch", [T, C], F32, kind="ExternalInput").ap()
    d_wqkv = nc.dram_tensor("wqkvT", [128, 8, 192], F32, kind="ExternalInput").ap()
    d_qkb = nc.dram_tensor("qkb", [128, 2], F32, kind="ExternalInput").ap()
    d_bvr = nc.dram_tensor("bvrow", [1, HS], F32, kind="ExternalInput").ap()
    d_b1t = nc.dram_tensor("b1t", [128, E, 32], F32, kind="ExternalInput").ap()
    d_b2w = nc.dram_tensor("b2w", [E, C], F32, kind="ExternalInput").ap()
    d_rw = nc.dram_tensor("rw", [128, 8, E], F32, kind="ExternalInput").ap()
    d_w1 = nc.dram_tensor("w1", [E, C, FF], F32, kind="ExternalInput").ap()
    d_w2 = nc.dram_tensor("w2", [E, FF, C], F32, kind="ExternalInput").ap()
    d_idb = nc.dram_tensor("identb", [128, 128], BF16, kind="ExternalInput").ap()
    d_idf = nc.dram_tensor("identf", [128, 128], F32, kind="ExternalInput").ap()
    d_qpi = nc.dram_tensor("qpiota", [128, 512], F32, kind="ExternalInput").ap()
    d_ones = nc.dram_tensor("onesb", [1, 128], BF16, kind="ExternalInput").ap()
    d_qoff = nc.dram_tensor("qoffb", [128, 1], F32, kind="ExternalInput").ap()
    d_cst = nc.dram_tensor("consts", [128, 12], F32, kind="ExternalInput").ap()
    d_sel = nc.dram_tensor("selb", [E, E * 128], BF16, kind="ExternalInput").ap()
    d_mth = nc.dram_tensor("mth", [128, 4], F32, kind="ExternalInput").ap()
    d_tokid = nc.dram_tensor("tokid", [128, NCH], F32, kind="ExternalInput").ap()
    d_evals = nc.dram_tensor("evals", [128, E], F32, kind="ExternalInput").ap()
    d_b2t = nc.dram_tensor("b2t", [128, 8, E], F32, kind="ExternalInput").ap()
    d_y1bf = nc.dram_tensor("y1bf", [TOK, C + 8], BF16, kind="Internal").ap()
    d_idxf = nc.dram_tensor("idxf", [E, CAP], F32, kind="Internal").ap()
    d_gc = nc.dram_tensor("gc", [TOK, E], F32, kind="Internal").ap()
    d_y1sp = nc.dram_tensor("y1sp", [TOK, C], F32, kind="Internal").ap()
    d_y2 = nc.dram_tensor("y2", [TOK, C], F32, kind="ExternalOutput").ap()
    taps = {}
    for name, shape in debug_taps:
        taps[name] = nc.dram_tensor("tap_" + name, shape, F32, kind="ExternalOutput").ap()

    esP = ExitStack()
    sb = lambda name, shape, dt: esP.enter_context(nc.sbuf_tensor(name, shape, dt))

    idb = sb("idb", [128, 128], BF16)
    idf = sb("idf", [128, 128], F32)
    onesb = sb("onesb_s", [1, 128], BF16)
    qoffb = sb("qoffb_s", [128, 1], F32)
    mth = sb("mth_s", [128, 4], F32)
    tokid = sb("tokid_s", [128, NCH], F32)
    evals = sb("evals_s", [128, E], F32)
    b2ts = sb("b2t_s", [128, 8 * E], F32)
    cst = sb("cst", [128, 12], F32)
    b1t = sb("b1t_s", [128, E * 32], F32)
    b2b = sb("b2b", [E, C], BF16)
    rw = sb("rw_s", [128, 8 * E], F32)
    gatesT = sb("gatesT", [E, TOK], BF16)
    selb = sb("selb_s", [E, E * 128], BF16)

    esA = ExitStack()
    sbA = lambda name, shape, dt: esA.enter_context(nc.sbuf_tensor(name, shape, dt))
    qpi = sbA("qpi", [128, 512], F32)
    maskadd = sbA("maskadd", [128, 512], F32)
    qT = sbA("qT", [65, TOK], BF16)
    kT = sbA("kT", [65, T], BF16)
    vtm = sbA("vtm", [128, KCH * 65], BF16)
    condt = sbA("condt", [128, 4 * 128], BF16)
    o_tm = sbA("o_tm", [128, NCH * HS], F32)
    mA = sbA("mA", [128, 1], F32)
    mB = sbA("mB", [128, 1], F32)
    dm = sbA("dm", [128, 1], F32)
    edm = sbA("edm", [128, 1], F32)
    denA = sbA("denA", [128, 1], F32)
    denB = sbA("denB", [128, 1], F32)
    negm32 = sbA("negm32", [128, 1], F32)
    lden = sbA("lden", [128, 1], F32)
    bias2 = sbA("bias2", [128, 1], F32)
    bsum = [sbA("bsum0", [128, 1], F32), sbA("bsum1", [128, 1], F32)]

    esS = ExitStack()
    mks = lambda name: Sem(nc, esS, name)
    sC1 = mks("sC1")
    sC2 = mks("sC2")
    sTAP = mks("sTAP")
    sPL = mks("sPL")
    sPE = mks("sPE")
    sAC = mks("sAC")
    sDV = mks("sDV")
    sBAR = mks("sBAR")
    dXB = DmaSlots(nc, esS, "dXB")
    dZC = DmaSlots(nc, esS, "dZC", n=3)
    dSPp = DmaSlots(nc, esS, "dSPp", n=3)
    dW1 = DmaSlots(nc, esS, "dW1", n=4)
    dW2 = DmaSlots(nc, esS, "dW2", n=4)
    dYC = DmaSlots(nc, esS, "dYC", n=3)
    dSTp = DmaSlots(nc, esS, "dSTp", n=3)
    dFD = DmaSlots(nc, esS, "dFD")
    dSPb = DmaSlots(nc, esS, "dSPb", n=3)
    sGC = mks("sGC")
    sSCT = mks("sSCT")
    sSCTp = [mks("sSCTp0"), mks("sSCTp1")]
    sSCA = mks("sSCA")
    dPF = DmaSlots(nc, esS, "dPF", n=1)
    dIX = DmaSlots(nc, esS, "dIX", n=2)
    dGA = DmaSlots(nc, esS, "dGA", n=2)
    dGV = DmaSlots(nc, esS, "dGV", n=2)

    c_inv1024 = cst[:, C_INV1024:C_INV1024 + 1]
    c_neg132 = cst[:, C_NEG132:C_NEG132 + 1]
    c_one = cst[:, C_ONE:C_ONE + 1]
    c_neg1e30 = cst[:, C_NEG1E30:C_NEG1E30 + 1]
    c_zero = cst[:, C_ZERO:C_ZERO + 1]
    c_neg1 = cst[:, C_NEG1:C_NEG1 + 1]
    c_big = cst[:, C_BIG:C_BIG + 1]
    c_eight = cst[:, C_EIGHT:C_EIGHT + 1]
    c_cap = cst[:, C_CAP:C_CAP + 1]

    regB_cap = nc.gpsimd.alloc_register("bnd_cap")
    regB_tok = nc.gpsimd.alloc_register("bnd_tok")
    regB_gc = nc.gpsimd.alloc_register("bnd_gc")
    regB_all = nc.gpsimd.alloc_register("bnd_all")

    barrier_count = [0]

    def barrier(P):
        barrier_count[0] += 1
        for eng in ENGINES:
            P.op(eng, lambda Eo: Eo.nop(), inc=sBAR)
        for eng in ENGINES:
            P.wait(eng, sBAR, 5 * barrier_count[0])

    es1 = ExitStack()
    sb1 = lambda name, shape, dt: es1.enter_context(nc.sbuf_tensor(name, shape, dt))
    ps1 = lambda name, shape, dt: es1.enter_context(nc.psum_tensor(name, shape, dt))
    xb = [sb1("xb0", [128, C], BF16), sb1("xb1", [128, C], BF16)]
    xT = [sb1("xT0", [128, C], BF16), sb1("xT1", [128, C], BF16)]
    wqkv = sb1("wqkv", [128, 8 * 192], BF16)
    bvr = sb1("bvr", [1, HS], BF16)
    qkb = sb1("qkb_s", [128, 2], F32)
    pXT = [ps1("pXT0", [128, C], BF16), ps1("pXT1", [128, C], BF16)]
    pQK = [ps1("pQK0", [64, 128], F32), ps1("pQK1", [64, 128], F32)]
    pV = [ps1("pV0", [128, HS], F32), ps1("pV1", [128, HS], F32)]

    P = Prog()
    for dst, src in [(idb, d_idb), (idf, d_idf), (qpi, d_qpi), (qoffb, d_qoff),
                     (mth, d_mth), (tokid, d_tokid), (evals, d_evals),
                     (b2ts, d_b2t.rearrange("p a b -> p (a b)")),
                     (cst, d_cst), (qkb[:], d_qkb),
                     (b1t, d_b1t.rearrange("p e k -> p (e k)")),
                     (rw, d_rw.rearrange("p c e -> p (c e)"))]:
        ap = dst[:] if hasattr(dst, "ap") else dst
        P.op("sync", lambda Eo, a=ap, s=src: Eo.dma_start(out=a, in_=s), inc=sC1, inc_n=16)
    for dst, src in [(onesb[:], d_ones), (selb[:], d_sel),
                     (wqkv[:], d_wqkv.rearrange("p c h -> p (c h)")),
                     (bvr[:], d_bvr), (b2b[:], d_b2w)]:
        P.op("gpsimd", lambda Eo, a=dst, s=src: Eo.dma_start(out=a, in_=s), inc=sC2, inc_n=16)
    ld_consts = sC1.n
    wl_consts = sC2.n
    for _eng in ("tensor", "scalar", "vector", "gpsimd"):
        P.wait(_eng, sC1, ld_consts)
        P.wait(_eng, sC2, wl_consts)
    P.op("gpsimd", lambda Eo: Eo.reg_mov(regB_cap, CAP - 1))
    P.op("gpsimd", lambda Eo: Eo.reg_mov(regB_tok, TOK - 1))
    P.op("gpsimd", lambda Eo: Eo.reg_mov(regB_gc, TOK * E - 1))
    P.op("gpsimd", lambda Eo: Eo.reg_mov(regB_all, E * CAP - 1))
    P.op("gpsimd", lambda Eo: Eo.memset(kT[64:65, :], 1.0), inc=sPL)
    P.op("gpsimd", lambda Eo: Eo.memset(vtm[:], 1.0), inc=sPL)
    P.wait("vector", sC1, ld_consts)
    P.eop("vector", lambda Eo: Eo.tensor_scalar(maskadd[0:128, 0:128], qpi[:, 0:128], c_neg1, None, op0=ALU.mult), sDV)
    for _t in range(4):
        P.eop("vector", lambda Eo, t=_t: Eo.tensor_scalar(
            condt[:, t * 128:(t + 1) * 128], maskadd[0:128, 0:128],
            mth[:, t:t + 1], None, op0=ALU.is_ge), sDV)
    P.eop("vector", lambda Eo: Eo.tensor_scalar(maskadd[:], qpi[:], qoffb[:, 0:1], None, op0=ALU.add), sDV)
    P.eop("vector", lambda Eo: Eo.tensor_scalar(maskadd[:], maskadd[:], c_zero, None, op0=ALU.is_lt), sDV)
    P.op("vector", lambda Eo: Eo.tensor_scalar(maskadd[:], maskadd[:], c_neg1e30, None, op0=ALU.mult), inc=sDV)

    rXB = Rot(2)
    rXT = Rot(2)
    rQK = Rot(2)
    rV = Rot(2)

    rPXT = Rot(2)

    def chunk_setup(i, src_ap, is_q, kc):
        pi = i % 2
        rXB.acquire(P, "gpsimd", i)
        dXB.load(P, "gpsimd", i, lambda Eo, a=xb[pi][:], s=src_ap: Eo.dma_start(out=a, in_=s))
        dXB.wait(P, "tensor", i)
        rPXT.acquire(P, "tensor", i)
        rXT_psum = pXT[pi]
        for dc in range(8):
            P.op("tensor", lambda Eo, o=rXT_psum[:, dc * 128:(dc + 1) * 128],
                 a=xb[pi][:, dc * 128:(dc + 1) * 128]: Eo.transpose(o, a, idb[:]),
                 inc=(sPE if dc == 7 else None))
        transposes_done = sPE.n
        P.wait("vector", sPE)
        rXT.acquire(P, "vector", i)
        P.op("vector", lambda Eo, o=xT[pi][:], a=rXT_psum[:]: Eo.tensor_copy(o, a), inc=sDV)
        rPXT.release(i, sDV)
        evac_cnt = sDV.n
        P.wait("tensor", sDV, evac_cnt)
        if is_q:
            rQK.acquire(P, "tensor", i)
            for dc in range(8):
                P.op("tensor", lambda Eo, p=pQK[pi][:],
                     w=wqkv[:, dc * 192:dc * 192 + 64],
                     x=xT[pi][:, dc * 128:(dc + 1) * 128], st=(dc == 0), sp=(dc == 7):
                     Eo.matmul(p, w, x, start=st, stop=sp),
                     inc=(sPE if dc == 7 else None))
            P.wait("scalar", sPE)
            P.op("scalar", lambda Eo, o=qT[0:64, i * 128:(i + 1) * 128], p=pQK[pi][:]:
                 Eo.activation(o, p, AF.Identity, bias=qkb[0:64, 0:1], scale=f32(1.0)),
                 inc=sAC)
            rQK.release(i, sAC)
        else:
            rQK.acquire(P, "tensor", i)
            for dc in range(8):
                P.op("tensor", lambda Eo, p=pQK[pi][:],
                     w=wqkv[:, dc * 192 + 64:dc * 192 + 128],
                     x=xT[pi][:, dc * 128:(dc + 1) * 128], st=(dc == 0), sp=(dc == 7):
                     Eo.matmul(p, w, x, start=st, stop=sp),
                     inc=(sPE if dc == 7 else None))
            P.wait("scalar", sPE)
            P.op("scalar", lambda Eo, o=kT[0:64, kc * 128:(kc + 1) * 128], p=pQK[pi][:]:
                 Eo.activation(o, p, AF.Identity, bias=qkb[0:64, 1:2], scale=f32(1.0)),
                 inc=sAC)
            rQK.release(i, sAC)
            rV.acquire(P, "tensor", i)
            for dc in range(8):
                P.op("tensor", lambda Eo, p=pV[pi][:],
                     x=xT[pi][:, dc * 128:(dc + 1) * 128],
                     w=wqkv[:, dc * 192 + 128:dc * 192 + 192], st=(dc == 0):
                     Eo.matmul(p, x, w, start=st, stop=False))
            P.op("tensor", lambda Eo, p=pV[pi][:]:
                 Eo.matmul(p, onesb[:], bvr[:], start=False, stop=True), inc=sPE)
            P.wait("vector", sPE)
            P.wait("vector", sPL, 2)
            P.op("vector", lambda Eo, o=vtm[:, kc * 65:kc * 65 + HS], p=pV[pi][:]:
                 Eo.tensor_copy(o, p), inc=sDV)
            rV.release(i, sDV)
        rXB.rel[pi] = [(sPE, transposes_done)]
        rXT.rel[pi] = [(sPE, sPE.n)]

    for i in range(NCH):
        chunk_setup(i, d_xown[i * 128:(i + 1) * 128, :], True, None)
    for kc in range(KCH):
        chunk_setup(NCH + kc, d_xbat[kc * 128:(kc + 1) * 128, :], False, kc)

    P.wait("sync", sTAP)
    barrier(P)
    with nc.Block("b1") as blk:
        P.run(blk)
    es1.close()

    es2 = ExitStack()
    sb2 = lambda name, shape, dt: es2.enter_context(nc.sbuf_tensor(name, shape, dt))
    ps2 = lambda name, shape, dt: es2.enter_context(nc.psum_tensor(name, shape, dt))
    PT4 = [sb2("PT40", [128, 512], BF16), sb2("PT41", [128, 512], BF16)]
    ots = sb2("ots", [65, 128], F32)
    mtab = [sb2("mtab0", [128, NCH], F32), sb2("mtab1", [128, NCH], F32)]
    mfin = sb2("mfin", [128, 1], F32)
    foldv = [sb2("foldv0", [128, 1], F32), sb2("foldv1", [128, 1], F32)]
    dtmp = sb2("dtmp", [128, 1], F32)
    rcp = sb2("rcp", [128, 1], F32)
    pSA = [ps2("pSA0", [128, 512], F32), ps2("pSA1", [128, 512], F32)]
    pSk = [ps2("pSk0", [128, 512], F32), ps2("pSk1", [128, 512], F32)]
    pO = ps2("pO", [65, 128], F32)
    pOt = ps2("pOt", [128, 65], F32)

    P = Prog()
    for _eng in ENGINES:
        P.wait(_eng, sC1, ld_consts)
        P.wait(_eng, sC2, wl_consts)
    rSA = Rot(2)
    rSK = Rot(2)
    rPT4 = Rot(2)
    rPO = Rot(1)
    rMT = Rot(2)
    fold_done = {}

    def pass_a(j):
        par2 = j % 2
        rMT.acquire(P, "vector", j)
        for kb in range(j + 1):
            par = kb % 2
            rSA.acquire(P, "tensor", kb)
            P.op("tensor", lambda Eo, p=pSA[par][:],
                 q=qT[0:64, j * 128:(j + 1) * 128],
                 k=kT[0:64, kb * 512:(kb + 1) * 512]:
                 Eo.matmul(p, q, k, start=True, stop=True), inc=sPE)
            P.wait("vector", sPE)
            if kb == j:
                P.eop("vector", lambda Eo, p=pSA[par][:]:
                      Eo.tensor_add(p, p, maskadd[:]), sDV)
            P.op("vector", lambda Eo, o=mtab[par2][:, kb:kb + 1], p=pSA[par][:]:
                 Eo.reduce_max(o, p, axis=AX.X), inc=sDV)
            rSA.release(kb, sDV)
        P.wait("vector", sDV)
        P.eop("vector", lambda Eo, mt=mtab[par2][:, 0:j + 1]:
              Eo.reduce_max(mfin[:], mt, axis=AX.X), sDV)
        dFD.wait(P, "vector", j)
        P.eop("vector", lambda Eo, fv=foldv[j % 2][:]:
              Eo.tensor_scalar(fv, mfin[:], c_neg1, None, op0=ALU.mult), sDV)
        dFD.wait(P, "gpsimd", j)
        P.wait("gpsimd", sDV)
        dFD.load(P, "gpsimd", j, lambda Eo, o=qT[64:65, j * 128:(j + 1) * 128],
                 fv=foldv[j % 2][:]: Eo.dma_start(out=o, in_=fv))
        fold_done[j] = None

    def pass_b(j):
        par2 = j % 2
        ngrp = j + 1
        dFD.wait(P, "tensor", j)

        def b_mms(kg):
            par = kg % 2
            rSK.acquire(P, "tensor", kg)
            for t in range(4):
                kk = 4 * kg + t
                P.op("tensor", lambda Eo, p=pSk[par][:, t * 128:(t + 1) * 128],
                     k=kT[:, kk * 128:(kk + 1) * 128],
                     q=qT[:, j * 128:(j + 1) * 128]:
                     Eo.matmul(p, k, q, start=True, stop=True),
                     inc=(sPE if t == 3 else None))
            return sPE.n

        mm_done = {0: b_mms(0)}
        for kg in range(ngrp):
            par = kg % 2
            if kg + 1 < ngrp:
                mm_done[kg + 1] = b_mms(kg + 1)
            P.wait("scalar", sPE, mm_done[kg])
            rPT4.acquire(P, "scalar", kg)
            P.op("scalar", lambda Eo, o=PT4[par][:], p=pSk[par][:]:
                 Eo.activation(o, p, AF.Exp, scale=f32(1.0 / 32.0)), inc=sAC)
            rSK.release(kg, sAC)
            if kg == j:
                P.wait("vector", sAC)
                for t in range(4):
                    P.op("vector", lambda Eo, t=t, o=PT4[par][:, t * 128:(t + 1) * 128],
                         c=condt[:, t * 128:(t + 1) * 128]:
                         Eo.tensor_tensor(o, o, c, op=ALU.mult),
                         inc=(sDV if t == 3 else None))
                P.wait("tensor", sDV)
            else:
                P.wait("tensor", sAC)
            if kg == 0:
                rPO.acquire(P, "tensor", 0)
            for t in range(4):
                kk = 4 * kg + t
                P.op("tensor", lambda Eo, v=vtm[:, kk * 65:(kk + 1) * 65],
                     pp=PT4[par][:, t * 128:(t + 1) * 128],
                     st=(kg == 0 and t == 0), sp=(kg == j and t == 3):
                     Eo.matmul(pO[:], v, pp, start=st, stop=sp),
                     inc=(sPE if t == 3 else None))
            rPT4.release(kg, sPE)
        P.wait("scalar", sPE)
        P.op("scalar", lambda Eo: Eo.activation(ots[:], pO[:], AF.Copy), inc=sAC)
        rPO.rel[0] = [(sAC, sAC.n)]
        P.wait("tensor", sAC)
        P.op("tensor", lambda Eo: Eo.transpose(pOt[:], ots[:], idf[0:65, 0:65]), inc=sPE)
        P.wait("vector", sPE)
        P.eop("vector", lambda Eo: Eo.tensor_copy(dtmp[:], pOt[:, 64:65]), sDV)
        P.eop("vector", lambda Eo: Eo.reciprocal(rcp[:], dtmp[:]), sDV)
        P.op("vector", lambda Eo, o=o_tm[:, j * HS:(j + 1) * HS]:
             Eo.tensor_scalar(o, pOt[:, 0:HS], rcp[:, 0:1], None, op0=ALU.mult), inc=sDV)
        P.wait("tensor", sDV)
        P.wait("scalar", sDV)

    for j in range(NCH):
        pass_a(j)
        if j > 0:
            pass_b(j - 1)
    pass_b(NCH - 1)

    if "o_tm" in taps:
        P.wait("sync", sDV)
        P.op("sync", lambda Eo: Eo.dma_start(out=taps["o_tm"], in_=o_tm[:]),
             inc=sTAP, inc_n=16)
    if "condt" in taps:
        P.wait("gpsimd", sDV)
        P.op("gpsimd", lambda Eo: Eo.dma_start(out=taps["condt"], in_=condt[:]),
             inc=sTAP, inc_n=16)
    if "vtm" in taps:
        P.wait("gpsimd", sDV)
        P.op("gpsimd", lambda Eo: Eo.dma_start(out=taps["vtm"], in_=vtm[:]),
             inc=sTAP, inc_n=16)
    if "qTe" in taps:
        P.wait("gpsimd", sDV)
        P.op("gpsimd", lambda Eo: Eo.dma_start(out=taps["qTe"], in_=qT[:]),
             inc=sTAP, inc_n=16)
    if "qT" in taps:
        P.wait("gpsimd", sDV)
        P.op("gpsimd", lambda Eo: Eo.dma_start(out=taps["qT"], in_=qT[:]),
             inc=sTAP, inc_n=16)
    if "kT" in taps:
        P.wait("gpsimd", sDV)
        P.op("gpsimd", lambda Eo: Eo.dma_start(out=taps["kT"], in_=kT[:]),
             inc=sTAP, inc_n=16)
    if "vtm" in taps:
        P.wait("gpsimd", sDV)
        P.op("gpsimd", lambda Eo: Eo.dma_start(out=taps["vtm"], in_=vtm[:]),
             inc=sTAP, inc_n=16)
    P.wait("sync", sTAP)
    P.wait("gpsimd", sTAP)
    barrier(P)
    with nc.Block("b2") as blk:
        P.run(blk)
    es2.close()

    es3 = ExitStack()
    sb3 = lambda name, shape, dt: es3.enter_context(nc.sbuf_tensor(name, shape, dt))
    ps3 = lambda name, shape, dt: es3.enter_context(nc.psum_tensor(name, shape, dt))
    zch = [sb3("zch%d" % i, [128, C], F32) for i in range(NCH)]
    sqb = [sb3("sqb%d" % i, [128, C], F32) for i in range(3)]
    tmp4 = [sb3("tmp4%d" % i, [128, 512], F32) for i in range(3)]
    logits = sb3("logits", [E, TOK], F32)
    gates_tm = sb3("gates_tm", [128, 128], F32)
    s1c = sb3("s1c", [128, NCH], F32)
    s2c = sb3("s2c", [128, NCH], F32)
    mu_a = sb3("mu_a", [128, NCH], F32)
    musq = sb3("musq", [128, NCH], F32)
    var_a = sb3("var_a", [128, NCH], F32)
    lnv_a = sb3("lnv_a", [128, NCH], F32)
    rstd_a = sb3("rstd_a", [128, NCH], F32)
    gtm_all = sb3("gtm_all", [128, NCH * 8], F32)
    eq1_all = sb3("eq1_all", [128, NCH * 8], F32)
    eq2_all = sb3("eq2_all", [128, NCH * 8], F32)
    mskl_all = sb3("mskl_all", [128, NCH * 8], F32)
    tA_all = sb3("tA_all", [128, NCH * 8], F32)
    tB_all = sb3("tB_all", [128, NCH * 8], F32)
    slots_all = sb3("slots_all", [128, NCH * 8], F32)
    selw_all = sb3("selw_all", [128, NCH * 8], F32)
    m1_all = sb3("m1_all", [128, NCH], F32)
    m2_all = sb3("m2_all", [128, NCH], F32)
    edv_all = sb3("edv_all", [128, NCH], F32)
    edv2_all = sb3("edv2_all", [128, NCH], F32)
    s1_all = sb3("s1_all", [128, NCH], F32)
    rv_all = sb3("rv_all", [128, NCH], F32)
    w2g_all = sb3("w2g_all", [128, NCH], F32)
    idx1_all = sb3("idx1_all", [128, NCH], F32)
    idx2_all = sb3("idx2_all", [128, NCH], F32)
    idxi1 = sb3("idxi1", [128, NCH], mybir.dt.int32)
    idxi2 = sb3("idxi2", [128, NCH], mybir.dt.int32)
    y1bch = [sb3("y1bch%d" % i, [128, C], BF16) for i in range(3)]
    ones8 = sb3("ones8", [E, TOK], F32)
    mask_fm = sb3("mask_fm", [E, TOK], F32)
    pos_fm = sb3("pos_fm", [E, TOK], F32)
    w1m_fm = sb3("w1m_fm", [E, TOK], F32)
    slot_fm = sb3("slot_fm", [E, TOK], F32)
    ecap = sb3("ecap", [128, 8], F32)
    pft = sb3("pft", [E, CAP], F32)
    pTf = [ps3("pTf%d" % i, [128, 512], F32) for i in range(3)]
    pLr = [ps3("pLr0", [E, 128], F32), ps3("pLr1", [E, 128], F32)]
    pG8 = ps3("pG8", [128, 16], F32)
    pTg = ps3("pTg", [E, 256], F32)

    P = Prog()
    for _eng in ENGINES:
        P.wait(_eng, sC1, ld_consts)
        P.wait(_eng, sC2, wl_consts)
    rSQ = Rot(3)
    rTF = Rot(3)
    rT4 = Rot(3)
    rPL = Rot(2)

    sq_done = {}

    def emit_red2(tc):
        P.wait("vector", sAC, sq_done[tc])
        P.op("vector", lambda Eo, q=sqb[tc % 3][:], o=s2c[:, tc:tc + 1]:
             Eo.reduce_sum(o, q, axis=AX.X), inc=sDV)
        rSQ.rel[tc % 3] = [(sDV, sDV.n)]

    for tc in range(NCH):
        z = zch[tc]
        dZC.load(P, "sync", tc, lambda Eo, o=z[:], s=d_xown[tc * 128:(tc + 1) * 128, :]:
                 Eo.dma_start(out=o, in_=s))
        dZC.wait(P, "gpsimd", tc)
        orep = o_tm[:, tc * HS:(tc + 1) * HS].unsqueeze(1).broadcast_to((128, 16, HS))
        P.op("gpsimd", lambda Eo, z_=z[:], r=orep:
             Eo.tensor_add(z_.rearrange("p (r h) -> p r h", r=16), z_.rearrange("p (r h) -> p r h", r=16), r), inc=sPL)
        P.wait("vector", sPL)
        P.op("vector", lambda Eo, z_=z[:], m=s1c[:, tc:tc + 1]:
             Eo.reduce_sum(m, z_, axis=AX.X), inc=sDV)
        P.wait("scalar", sPL)
        rSQ.acquire(P, "scalar", tc)
        P.op("scalar", lambda Eo, z_=z[:], q=sqb[tc % 3][:]:
             Eo.activation(q, z_, AF.Square), inc=sAC)
        sq_done[tc] = sAC.n
        if tc >= 1:
            emit_red2(tc - 1)
    emit_red2(NCH - 1)
    P.wait("vector", sDV)
    P.eop("vector", lambda Eo: Eo.tensor_scalar(mu_a[:], s1c[:], c_inv1024, None, op0=ALU.mult), sDV)
    P.eop("vector", lambda Eo: Eo.tensor_mul(musq[:], mu_a[:], mu_a[:]), sDV)
    P.eop("vector", lambda Eo: Eo.scalar_tensor_tensor(var_a[:], s2c[:], c_inv1024, musq[:],
                                                       op0=ALU.mult, op1=ALU.subtract), sDV)
    P.wait("scalar", sDV)
    P.eop("scalar", lambda Eo: Eo.activation(lnv_a[:], var_a[:], AF.Ln, bias=f32(EPS)), sAC)
    P.op("scalar", lambda Eo: Eo.activation(rstd_a[:], lnv_a[:], AF.Exp, scale=f32(-0.5)), inc=sAC)
    P.wait("vector", sAC)
    for tc in range(NCH):
        z = zch[tc]
        P.op("vector", lambda Eo, z_=z[:], m=mu_a[:, tc:tc + 1], r=rstd_a[:, tc:tc + 1]:
             Eo.tensor_scalar(z_, z_, m, r, op0=ALU.subtract, op1=ALU.mult), inc=sDV)
        y1_done = sDV.n
        P.wait("sync", sDV, y1_done)
        dSPp.load(P, "sync", tc, lambda Eo, o=d_y1sp[tc * 128:(tc + 1) * 128, :], z_=z[:]:
                  Eo.dma_start(out=o, in_=z_))
        P.wait("gpsimd", sDV, y1_done)
        dSPb.wait(P, "gpsimd", tc)
        P.op("gpsimd", lambda Eo, o=y1bch[tc % 3][:], z_=z[:]: Eo.tensor_copy(o, z_), inc=sPL)
        P.wait("sync", sPL)
        dSPb.load(P, "sync", tc, lambda Eo, o=d_y1bf[tc * 128:(tc + 1) * 128, 0:C],
                  a=y1bch[tc % 3][:]: Eo.dma_start(out=o, in_=a))
        P.wait("tensor", sDV, y1_done)
        for g in range(2):
            tfp = pTf[(tc * 2 + g) % 3]
            rTF.acquire(P, "tensor", tc * 2 + g)
            for t in range(4):
                dc = g * 4 + t
                P.op("tensor", lambda Eo, o=tfp[:, t * 128:(t + 1) * 128],
                     a=z[:, dc * 128:(dc + 1) * 128]: Eo.transpose(o, a, idf[:]),
                     inc=(sPE if t == 3 else None))
            P.wait("scalar", sPE)
            rT4.acquire(P, "scalar", tc * 2 + g)
            tm4 = tmp4[(tc * 2 + g) % 3]
            P.op("scalar", lambda Eo, o=tm4[:], a=tfp[:]: Eo.activation(o, a, AF.Copy), inc=sAC)
            rTF.release(tc * 2 + g, sAC)
            P.wait("tensor", sAC)
            if g == 0:
                rPL.acquire(P, "tensor", tc)
            for t in range(4):
                dc = g * 4 + t
                P.op("tensor", lambda Eo, p=pLr[tc % 2][:],
                     w=rw[:, dc * 8:(dc + 1) * 8],
                     x=tm4[:, t * 128:(t + 1) * 128], st=(dc == 0), sp=(dc == 7):
                     Eo.matmul(p, w, x, start=st, stop=sp),
                     inc=(sPE if t == 3 else None))
            rT4.release(tc * 2 + g, sPE)
        P.wait("scalar", sPE)
        P.op("scalar", lambda Eo, o=logits[:, tc * 128:(tc + 1) * 128],
             p=pLr[tc % 2][:]: Eo.activation(o, p, AF.Copy), inc=sAC)
        rPL.release(tc, sAC)
    logits_done = sAC.n
    r3 = lambda ap: ap.rearrange("p (t e) -> p t e", e=8)
    rm = lambda ap: ap.rearrange("p (t o) -> p t o", o=1)
    bc = lambda ap: ap.rearrange("p (t o) -> p t o", o=1).broadcast_to((128, NCH, 8))
    cp_done = {}
    for tc in range(NCH):
        sl = tc % 2
        P.wait("tensor", sAC)
        if tc >= 2:
            P.wait("tensor", sDV, cp_done[sl])
        P.op("tensor", lambda Eo, o=pG8[:, sl * 8:(sl + 1) * 8],
             a=logits[:, tc * 128:(tc + 1) * 128]:
             Eo.transpose(o, a, idf[0:8, 0:8]), inc=sPE)
        P.wait("vector", sPE)
        P.op("vector", lambda Eo, o=gtm_all[:, tc * 8:(tc + 1) * 8],
             p=pG8[:, sl * 8:(sl + 1) * 8]: Eo.tensor_copy(o, p), inc=sDV)
        cp_done[sl] = sDV.n
    P.wait("vector", sDV)
    P.eop("vector", lambda Eo: Eo.reduce_max(rm(m1_all[:]), r3(gtm_all[:]), axis=AX.X), sDV)
    P.eop("vector", lambda Eo: Eo.tensor_tensor(r3(eq1_all[:]), r3(gtm_all[:]), bc(m1_all[:]),
                                                op=ALU.is_equal), sDV)
    P.eop("vector", lambda Eo: Eo.scalar_tensor_tensor(mskl_all[:], eq1_all[:], c_neg1e30,
                                                       gtm_all[:], op0=ALU.mult, op1=ALU.add), sDV)
    P.eop("vector", lambda Eo: Eo.reduce_max(rm(m2_all[:]), r3(mskl_all[:]), axis=AX.X), sDV)
    P.eop("vector", lambda Eo: Eo.tensor_tensor(r3(eq2_all[:]), r3(mskl_all[:]), bc(m2_all[:]),
                                                op=ALU.is_equal), sDV)
    P.eop("vector", lambda Eo: Eo.tensor_sub(edv2_all[:], m2_all[:], m1_all[:]), sDV)
    P.wait("scalar", sDV)
    P.eop("scalar", lambda Eo: Eo.activation(edv_all[:], edv2_all[:], AF.Exp), sAC)
    P.wait("vector", sAC)
    P.eop("vector", lambda Eo: Eo.tensor_scalar(s1_all[:], edv_all[:], c_one, None, op0=ALU.add), sDV)
    P.eop("vector", lambda Eo: Eo.reciprocal(rv_all[:], s1_all[:]), sDV)
    P.eop("vector", lambda Eo: Eo.tensor_mul(w2g_all[:], edv_all[:], rv_all[:]), sDV)
    P.eop("vector", lambda Eo: Eo.tensor_tensor(r3(tA_all[:]), r3(eq1_all[:]), bc(rv_all[:]),
                                                op=ALU.mult), sDV)
    P.eop("vector", lambda Eo: Eo.tensor_tensor(r3(tB_all[:]), r3(eq2_all[:]), bc(w2g_all[:]),
                                                op=ALU.mult), sDV)
    P.eop("vector", lambda Eo: Eo.tensor_add(gates_tm[:], tA_all[:], tB_all[:]), sDV)
    P.wait("gpsimd", sDV)
    P.op("gpsimd", lambda Eo, o=d_y1bf.rearrange("(t p) c -> p t c", p=128)[:, :, C:C + 8],
         a=r3(gates_tm[:]): Eo.dma_start(out=o, in_=a), inc=sGC, inc_n=16)
    P.wait("tensor", sDV)
    for tc in range(NCH):
        sl = tc % 2
        P.op("tensor", lambda Eo, o=pTg[:, sl * 128:(sl + 1) * 128],
             a=gates_tm[:, tc * 8:(tc + 1) * 8]:
             Eo.transpose(o, a, idf[:]), inc=sPE)
        P.wait("scalar", sPE)
        P.op("scalar", lambda Eo, o=gatesT[:, tc * 128:(tc + 1) * 128],
             p=pTg[:, sl * 128:(sl + 1) * 128]: Eo.activation(o, p, AF.Copy), inc=sAC)
        P.wait("tensor", sAC)

    P.eop("vector", lambda Eo: Eo.tensor_scalar(ecap[:], evals[:], c_cap, None, op0=ALU.mult), sDV)
    P.op("gpsimd", lambda Eo: Eo.memset(ones8[:], 1.0), inc=sPL)
    P.op("gpsimd", lambda Eo: Eo.memset(pft[:], 2.0e6), inc=sPL)
    P.wait("sync", sPL)
    dPF.load(P, "sync", 0, lambda Eo: Eo.dma_start(out=d_idxf, in_=pft[:]))
    P.wait("vector", sAC)
    P.wait("vector", sPL)
    P.eop("vector", lambda Eo: Eo.tensor_scalar(mask_fm[:], gatesT[:], c_zero[0:E, :], None,
                                                op0=ALU.is_gt), sDV)
    P.eop("vector", lambda Eo: Eo.tensor_tensor_scan(pos_fm[:], ones8[:], mask_fm[:], 0.0,
                                                     op0=ALU.mult, op1=ALU.add), sDV)
    P.eop("vector", lambda Eo: Eo.tensor_scalar(pos_fm[:], pos_fm[:], c_one[0:E, :], None,
                                                op0=ALU.subtract), sDV)
    P.eop("vector", lambda Eo: Eo.tensor_scalar(w1m_fm[:], mask_fm[:], c_neg1[0:E, :],
                                                c_one[0:E, :], op0=ALU.mult, op1=ALU.add), sDV)
    P.eop("vector", lambda Eo: Eo.scalar_tensor_tensor(slot_fm[:], w1m_fm[:], c_big[0:E, :],
                                                       pos_fm[:], op0=ALU.mult, op1=ALU.add), sDV)
    slot_ready = sDV.n
    P.wait("tensor", sDV, slot_ready)
    dPF.wait(P, "gpsimd", 0)
    for tc in range(NCH):
        sl = tc % 2
        if tc >= 2:
            P.wait("tensor", sDV, cp_done[sl])
        P.op("tensor", lambda Eo, o=pG8[:, sl * 8:(sl + 1) * 8],
             a=slot_fm[:, tc * 128:(tc + 1) * 128]:
             Eo.transpose(o, a, idf[0:8, 0:8]), inc=sPE)
        P.wait("vector", sPE)
        P.op("vector", lambda Eo, o=slots_all[:, tc * 8:(tc + 1) * 8],
             p=pG8[:, sl * 8:(sl + 1) * 8]: Eo.tensor_copy(o, p), inc=sDV)
        cp_done[sl] = sDV.n
    P.wait("vector", sDV)
    P.eop("vector", lambda Eo: Eo.tensor_scalar(tA_all[:], slots_all[:], c_cap, None,
                                                op0=ALU.is_ge), sDV)
    P.eop("vector", lambda Eo: Eo.scalar_tensor_tensor(selw_all[:], tA_all[:], c_big,
                                                       slots_all[:], op0=ALU.mult, op1=ALU.add), sDV)
    ecap_b = ecap[:].rearrange("p (o e) -> p o e", o=1).broadcast_to((128, NCH, 8))
    P.eop("vector", lambda Eo: Eo.tensor_tensor(r3(selw_all[:]), r3(selw_all[:]), ecap_b,
                                                op=ALU.add), sDV)
    P.eop("vector", lambda Eo: Eo.reduce_max(rm(m1_all[:]), r3(gates_tm[:]), axis=AX.X), sDV)
    P.eop("vector", lambda Eo: Eo.tensor_tensor(r3(tA_all[:]), r3(gates_tm[:]), bc(m1_all[:]),
                                                op=ALU.is_equal), sDV)
    P.eop("vector", lambda Eo: Eo.tensor_scalar(tB_all[:], gates_tm[:], c_zero, None,
                                                op0=ALU.is_gt), sDV)
    P.eop("vector", lambda Eo: Eo.tensor_sub(tB_all[:], tB_all[:], tA_all[:]), sDV)
    P.eop("vector", lambda Eo: Eo.tensor_mul(tA_all[:], tA_all[:], selw_all[:]), sDV)
    P.eop("vector", lambda Eo: Eo.reduce_sum(rm(idx1_all[:]), r3(tA_all[:]), axis=AX.X), sDV)
    P.eop("vector", lambda Eo: Eo.tensor_mul(tB_all[:], tB_all[:], selw_all[:]), sDV)
    P.eop("vector", lambda Eo: Eo.reduce_sum(rm(idx2_all[:]), r3(tB_all[:]), axis=AX.X), sDV)
    P.eop("vector", lambda Eo: Eo.tensor_copy(idxi1[:], idx1_all[:]), sDV)
    P.eop("vector", lambda Eo: Eo.tensor_copy(idxi2[:], idx2_all[:]), sDV)
    P.wait("gpsimd", sDV)
    P.wait("gpsimd", sGC)
    sct_base = sSCT.n
    kb_ = 0
    for tc in range(NCH):
        for ib in (idxi1, idxi2):
            P.op("gpsimd", lambda Eo, tc=tc, ib=ib:
                 Eo.indirect_dma_start(
                     out=d_idxf.rearrange("e (s o) -> (e s) o", o=1),
                     out_offset=IndirectOffsetOnAxis(ap=ib[:, tc:tc + 1], axis=0),
                     in_=tokid[:, tc:tc + 1], in_offset=None,
                     bounds_check=regB_all, oob_is_err=False),
                 inc=sSCT, inc_n=16)
            kb_ += 1
            P.wait("gpsimd", sSCT, sct_base + 16 * kb_)

    if "logits" in taps:
        P.wait("sync", sAC, logits_done)
        P.op("sync", lambda Eo: Eo.dma_start(out=taps["logits"], in_=logits[:]),
             inc=sTAP, inc_n=16)
    if "gates" in taps:
        P.wait("sync", sDV)
        P.op("sync", lambda Eo: Eo.dma_start(out=taps["gates"], in_=gates_tm[:]),
             inc=sTAP, inc_n=16)
    if "gatesT" in taps:
        P.wait("gpsimd", sAC)
        P.op("gpsimd", lambda Eo: Eo.dma_start(out=taps["gatesT"], in_=gatesT[:]),
             inc=sTAP, inc_n=16)
    if "y1" in taps:
        for s_ in dSPp.sems:
            P.wait("sync", s_)
        P.op("sync", lambda Eo: Eo.dma_start(out=taps["y1"], in_=d_y1sp), inc=sTAP, inc_n=16)
    for s_ in dSPp.sems:
        P.wait("sync", s_)
    P.wait("sync", sTAP)
    barrier(P)
    with nc.Block("b3") as blk:
        P.run(blk)
    es3.close()
    esA.close()

    es4 = ExitStack()
    sb4 = lambda name, shape, dt: es4.enter_context(nc.sbuf_tensor(name, shape, dt))
    ps4 = lambda name, shape, dt: es4.enter_context(nc.psum_tensor(name, shape, dt))
    H = sb4("H", [128, 32 * CAP], BF16)
    y1gT = [sb4("y1gT%d" % i, [128, 8 * CAP], BF16) for i in range(2)]
    o_sb = sb4("o_sb", [128, 8 * CAP], F32)
    w1b = [sb4("w1b%d" % i, [128, 8 * 256], BF16) for i in range(4)]
    w2b = [sb4("w2b%d" % i, [128, 16 * 128], BF16) for i in range(4)]
    gbuf = [sb4("gbuf%d" % i, [128, C + 8], BF16) for i in range(NG)]
    scatb = [sb4("scatb%d" % i, [128, C], F32) for i in range(NG)]
    idxf_s = [sb4("idxf_s%d" % i, [128, NG], F32) for i in range(2)]
    idxi_s = [sb4("idxi_s%d" % i, [128, NG], mybir.dt.int32) for i in range(2)]
    gv = [sb4("gv%d" % i, [128, NG], F32) for i in range(2)]
    pH = [ps4("pH%d" % i, [128, 1024], F32) for i in range(2)]
    pTin = ps4("pTin", [128, 512], BF16)
    pTout = ps4("pTout", [128, 512], F32)

    P = Prog()
    w1_view = d_w1
    w2_view = d_w2
    for _eng in ENGINES:
        P.wait(_eng, sC1, ld_consts)
        P.wait(_eng, sC2, wl_consts)
    for _eng in ("gpsimd", "sync"):
        P.wait(_eng, sSCT)
        P.wait(_eng, sGC)
        for s_ in dSPb.sems:
            P.wait(_eng, s_)
        for s_ in dSPp.sems:
            P.wait(_eng, s_)
        P.wait(_eng, dPF.sems[0])
    P.wait("gpsimd", sGC)
    for s_ in dSPb.sems:
        P.wait("gpsimd", s_)
    for _gb in gbuf:
        P.op("gpsimd", lambda Eo, b=_gb: Eo.memset(b[:], 0.0))
    P.op("gpsimd", lambda Eo: Eo.memset(gv[0][:], 0.0))
    P.op("gpsimd", lambda Eo: Eo.memset(gv[1][:], 0.0), inc=sPL)
    P.wait("gpsimd", sPL)
    P.wait("tensor", sPL)

    rW1 = Rot(4)
    rW2 = Rot(4)
    rGB = Rot(NG)
    rIX = Rot(2)
    rGV = Rot(2)
    rSCB = Rot(NG)
    rPH = Rot(2)
    rTIN = Rot(1)
    rTOUT = Rot(1)
    NB = ((0, 512), (512, CAP))

    in_done = {}

    idx_ready = {}

    def emit_meta(e):
        ep = e % 2
        rIX.acquire(P, "sync", e)
        dIX.load(P, "sync", e, lambda Eo, o=idxf_s[ep][:],
                 s=d_idxf[e, :].rearrange("(p g) -> p g", g=NG): Eo.dma_start(out=o, in_=s))
        dIX.wait(P, "vector", e)
        P.eop("vector", lambda Eo, o=idxi_s[ep][:], a=idxf_s[ep][:]: Eo.tensor_copy(o, a), sDV)
        rGV.acquire(P, "vector", e)
        idx_ready[e] = sDV.n

    def emit_gath1(e, g):
        ep = e % 2
        if g == 0:
            P.wait("gpsimd", sDV, idx_ready[e])
            for s_ in dW1.sems:
                P.wait("gpsimd", s_)
            for s_ in dW2.sems:
                P.wait("gpsimd", s_)
        rGB.acquire(P, "gpsimd", g)
        dGA.load(P, "gpsimd", e * NG + g, lambda Eo, g=g, ep=ep:
                 Eo.indirect_dma_start(
                     out=gbuf[g][:], out_offset=None,
                     in_=d_y1bf, in_offset=IndirectOffsetOnAxis(ap=idxi_s[ep][:, g:g + 1], axis=0),
                     bounds_check=regB_tok, oob_is_err=False))
        dGA.wait(P, "gpsimd", e * NG + g)

    def emit_gvx(e):
        ep = e % 2
        for g in range(NG):
            dGA.wait(P, "vector", e * NG + g)
            P.eop("vector", lambda Eo, g=g, ep=ep, e=e, o=gv[ep][:, g:g + 1]:
                  Eo.tensor_copy(o, gbuf[g][:, C + e:C + e + 1]), sDV)

    def emit_int(e):
        ep = e % 2
        for g in range(NG):
            dGA.wait(P, "tensor", e * NG + g)
            rTIN.acquire(P, "tensor", 0)
            for kc in range(8):
                P.op("tensor", lambda Eo, kc=kc, g=g,
                     o=pTin[:, (kc % 4) * 128:(kc % 4 + 1) * 128],
                     a=gbuf[g][:, kc * 128:(kc + 1) * 128]:
                     Eo.transpose(o, a, idb[:]),
                     inc=(sPE if kc % 4 == 3 else None))
                if kc % 4 == 3:
                    P.wait("vector", sPE)
                    kc0 = kc - 3
                    P.op("vector", lambda Eo, kc0=kc0, g=g, o4=[
                         y1gT[ep][:, (kc0 + i) * CAP + g * 128:(kc0 + i) * CAP + (g + 1) * 128]
                         for i in range(4)], pt=pTin[:]:
                         [Eo.tensor_copy(o4[i], pt[:, i * 128:(i + 1) * 128]) for i in range(4)][-1],
                         inc=sDV)
                    if kc == 3:
                        rTIN.rel[0] = [(sDV, sDV.n)]
                        rTIN.acquire(P, "tensor", 0)
                    else:
                        rTIN.rel[0] = [(sDV, sDV.n)]
            rGB.release(g, sPE)
            rGB.release(g, sDV)
        in_done[e] = sDV.n

    PIPE_META = True
    prefetched = set()

    def load_w1(e, fp):
        wpar = (e * 16 + fp) % 4
        rW1.acquire(P, "gpsimd", e * 16 + fp)
        srcw = w1_view[e, :, fp * 256:(fp + 1) * 256].rearrange("(kc p) f -> p kc f", p=128)
        dW1.load(P, "gpsimd", e * 16 + fp,
                 lambda Eo, o=w1b[wpar][:].rearrange("p (kc f) -> p kc f", kc=8),
                 s=srcw: Eo.dma_start(out=o, in_=s))

    emit_meta(0)
    for _g in range(NG):
        emit_gath1(0, _g)
    dGA.wait(P, "gpsimd", NG - 1)
    emit_gvx(0)
    emit_int(0)
    for e in range(n_experts):
        ep = e % 2
        if e > 0:
            emit_int(e)
        P.wait("tensor", sDV, in_done[e])
        for fp in range(16):
            wpar = (e * 16 + fp) % 4
            if (e, fp) not in prefetched:
                load_w1(e, fp)
            dW1.wait(P, "tensor", e * 16 + fp)
            for fc in range(2):
                fcg = fp * 2 + fc
                hpar = fcg % 2
                rPH.acquire(P, "tensor", fcg)
                for kc in range(8):
                    for nbi, (lo, hi) in enumerate(NB):
                        P.op("tensor", lambda Eo, p=pH[hpar][:, lo:hi],
                             w=w1b[wpar][:, kc * 256 + fc * 128:kc * 256 + (fc + 1) * 128],
                             x=y1gT[ep][:, kc * CAP + lo:kc * CAP + hi],
                             st=(kc == 0), sp=(kc == 7):
                             Eo.matmul(p, w, x, start=st, stop=sp),
                             inc=(sPE if (nbi == 1 and kc == 7) else None))
                P.wait("scalar", sPE)
                hsl = H[:, fcg * CAP:(fcg + 1) * CAP]
                P.op("scalar", lambda Eo, o=hsl, p=pH[hpar][:, 0:CAP],
                     b=b1t[:, e * 32 + fcg:e * 32 + fcg + 1]:
                     Eo.activation(o, p, AF.Gelu_apprx_tanh, bias=b, scale=f32(1.0)),
                     inc=sAC)
                rPH.release(fcg, sAC)
            rW1.release(e * 16 + fp, sPE)
        h_done = sAC.n
        if PIPE_META and e + 1 < n_experts:
            emit_meta(e + 1)
        P.wait("tensor", sAC, h_done)
        for dc in range(8):
            opar = dc % 2
            if dc == 4 and e + 1 < n_experts:
                for g in range(NG):
                    emit_gath1(e + 1, g)
                dGA.wait(P, "gpsimd", (e + 1) * NG + NG - 1)
                emit_gvx(e + 1)
                for fpp in range(4):
                    load_w1(e + 1, fpp)
                    prefetched.add((e + 1, fpp))
            for fh in range(2):
                w2i = (e * 16 + dc * 2 + fh) % 4
                rW2.acquire(P, "gpsimd", e * 16 + dc * 2 + fh)
                srcw = w2_view[e, fh * 2048:(fh + 1) * 2048,
                               dc * 128:(dc + 1) * 128].rearrange("(fc p) d -> p fc d", p=128)
                dW2.load(P, "gpsimd", e * 16 + dc * 2 + fh,
                         lambda Eo, o=w2b[w2i][:].rearrange("p (fc d) -> p fc d", fc=16),
                         s=srcw: Eo.dma_start(out=o, in_=s))
                dW2.wait(P, "tensor", e * 16 + dc * 2 + fh)
                if fh == 0:
                    rPH.acquire(P, "tensor", dc)
                for fcL in range(16):
                    fcg = fh * 16 + fcL
                    for nbi, (lo, hi) in enumerate(NB):
                        P.op("tensor", lambda Eo, p=pH[opar][:, lo:hi],
                             w=w2b[w2i][:, fcL * 128:(fcL + 1) * 128],
                             x=H[:, fcg * CAP + lo:fcg * CAP + hi],
                             st=(fh == 0 and fcL == 0), sp=(fh == 1 and fcL == 15):
                             Eo.matmul(p, w, x, start=st, stop=sp),
                             inc=(sPE if (nbi == 1 and fcL == 15) else None))
                rW2.release(e * 16 + dc * 2 + fh, sPE)
            P.wait("vector", sPE)
            P.op("vector", lambda Eo, o=o_sb[:, dc * CAP:(dc + 1) * CAP],
                 p=pH[opar][:, 0:CAP], b=b2ts[:, dc * 8 + e:dc * 8 + e + 1]:
                 Eo.tensor_scalar(o, p, b, None, op0=ALU.add), inc=sDV)
            rPH.release(dc, sDV)
        o_done = sDV.n
        P.wait("tensor", sDV, o_done)
        for g in range(NG):
            sbi = g
            rSCB.acquire(P, "tensor", e * NG + g)
            rTOUT.acquire(P, "tensor", 0)
            for dc in range(8):
                P.op("tensor", lambda Eo, dc=dc, g=g,
                     o=pTout[:, (dc % 4) * 128:(dc % 4 + 1) * 128],
                     a=o_sb[:, dc * CAP + g * 128:dc * CAP + (g + 1) * 128]:
                     Eo.transpose(o, a, idf[:]),
                     inc=(sPE if dc % 4 == 3 else None))
                if dc % 4 == 3:
                    P.wait("vector", sPE)
                    dc0 = dc - 3
                    for i in range(4):
                        P.op("vector", lambda Eo, i=i, dc0=dc0, g=g, sbi=sbi,
                             o=scatb[sbi][:, (dc0 + i) * 128:(dc0 + i + 1) * 128],
                             p=pTout[:, i * 128:(i + 1) * 128],
                             gvc=gv[ep][:, g:g + 1]:
                             Eo.tensor_scalar(o, p, gvc, None, op0=ALU.mult),
                             inc=(sDV if i == 3 else None))
                    rTOUT.rel[0] = [(sDV, sDV.n)]
                    if dc == 3:
                        rTOUT.acquire(P, "tensor", 0)
            P.wait("gpsimd", sDV)
            P.wait("gpsimd", sSCA)
            P.op("gpsimd", lambda Eo, g=g, ep=ep, sbi=sbi:
                 Eo.indirect_dma_start(
                     out=d_y1sp, out_offset=IndirectOffsetOnAxis(ap=idxi_s[ep][:, g:g + 1], axis=0),
                     in_=scatb[sbi][:], in_offset=None,
                     bounds_check=regB_tok, oob_is_err=False,
                     compute_op=ALU.add),
                 inc=sSCA, inc_n=16)
            rSCB.release(e * NG + g, sSCA)
        rIX.release(e, sSCA)
        rGV.release(e, sSCA)

    if "O_acc" in taps:
        P.wait("sync", sSCA)
        P.op("sync", lambda Eo: Eo.dma_start(out=taps["O_acc"], in_=o_sb[:]),
             inc=sTAP, inc_n=16)
    P.wait("sync", sTAP)
    barrier(P)
    with nc.Block("b4") as blk:
        P.run(blk)
    es4.close()

    es5 = ExitStack()
    sb5 = lambda name, shape, dt: es5.enter_context(nc.sbuf_tensor(name, shape, dt))
    z2ch = [sb5("z2ch%d" % i, [128, C], F32) for i in range(NCH)]
    sq2 = [sb5("sq2%d" % i, [128, C], F32) for i in range(3)]
    s1b = sb5("s1b", [128, NCH], F32)
    s2b = sb5("s2b", [128, NCH], F32)
    mu2a = sb5("mu2a", [128, NCH], F32)
    musq2 = sb5("musq2", [128, NCH], F32)
    var2a = sb5("var2a", [128, NCH], F32)
    lnv2a = sb5("lnv2a", [128, NCH], F32)
    rstd2a = sb5("rstd2a", [128, NCH], F32)

    P = Prog()
    for _eng in ENGINES:
        P.wait(_eng, sC1, ld_consts)
        P.wait(_eng, sC2, wl_consts)
    P.wait("sync", sSCA)
    rSQ2 = Rot(3)
    sq2_done = {}

    def emit_red2b(tc):
        P.wait("vector", sAC, sq2_done[tc])
        P.op("vector", lambda Eo, q=sq2[tc % 3][:], o=s2b[:, tc:tc + 1]:
             Eo.reduce_sum(o, q, axis=AX.X), inc=sDV)
        rSQ2.rel[tc % 3] = [(sDV, sDV.n)]

    for tc in range(NCH):
        z2 = z2ch[tc]
        dYC.load(P, "sync", tc, lambda Eo, o=z2[:], s=d_y1sp[tc * 128:(tc + 1) * 128, :]:
                 Eo.dma_start(out=o, in_=s))
        dYC.wait(P, "vector", tc)
        P.op("vector", lambda Eo, m=s1b[:, tc:tc + 1], z_=z2[:]:
             Eo.reduce_sum(m, z_, axis=AX.X), inc=sDV)
        dYC.wait(P, "scalar", tc)
        rSQ2.acquire(P, "scalar", tc)
        P.op("scalar", lambda Eo, q=sq2[tc % 3][:], z_=z2[:]:
             Eo.activation(q, z_, AF.Square), inc=sAC)
        sq2_done[tc] = sAC.n
        if tc >= 1:
            emit_red2b(tc - 1)
    emit_red2b(NCH - 1)
    P.wait("vector", sDV)
    P.eop("vector", lambda Eo: Eo.tensor_scalar(mu2a[:], s1b[:], c_inv1024, None, op0=ALU.mult), sDV)
    P.eop("vector", lambda Eo: Eo.tensor_mul(musq2[:], mu2a[:], mu2a[:]), sDV)
    P.eop("vector", lambda Eo: Eo.scalar_tensor_tensor(var2a[:], s2b[:], c_inv1024, musq2[:],
                                                       op0=ALU.mult, op1=ALU.subtract), sDV)
    P.wait("scalar", sDV)
    P.eop("scalar", lambda Eo: Eo.activation(lnv2a[:], var2a[:], AF.Ln, bias=f32(EPS)), sAC)
    P.op("scalar", lambda Eo: Eo.activation(rstd2a[:], lnv2a[:], AF.Exp, scale=f32(-0.5)), inc=sAC)
    P.wait("vector", sAC)
    for tc in range(NCH):
        z2 = z2ch[tc]
        P.op("vector", lambda Eo, z_=z2[:], m=mu2a[:, tc:tc + 1], r=rstd2a[:, tc:tc + 1]:
             Eo.tensor_scalar(z_, z_, m, r, op0=ALU.subtract, op1=ALU.mult), inc=sDV)
        P.wait("sync", sDV)
        dSTp.load(P, "sync", tc, lambda Eo, o=d_y2[tc * 128:(tc + 1) * 128, :], z_=z2[:]:
                  Eo.dma_start(out=o, in_=z_))
    for s_ in dSTp.sems:
        P.wait("sync", s_)
    P.wait("sync", sTAP)
    with nc.Block("b5") as blk:
        P.run(blk)
    es5.close()
    esP.close()
    esS.close()
    return nc


_CACHE = {}


def kernel(L0_attn_x, L0_attn_Wq, L0_attn_bq, L0_attn_Wk, L0_attn_bk,
           L0_attn_Wv, L0_attn_bv, L0_ffn_router_w, L0_ffn_w1, L0_ffn_b1,
           L0_ffn_w2, L0_ffn_b2):
    x = np.ascontiguousarray(np.asarray(L0_attn_x, dtype=np.float32))
    Wq = np.asarray(L0_attn_Wq, np.float32)
    Wk = np.asarray(L0_attn_Wk, np.float32)
    Wv = np.asarray(L0_attn_Wv, np.float32)
    bq = np.asarray(L0_attn_bq, np.float32)
    bk = np.asarray(L0_attn_bk, np.float32)
    bv = np.asarray(L0_attn_bv, np.float32)
    rw_in = np.asarray(L0_ffn_router_w, np.float32)
    w1 = np.ascontiguousarray(np.asarray(L0_ffn_w1, np.float32))
    b1 = np.asarray(L0_ffn_b1, np.float32)
    w2 = np.ascontiguousarray(np.asarray(L0_ffn_w2, np.float32))
    b2 = np.asarray(L0_ffn_b2, np.float32)

    wqkvT = np.concatenate([Wq.T, Wk.T, Wv.T], axis=1)
    wqkvT = np.ascontiguousarray(wqkvT.reshape(8, 128, 192).transpose(1, 0, 2))
    qkb = np.zeros((128, 2), np.float32)
    qkb[0:64, 0] = bq
    qkb[0:64, 1] = bk
    bvrow = bv.reshape(1, HS)
    b1t = np.ascontiguousarray(
        b1.reshape(E, 32, 128).transpose(2, 0, 1))
    rw_pre = np.ascontiguousarray(rw_in.reshape(8, 128, E).transpose(1, 0, 2))
    identb = np.eye(128, dtype=bf16)
    identf = np.eye(128, dtype=np.float32)
    qpiota = (np.arange(128, dtype=np.float32)[:, None]
              - np.arange(512, dtype=np.float32)[None, :])
    onesb = np.ones((1, 128), dtype=bf16)
    consts = np.zeros((128, 12), np.float32)
    consts[:, C_INV1024] = 1.0 / 1024.0
    consts[:, C_NEG132] = -1.0 / 32.0
    consts[:, C_ONE] = 1.0
    consts[:, C_NEG1E30] = -1e30
    consts[:, C_NEG1] = -1.0
    selb = np.zeros((E, E * 128), dtype=bf16)
    for e in range(E):
        selb[e, e * 128:(e + 1) * 128] = 1
    tokid = (np.arange(NCH, dtype=np.float32)[None, :] * 128.0
             + np.arange(128, dtype=np.float32)[:, None])
    evals = np.tile(np.arange(E, dtype=np.float32)[None, :], (128, 1))
    consts[:, C_BIG] = 1.0e6
    consts[:, C_EIGHT] = 8.0
    consts[:, C_CAP] = float(CAP)

    nc = build_nc()
    in_maps = []
    for c in range(NCORES):
        b = c // 4
        ph = c % 4
        chunks = [4 * j + ph for j in range(NCH)]
        x_b = x[b]
        x_own = np.ascontiguousarray(
            x_b.reshape(64, 128, C)[chunks].reshape(TOK, C))
        qoffb = np.full((128, 1), 128.0 * ph, np.float32)
        mth = np.zeros((128, 4), np.float32)
        for t in range(4):
            mth[:, t] = (t - ph) * 128.0
        in_maps.append(dict(
            x_own=x_own, x_batch=x_b, wqkvT=wqkvT, qkb=qkb, bvrow=bvrow,
            b1t=b1t, b2w=b2, rw=rw_pre, w1=w1, w2=w2, identb=identb,
            identf=identf, qpiota=qpiota, onesb=onesb, qoffb=qoffb,
            consts=consts, selb=selb, mth=mth, tokid=tokid, evals=evals,
            b2t=np.ascontiguousarray(b2.T.reshape(8, 128, E).transpose(1, 0, 2))))
    res = run_bass_kernel_spmd(nc, in_maps, core_ids=list(range(NCORES)))
    out = np.zeros((B, T, C), np.float32)
    for c in range(NCORES):
        b = c // 4
        ph = c % 4
        y2 = np.asarray(res.results[c]["y2"]).reshape(NCH, 128, C)
        for j in range(NCH):
            out[b, (4 * j + ph) * 128:(4 * j + ph + 1) * 128, :] = y2[j]
    return out


# version 20
# speedup vs baseline: 2.1462x; 1.0464x over previous
import numpy as np
import ml_dtypes
from contextlib import ExitStack

import concourse.bass as bass
import concourse.mybir as mybir
from concourse.bass_utils import run_bass_kernel_spmd
from concourse.bass_core_rust import IndirectOffsetOnAxis

F32 = mybir.dt.float32
BF16 = mybir.dt.bfloat16
AF = mybir.ActivationFunctionType
ALU = mybir.AluOpType
AX = mybir.AxisListType
f32 = np.float32
bf16 = ml_dtypes.bfloat16

B, T, C = 2, 8192, 1024
HS = 64
E, FF = 8, 4096
NCORES = 8
TOK = 2048
NCH = 16
KCH = 64
NT = 1024
EPS = 1e-5

C_INV1024, C_NEG132, C_ONE, C_NEG1E30, C_ZERO, C_NEG1, C_BIG, C_EIGHT, C_CAP = 0, 1, 2, 3, 4, 5, 6, 7, 8
CAP, NG = 640, 5

ENGINES = ("sync", "scalar", "vector", "tensor", "gpsimd")


class Sem:
    def __init__(self, nc, es, name):
        self.h = es.enter_context(nc.semaphore(name))
        self.n = 0

    def bump(self, k):
        self.n += k
        return self.n


class Prog:

    def __init__(self):
        self.streams = {e: [] for e in ENGINES}

    def op(self, eng, fn, inc=None, inc_n=1):
        if inc is not None:
            inc.bump(inc_n)
            self.streams[eng].append(
                lambda Eo, fn=fn, s=inc.h, n=inc_n: fn(Eo).then_inc(s, n))
            if (PARANOID[0] is True and eng == "gpsimd" and inc_n == 16) or \
               (PARANOID[0] == "ind" and eng == "gpsimd" and inc_n == 16 and getattr(self, "_mark_ind", False)):
                self.wait(eng, inc)
        else:
            self.streams[eng].append(lambda Eo, fn=fn: fn(Eo))

    def eop(self, eng, fn, sem):
        self.op(eng, fn, inc=sem)
        self.wait(eng, sem)

    def wait(self, eng, sem, val=None):
        v = sem.n if val is None else val
        if v > 0:
            self.streams[eng].append(lambda Eo, s=sem.h, v=v: Eo.wait_ge(s, v))

    def run(self, block):
        for eng in ENGINES:
            fns = tuple(self.streams[eng])

            def body(Eo, fns=fns):
                for f in fns:
                    f(Eo)

            getattr(block, eng)(body)


class Rot:

    def __init__(self, n):
        self.n = n
        self.rel = [[] for _ in range(n)]

    def acquire(self, P, eng, i):
        for sem, cnt in self.rel[i % self.n]:
            P.wait(eng, sem, cnt)
        self.rel[i % self.n] = []

    def release(self, i, sem):
        self.rel[i % self.n].append((sem, sem.n))


class DmaSlots:

    def __init__(self, nc, es, name, n=2):
        self.sems = [Sem(nc, es, "%s%d" % (name, i)) for i in range(n)]
        self.n = n

    def load(self, P, eng, i, fn):
        P.op(eng, fn, inc=self.sems[i % self.n], inc_n=16)

    def wait(self, P, eng, i):
        P.wait(eng, self.sems[i % self.n])

    def sem_count(self, i):
        s = self.sems[i % self.n]
        return (s, s.n)


PARANOID = [False]


def build_nc(debug_taps=(), n_experts=E):
    nc = bass.Bass("TRN2", target_bir_lowering=False)

    d_xown = nc.dram_tensor("x_own", [TOK, C], F32, kind="ExternalInput").ap()
    d_xbat = nc.dram_tensor("x_batch", [T, C], F32, kind="ExternalInput").ap()
    d_wqkv = nc.dram_tensor("wqkvT", [128, 8, 192], F32, kind="ExternalInput").ap()
    d_qkb = nc.dram_tensor("qkb", [128, 2], F32, kind="ExternalInput").ap()
    d_bvr = nc.dram_tensor("bvrow", [1, HS], F32, kind="ExternalInput").ap()
    d_b1t = nc.dram_tensor("b1t", [128, E, 32], F32, kind="ExternalInput").ap()
    d_b2w = nc.dram_tensor("b2w", [E, C], F32, kind="ExternalInput").ap()
    d_rw = nc.dram_tensor("rw", [128, 8, E], F32, kind="ExternalInput").ap()
    d_w1 = nc.dram_tensor("w1", [E, C, FF], F32, kind="ExternalInput").ap()
    d_w2 = nc.dram_tensor("w2", [E, FF, C], F32, kind="ExternalInput").ap()
    d_idb = nc.dram_tensor("identb", [128, 128], BF16, kind="ExternalInput").ap()
    d_idf = nc.dram_tensor("identf", [128, 128], F32, kind="ExternalInput").ap()
    d_qpi = nc.dram_tensor("qpiota", [128, 512], F32, kind="ExternalInput").ap()
    d_ones = nc.dram_tensor("onesb", [1, 128], BF16, kind="ExternalInput").ap()
    d_qoff = nc.dram_tensor("qoffb", [128, 1], F32, kind="ExternalInput").ap()
    d_cst = nc.dram_tensor("consts", [128, 12], F32, kind="ExternalInput").ap()
    d_sel = nc.dram_tensor("selb", [E, E * 128], BF16, kind="ExternalInput").ap()
    d_mth = nc.dram_tensor("mth", [128, 4], F32, kind="ExternalInput").ap()
    d_tokid = nc.dram_tensor("tokid", [128, NCH], F32, kind="ExternalInput").ap()
    d_evals = nc.dram_tensor("evals", [128, E], F32, kind="ExternalInput").ap()
    d_b2t = nc.dram_tensor("b2t", [128, 8, E], F32, kind="ExternalInput").ap()
    d_y1bf = nc.dram_tensor("y1bf", [TOK, C + 8], BF16, kind="Internal").ap()
    d_idxf = nc.dram_tensor("idxf", [E, CAP], F32, kind="Internal").ap()
    d_gc = nc.dram_tensor("gc", [TOK, E], F32, kind="Internal").ap()
    d_y1sp = nc.dram_tensor("y1sp", [TOK, C], F32, kind="Internal").ap()
    d_y2 = nc.dram_tensor("y2", [TOK, C], F32, kind="ExternalOutput").ap()
    taps = {}
    for name, shape in debug_taps:
        taps[name] = nc.dram_tensor("tap_" + name, shape, F32, kind="ExternalOutput").ap()

    esP = ExitStack()
    sb = lambda name, shape, dt: esP.enter_context(nc.sbuf_tensor(name, shape, dt))

    idb = sb("idb", [128, 128], BF16)
    idf = sb("idf", [128, 128], F32)
    onesb = sb("onesb_s", [1, 128], BF16)
    qoffb = sb("qoffb_s", [128, 1], F32)
    mth = sb("mth_s", [128, 4], F32)
    tokid = sb("tokid_s", [128, NCH], F32)
    evals = sb("evals_s", [128, E], F32)
    b2ts = sb("b2t_s", [128, 8 * E], F32)
    cst = sb("cst", [128, 12], F32)
    b1t = sb("b1t_s", [128, E * 32], F32)
    b2b = sb("b2b", [E, C], BF16)
    rw = sb("rw_s", [128, 8 * E], F32)
    gatesT = sb("gatesT", [E, TOK], BF16)
    selb = sb("selb_s", [E, E * 128], BF16)

    esA = ExitStack()
    sbA = lambda name, shape, dt: esA.enter_context(nc.sbuf_tensor(name, shape, dt))
    qpi = sbA("qpi", [128, 512], F32)
    maskadd = sbA("maskadd", [128, 512], F32)
    qT = sbA("qT", [65, TOK], BF16)
    kT = sbA("kT", [65, T], BF16)
    vtm = sbA("vtm", [128, KCH * 65], BF16)
    condt = sbA("condt", [128, 4 * 128], BF16)
    o_tm = sbA("o_tm", [128, NCH * HS], F32)
    mA = sbA("mA", [128, 1], F32)
    mB = sbA("mB", [128, 1], F32)
    dm = sbA("dm", [128, 1], F32)
    edm = sbA("edm", [128, 1], F32)
    denA = sbA("denA", [128, 1], F32)
    denB = sbA("denB", [128, 1], F32)
    negm32 = sbA("negm32", [128, 1], F32)
    lden = sbA("lden", [128, 1], F32)
    bias2 = sbA("bias2", [128, 1], F32)
    bsum = [sbA("bsum0", [128, 1], F32), sbA("bsum1", [128, 1], F32)]

    esS = ExitStack()
    mks = lambda name: Sem(nc, esS, name)
    sC1 = mks("sC1")
    sC2 = mks("sC2")
    sTAP = mks("sTAP")
    sPL = mks("sPL")
    sPE = mks("sPE")
    sAC = mks("sAC")
    sDV = mks("sDV")
    sBAR = mks("sBAR")
    dXB = DmaSlots(nc, esS, "dXB")
    dZC = DmaSlots(nc, esS, "dZC", n=3)
    dSPp = DmaSlots(nc, esS, "dSPp", n=3)
    dW1 = DmaSlots(nc, esS, "dW1", n=4)
    dW2 = DmaSlots(nc, esS, "dW2", n=4)
    dYC = DmaSlots(nc, esS, "dYC", n=3)
    dSTp = DmaSlots(nc, esS, "dSTp", n=3)
    dFD = DmaSlots(nc, esS, "dFD")
    dSPb = DmaSlots(nc, esS, "dSPb", n=3)
    sGC = mks("sGC")
    sSCT = mks("sSCT")
    sSCTp = [mks("sSCTp0"), mks("sSCTp1")]
    sSCA = mks("sSCA")
    dPF = DmaSlots(nc, esS, "dPF", n=1)
    dIX = DmaSlots(nc, esS, "dIX", n=2)
    dGA = DmaSlots(nc, esS, "dGA", n=2)
    dGV = DmaSlots(nc, esS, "dGV", n=2)

    c_inv1024 = cst[:, C_INV1024:C_INV1024 + 1]
    c_neg132 = cst[:, C_NEG132:C_NEG132 + 1]
    c_one = cst[:, C_ONE:C_ONE + 1]
    c_neg1e30 = cst[:, C_NEG1E30:C_NEG1E30 + 1]
    c_zero = cst[:, C_ZERO:C_ZERO + 1]
    c_neg1 = cst[:, C_NEG1:C_NEG1 + 1]
    c_big = cst[:, C_BIG:C_BIG + 1]
    c_eight = cst[:, C_EIGHT:C_EIGHT + 1]
    c_cap = cst[:, C_CAP:C_CAP + 1]

    regB_cap = nc.gpsimd.alloc_register("bnd_cap")
    regB_tok = nc.gpsimd.alloc_register("bnd_tok")
    regB_gc = nc.gpsimd.alloc_register("bnd_gc")
    regB_all = nc.gpsimd.alloc_register("bnd_all")

    barrier_count = [0]

    def barrier(P):
        barrier_count[0] += 1
        for eng in ENGINES:
            P.op(eng, lambda Eo: Eo.nop(), inc=sBAR)
        for eng in ENGINES:
            P.wait(eng, sBAR, 5 * barrier_count[0])

    es1 = ExitStack()
    sb1 = lambda name, shape, dt: es1.enter_context(nc.sbuf_tensor(name, shape, dt))
    ps1 = lambda name, shape, dt: es1.enter_context(nc.psum_tensor(name, shape, dt))
    xb = [sb1("xb0", [128, C], BF16), sb1("xb1", [128, C], BF16)]
    xT = [sb1("xT0", [128, C], BF16), sb1("xT1", [128, C], BF16)]
    wqkv = sb1("wqkv", [128, 8 * 192], BF16)
    bvr = sb1("bvr", [1, HS], BF16)
    qkb = sb1("qkb_s", [128, 2], F32)
    pXT = [ps1("pXT0", [128, C], BF16), ps1("pXT1", [128, C], BF16)]
    pQK = [ps1("pQK0", [64, 128], F32), ps1("pQK1", [64, 128], F32)]
    pV = [ps1("pV0", [128, HS], F32), ps1("pV1", [128, HS], F32)]

    P = Prog()
    for dst, src in [(idb, d_idb), (idf, d_idf), (qpi, d_qpi), (qoffb, d_qoff),
                     (mth, d_mth), (tokid, d_tokid), (evals, d_evals),
                     (b2ts, d_b2t.rearrange("p a b -> p (a b)")),
                     (cst, d_cst), (qkb[:], d_qkb),
                     (b1t, d_b1t.rearrange("p e k -> p (e k)")),
                     (rw, d_rw.rearrange("p c e -> p (c e)"))]:
        ap = dst[:] if hasattr(dst, "ap") else dst
        P.op("sync", lambda Eo, a=ap, s=src: Eo.dma_start(out=a, in_=s), inc=sC1, inc_n=16)
    for dst, src in [(onesb[:], d_ones), (selb[:], d_sel),
                     (wqkv[:], d_wqkv.rearrange("p c h -> p (c h)")),
                     (bvr[:], d_bvr), (b2b[:], d_b2w)]:
        P.op("gpsimd", lambda Eo, a=dst, s=src: Eo.dma_start(out=a, in_=s), inc=sC2, inc_n=16)
    ld_consts = sC1.n
    wl_consts = sC2.n
    for _eng in ("tensor", "scalar", "vector", "gpsimd"):
        P.wait(_eng, sC1, ld_consts)
        P.wait(_eng, sC2, wl_consts)
    P.op("gpsimd", lambda Eo: Eo.reg_mov(regB_cap, CAP - 1))
    P.op("gpsimd", lambda Eo: Eo.reg_mov(regB_tok, TOK - 1))
    P.op("gpsimd", lambda Eo: Eo.reg_mov(regB_gc, TOK * E - 1))
    P.op("gpsimd", lambda Eo: Eo.reg_mov(regB_all, E * CAP - 1))
    P.op("gpsimd", lambda Eo: Eo.memset(kT[64:65, :], 1.0), inc=sPL)
    P.op("gpsimd", lambda Eo: Eo.memset(vtm[:], 1.0), inc=sPL)
    P.wait("vector", sC1, ld_consts)
    P.eop("vector", lambda Eo: Eo.tensor_scalar(maskadd[0:128, 0:128], qpi[:, 0:128], c_neg1, None, op0=ALU.mult), sDV)
    for _t in range(4):
        P.eop("vector", lambda Eo, t=_t: Eo.tensor_scalar(
            condt[:, t * 128:(t + 1) * 128], maskadd[0:128, 0:128],
            mth[:, t:t + 1], None, op0=ALU.is_ge), sDV)
    P.eop("vector", lambda Eo: Eo.tensor_scalar(maskadd[:], qpi[:], qoffb[:, 0:1], None, op0=ALU.add), sDV)
    P.eop("vector", lambda Eo: Eo.tensor_scalar(maskadd[:], maskadd[:], c_zero, None, op0=ALU.is_lt), sDV)
    P.op("vector", lambda Eo: Eo.tensor_scalar(maskadd[:], maskadd[:], c_neg1e30, None, op0=ALU.mult), inc=sDV)

    rXB = Rot(2)
    rXT = Rot(2)
    rQK = Rot(2)
    rV = Rot(2)

    rPXT = Rot(2)

    def chunk_setup(i, src_ap, is_q, kc):
        pi = i % 2
        rXB.acquire(P, "gpsimd", i)
        dXB.load(P, "gpsimd", i, lambda Eo, a=xb[pi][:], s=src_ap: Eo.dma_start(out=a, in_=s))
        dXB.wait(P, "tensor", i)
        rPXT.acquire(P, "tensor", i)
        rXT_psum = pXT[pi]
        for dc in range(8):
            P.op("tensor", lambda Eo, o=rXT_psum[:, dc * 128:(dc + 1) * 128],
                 a=xb[pi][:, dc * 128:(dc + 1) * 128]: Eo.transpose(o, a, idb[:]),
                 inc=(sPE if dc == 7 else None))
        transposes_done = sPE.n
        P.wait("vector", sPE)
        rXT.acquire(P, "vector", i)
        P.op("vector", lambda Eo, o=xT[pi][:], a=rXT_psum[:]: Eo.tensor_copy(o, a), inc=sDV)
        rPXT.release(i, sDV)
        evac_cnt = sDV.n
        P.wait("tensor", sDV, evac_cnt)
        if is_q:
            rQK.acquire(P, "tensor", i)
            for dc in range(8):
                P.op("tensor", lambda Eo, p=pQK[pi][:],
                     w=wqkv[:, dc * 192:dc * 192 + 64],
                     x=xT[pi][:, dc * 128:(dc + 1) * 128], st=(dc == 0), sp=(dc == 7):
                     Eo.matmul(p, w, x, start=st, stop=sp),
                     inc=(sPE if dc == 7 else None))
            P.wait("scalar", sPE)
            P.op("scalar", lambda Eo, o=qT[0:64, i * 128:(i + 1) * 128], p=pQK[pi][:]:
                 Eo.activation(o, p, AF.Identity, bias=qkb[0:64, 0:1], scale=f32(1.0)),
                 inc=sAC)
            rQK.release(i, sAC)
        else:
            rQK.acquire(P, "tensor", i)
            for dc in range(8):
                P.op("tensor", lambda Eo, p=pQK[pi][:],
                     w=wqkv[:, dc * 192 + 64:dc * 192 + 128],
                     x=xT[pi][:, dc * 128:(dc + 1) * 128], st=(dc == 0), sp=(dc == 7):
                     Eo.matmul(p, w, x, start=st, stop=sp),
                     inc=(sPE if dc == 7 else None))
            P.wait("scalar", sPE)
            P.op("scalar", lambda Eo, o=kT[0:64, kc * 128:(kc + 1) * 128], p=pQK[pi][:]:
                 Eo.activation(o, p, AF.Identity, bias=qkb[0:64, 1:2], scale=f32(1.0)),
                 inc=sAC)
            rQK.release(i, sAC)
            rV.acquire(P, "tensor", i)
            for dc in range(8):
                P.op("tensor", lambda Eo, p=pV[pi][:],
                     x=xT[pi][:, dc * 128:(dc + 1) * 128],
                     w=wqkv[:, dc * 192 + 128:dc * 192 + 192], st=(dc == 0):
                     Eo.matmul(p, x, w, start=st, stop=False))
            P.op("tensor", lambda Eo, p=pV[pi][:]:
                 Eo.matmul(p, onesb[:], bvr[:], start=False, stop=True), inc=sPE)
            P.wait("vector", sPE)
            P.wait("vector", sPL, 2)
            P.op("vector", lambda Eo, o=vtm[:, kc * 65:kc * 65 + HS], p=pV[pi][:]:
                 Eo.tensor_copy(o, p), inc=sDV)
            rV.release(i, sDV)
        rXB.rel[pi] = [(sPE, transposes_done)]
        rXT.rel[pi] = [(sPE, sPE.n)]

    for i in range(NCH):
        chunk_setup(i, d_xown[i * 128:(i + 1) * 128, :], True, None)
    for kc in range(KCH):
        chunk_setup(NCH + kc, d_xbat[kc * 128:(kc + 1) * 128, :], False, kc)

    P.wait("sync", sTAP)
    barrier(P)
    with nc.Block("b1") as blk:
        P.run(blk)
    es1.close()

    es2 = ExitStack()
    sb2 = lambda name, shape, dt: es2.enter_context(nc.sbuf_tensor(name, shape, dt))
    ps2 = lambda name, shape, dt: es2.enter_context(nc.psum_tensor(name, shape, dt))
    PT4 = [sb2("PT40", [128, 512], BF16), sb2("PT41", [128, 512], BF16)]
    ots = sb2("ots", [65, 128], F32)
    mtab = [sb2("mtab0", [128, NCH], F32), sb2("mtab1", [128, NCH], F32)]
    mfin = sb2("mfin", [128, 1], F32)
    foldv = [sb2("foldv0", [128, 1], F32), sb2("foldv1", [128, 1], F32)]
    dtmp = sb2("dtmp", [128, 1], F32)
    rcp = sb2("rcp", [128, 1], F32)
    pSA = [ps2("pSA0", [128, 512], F32), ps2("pSA1", [128, 512], F32)]
    pSk = [ps2("pSk0", [128, 512], F32), ps2("pSk1", [128, 512], F32)]
    pO = ps2("pO", [65, 128], F32)
    pOt = ps2("pOt", [128, 65], F32)

    P = Prog()
    for _eng in ENGINES:
        P.wait(_eng, sC1, ld_consts)
        P.wait(_eng, sC2, wl_consts)
    rSA = Rot(2)
    rSK = Rot(2)
    rPT4 = Rot(2)
    rPO = Rot(1)
    rMT = Rot(2)
    fold_done = {}

    def pass_a(j):
        par2 = j % 2
        rMT.acquire(P, "vector", j)
        for kb in range(j + 1):
            par = kb % 2
            rSA.acquire(P, "tensor", kb)
            P.op("tensor", lambda Eo, p=pSA[par][:],
                 q=qT[0:64, j * 128:(j + 1) * 128],
                 k=kT[0:64, kb * 512:(kb + 1) * 512]:
                 Eo.matmul(p, q, k, start=True, stop=True), inc=sPE)
            P.wait("vector", sPE)
            if kb == j:
                P.eop("vector", lambda Eo, p=pSA[par][:]:
                      Eo.tensor_add(p, p, maskadd[:]), sDV)
            P.op("vector", lambda Eo, o=mtab[par2][:, kb:kb + 1], p=pSA[par][:]:
                 Eo.reduce_max(o, p, axis=AX.X), inc=sDV)
            rSA.release(kb, sDV)
        P.wait("vector", sDV)
        P.eop("vector", lambda Eo, mt=mtab[par2][:, 0:j + 1]:
              Eo.reduce_max(mfin[:], mt, axis=AX.X), sDV)
        dFD.wait(P, "vector", j)
        P.eop("vector", lambda Eo, fv=foldv[j % 2][:]:
              Eo.tensor_scalar(fv, mfin[:], c_neg1, None, op0=ALU.mult), sDV)
        dFD.wait(P, "gpsimd", j)
        P.wait("gpsimd", sDV)
        dFD.load(P, "gpsimd", j, lambda Eo, o=qT[64:65, j * 128:(j + 1) * 128],
                 fv=foldv[j % 2][:]: Eo.dma_start(out=o, in_=fv))
        fold_done[j] = None

    def pass_b(j):
        par2 = j % 2
        ngrp = j + 1
        dFD.wait(P, "tensor", j)

        def b_mms(kg):
            par = kg % 2
            rSK.acquire(P, "tensor", kg)
            for t in range(4):
                kk = 4 * kg + t
                P.op("tensor", lambda Eo, p=pSk[par][:, t * 128:(t + 1) * 128],
                     k=kT[:, kk * 128:(kk + 1) * 128],
                     q=qT[:, j * 128:(j + 1) * 128]:
                     Eo.matmul(p, k, q, start=True, stop=True),
                     inc=(sPE if t == 3 else None))
            return sPE.n

        mm_done = {0: b_mms(0)}
        for kg in range(ngrp):
            par = kg % 2
            if kg + 1 < ngrp:
                mm_done[kg + 1] = b_mms(kg + 1)
            P.wait("scalar", sPE, mm_done[kg])
            rPT4.acquire(P, "scalar", kg)
            P.op("scalar", lambda Eo, o=PT4[par][:], p=pSk[par][:]:
                 Eo.activation(o, p, AF.Exp, scale=f32(1.0 / 32.0)), inc=sAC)
            rSK.release(kg, sAC)
            if kg == j:
                P.wait("vector", sAC)
                for t in range(4):
                    P.op("vector", lambda Eo, t=t, o=PT4[par][:, t * 128:(t + 1) * 128],
                         c=condt[:, t * 128:(t + 1) * 128]:
                         Eo.tensor_tensor(o, o, c, op=ALU.mult),
                         inc=(sDV if t == 3 else None))
                P.wait("tensor", sDV)
            else:
                P.wait("tensor", sAC)
            if kg == 0:
                rPO.acquire(P, "tensor", 0)
            for t in range(4):
                kk = 4 * kg + t
                P.op("tensor", lambda Eo, v=vtm[:, kk * 65:(kk + 1) * 65],
                     pp=PT4[par][:, t * 128:(t + 1) * 128],
                     st=(kg == 0 and t == 0), sp=(kg == j and t == 3):
                     Eo.matmul(pO[:], v, pp, start=st, stop=sp),
                     inc=(sPE if t == 3 else None))
            rPT4.release(kg, sPE)
        P.wait("scalar", sPE)
        P.op("scalar", lambda Eo: Eo.activation(ots[:], pO[:], AF.Copy), inc=sAC)
        rPO.rel[0] = [(sAC, sAC.n)]
        P.wait("tensor", sAC)
        P.op("tensor", lambda Eo: Eo.transpose(pOt[:], ots[:], idf[0:65, 0:65]), inc=sPE)
        P.wait("vector", sPE)
        P.eop("vector", lambda Eo: Eo.tensor_copy(dtmp[:], pOt[:, 64:65]), sDV)
        P.eop("vector", lambda Eo: Eo.reciprocal(rcp[:], dtmp[:]), sDV)
        P.op("vector", lambda Eo, o=o_tm[:, j * HS:(j + 1) * HS]:
             Eo.tensor_scalar(o, pOt[:, 0:HS], rcp[:, 0:1], None, op0=ALU.mult), inc=sDV)
        P.wait("tensor", sDV)
        P.wait("scalar", sDV)

    for j in range(NCH):
        pass_a(j)
        if j > 0:
            pass_b(j - 1)
    pass_b(NCH - 1)

    if "o_tm" in taps:
        P.wait("sync", sDV)
        P.op("sync", lambda Eo: Eo.dma_start(out=taps["o_tm"], in_=o_tm[:]),
             inc=sTAP, inc_n=16)
    if "condt" in taps:
        P.wait("gpsimd", sDV)
        P.op("gpsimd", lambda Eo: Eo.dma_start(out=taps["condt"], in_=condt[:]),
             inc=sTAP, inc_n=16)
    if "vtm" in taps:
        P.wait("gpsimd", sDV)
        P.op("gpsimd", lambda Eo: Eo.dma_start(out=taps["vtm"], in_=vtm[:]),
             inc=sTAP, inc_n=16)
    if "qTe" in taps:
        P.wait("gpsimd", sDV)
        P.op("gpsimd", lambda Eo: Eo.dma_start(out=taps["qTe"], in_=qT[:]),
             inc=sTAP, inc_n=16)
    if "qT" in taps:
        P.wait("gpsimd", sDV)
        P.op("gpsimd", lambda Eo: Eo.dma_start(out=taps["qT"], in_=qT[:]),
             inc=sTAP, inc_n=16)
    if "kT" in taps:
        P.wait("gpsimd", sDV)
        P.op("gpsimd", lambda Eo: Eo.dma_start(out=taps["kT"], in_=kT[:]),
             inc=sTAP, inc_n=16)
    if "vtm" in taps:
        P.wait("gpsimd", sDV)
        P.op("gpsimd", lambda Eo: Eo.dma_start(out=taps["vtm"], in_=vtm[:]),
             inc=sTAP, inc_n=16)
    P.wait("sync", sTAP)
    P.wait("gpsimd", sTAP)
    barrier(P)
    with nc.Block("b2") as blk:
        P.run(blk)
    es2.close()

    es3 = ExitStack()
    sb3 = lambda name, shape, dt: es3.enter_context(nc.sbuf_tensor(name, shape, dt))
    ps3 = lambda name, shape, dt: es3.enter_context(nc.psum_tensor(name, shape, dt))
    zch = [sb3("zch%d" % i, [128, C], F32) for i in range(NCH)]
    sqb = [sb3("sqb%d" % i, [128, C], F32) for i in range(3)]
    tmp4 = [sb3("tmp4%d" % i, [128, 512], F32) for i in range(3)]
    logits = sb3("logits", [E, TOK], F32)
    gates_tm = sb3("gates_tm", [128, 128], F32)
    s1c = sb3("s1c", [128, NCH], F32)
    s2c = sb3("s2c", [128, NCH], F32)
    mu_a = sb3("mu_a", [128, NCH], F32)
    musq = sb3("musq", [128, NCH], F32)
    var_a = sb3("var_a", [128, NCH], F32)
    lnv_a = sb3("lnv_a", [128, NCH], F32)
    rstd_a = sb3("rstd_a", [128, NCH], F32)
    gtm_all = sb3("gtm_all", [128, NCH * 8], F32)
    eq1_all = sb3("eq1_all", [128, NCH * 8], F32)
    eq2_all = sb3("eq2_all", [128, NCH * 8], F32)
    mskl_all = sb3("mskl_all", [128, NCH * 8], F32)
    tA_all = sb3("tA_all", [128, NCH * 8], F32)
    tB_all = sb3("tB_all", [128, NCH * 8], F32)
    slots_all = sb3("slots_all", [128, NCH * 8], F32)
    selw_all = sb3("selw_all", [128, NCH * 8], F32)
    m1_all = sb3("m1_all", [128, NCH], F32)
    m2_all = sb3("m2_all", [128, NCH], F32)
    edv_all = sb3("edv_all", [128, NCH], F32)
    edv2_all = sb3("edv2_all", [128, NCH], F32)
    s1_all = sb3("s1_all", [128, NCH], F32)
    rv_all = sb3("rv_all", [128, NCH], F32)
    w2g_all = sb3("w2g_all", [128, NCH], F32)
    idx1_all = sb3("idx1_all", [128, NCH], F32)
    idx2_all = sb3("idx2_all", [128, NCH], F32)
    idxi1 = sb3("idxi1", [128, NCH], mybir.dt.int32)
    idxi2 = sb3("idxi2", [128, NCH], mybir.dt.int32)
    y1bch = [sb3("y1bch%d" % i, [128, C], BF16) for i in range(3)]
    ones8 = sb3("ones8", [E, TOK], F32)
    mask_fm = sb3("mask_fm", [E, TOK], F32)
    pos_fm = sb3("pos_fm", [E, TOK], F32)
    w1m_fm = sb3("w1m_fm", [E, TOK], F32)
    slot_fm = sb3("slot_fm", [E, TOK], F32)
    ecap = sb3("ecap", [128, 8], F32)
    pft = sb3("pft", [E, CAP], F32)
    pTf = [ps3("pTf%d" % i, [128, 512], F32) for i in range(3)]
    pLr = [ps3("pLr0", [E, 128], F32), ps3("pLr1", [E, 128], F32)]
    pG8 = ps3("pG8", [128, 16], F32)
    pTg = ps3("pTg", [E, 256], F32)

    P = Prog()
    for _eng in ENGINES:
        P.wait(_eng, sC1, ld_consts)
        P.wait(_eng, sC2, wl_consts)
    rSQ = Rot(3)
    rTF = Rot(3)
    rT4 = Rot(3)
    rPL = Rot(2)

    sq_done = {}

    def emit_red2(tc):
        P.wait("vector", sAC, sq_done[tc])
        P.op("vector", lambda Eo, q=sqb[tc % 3][:], o=s2c[:, tc:tc + 1]:
             Eo.reduce_sum(o, q, axis=AX.X), inc=sDV)
        rSQ.rel[tc % 3] = [(sDV, sDV.n)]

    for tc in range(NCH):
        z = zch[tc]
        dZC.load(P, "sync", tc, lambda Eo, o=z[:], s=d_xown[tc * 128:(tc + 1) * 128, :]:
                 Eo.dma_start(out=o, in_=s))
        dZC.wait(P, "gpsimd", tc)
        orep = o_tm[:, tc * HS:(tc + 1) * HS].unsqueeze(1).broadcast_to((128, 16, HS))
        P.op("gpsimd", lambda Eo, z_=z[:], r=orep:
             Eo.tensor_add(z_.rearrange("p (r h) -> p r h", r=16), z_.rearrange("p (r h) -> p r h", r=16), r), inc=sPL)
        P.wait("vector", sPL)
        P.op("vector", lambda Eo, z_=z[:], m=s1c[:, tc:tc + 1]:
             Eo.reduce_sum(m, z_, axis=AX.X), inc=sDV)
        P.wait("scalar", sPL)
        rSQ.acquire(P, "scalar", tc)
        P.op("scalar", lambda Eo, z_=z[:], q=sqb[tc % 3][:]:
             Eo.activation(q, z_, AF.Square), inc=sAC)
        sq_done[tc] = sAC.n
        if tc >= 1:
            emit_red2(tc - 1)
    emit_red2(NCH - 1)
    P.wait("vector", sDV)
    P.eop("vector", lambda Eo: Eo.tensor_scalar(mu_a[:], s1c[:], c_inv1024, None, op0=ALU.mult), sDV)
    P.eop("vector", lambda Eo: Eo.tensor_mul(musq[:], mu_a[:], mu_a[:]), sDV)
    P.eop("vector", lambda Eo: Eo.scalar_tensor_tensor(var_a[:], s2c[:], c_inv1024, musq[:],
                                                       op0=ALU.mult, op1=ALU.subtract), sDV)
    P.wait("scalar", sDV)
    P.eop("scalar", lambda Eo: Eo.activation(lnv_a[:], var_a[:], AF.Ln, bias=f32(EPS)), sAC)
    P.op("scalar", lambda Eo: Eo.activation(rstd_a[:], lnv_a[:], AF.Exp, scale=f32(-0.5)), inc=sAC)
    P.wait("vector", sAC)
    for tc in range(NCH):
        z = zch[tc]
        P.op("vector", lambda Eo, z_=z[:], m=mu_a[:, tc:tc + 1], r=rstd_a[:, tc:tc + 1]:
             Eo.tensor_scalar(z_, z_, m, r, op0=ALU.subtract, op1=ALU.mult), inc=sDV)
        y1_done = sDV.n
        P.wait("sync", sDV, y1_done)
        dSPp.load(P, "sync", tc, lambda Eo, o=d_y1sp[tc * 128:(tc + 1) * 128, :], z_=z[:]:
                  Eo.dma_start(out=o, in_=z_))
        P.wait("gpsimd", sDV, y1_done)
        dSPb.wait(P, "gpsimd", tc)
        P.op("gpsimd", lambda Eo, o=y1bch[tc % 3][:], z_=z[:]: Eo.tensor_copy(o, z_), inc=sPL)
        P.wait("sync", sPL)
        dSPb.load(P, "sync", tc, lambda Eo, o=d_y1bf[tc * 128:(tc + 1) * 128, 0:C],
                  a=y1bch[tc % 3][:]: Eo.dma_start(out=o, in_=a))
        P.wait("tensor", sDV, y1_done)
        for g in range(2):
            tfp = pTf[(tc * 2 + g) % 3]
            rTF.acquire(P, "tensor", tc * 2 + g)
            for t in range(4):
                dc = g * 4 + t
                P.op("tensor", lambda Eo, o=tfp[:, t * 128:(t + 1) * 128],
                     a=z[:, dc * 128:(dc + 1) * 128]: Eo.transpose(o, a, idf[:]),
                     inc=(sPE if t == 3 else None))
            P.wait("scalar", sPE)
            rT4.acquire(P, "scalar", tc * 2 + g)
            tm4 = tmp4[(tc * 2 + g) % 3]
            P.op("scalar", lambda Eo, o=tm4[:], a=tfp[:]: Eo.activation(o, a, AF.Copy), inc=sAC)
            rTF.release(tc * 2 + g, sAC)
            P.wait("tensor", sAC)
            if g == 0:
                rPL.acquire(P, "tensor", tc)
            for t in range(4):
                dc = g * 4 + t
                P.op("tensor", lambda Eo, p=pLr[tc % 2][:],
                     w=rw[:, dc * 8:(dc + 1) * 8],
                     x=tm4[:, t * 128:(t + 1) * 128], st=(dc == 0), sp=(dc == 7):
                     Eo.matmul(p, w, x, start=st, stop=sp),
                     inc=(sPE if t == 3 else None))
            rT4.release(tc * 2 + g, sPE)
        P.wait("scalar", sPE)
        P.op("scalar", lambda Eo, o=logits[:, tc * 128:(tc + 1) * 128],
             p=pLr[tc % 2][:]: Eo.activation(o, p, AF.Copy), inc=sAC)
        rPL.release(tc, sAC)
    logits_done = sAC.n
    r3 = lambda ap: ap.rearrange("p (t e) -> p t e", e=8)
    rm = lambda ap: ap.rearrange("p (t o) -> p t o", o=1)
    bc = lambda ap: ap.rearrange("p (t o) -> p t o", o=1).broadcast_to((128, NCH, 8))
    cp_done = {}
    for tc in range(NCH):
        sl = tc % 2
        P.wait("tensor", sAC)
        if tc >= 2:
            P.wait("tensor", sDV, cp_done[sl])
        P.op("tensor", lambda Eo, o=pG8[:, sl * 8:(sl + 1) * 8],
             a=logits[:, tc * 128:(tc + 1) * 128]:
             Eo.transpose(o, a, idf[0:8, 0:8]), inc=sPE)
        P.wait("vector", sPE)
        P.op("vector", lambda Eo, o=gtm_all[:, tc * 8:(tc + 1) * 8],
             p=pG8[:, sl * 8:(sl + 1) * 8]: Eo.tensor_copy(o, p), inc=sDV)
        cp_done[sl] = sDV.n
    P.wait("vector", sDV)
    P.eop("vector", lambda Eo: Eo.reduce_max(rm(m1_all[:]), r3(gtm_all[:]), axis=AX.X), sDV)
    P.eop("vector", lambda Eo: Eo.tensor_tensor(r3(eq1_all[:]), r3(gtm_all[:]), bc(m1_all[:]),
                                                op=ALU.is_equal), sDV)
    P.eop("vector", lambda Eo: Eo.scalar_tensor_tensor(mskl_all[:], eq1_all[:], c_neg1e30,
                                                       gtm_all[:], op0=ALU.mult, op1=ALU.add), sDV)
    P.eop("vector", lambda Eo: Eo.reduce_max(rm(m2_all[:]), r3(mskl_all[:]), axis=AX.X), sDV)
    P.eop("vector", lambda Eo: Eo.tensor_tensor(r3(eq2_all[:]), r3(mskl_all[:]), bc(m2_all[:]),
                                                op=ALU.is_equal), sDV)
    P.eop("vector", lambda Eo: Eo.tensor_sub(edv2_all[:], m2_all[:], m1_all[:]), sDV)
    P.wait("scalar", sDV)
    P.eop("scalar", lambda Eo: Eo.activation(edv_all[:], edv2_all[:], AF.Exp), sAC)
    P.wait("vector", sAC)
    P.eop("vector", lambda Eo: Eo.tensor_scalar(s1_all[:], edv_all[:], c_one, None, op0=ALU.add), sDV)
    P.eop("vector", lambda Eo: Eo.reciprocal(rv_all[:], s1_all[:]), sDV)
    P.eop("vector", lambda Eo: Eo.tensor_mul(w2g_all[:], edv_all[:], rv_all[:]), sDV)
    P.eop("vector", lambda Eo: Eo.tensor_tensor(r3(tA_all[:]), r3(eq1_all[:]), bc(rv_all[:]),
                                                op=ALU.mult), sDV)
    P.eop("vector", lambda Eo: Eo.tensor_tensor(r3(tB_all[:]), r3(eq2_all[:]), bc(w2g_all[:]),
                                                op=ALU.mult), sDV)
    P.eop("vector", lambda Eo: Eo.tensor_add(gates_tm[:], tA_all[:], tB_all[:]), sDV)
    P.wait("gpsimd", sDV)
    P.op("gpsimd", lambda Eo, o=d_y1bf.rearrange("(t p) c -> p t c", p=128)[:, :, C:C + 8],
         a=r3(gates_tm[:]): Eo.dma_start(out=o, in_=a), inc=sGC, inc_n=16)
    P.wait("tensor", sDV)
    for tc in range(NCH):
        sl = tc % 2
        P.op("tensor", lambda Eo, o=pTg[:, sl * 128:(sl + 1) * 128],
             a=gates_tm[:, tc * 8:(tc + 1) * 8]:
             Eo.transpose(o, a, idf[:]), inc=sPE)
        P.wait("scalar", sPE)
        P.op("scalar", lambda Eo, o=gatesT[:, tc * 128:(tc + 1) * 128],
             p=pTg[:, sl * 128:(sl + 1) * 128]: Eo.activation(o, p, AF.Copy), inc=sAC)
        P.wait("tensor", sAC)

    P.eop("vector", lambda Eo: Eo.tensor_scalar(ecap[:], evals[:], c_cap, None, op0=ALU.mult), sDV)
    P.op("gpsimd", lambda Eo: Eo.memset(ones8[:], 1.0), inc=sPL)
    P.op("gpsimd", lambda Eo: Eo.memset(pft[:], 2.0e6), inc=sPL)
    P.wait("sync", sPL)
    dPF.load(P, "sync", 0, lambda Eo: Eo.dma_start(out=d_idxf, in_=pft[:]))
    P.wait("vector", sAC)
    P.wait("vector", sPL)
    P.eop("vector", lambda Eo: Eo.tensor_scalar(mask_fm[:], gatesT[:], c_zero[0:E, :], None,
                                                op0=ALU.is_gt), sDV)
    P.eop("vector", lambda Eo: Eo.tensor_tensor_scan(pos_fm[:], ones8[:], mask_fm[:], 0.0,
                                                     op0=ALU.mult, op1=ALU.add), sDV)
    P.eop("vector", lambda Eo: Eo.tensor_scalar(pos_fm[:], pos_fm[:], c_one[0:E, :], None,
                                                op0=ALU.subtract), sDV)
    P.eop("vector", lambda Eo: Eo.tensor_scalar(w1m_fm[:], mask_fm[:], c_neg1[0:E, :],
                                                c_one[0:E, :], op0=ALU.mult, op1=ALU.add), sDV)
    P.eop("vector", lambda Eo: Eo.scalar_tensor_tensor(slot_fm[:], w1m_fm[:], c_big[0:E, :],
                                                       pos_fm[:], op0=ALU.mult, op1=ALU.add), sDV)
    slot_ready = sDV.n
    P.wait("tensor", sDV, slot_ready)
    dPF.wait(P, "gpsimd", 0)
    for tc in range(NCH):
        sl = tc % 2
        if tc >= 2:
            P.wait("tensor", sDV, cp_done[sl])
        P.op("tensor", lambda Eo, o=pG8[:, sl * 8:(sl + 1) * 8],
             a=slot_fm[:, tc * 128:(tc + 1) * 128]:
             Eo.transpose(o, a, idf[0:8, 0:8]), inc=sPE)
        P.wait("vector", sPE)
        P.op("vector", lambda Eo, o=slots_all[:, tc * 8:(tc + 1) * 8],
             p=pG8[:, sl * 8:(sl + 1) * 8]: Eo.tensor_copy(o, p), inc=sDV)
        cp_done[sl] = sDV.n
    P.wait("vector", sDV)
    P.eop("vector", lambda Eo: Eo.tensor_scalar(tA_all[:], slots_all[:], c_cap, None,
                                                op0=ALU.is_ge), sDV)
    P.eop("vector", lambda Eo: Eo.scalar_tensor_tensor(selw_all[:], tA_all[:], c_big,
                                                       slots_all[:], op0=ALU.mult, op1=ALU.add), sDV)
    ecap_b = ecap[:].rearrange("p (o e) -> p o e", o=1).broadcast_to((128, NCH, 8))
    P.eop("vector", lambda Eo: Eo.tensor_tensor(r3(selw_all[:]), r3(selw_all[:]), ecap_b,
                                                op=ALU.add), sDV)
    P.eop("vector", lambda Eo: Eo.reduce_max(rm(m1_all[:]), r3(gates_tm[:]), axis=AX.X), sDV)
    P.eop("vector", lambda Eo: Eo.tensor_tensor(r3(tA_all[:]), r3(gates_tm[:]), bc(m1_all[:]),
                                                op=ALU.is_equal), sDV)
    P.eop("vector", lambda Eo: Eo.tensor_scalar(tB_all[:], gates_tm[:], c_zero, None,
                                                op0=ALU.is_gt), sDV)
    P.eop("vector", lambda Eo: Eo.tensor_sub(tB_all[:], tB_all[:], tA_all[:]), sDV)
    P.eop("vector", lambda Eo: Eo.tensor_mul(tA_all[:], tA_all[:], selw_all[:]), sDV)
    P.eop("vector", lambda Eo: Eo.reduce_sum(rm(idx1_all[:]), r3(tA_all[:]), axis=AX.X), sDV)
    P.eop("vector", lambda Eo: Eo.tensor_mul(tB_all[:], tB_all[:], selw_all[:]), sDV)
    P.eop("vector", lambda Eo: Eo.reduce_sum(rm(idx2_all[:]), r3(tB_all[:]), axis=AX.X), sDV)
    P.eop("vector", lambda Eo: Eo.tensor_copy(idxi1[:], idx1_all[:]), sDV)
    P.eop("vector", lambda Eo: Eo.tensor_copy(idxi2[:], idx2_all[:]), sDV)
    P.wait("gpsimd", sDV)
    P.wait("gpsimd", sGC)
    sct_base = sSCT.n
    kb_ = 0
    for tc in range(NCH):
        for ib in (idxi1, idxi2):
            P.op("gpsimd", lambda Eo, tc=tc, ib=ib:
                 Eo.indirect_dma_start(
                     out=d_idxf.rearrange("e (s o) -> (e s) o", o=1),
                     out_offset=IndirectOffsetOnAxis(ap=ib[:, tc:tc + 1], axis=0),
                     in_=tokid[:, tc:tc + 1], in_offset=None,
                     bounds_check=regB_all, oob_is_err=False),
                 inc=sSCT, inc_n=16)
            kb_ += 1
            if kb_ >= 2:
                P.wait("gpsimd", sSCT, sct_base + 16 * (kb_ - 1))
    P.wait("gpsimd", sSCT)

    if "logits" in taps:
        P.wait("sync", sAC, logits_done)
        P.op("sync", lambda Eo: Eo.dma_start(out=taps["logits"], in_=logits[:]),
             inc=sTAP, inc_n=16)
    if "gates" in taps:
        P.wait("sync", sDV)
        P.op("sync", lambda Eo: Eo.dma_start(out=taps["gates"], in_=gates_tm[:]),
             inc=sTAP, inc_n=16)
    if "gatesT" in taps:
        P.wait("gpsimd", sAC)
        P.op("gpsimd", lambda Eo: Eo.dma_start(out=taps["gatesT"], in_=gatesT[:]),
             inc=sTAP, inc_n=16)
    if "y1" in taps:
        for s_ in dSPp.sems:
            P.wait("sync", s_)
        P.op("sync", lambda Eo: Eo.dma_start(out=taps["y1"], in_=d_y1sp), inc=sTAP, inc_n=16)
    for s_ in dSPp.sems:
        P.wait("sync", s_)
    P.wait("sync", sTAP)
    barrier(P)
    with nc.Block("b3") as blk:
        P.run(blk)
    es3.close()
    esA.close()

    es4 = ExitStack()
    sb4 = lambda name, shape, dt: es4.enter_context(nc.sbuf_tensor(name, shape, dt))
    ps4 = lambda name, shape, dt: es4.enter_context(nc.psum_tensor(name, shape, dt))
    H = sb4("H", [128, 32 * CAP], BF16)
    y1gT = [sb4("y1gT%d" % i, [128, 8 * CAP], BF16) for i in range(2)]
    o_sb = sb4("o_sb", [128, 8 * CAP], F32)
    w1b = [sb4("w1b%d" % i, [128, 8 * 256], BF16) for i in range(4)]
    w2b = [sb4("w2b%d" % i, [128, 16 * 128], BF16) for i in range(4)]
    gbuf = [sb4("gbuf%d" % i, [128, C + 8], BF16) for i in range(NG)]
    scatb = [sb4("scatb%d" % i, [128, C], F32) for i in range(NG)]
    idxf_s = [sb4("idxf_s%d" % i, [128, NG], F32) for i in range(2)]
    idxi_s = [sb4("idxi_s%d" % i, [128, NG], mybir.dt.int32) for i in range(2)]
    gv = [sb4("gv%d" % i, [128, NG], F32) for i in range(2)]
    pH = [ps4("pH%d" % i, [128, 1024], F32) for i in range(2)]
    pTin = ps4("pTin", [128, 512], BF16)
    pTout = ps4("pTout", [128, 512], F32)

    P = Prog()
    w1_view = d_w1
    w2_view = d_w2
    for _eng in ENGINES:
        P.wait(_eng, sC1, ld_consts)
        P.wait(_eng, sC2, wl_consts)
    for _eng in ("gpsimd", "sync"):
        P.wait(_eng, sSCT)
        P.wait(_eng, sGC)
        for s_ in dSPb.sems:
            P.wait(_eng, s_)
        for s_ in dSPp.sems:
            P.wait(_eng, s_)
        P.wait(_eng, dPF.sems[0])
    P.wait("gpsimd", sGC)
    for s_ in dSPb.sems:
        P.wait("gpsimd", s_)
    for _gb in gbuf:
        P.op("gpsimd", lambda Eo, b=_gb: Eo.memset(b[:], 0.0))
    P.op("gpsimd", lambda Eo: Eo.memset(gv[0][:], 0.0))
    P.op("gpsimd", lambda Eo: Eo.memset(gv[1][:], 0.0), inc=sPL)
    P.wait("gpsimd", sPL)
    P.wait("tensor", sPL)

    rW1 = Rot(4)
    rW2 = Rot(4)
    rGB = Rot(NG)
    rIX = Rot(2)
    rGV = Rot(2)
    rSCB = Rot(NG)
    rPH = Rot(2)
    rTIN = Rot(1)
    rTOUT = Rot(1)
    NB = ((0, 512), (512, CAP))

    in_done = {}

    idx_ready = {}

    def emit_meta(e):
        ep = e % 2
        rIX.acquire(P, "sync", e)
        dIX.load(P, "sync", e, lambda Eo, o=idxf_s[ep][:],
                 s=d_idxf[e, :].rearrange("(p g) -> p g", g=NG): Eo.dma_start(out=o, in_=s))
        dIX.wait(P, "vector", e)
        P.eop("vector", lambda Eo, o=idxi_s[ep][:], a=idxf_s[ep][:]: Eo.tensor_copy(o, a), sDV)
        rGV.acquire(P, "vector", e)
        idx_ready[e] = sDV.n

    def emit_gath1(e, g):
        ep = e % 2
        if g == 0:
            P.wait("gpsimd", sDV, idx_ready[e])
            for s_ in dW1.sems:
                P.wait("gpsimd", s_)
            for s_ in dW2.sems:
                P.wait("gpsimd", s_)
        rGB.acquire(P, "gpsimd", g)
        dGA.load(P, "gpsimd", e * NG + g, lambda Eo, g=g, ep=ep:
                 Eo.indirect_dma_start(
                     out=gbuf[g][:], out_offset=None,
                     in_=d_y1bf, in_offset=IndirectOffsetOnAxis(ap=idxi_s[ep][:, g:g + 1], axis=0),
                     bounds_check=regB_tok, oob_is_err=False))
        if g >= 1:
            dGA.wait(P, "gpsimd", e * NG + g - 1)

    def emit_gvx(e):
        ep = e % 2
        for g in range(NG):
            dGA.wait(P, "vector", e * NG + g)
            P.eop("vector", lambda Eo, g=g, ep=ep, e=e, o=gv[ep][:, g:g + 1]:
                  Eo.tensor_copy(o, gbuf[g][:, C + e:C + e + 1]), sDV)

    def emit_int(e):
        ep = e % 2
        for g in range(NG):
            dGA.wait(P, "tensor", e * NG + g)
            rTIN.acquire(P, "tensor", 0)
            for kc in range(8):
                P.op("tensor", lambda Eo, kc=kc, g=g,
                     o=pTin[:, (kc % 4) * 128:(kc % 4 + 1) * 128],
                     a=gbuf[g][:, kc * 128:(kc + 1) * 128]:
                     Eo.transpose(o, a, idb[:]),
                     inc=(sPE if kc % 4 == 3 else None))
                if kc % 4 == 3:
                    P.wait("vector", sPE)
                    kc0 = kc - 3
                    P.op("vector", lambda Eo, kc0=kc0, g=g, o4=[
                         y1gT[ep][:, (kc0 + i) * CAP + g * 128:(kc0 + i) * CAP + (g + 1) * 128]
                         for i in range(4)], pt=pTin[:]:
                         [Eo.tensor_copy(o4[i], pt[:, i * 128:(i + 1) * 128]) for i in range(4)][-1],
                         inc=sDV)
                    if kc == 3:
                        rTIN.rel[0] = [(sDV, sDV.n)]
                        rTIN.acquire(P, "tensor", 0)
                    else:
                        rTIN.rel[0] = [(sDV, sDV.n)]
            rGB.release(g, sPE)
            rGB.release(g, sDV)
        in_done[e] = sDV.n

    PIPE_META = True
    prefetched = set()

    def load_w1(e, fp):
        wpar = (e * 16 + fp) % 4
        rW1.acquire(P, "gpsimd", e * 16 + fp)
        srcw = w1_view[e, :, fp * 256:(fp + 1) * 256].rearrange("(kc p) f -> p kc f", p=128)
        dW1.load(P, "gpsimd", e * 16 + fp,
                 lambda Eo, o=w1b[wpar][:].rearrange("p (kc f) -> p kc f", kc=8),
                 s=srcw: Eo.dma_start(out=o, in_=s))

    emit_meta(0)
    for _g in range(NG):
        emit_gath1(0, _g)
    dGA.wait(P, "gpsimd", NG - 1)
    emit_gvx(0)
    emit_int(0)
    for e in range(n_experts):
        ep = e % 2
        if e > 0:
            emit_int(e)
        P.wait("tensor", sDV, in_done[e])
        for fp in range(16):
            wpar = (e * 16 + fp) % 4
            if (e, fp) not in prefetched:
                load_w1(e, fp)
            dW1.wait(P, "tensor", e * 16 + fp)
            for fc in range(2):
                fcg = fp * 2 + fc
                hpar = fcg % 2
                rPH.acquire(P, "tensor", fcg)
                for kc in range(8):
                    for nbi, (lo, hi) in enumerate(NB):
                        P.op("tensor", lambda Eo, p=pH[hpar][:, lo:hi],
                             w=w1b[wpar][:, kc * 256 + fc * 128:kc * 256 + (fc + 1) * 128],
                             x=y1gT[ep][:, kc * CAP + lo:kc * CAP + hi],
                             st=(kc == 0), sp=(kc == 7):
                             Eo.matmul(p, w, x, start=st, stop=sp),
                             inc=(sPE if (nbi == 1 and kc == 7) else None))
                P.wait("scalar", sPE)
                hsl = H[:, fcg * CAP:(fcg + 1) * CAP]
                P.op("scalar", lambda Eo, o=hsl, p=pH[hpar][:, 0:CAP],
                     b=b1t[:, e * 32 + fcg:e * 32 + fcg + 1]:
                     Eo.activation(o, p, AF.Gelu_apprx_tanh, bias=b, scale=f32(1.0)),
                     inc=sAC)
                rPH.release(fcg, sAC)
            rW1.release(e * 16 + fp, sPE)
        h_done = sAC.n
        if PIPE_META and e + 1 < n_experts:
            emit_meta(e + 1)
        P.wait("tensor", sAC, h_done)
        for dc in range(8):
            opar = dc % 2
            if dc == 4 and e + 1 < n_experts:
                for g in range(NG):
                    emit_gath1(e + 1, g)
                dGA.wait(P, "gpsimd", (e + 1) * NG + NG - 1)
                emit_gvx(e + 1)
                for fpp in range(4):
                    load_w1(e + 1, fpp)
                    prefetched.add((e + 1, fpp))
            for fh in range(2):
                w2i = (e * 16 + dc * 2 + fh) % 4
                rW2.acquire(P, "gpsimd", e * 16 + dc * 2 + fh)
                srcw = w2_view[e, fh * 2048:(fh + 1) * 2048,
                               dc * 128:(dc + 1) * 128].rearrange("(fc p) d -> p fc d", p=128)
                dW2.load(P, "gpsimd", e * 16 + dc * 2 + fh,
                         lambda Eo, o=w2b[w2i][:].rearrange("p (fc d) -> p fc d", fc=16),
                         s=srcw: Eo.dma_start(out=o, in_=s))
                dW2.wait(P, "tensor", e * 16 + dc * 2 + fh)
                if fh == 0:
                    rPH.acquire(P, "tensor", dc)
                for fcL in range(16):
                    fcg = fh * 16 + fcL
                    for nbi, (lo, hi) in enumerate(NB):
                        P.op("tensor", lambda Eo, p=pH[opar][:, lo:hi],
                             w=w2b[w2i][:, fcL * 128:(fcL + 1) * 128],
                             x=H[:, fcg * CAP + lo:fcg * CAP + hi],
                             st=(fh == 0 and fcL == 0), sp=(fh == 1 and fcL == 15):
                             Eo.matmul(p, w, x, start=st, stop=sp),
                             inc=(sPE if (nbi == 1 and fcL == 15) else None))
                rW2.release(e * 16 + dc * 2 + fh, sPE)
            P.wait("vector", sPE)
            P.op("vector", lambda Eo, o=o_sb[:, dc * CAP:(dc + 1) * CAP],
                 p=pH[opar][:, 0:CAP], b=b2ts[:, dc * 8 + e:dc * 8 + e + 1]:
                 Eo.tensor_scalar(o, p, b, None, op0=ALU.add), inc=sDV)
            rPH.release(dc, sDV)
        o_done = sDV.n
        P.wait("tensor", sDV, o_done)
        for g in range(NG):
            sbi = g
            rSCB.acquire(P, "tensor", e * NG + g)
            rTOUT.acquire(P, "tensor", 0)
            for dc in range(8):
                P.op("tensor", lambda Eo, dc=dc, g=g,
                     o=pTout[:, (dc % 4) * 128:(dc % 4 + 1) * 128],
                     a=o_sb[:, dc * CAP + g * 128:dc * CAP + (g + 1) * 128]:
                     Eo.transpose(o, a, idf[:]),
                     inc=(sPE if dc % 4 == 3 else None))
                if dc % 4 == 3:
                    P.wait("vector", sPE)
                    dc0 = dc - 3
                    for i in range(4):
                        P.op("vector", lambda Eo, i=i, dc0=dc0, g=g, sbi=sbi,
                             o=scatb[sbi][:, (dc0 + i) * 128:(dc0 + i + 1) * 128],
                             p=pTout[:, i * 128:(i + 1) * 128],
                             gvc=gv[ep][:, g:g + 1]:
                             Eo.tensor_scalar(o, p, gvc, None, op0=ALU.mult),
                             inc=(sDV if i == 3 else None))
                    rTOUT.rel[0] = [(sDV, sDV.n)]
                    if dc == 3:
                        rTOUT.acquire(P, "tensor", 0)
            P.wait("gpsimd", sDV)
            P.wait("gpsimd", sSCA)
            P.op("gpsimd", lambda Eo, g=g, ep=ep, sbi=sbi:
                 Eo.indirect_dma_start(
                     out=d_y1sp, out_offset=IndirectOffsetOnAxis(ap=idxi_s[ep][:, g:g + 1], axis=0),
                     in_=scatb[sbi][:], in_offset=None,
                     bounds_check=regB_tok, oob_is_err=False,
                     compute_op=ALU.add),
                 inc=sSCA, inc_n=16)
            rSCB.release(e * NG + g, sSCA)
        rIX.release(e, sSCA)
        rGV.release(e, sSCA)

    if "O_acc" in taps:
        P.wait("sync", sSCA)
        P.op("sync", lambda Eo: Eo.dma_start(out=taps["O_acc"], in_=o_sb[:]),
             inc=sTAP, inc_n=16)
    P.wait("sync", sTAP)
    barrier(P)
    with nc.Block("b4") as blk:
        P.run(blk)
    es4.close()

    es5 = ExitStack()
    sb5 = lambda name, shape, dt: es5.enter_context(nc.sbuf_tensor(name, shape, dt))
    z2ch = [sb5("z2ch%d" % i, [128, C], F32) for i in range(NCH)]
    sq2 = [sb5("sq2%d" % i, [128, C], F32) for i in range(3)]
    s1b = sb5("s1b", [128, NCH], F32)
    s2b = sb5("s2b", [128, NCH], F32)
    mu2a = sb5("mu2a", [128, NCH], F32)
    musq2 = sb5("musq2", [128, NCH], F32)
    var2a = sb5("var2a", [128, NCH], F32)
    lnv2a = sb5("lnv2a", [128, NCH], F32)
    rstd2a = sb5("rstd2a", [128, NCH], F32)

    P = Prog()
    for _eng in ENGINES:
        P.wait(_eng, sC1, ld_consts)
        P.wait(_eng, sC2, wl_consts)
    P.wait("sync", sSCA)
    rSQ2 = Rot(3)
    sq2_done = {}

    def emit_red2b(tc):
        P.wait("vector", sAC, sq2_done[tc])
        P.op("vector", lambda Eo, q=sq2[tc % 3][:], o=s2b[:, tc:tc + 1]:
             Eo.reduce_sum(o, q, axis=AX.X), inc=sDV)
        rSQ2.rel[tc % 3] = [(sDV, sDV.n)]

    for tc in range(NCH):
        z2 = z2ch[tc]
        dYC.load(P, "sync", tc, lambda Eo, o=z2[:], s=d_y1sp[tc * 128:(tc + 1) * 128, :]:
                 Eo.dma_start(out=o, in_=s))
        dYC.wait(P, "vector", tc)
        P.op("vector", lambda Eo, m=s1b[:, tc:tc + 1], z_=z2[:]:
             Eo.reduce_sum(m, z_, axis=AX.X), inc=sDV)
        dYC.wait(P, "scalar", tc)
        rSQ2.acquire(P, "scalar", tc)
        P.op("scalar", lambda Eo, q=sq2[tc % 3][:], z_=z2[:]:
             Eo.activation(q, z_, AF.Square), inc=sAC)
        sq2_done[tc] = sAC.n
        if tc >= 1:
            emit_red2b(tc - 1)
    emit_red2b(NCH - 1)
    P.wait("vector", sDV)
    P.eop("vector", lambda Eo: Eo.tensor_scalar(mu2a[:], s1b[:], c_inv1024, None, op0=ALU.mult), sDV)
    P.eop("vector", lambda Eo: Eo.tensor_mul(musq2[:], mu2a[:], mu2a[:]), sDV)
    P.eop("vector", lambda Eo: Eo.scalar_tensor_tensor(var2a[:], s2b[:], c_inv1024, musq2[:],
                                                       op0=ALU.mult, op1=ALU.subtract), sDV)
    P.wait("scalar", sDV)
    P.eop("scalar", lambda Eo: Eo.activation(lnv2a[:], var2a[:], AF.Ln, bias=f32(EPS)), sAC)
    P.op("scalar", lambda Eo: Eo.activation(rstd2a[:], lnv2a[:], AF.Exp, scale=f32(-0.5)), inc=sAC)
    P.wait("vector", sAC)
    for tc in range(NCH):
        z2 = z2ch[tc]
        P.op("vector", lambda Eo, z_=z2[:], m=mu2a[:, tc:tc + 1], r=rstd2a[:, tc:tc + 1]:
             Eo.tensor_scalar(z_, z_, m, r, op0=ALU.subtract, op1=ALU.mult), inc=sDV)
        P.wait("sync", sDV)
        dSTp.load(P, "sync", tc, lambda Eo, o=d_y2[tc * 128:(tc + 1) * 128, :], z_=z2[:]:
                  Eo.dma_start(out=o, in_=z_))
    for s_ in dSTp.sems:
        P.wait("sync", s_)
    P.wait("sync", sTAP)
    with nc.Block("b5") as blk:
        P.run(blk)
    es5.close()
    esP.close()
    esS.close()
    return nc


_CACHE = {}


def kernel(L0_attn_x, L0_attn_Wq, L0_attn_bq, L0_attn_Wk, L0_attn_bk,
           L0_attn_Wv, L0_attn_bv, L0_ffn_router_w, L0_ffn_w1, L0_ffn_b1,
           L0_ffn_w2, L0_ffn_b2):
    x = np.ascontiguousarray(np.asarray(L0_attn_x, dtype=np.float32))
    Wq = np.asarray(L0_attn_Wq, np.float32)
    Wk = np.asarray(L0_attn_Wk, np.float32)
    Wv = np.asarray(L0_attn_Wv, np.float32)
    bq = np.asarray(L0_attn_bq, np.float32)
    bk = np.asarray(L0_attn_bk, np.float32)
    bv = np.asarray(L0_attn_bv, np.float32)
    rw_in = np.asarray(L0_ffn_router_w, np.float32)
    w1 = np.ascontiguousarray(np.asarray(L0_ffn_w1, np.float32))
    b1 = np.asarray(L0_ffn_b1, np.float32)
    w2 = np.ascontiguousarray(np.asarray(L0_ffn_w2, np.float32))
    b2 = np.asarray(L0_ffn_b2, np.float32)

    wqkvT = np.concatenate([Wq.T, Wk.T, Wv.T], axis=1)
    wqkvT = np.ascontiguousarray(wqkvT.reshape(8, 128, 192).transpose(1, 0, 2))
    qkb = np.zeros((128, 2), np.float32)
    qkb[0:64, 0] = bq
    qkb[0:64, 1] = bk
    bvrow = bv.reshape(1, HS)
    b1t = np.ascontiguousarray(
        b1.reshape(E, 32, 128).transpose(2, 0, 1))
    rw_pre = np.ascontiguousarray(rw_in.reshape(8, 128, E).transpose(1, 0, 2))
    identb = np.eye(128, dtype=bf16)
    identf = np.eye(128, dtype=np.float32)
    qpiota = (np.arange(128, dtype=np.float32)[:, None]
              - np.arange(512, dtype=np.float32)[None, :])
    onesb = np.ones((1, 128), dtype=bf16)
    consts = np.zeros((128, 12), np.float32)
    consts[:, C_INV1024] = 1.0 / 1024.0
    consts[:, C_NEG132] = -1.0 / 32.0
    consts[:, C_ONE] = 1.0
    consts[:, C_NEG1E30] = -1e30
    consts[:, C_NEG1] = -1.0
    selb = np.zeros((E, E * 128), dtype=bf16)
    for e in range(E):
        selb[e, e * 128:(e + 1) * 128] = 1
    tokid = (np.arange(NCH, dtype=np.float32)[None, :] * 128.0
             + np.arange(128, dtype=np.float32)[:, None])
    evals = np.tile(np.arange(E, dtype=np.float32)[None, :], (128, 1))
    consts[:, C_BIG] = 1.0e6
    consts[:, C_EIGHT] = 8.0
    consts[:, C_CAP] = float(CAP)

    nc = build_nc()
    in_maps = []
    for c in range(NCORES):
        b = c // 4
        ph = c % 4
        chunks = [4 * j + ph for j in range(NCH)]
        x_b = x[b]
        x_own = np.ascontiguousarray(
            x_b.reshape(64, 128, C)[chunks].reshape(TOK, C))
        qoffb = np.full((128, 1), 128.0 * ph, np.float32)
        mth = np.zeros((128, 4), np.float32)
        for t in range(4):
            mth[:, t] = (t - ph) * 128.0
        in_maps.append(dict(
            x_own=x_own, x_batch=x_b, wqkvT=wqkvT, qkb=qkb, bvrow=bvrow,
            b1t=b1t, b2w=b2, rw=rw_pre, w1=w1, w2=w2, identb=identb,
            identf=identf, qpiota=qpiota, onesb=onesb, qoffb=qoffb,
            consts=consts, selb=selb, mth=mth, tokid=tokid, evals=evals,
            b2t=np.ascontiguousarray(b2.T.reshape(8, 128, E).transpose(1, 0, 2))))
    res = run_bass_kernel_spmd(nc, in_maps, core_ids=list(range(NCORES)))
    out = np.zeros((B, T, C), np.float32)
    for c in range(NCORES):
        b = c // 4
        ph = c % 4
        y2 = np.asarray(res.results[c]["y2"]).reshape(NCH, 128, C)
        for j in range(NCH):
            out[b, (4 * j + ph) * 128:(4 * j + ph + 1) * 128, :] = y2[j]
    return out


# version 21
# speedup vs baseline: 2.3165x; 1.0793x over previous
import numpy as np
import ml_dtypes
from contextlib import ExitStack

import concourse.bass as bass
import concourse.mybir as mybir
from concourse.bass_utils import run_bass_kernel_spmd
from concourse.bass_core_rust import IndirectOffsetOnAxis

F32 = mybir.dt.float32
BF16 = mybir.dt.bfloat16
AF = mybir.ActivationFunctionType
ALU = mybir.AluOpType
AX = mybir.AxisListType
f32 = np.float32
bf16 = ml_dtypes.bfloat16

B, T, C = 2, 8192, 1024
HS = 64
E, FF = 8, 4096
NCORES = 8
TOK = 2048
NCH = 16
KCH = 64
NT = 1024
EPS = 1e-5

C_INV1024, C_NEG132, C_ONE, C_NEG1E30, C_ZERO, C_NEG1, C_BIG, C_EIGHT, C_CAP = 0, 1, 2, 3, 4, 5, 6, 7, 8
CAP, NG = 640, 5

ENGINES = ("sync", "scalar", "vector", "tensor", "gpsimd")


class Sem:
    def __init__(self, nc, es, name):
        self.h = es.enter_context(nc.semaphore(name))
        self.n = 0

    def bump(self, k):
        self.n += k
        return self.n


class Prog:

    def __init__(self):
        self.streams = {e: [] for e in ENGINES}

    def op(self, eng, fn, inc=None, inc_n=1):
        if inc is not None:
            inc.bump(inc_n)
            self.streams[eng].append(
                lambda Eo, fn=fn, s=inc.h, n=inc_n: fn(Eo).then_inc(s, n))
            if (PARANOID[0] is True and eng == "gpsimd" and inc_n == 16) or \
               (PARANOID[0] == "ind" and eng == "gpsimd" and inc_n == 16 and getattr(self, "_mark_ind", False)):
                self.wait(eng, inc)
        else:
            self.streams[eng].append(lambda Eo, fn=fn: fn(Eo))

    def eop(self, eng, fn, sem):
        self.op(eng, fn, inc=sem)
        self.wait(eng, sem)

    def wait(self, eng, sem, val=None):
        v = sem.n if val is None else val
        if v > 0:
            self.streams[eng].append(lambda Eo, s=sem.h, v=v: Eo.wait_ge(s, v))

    def run(self, block):
        for eng in ENGINES:
            fns = tuple(self.streams[eng])

            def body(Eo, fns=fns):
                for f in fns:
                    f(Eo)

            getattr(block, eng)(body)


class Rot:

    def __init__(self, n):
        self.n = n
        self.rel = [[] for _ in range(n)]

    def acquire(self, P, eng, i):
        for sem, cnt in self.rel[i % self.n]:
            P.wait(eng, sem, cnt)
        self.rel[i % self.n] = []

    def release(self, i, sem):
        self.rel[i % self.n].append((sem, sem.n))


class DmaSlots:

    def __init__(self, nc, es, name, n=2):
        self.sems = [Sem(nc, es, "%s%d" % (name, i)) for i in range(n)]
        self.n = n

    def load(self, P, eng, i, fn):
        P.op(eng, fn, inc=self.sems[i % self.n], inc_n=16)

    def wait(self, P, eng, i):
        P.wait(eng, self.sems[i % self.n])

    def sem_count(self, i):
        s = self.sems[i % self.n]
        return (s, s.n)


PARANOID = [False]


def build_nc(debug_taps=(), n_experts=E):
    nc = bass.Bass("TRN2", target_bir_lowering=False)

    d_xown = nc.dram_tensor("x_own", [TOK, C], F32, kind="ExternalInput").ap()
    d_xbat = nc.dram_tensor("x_batch", [T, C], F32, kind="ExternalInput").ap()
    d_wqkv = nc.dram_tensor("wqkvT", [128, 8, 192], F32, kind="ExternalInput").ap()
    d_qkb = nc.dram_tensor("qkb", [128, 2], F32, kind="ExternalInput").ap()
    d_bvr = nc.dram_tensor("bvrow", [1, HS], F32, kind="ExternalInput").ap()
    d_b1t = nc.dram_tensor("b1t", [128, E, 32], F32, kind="ExternalInput").ap()
    d_b2w = nc.dram_tensor("b2w", [E, C], F32, kind="ExternalInput").ap()
    d_rw = nc.dram_tensor("rw", [128, 8, E], F32, kind="ExternalInput").ap()
    d_w1 = nc.dram_tensor("w1", [E, C, FF], F32, kind="ExternalInput").ap()
    d_w2 = nc.dram_tensor("w2", [E, FF, C], F32, kind="ExternalInput").ap()
    d_idb = nc.dram_tensor("identb", [128, 128], BF16, kind="ExternalInput").ap()
    d_idf = nc.dram_tensor("identf", [128, 128], F32, kind="ExternalInput").ap()
    d_qpi = nc.dram_tensor("qpiota", [128, 512], F32, kind="ExternalInput").ap()
    d_ones = nc.dram_tensor("onesb", [1, 128], BF16, kind="ExternalInput").ap()
    d_qoff = nc.dram_tensor("qoffb", [128, 1], F32, kind="ExternalInput").ap()
    d_cst = nc.dram_tensor("consts", [128, 12], F32, kind="ExternalInput").ap()
    d_sel = nc.dram_tensor("selb", [E, E * 128], BF16, kind="ExternalInput").ap()
    d_mth = nc.dram_tensor("mth", [128, 4], F32, kind="ExternalInput").ap()
    d_tokid = nc.dram_tensor("tokid", [128, NCH], F32, kind="ExternalInput").ap()
    d_evals = nc.dram_tensor("evals", [128, E], F32, kind="ExternalInput").ap()
    d_b2t = nc.dram_tensor("b2t", [128, 8, E], F32, kind="ExternalInput").ap()
    d_y1bf = nc.dram_tensor("y1bf", [TOK, C + 8], BF16, kind="Internal").ap()
    d_idxf = nc.dram_tensor("idxf", [E, CAP], F32, kind="Internal").ap()
    d_gc = nc.dram_tensor("gc", [TOK, E], F32, kind="Internal").ap()
    d_y1sp = nc.dram_tensor("y1sp", [TOK, C], F32, kind="Internal").ap()
    d_y2 = nc.dram_tensor("y2", [TOK, C], F32, kind="ExternalOutput").ap()
    taps = {}
    for name, shape in debug_taps:
        taps[name] = nc.dram_tensor("tap_" + name, shape, F32, kind="ExternalOutput").ap()

    esP = ExitStack()
    sb = lambda name, shape, dt: esP.enter_context(nc.sbuf_tensor(name, shape, dt))

    idb = sb("idb", [128, 128], BF16)
    idf = sb("idf", [128, 128], F32)
    onesb = sb("onesb_s", [1, 128], BF16)
    qoffb = sb("qoffb_s", [128, 1], F32)
    mth = sb("mth_s", [128, 4], F32)
    tokid = sb("tokid_s", [128, NCH], F32)
    evals = sb("evals_s", [128, E], F32)
    b2ts = sb("b2t_s", [128, 8 * E], F32)
    cst = sb("cst", [128, 12], F32)
    b1t = sb("b1t_s", [128, E * 32], F32)
    b2b = sb("b2b", [E, C], BF16)
    rw = sb("rw_s", [128, 8 * E], F32)
    gatesT = sb("gatesT", [E, TOK], BF16)
    selb = sb("selb_s", [E, E * 128], BF16)

    esA = ExitStack()
    sbA = lambda name, shape, dt: esA.enter_context(nc.sbuf_tensor(name, shape, dt))
    qpi = sbA("qpi", [128, 512], F32)
    maskadd = sbA("maskadd", [128, 512], F32)
    qT = sbA("qT", [65, TOK], BF16)
    kT = sbA("kT", [65, T], BF16)
    vtm = sbA("vtm", [128, KCH * 65], BF16)
    condt = sbA("condt", [128, 4 * 128], BF16)
    o_tm = sbA("o_tm", [128, NCH * HS], F32)
    mA = sbA("mA", [128, 1], F32)
    mB = sbA("mB", [128, 1], F32)
    dm = sbA("dm", [128, 1], F32)
    edm = sbA("edm", [128, 1], F32)
    denA = sbA("denA", [128, 1], F32)
    denB = sbA("denB", [128, 1], F32)
    negm32 = sbA("negm32", [128, 1], F32)
    lden = sbA("lden", [128, 1], F32)
    bias2 = sbA("bias2", [128, 1], F32)
    bsum = [sbA("bsum0", [128, 1], F32), sbA("bsum1", [128, 1], F32)]

    esS = ExitStack()
    mks = lambda name: Sem(nc, esS, name)
    sC1 = mks("sC1")
    sC2 = mks("sC2")
    sTAP = mks("sTAP")
    sPL = mks("sPL")
    sPE = mks("sPE")
    sAC = mks("sAC")
    sDV = mks("sDV")
    sBAR = mks("sBAR")
    dXB = DmaSlots(nc, esS, "dXB")
    dZC = DmaSlots(nc, esS, "dZC", n=3)
    dSPp = DmaSlots(nc, esS, "dSPp", n=3)
    dW1 = DmaSlots(nc, esS, "dW1", n=4)
    dW2 = DmaSlots(nc, esS, "dW2", n=4)
    dYC = DmaSlots(nc, esS, "dYC", n=3)
    dSTp = DmaSlots(nc, esS, "dSTp", n=3)
    dFD = DmaSlots(nc, esS, "dFD")
    dSPb = DmaSlots(nc, esS, "dSPb", n=3)
    sGC = mks("sGC")
    sSCT = mks("sSCT")
    sSCTp = [mks("sSCTp0"), mks("sSCTp1")]
    sSCA = mks("sSCA")
    dPF = DmaSlots(nc, esS, "dPF", n=1)
    dIX = DmaSlots(nc, esS, "dIX", n=2)
    dGA = DmaSlots(nc, esS, "dGA", n=2)
    dGV = DmaSlots(nc, esS, "dGV", n=2)

    c_inv1024 = cst[:, C_INV1024:C_INV1024 + 1]
    c_neg132 = cst[:, C_NEG132:C_NEG132 + 1]
    c_one = cst[:, C_ONE:C_ONE + 1]
    c_neg1e30 = cst[:, C_NEG1E30:C_NEG1E30 + 1]
    c_zero = cst[:, C_ZERO:C_ZERO + 1]
    c_neg1 = cst[:, C_NEG1:C_NEG1 + 1]
    c_big = cst[:, C_BIG:C_BIG + 1]
    c_eight = cst[:, C_EIGHT:C_EIGHT + 1]
    c_cap = cst[:, C_CAP:C_CAP + 1]

    regB_cap = nc.gpsimd.alloc_register("bnd_cap")
    regB_tok = nc.gpsimd.alloc_register("bnd_tok")
    regB_gc = nc.gpsimd.alloc_register("bnd_gc")
    regB_all = nc.gpsimd.alloc_register("bnd_all")

    barrier_count = [0]

    def barrier(P):
        barrier_count[0] += 1
        for eng in ENGINES:
            P.op(eng, lambda Eo: Eo.nop(), inc=sBAR)
        for eng in ENGINES:
            P.wait(eng, sBAR, 5 * barrier_count[0])

    es1 = ExitStack()
    sb1 = lambda name, shape, dt: es1.enter_context(nc.sbuf_tensor(name, shape, dt))
    ps1 = lambda name, shape, dt: es1.enter_context(nc.psum_tensor(name, shape, dt))
    xb = [sb1("xb0", [128, C], BF16), sb1("xb1", [128, C], BF16)]
    xT = [sb1("xT0", [128, C], BF16), sb1("xT1", [128, C], BF16)]
    wqkv = sb1("wqkv", [128, 8 * 192], BF16)
    bvr = sb1("bvr", [1, HS], BF16)
    qkb = sb1("qkb_s", [128, 2], F32)
    pXT = [ps1("pXT0", [128, C], BF16), ps1("pXT1", [128, C], BF16)]
    pQK = [ps1("pQK0", [64, 128], F32), ps1("pQK1", [64, 128], F32)]
    pV = [ps1("pV0", [128, HS], F32), ps1("pV1", [128, HS], F32)]

    P = Prog()
    for dst, src in [(idb, d_idb), (idf, d_idf), (qpi, d_qpi), (qoffb, d_qoff),
                     (mth, d_mth), (tokid, d_tokid), (evals, d_evals),
                     (b2ts, d_b2t.rearrange("p a b -> p (a b)")),
                     (cst, d_cst), (qkb[:], d_qkb),
                     (b1t, d_b1t.rearrange("p e k -> p (e k)")),
                     (rw, d_rw.rearrange("p c e -> p (c e)"))]:
        ap = dst[:] if hasattr(dst, "ap") else dst
        P.op("sync", lambda Eo, a=ap, s=src: Eo.dma_start(out=a, in_=s), inc=sC1, inc_n=16)
    for dst, src in [(onesb[:], d_ones), (selb[:], d_sel),
                     (wqkv[:], d_wqkv.rearrange("p c h -> p (c h)")),
                     (bvr[:], d_bvr), (b2b[:], d_b2w)]:
        P.op("gpsimd", lambda Eo, a=dst, s=src: Eo.dma_start(out=a, in_=s), inc=sC2, inc_n=16)
    ld_consts = sC1.n
    wl_consts = sC2.n
    for _eng in ("tensor", "scalar", "vector", "gpsimd"):
        P.wait(_eng, sC1, ld_consts)
        P.wait(_eng, sC2, wl_consts)
    P.op("gpsimd", lambda Eo: Eo.reg_mov(regB_cap, CAP - 1))
    P.op("gpsimd", lambda Eo: Eo.reg_mov(regB_tok, TOK - 1))
    P.op("gpsimd", lambda Eo: Eo.reg_mov(regB_gc, TOK * E - 1))
    P.op("gpsimd", lambda Eo: Eo.reg_mov(regB_all, E * CAP - 1))
    P.op("gpsimd", lambda Eo: Eo.memset(kT[64:65, :], 1.0), inc=sPL)
    P.op("gpsimd", lambda Eo: Eo.memset(vtm[:], 1.0), inc=sPL)
    P.wait("vector", sC1, ld_consts)
    P.eop("vector", lambda Eo: Eo.tensor_scalar(maskadd[0:128, 0:128], qpi[:, 0:128], c_neg1, None, op0=ALU.mult), sDV)
    for _t in range(4):
        P.eop("vector", lambda Eo, t=_t: Eo.tensor_scalar(
            condt[:, t * 128:(t + 1) * 128], maskadd[0:128, 0:128],
            mth[:, t:t + 1], None, op0=ALU.is_ge), sDV)
    P.eop("vector", lambda Eo: Eo.tensor_scalar(maskadd[:], qpi[:], qoffb[:, 0:1], None, op0=ALU.add), sDV)
    P.eop("vector", lambda Eo: Eo.tensor_scalar(maskadd[:], maskadd[:], c_zero, None, op0=ALU.is_lt), sDV)
    P.op("vector", lambda Eo: Eo.tensor_scalar(maskadd[:], maskadd[:], c_neg1e30, None, op0=ALU.mult), inc=sDV)

    rXB = Rot(2)
    rXT = Rot(2)
    rQK = Rot(2)
    rV = Rot(2)

    rPXT = Rot(2)

    def chunk_setup(i, src_ap, is_q, kc):
        pi = i % 2
        rXB.acquire(P, "gpsimd", i)
        dXB.load(P, "gpsimd", i, lambda Eo, a=xb[pi][:], s=src_ap: Eo.dma_start(out=a, in_=s))
        dXB.wait(P, "tensor", i)
        rPXT.acquire(P, "tensor", i)
        rXT_psum = pXT[pi]
        for dc in range(8):
            P.op("tensor", lambda Eo, o=rXT_psum[:, dc * 128:(dc + 1) * 128],
                 a=xb[pi][:, dc * 128:(dc + 1) * 128]: Eo.transpose(o, a, idb[:]),
                 inc=(sPE if dc == 7 else None))
        transposes_done = sPE.n
        P.wait("vector", sPE)
        rXT.acquire(P, "vector", i)
        P.op("vector", lambda Eo, o=xT[pi][:], a=rXT_psum[:]: Eo.tensor_copy(o, a), inc=sDV)
        rPXT.release(i, sDV)
        evac_cnt = sDV.n
        P.wait("tensor", sDV, evac_cnt)
        if is_q:
            rQK.acquire(P, "tensor", i)
            for dc in range(8):
                P.op("tensor", lambda Eo, p=pQK[pi][:],
                     w=wqkv[:, dc * 192:dc * 192 + 64],
                     x=xT[pi][:, dc * 128:(dc + 1) * 128], st=(dc == 0), sp=(dc == 7):
                     Eo.matmul(p, w, x, start=st, stop=sp),
                     inc=(sPE if dc == 7 else None))
            P.wait("scalar", sPE)
            P.op("scalar", lambda Eo, o=qT[0:64, i * 128:(i + 1) * 128], p=pQK[pi][:]:
                 Eo.activation(o, p, AF.Identity, bias=qkb[0:64, 0:1], scale=f32(1.0)),
                 inc=sAC)
            rQK.release(i, sAC)
        else:
            rQK.acquire(P, "tensor", i)
            for dc in range(8):
                P.op("tensor", lambda Eo, p=pQK[pi][:],
                     w=wqkv[:, dc * 192 + 64:dc * 192 + 128],
                     x=xT[pi][:, dc * 128:(dc + 1) * 128], st=(dc == 0), sp=(dc == 7):
                     Eo.matmul(p, w, x, start=st, stop=sp),
                     inc=(sPE if dc == 7 else None))
            P.wait("scalar", sPE)
            P.op("scalar", lambda Eo, o=kT[0:64, kc * 128:(kc + 1) * 128], p=pQK[pi][:]:
                 Eo.activation(o, p, AF.Identity, bias=qkb[0:64, 1:2], scale=f32(1.0)),
                 inc=sAC)
            rQK.release(i, sAC)
            rV.acquire(P, "tensor", i)
            for dc in range(8):
                P.op("tensor", lambda Eo, p=pV[pi][:],
                     x=xT[pi][:, dc * 128:(dc + 1) * 128],
                     w=wqkv[:, dc * 192 + 128:dc * 192 + 192], st=(dc == 0):
                     Eo.matmul(p, x, w, start=st, stop=False))
            P.op("tensor", lambda Eo, p=pV[pi][:]:
                 Eo.matmul(p, onesb[:], bvr[:], start=False, stop=True), inc=sPE)
            P.wait("vector", sPE)
            P.wait("vector", sPL, 2)
            P.op("vector", lambda Eo, o=vtm[:, kc * 65:kc * 65 + HS], p=pV[pi][:]:
                 Eo.tensor_copy(o, p), inc=sDV)
            rV.release(i, sDV)
        rXB.rel[pi] = [(sPE, transposes_done)]
        rXT.rel[pi] = [(sPE, sPE.n)]

    for i in range(NCH):
        chunk_setup(i, d_xown[i * 128:(i + 1) * 128, :], True, None)
    for kc in range(KCH):
        chunk_setup(NCH + kc, d_xbat[kc * 128:(kc + 1) * 128, :], False, kc)

    P.wait("sync", sTAP)
    barrier(P)
    with nc.Block("b1") as blk:
        P.run(blk)
    es1.close()

    es2 = ExitStack()
    sb2 = lambda name, shape, dt: es2.enter_context(nc.sbuf_tensor(name, shape, dt))
    ps2 = lambda name, shape, dt: es2.enter_context(nc.psum_tensor(name, shape, dt))
    PT4 = [sb2("PT40", [128, 512], BF16), sb2("PT41", [128, 512], BF16)]
    ots = sb2("ots", [65, 128], F32)
    mtab = [sb2("mtab0", [128, NCH], F32), sb2("mtab1", [128, NCH], F32)]
    mfin = sb2("mfin", [128, 1], F32)
    foldv = [sb2("foldv0", [128, 1], F32), sb2("foldv1", [128, 1], F32)]
    dtmp = sb2("dtmp", [128, 1], F32)
    rcp = sb2("rcp", [128, 1], F32)
    pSA = [ps2("pSA0", [128, 512], F32), ps2("pSA1", [128, 512], F32)]
    pSk = [ps2("pSk0", [128, 512], F32), ps2("pSk1", [128, 512], F32)]
    pO = ps2("pO", [65, 128], F32)
    pOt = ps2("pOt", [128, 65], F32)

    P = Prog()
    for _eng in ENGINES:
        P.wait(_eng, sC1, ld_consts)
        P.wait(_eng, sC2, wl_consts)
    rSA = Rot(2)
    rSK = Rot(2)
    rPT4 = Rot(2)
    rPO = Rot(1)
    rMT = Rot(2)
    fold_done = {}

    def pass_a(j):
        par2 = j % 2
        rMT.acquire(P, "vector", j)
        for kb in range(j + 1):
            par = kb % 2
            rSA.acquire(P, "tensor", kb)
            P.op("tensor", lambda Eo, p=pSA[par][:],
                 q=qT[0:64, j * 128:(j + 1) * 128],
                 k=kT[0:64, kb * 512:(kb + 1) * 512]:
                 Eo.matmul(p, q, k, start=True, stop=True), inc=sPE)
            P.wait("vector", sPE)
            if kb == j:
                P.eop("vector", lambda Eo, p=pSA[par][:]:
                      Eo.tensor_add(p, p, maskadd[:]), sDV)
            P.op("vector", lambda Eo, o=mtab[par2][:, kb:kb + 1], p=pSA[par][:]:
                 Eo.reduce_max(o, p, axis=AX.X), inc=sDV)
            rSA.release(kb, sDV)
        P.wait("vector", sDV)
        P.eop("vector", lambda Eo, mt=mtab[par2][:, 0:j + 1]:
              Eo.reduce_max(mfin[:], mt, axis=AX.X), sDV)
        dFD.wait(P, "vector", j)
        P.eop("vector", lambda Eo, fv=foldv[j % 2][:]:
              Eo.tensor_scalar(fv, mfin[:], c_neg1, None, op0=ALU.mult), sDV)
        dFD.wait(P, "gpsimd", j)
        P.wait("gpsimd", sDV)
        dFD.load(P, "gpsimd", j, lambda Eo, o=qT[64:65, j * 128:(j + 1) * 128],
                 fv=foldv[j % 2][:]: Eo.dma_start(out=o, in_=fv))
        fold_done[j] = None

    def pass_b(j):
        par2 = j % 2
        ngrp = j + 1
        dFD.wait(P, "tensor", j)

        def b_mms(kg):
            par = kg % 2
            rSK.acquire(P, "tensor", kg)
            for t in range(4):
                kk = 4 * kg + t
                P.op("tensor", lambda Eo, p=pSk[par][:, t * 128:(t + 1) * 128],
                     k=kT[:, kk * 128:(kk + 1) * 128],
                     q=qT[:, j * 128:(j + 1) * 128]:
                     Eo.matmul(p, k, q, start=True, stop=True),
                     inc=(sPE if t == 3 else None))
            return sPE.n

        mm_done = {0: b_mms(0)}
        for kg in range(ngrp):
            par = kg % 2
            if kg + 1 < ngrp:
                mm_done[kg + 1] = b_mms(kg + 1)
            P.wait("scalar", sPE, mm_done[kg])
            rPT4.acquire(P, "scalar", kg)
            P.op("scalar", lambda Eo, o=PT4[par][:], p=pSk[par][:]:
                 Eo.activation(o, p, AF.Exp, scale=f32(1.0 / 32.0)), inc=sAC)
            rSK.release(kg, sAC)
            if kg == j:
                P.wait("vector", sAC)
                for t in range(4):
                    P.op("vector", lambda Eo, t=t, o=PT4[par][:, t * 128:(t + 1) * 128],
                         c=condt[:, t * 128:(t + 1) * 128]:
                         Eo.tensor_tensor(o, o, c, op=ALU.mult),
                         inc=(sDV if t == 3 else None))
                P.wait("tensor", sDV)
            else:
                P.wait("tensor", sAC)
            if kg == 0:
                rPO.acquire(P, "tensor", 0)
            for t in range(4):
                kk = 4 * kg + t
                P.op("tensor", lambda Eo, v=vtm[:, kk * 65:(kk + 1) * 65],
                     pp=PT4[par][:, t * 128:(t + 1) * 128],
                     st=(kg == 0 and t == 0), sp=(kg == j and t == 3):
                     Eo.matmul(pO[:], v, pp, start=st, stop=sp),
                     inc=(sPE if t == 3 else None))
            rPT4.release(kg, sPE)
        P.wait("scalar", sPE)
        P.op("scalar", lambda Eo: Eo.activation(ots[:], pO[:], AF.Copy), inc=sAC)
        rPO.rel[0] = [(sAC, sAC.n)]
        P.wait("tensor", sAC)
        P.op("tensor", lambda Eo: Eo.transpose(pOt[:], ots[:], idf[0:65, 0:65]), inc=sPE)
        P.wait("vector", sPE)
        P.eop("vector", lambda Eo: Eo.tensor_copy(dtmp[:], pOt[:, 64:65]), sDV)
        P.eop("vector", lambda Eo: Eo.reciprocal(rcp[:], dtmp[:]), sDV)
        P.op("vector", lambda Eo, o=o_tm[:, j * HS:(j + 1) * HS]:
             Eo.tensor_scalar(o, pOt[:, 0:HS], rcp[:, 0:1], None, op0=ALU.mult), inc=sDV)
        P.wait("tensor", sDV)
        P.wait("scalar", sDV)

    for j in range(NCH):
        pass_a(j)
        if j > 0:
            pass_b(j - 1)
    pass_b(NCH - 1)

    if "o_tm" in taps:
        P.wait("sync", sDV)
        P.op("sync", lambda Eo: Eo.dma_start(out=taps["o_tm"], in_=o_tm[:]),
             inc=sTAP, inc_n=16)
    if "condt" in taps:
        P.wait("gpsimd", sDV)
        P.op("gpsimd", lambda Eo: Eo.dma_start(out=taps["condt"], in_=condt[:]),
             inc=sTAP, inc_n=16)
    if "vtm" in taps:
        P.wait("gpsimd", sDV)
        P.op("gpsimd", lambda Eo: Eo.dma_start(out=taps["vtm"], in_=vtm[:]),
             inc=sTAP, inc_n=16)
    if "qTe" in taps:
        P.wait("gpsimd", sDV)
        P.op("gpsimd", lambda Eo: Eo.dma_start(out=taps["qTe"], in_=qT[:]),
             inc=sTAP, inc_n=16)
    if "qT" in taps:
        P.wait("gpsimd", sDV)
        P.op("gpsimd", lambda Eo: Eo.dma_start(out=taps["qT"], in_=qT[:]),
             inc=sTAP, inc_n=16)
    if "kT" in taps:
        P.wait("gpsimd", sDV)
        P.op("gpsimd", lambda Eo: Eo.dma_start(out=taps["kT"], in_=kT[:]),
             inc=sTAP, inc_n=16)
    if "vtm" in taps:
        P.wait("gpsimd", sDV)
        P.op("gpsimd", lambda Eo: Eo.dma_start(out=taps["vtm"], in_=vtm[:]),
             inc=sTAP, inc_n=16)
    P.wait("sync", sTAP)
    P.wait("gpsimd", sTAP)
    barrier(P)
    with nc.Block("b2") as blk:
        P.run(blk)
    es2.close()

    es3 = ExitStack()
    sb3 = lambda name, shape, dt: es3.enter_context(nc.sbuf_tensor(name, shape, dt))
    ps3 = lambda name, shape, dt: es3.enter_context(nc.psum_tensor(name, shape, dt))
    zch = [sb3("zch%d" % i, [128, C], F32) for i in range(NCH)]
    sqb = [sb3("sqb%d" % i, [128, C], F32) for i in range(3)]
    tmp4 = [sb3("tmp4%d" % i, [128, 512], F32) for i in range(3)]
    logits = sb3("logits", [E, TOK], F32)
    gates_tm = sb3("gates_tm", [128, 128], F32)
    s1c = sb3("s1c", [128, NCH], F32)
    s2c = sb3("s2c", [128, NCH], F32)
    mu_a = sb3("mu_a", [128, NCH], F32)
    musq = sb3("musq", [128, NCH], F32)
    var_a = sb3("var_a", [128, NCH], F32)
    lnv_a = sb3("lnv_a", [128, NCH], F32)
    rstd_a = sb3("rstd_a", [128, NCH], F32)
    gtm_all = sb3("gtm_all", [128, NCH * 8], F32)
    eq1_all = sb3("eq1_all", [128, NCH * 8], F32)
    eq2_all = sb3("eq2_all", [128, NCH * 8], F32)
    mskl_all = sb3("mskl_all", [128, NCH * 8], F32)
    tA_all = sb3("tA_all", [128, NCH * 8], F32)
    tB_all = sb3("tB_all", [128, NCH * 8], F32)
    slots_all = sb3("slots_all", [128, NCH * 8], F32)
    selw_all = sb3("selw_all", [128, NCH * 8], F32)
    m1_all = sb3("m1_all", [128, NCH], F32)
    m2_all = sb3("m2_all", [128, NCH], F32)
    edv_all = sb3("edv_all", [128, NCH], F32)
    edv2_all = sb3("edv2_all", [128, NCH], F32)
    s1_all = sb3("s1_all", [128, NCH], F32)
    rv_all = sb3("rv_all", [128, NCH], F32)
    w2g_all = sb3("w2g_all", [128, NCH], F32)
    idx1_all = sb3("idx1_all", [128, NCH], F32)
    idx2_all = sb3("idx2_all", [128, NCH], F32)
    idxi1 = sb3("idxi1", [128, NCH], mybir.dt.int32)
    idxi2 = sb3("idxi2", [128, NCH], mybir.dt.int32)
    y1bch = [sb3("y1bch%d" % i, [128, C], BF16) for i in range(3)]
    ones8 = sb3("ones8", [E, TOK], F32)
    mask_fm = sb3("mask_fm", [E, TOK], F32)
    pos_fm = sb3("pos_fm", [E, TOK], F32)
    w1m_fm = sb3("w1m_fm", [E, TOK], F32)
    slot_fm = sb3("slot_fm", [E, TOK], F32)
    ecap = sb3("ecap", [128, 8], F32)
    pft = sb3("pft", [E, CAP], F32)
    pTf = [ps3("pTf%d" % i, [128, 512], F32) for i in range(3)]
    pLr = [ps3("pLr0", [E, 128], F32), ps3("pLr1", [E, 128], F32)]
    pG8 = ps3("pG8", [128, 16], F32)
    pTg = ps3("pTg", [E, 256], F32)

    P = Prog()
    for _eng in ENGINES:
        P.wait(_eng, sC1, ld_consts)
        P.wait(_eng, sC2, wl_consts)
    rSQ = Rot(3)
    rTF = Rot(3)
    rT4 = Rot(3)
    rPL = Rot(2)

    sq_done = {}

    def emit_red2(tc):
        P.wait("vector", sAC, sq_done[tc])
        P.op("vector", lambda Eo, q=sqb[tc % 3][:], o=s2c[:, tc:tc + 1]:
             Eo.reduce_sum(o, q, axis=AX.X), inc=sDV)
        rSQ.rel[tc % 3] = [(sDV, sDV.n)]

    for tc in range(NCH):
        z = zch[tc]
        dZC.load(P, "sync", tc, lambda Eo, o=z[:], s=d_xown[tc * 128:(tc + 1) * 128, :]:
                 Eo.dma_start(out=o, in_=s))
        dZC.wait(P, "gpsimd", tc)
        orep = o_tm[:, tc * HS:(tc + 1) * HS].unsqueeze(1).broadcast_to((128, 16, HS))
        P.op("gpsimd", lambda Eo, z_=z[:], r=orep:
             Eo.tensor_add(z_.rearrange("p (r h) -> p r h", r=16), z_.rearrange("p (r h) -> p r h", r=16), r), inc=sPL)
        P.wait("vector", sPL)
        P.op("vector", lambda Eo, z_=z[:], m=s1c[:, tc:tc + 1]:
             Eo.reduce_sum(m, z_, axis=AX.X), inc=sDV)
        P.wait("scalar", sPL)
        rSQ.acquire(P, "scalar", tc)
        P.op("scalar", lambda Eo, z_=z[:], q=sqb[tc % 3][:]:
             Eo.activation(q, z_, AF.Square), inc=sAC)
        sq_done[tc] = sAC.n
        if tc >= 1:
            emit_red2(tc - 1)
    emit_red2(NCH - 1)
    P.wait("vector", sDV)
    P.eop("vector", lambda Eo: Eo.tensor_scalar(mu_a[:], s1c[:], c_inv1024, None, op0=ALU.mult), sDV)
    P.eop("vector", lambda Eo: Eo.tensor_mul(musq[:], mu_a[:], mu_a[:]), sDV)
    P.eop("vector", lambda Eo: Eo.scalar_tensor_tensor(var_a[:], s2c[:], c_inv1024, musq[:],
                                                       op0=ALU.mult, op1=ALU.subtract), sDV)
    P.wait("scalar", sDV)
    P.eop("scalar", lambda Eo: Eo.activation(lnv_a[:], var_a[:], AF.Ln, bias=f32(EPS)), sAC)
    P.op("scalar", lambda Eo: Eo.activation(rstd_a[:], lnv_a[:], AF.Exp, scale=f32(-0.5)), inc=sAC)
    P.wait("vector", sAC)
    for tc in range(NCH):
        z = zch[tc]
        P.op("vector", lambda Eo, z_=z[:], m=mu_a[:, tc:tc + 1], r=rstd_a[:, tc:tc + 1]:
             Eo.tensor_scalar(z_, z_, m, r, op0=ALU.subtract, op1=ALU.mult), inc=sDV)
        y1_done = sDV.n
        P.wait("sync", sDV, y1_done)
        dSPp.load(P, "sync", tc, lambda Eo, o=d_y1sp[tc * 128:(tc + 1) * 128, :], z_=z[:]:
                  Eo.dma_start(out=o, in_=z_))
        P.wait("gpsimd", sDV, y1_done)
        dSPb.wait(P, "gpsimd", tc)
        P.op("gpsimd", lambda Eo, o=y1bch[tc % 3][:], z_=z[:]: Eo.tensor_copy(o, z_), inc=sPL)
        P.wait("sync", sPL)
        dSPb.load(P, "sync", tc, lambda Eo, o=d_y1bf[tc * 128:(tc + 1) * 128, 0:C],
                  a=y1bch[tc % 3][:]: Eo.dma_start(out=o, in_=a))
        P.wait("tensor", sDV, y1_done)
        for g in range(2):
            tfp = pTf[(tc * 2 + g) % 3]
            rTF.acquire(P, "tensor", tc * 2 + g)
            for t in range(4):
                dc = g * 4 + t
                P.op("tensor", lambda Eo, o=tfp[:, t * 128:(t + 1) * 128],
                     a=z[:, dc * 128:(dc + 1) * 128]: Eo.transpose(o, a, idf[:]),
                     inc=(sPE if t == 3 else None))
            P.wait("scalar", sPE)
            rT4.acquire(P, "scalar", tc * 2 + g)
            tm4 = tmp4[(tc * 2 + g) % 3]
            P.op("scalar", lambda Eo, o=tm4[:], a=tfp[:]: Eo.activation(o, a, AF.Copy), inc=sAC)
            rTF.release(tc * 2 + g, sAC)
            P.wait("tensor", sAC)
            if g == 0:
                rPL.acquire(P, "tensor", tc)
            for t in range(4):
                dc = g * 4 + t
                P.op("tensor", lambda Eo, p=pLr[tc % 2][:],
                     w=rw[:, dc * 8:(dc + 1) * 8],
                     x=tm4[:, t * 128:(t + 1) * 128], st=(dc == 0), sp=(dc == 7):
                     Eo.matmul(p, w, x, start=st, stop=sp),
                     inc=(sPE if t == 3 else None))
            rT4.release(tc * 2 + g, sPE)
        P.wait("scalar", sPE)
        P.op("scalar", lambda Eo, o=logits[:, tc * 128:(tc + 1) * 128],
             p=pLr[tc % 2][:]: Eo.activation(o, p, AF.Copy), inc=sAC)
        rPL.release(tc, sAC)
    logits_done = sAC.n
    r3 = lambda ap: ap.rearrange("p (t e) -> p t e", e=8)
    rm = lambda ap: ap.rearrange("p (t o) -> p t o", o=1)
    bc = lambda ap: ap.rearrange("p (t o) -> p t o", o=1).broadcast_to((128, NCH, 8))
    cp_done = {}
    for tc in range(NCH):
        sl = tc % 2
        P.wait("tensor", sAC)
        if tc >= 2:
            P.wait("tensor", sDV, cp_done[sl])
        P.op("tensor", lambda Eo, o=pG8[:, sl * 8:(sl + 1) * 8],
             a=logits[:, tc * 128:(tc + 1) * 128]:
             Eo.transpose(o, a, idf[0:8, 0:8]), inc=sPE)
        P.wait("vector", sPE)
        P.op("vector", lambda Eo, o=gtm_all[:, tc * 8:(tc + 1) * 8],
             p=pG8[:, sl * 8:(sl + 1) * 8]: Eo.tensor_copy(o, p), inc=sDV)
        cp_done[sl] = sDV.n
    P.wait("vector", sDV)
    P.eop("vector", lambda Eo: Eo.reduce_max(rm(m1_all[:]), r3(gtm_all[:]), axis=AX.X), sDV)
    P.eop("vector", lambda Eo: Eo.tensor_tensor(r3(eq1_all[:]), r3(gtm_all[:]), bc(m1_all[:]),
                                                op=ALU.is_equal), sDV)
    P.eop("vector", lambda Eo: Eo.scalar_tensor_tensor(mskl_all[:], eq1_all[:], c_neg1e30,
                                                       gtm_all[:], op0=ALU.mult, op1=ALU.add), sDV)
    P.eop("vector", lambda Eo: Eo.reduce_max(rm(m2_all[:]), r3(mskl_all[:]), axis=AX.X), sDV)
    P.eop("vector", lambda Eo: Eo.tensor_tensor(r3(eq2_all[:]), r3(mskl_all[:]), bc(m2_all[:]),
                                                op=ALU.is_equal), sDV)
    P.eop("vector", lambda Eo: Eo.tensor_sub(edv2_all[:], m2_all[:], m1_all[:]), sDV)
    P.wait("scalar", sDV)
    P.eop("scalar", lambda Eo: Eo.activation(edv_all[:], edv2_all[:], AF.Exp), sAC)
    P.wait("vector", sAC)
    P.eop("vector", lambda Eo: Eo.tensor_scalar(s1_all[:], edv_all[:], c_one, None, op0=ALU.add), sDV)
    P.eop("vector", lambda Eo: Eo.reciprocal(rv_all[:], s1_all[:]), sDV)
    P.eop("vector", lambda Eo: Eo.tensor_mul(w2g_all[:], edv_all[:], rv_all[:]), sDV)
    P.eop("vector", lambda Eo: Eo.tensor_tensor(r3(tA_all[:]), r3(eq1_all[:]), bc(rv_all[:]),
                                                op=ALU.mult), sDV)
    P.eop("vector", lambda Eo: Eo.tensor_tensor(r3(tB_all[:]), r3(eq2_all[:]), bc(w2g_all[:]),
                                                op=ALU.mult), sDV)
    P.eop("vector", lambda Eo: Eo.tensor_add(gates_tm[:], tA_all[:], tB_all[:]), sDV)
    P.wait("gpsimd", sDV)
    P.op("gpsimd", lambda Eo, o=d_y1bf.rearrange("(t p) c -> p t c", p=128)[:, :, C:C + 8],
         a=r3(gates_tm[:]): Eo.dma_start(out=o, in_=a), inc=sGC, inc_n=16)
    P.wait("tensor", sDV)
    for tc in range(NCH):
        sl = tc % 2
        P.op("tensor", lambda Eo, o=pTg[:, sl * 128:(sl + 1) * 128],
             a=gates_tm[:, tc * 8:(tc + 1) * 8]:
             Eo.transpose(o, a, idf[:]), inc=sPE)
        P.wait("scalar", sPE)
        P.op("scalar", lambda Eo, o=gatesT[:, tc * 128:(tc + 1) * 128],
             p=pTg[:, sl * 128:(sl + 1) * 128]: Eo.activation(o, p, AF.Copy), inc=sAC)
        P.wait("tensor", sAC)

    P.eop("vector", lambda Eo: Eo.tensor_scalar(ecap[:], evals[:], c_cap, None, op0=ALU.mult), sDV)
    P.op("gpsimd", lambda Eo: Eo.memset(ones8[:], 1.0), inc=sPL)
    P.op("gpsimd", lambda Eo: Eo.memset(pft[:], 2.0e6), inc=sPL)
    P.wait("sync", sPL)
    dPF.load(P, "sync", 0, lambda Eo: Eo.dma_start(out=d_idxf, in_=pft[:]))
    P.wait("vector", sAC)
    P.wait("vector", sPL)
    P.eop("vector", lambda Eo: Eo.tensor_scalar(mask_fm[:], gatesT[:], c_zero[0:E, :], None,
                                                op0=ALU.is_gt), sDV)
    P.eop("vector", lambda Eo: Eo.tensor_tensor_scan(pos_fm[:], ones8[:], mask_fm[:], 0.0,
                                                     op0=ALU.mult, op1=ALU.add), sDV)
    P.eop("vector", lambda Eo: Eo.tensor_scalar(pos_fm[:], pos_fm[:], c_one[0:E, :], None,
                                                op0=ALU.subtract), sDV)
    P.eop("vector", lambda Eo: Eo.tensor_scalar(w1m_fm[:], mask_fm[:], c_neg1[0:E, :],
                                                c_one[0:E, :], op0=ALU.mult, op1=ALU.add), sDV)
    P.eop("vector", lambda Eo: Eo.scalar_tensor_tensor(slot_fm[:], w1m_fm[:], c_big[0:E, :],
                                                       pos_fm[:], op0=ALU.mult, op1=ALU.add), sDV)
    slot_ready = sDV.n
    P.wait("tensor", sDV, slot_ready)
    dPF.wait(P, "gpsimd", 0)
    for tc in range(NCH):
        sl = tc % 2
        if tc >= 2:
            P.wait("tensor", sDV, cp_done[sl])
        P.op("tensor", lambda Eo, o=pG8[:, sl * 8:(sl + 1) * 8],
             a=slot_fm[:, tc * 128:(tc + 1) * 128]:
             Eo.transpose(o, a, idf[0:8, 0:8]), inc=sPE)
        P.wait("vector", sPE)
        P.op("vector", lambda Eo, o=slots_all[:, tc * 8:(tc + 1) * 8],
             p=pG8[:, sl * 8:(sl + 1) * 8]: Eo.tensor_copy(o, p), inc=sDV)
        cp_done[sl] = sDV.n
    P.wait("vector", sDV)
    P.eop("vector", lambda Eo: Eo.tensor_scalar(tA_all[:], slots_all[:], c_cap, None,
                                                op0=ALU.is_ge), sDV)
    P.eop("vector", lambda Eo: Eo.scalar_tensor_tensor(selw_all[:], tA_all[:], c_big,
                                                       slots_all[:], op0=ALU.mult, op1=ALU.add), sDV)
    ecap_b = ecap[:].rearrange("p (o e) -> p o e", o=1).broadcast_to((128, NCH, 8))
    P.eop("vector", lambda Eo: Eo.tensor_tensor(r3(selw_all[:]), r3(selw_all[:]), ecap_b,
                                                op=ALU.add), sDV)
    P.eop("vector", lambda Eo: Eo.reduce_max(rm(m1_all[:]), r3(gates_tm[:]), axis=AX.X), sDV)
    P.eop("vector", lambda Eo: Eo.tensor_tensor(r3(tA_all[:]), r3(gates_tm[:]), bc(m1_all[:]),
                                                op=ALU.is_equal), sDV)
    P.eop("vector", lambda Eo: Eo.tensor_scalar(tB_all[:], gates_tm[:], c_zero, None,
                                                op0=ALU.is_gt), sDV)
    P.eop("vector", lambda Eo: Eo.tensor_sub(tB_all[:], tB_all[:], tA_all[:]), sDV)
    P.eop("vector", lambda Eo: Eo.tensor_mul(tA_all[:], tA_all[:], selw_all[:]), sDV)
    P.eop("vector", lambda Eo: Eo.reduce_sum(rm(idx1_all[:]), r3(tA_all[:]), axis=AX.X), sDV)
    P.eop("vector", lambda Eo: Eo.tensor_mul(tB_all[:], tB_all[:], selw_all[:]), sDV)
    P.eop("vector", lambda Eo: Eo.reduce_sum(rm(idx2_all[:]), r3(tB_all[:]), axis=AX.X), sDV)
    P.eop("vector", lambda Eo: Eo.tensor_copy(idxi1[:], idx1_all[:]), sDV)
    P.eop("vector", lambda Eo: Eo.tensor_copy(idxi2[:], idx2_all[:]), sDV)
    P.wait("gpsimd", sDV)
    P.wait("gpsimd", sGC)
    sct_base = sSCT.n
    kb_ = 0
    for tc in range(NCH):
        for ib in (idxi1, idxi2):
            P.op("gpsimd", lambda Eo, tc=tc, ib=ib:
                 Eo.indirect_dma_start(
                     out=d_idxf.rearrange("e (s o) -> (e s) o", o=1),
                     out_offset=IndirectOffsetOnAxis(ap=ib[:, tc:tc + 1], axis=0),
                     in_=tokid[:, tc:tc + 1], in_offset=None,
                     bounds_check=regB_all, oob_is_err=False),
                 inc=sSCT, inc_n=16)
            kb_ += 1
            if kb_ >= 2:
                P.wait("gpsimd", sSCT, sct_base + 16 * (kb_ - 1))
    P.wait("gpsimd", sSCT)

    if "logits" in taps:
        P.wait("sync", sAC, logits_done)
        P.op("sync", lambda Eo: Eo.dma_start(out=taps["logits"], in_=logits[:]),
             inc=sTAP, inc_n=16)
    if "gates" in taps:
        P.wait("sync", sDV)
        P.op("sync", lambda Eo: Eo.dma_start(out=taps["gates"], in_=gates_tm[:]),
             inc=sTAP, inc_n=16)
    if "gatesT" in taps:
        P.wait("gpsimd", sAC)
        P.op("gpsimd", lambda Eo: Eo.dma_start(out=taps["gatesT"], in_=gatesT[:]),
             inc=sTAP, inc_n=16)
    if "y1" in taps:
        for s_ in dSPp.sems:
            P.wait("sync", s_)
        P.op("sync", lambda Eo: Eo.dma_start(out=taps["y1"], in_=d_y1sp), inc=sTAP, inc_n=16)
    for s_ in dSPp.sems:
        P.wait("sync", s_)
    P.wait("sync", sTAP)
    barrier(P)
    with nc.Block("b3") as blk:
        P.run(blk)
    es3.close()
    esA.close()

    es4 = ExitStack()
    sb4 = lambda name, shape, dt: es4.enter_context(nc.sbuf_tensor(name, shape, dt))
    ps4 = lambda name, shape, dt: es4.enter_context(nc.psum_tensor(name, shape, dt))
    H = sb4("H", [128, 32 * CAP], BF16)
    y1gT = [sb4("y1gT%d" % i, [128, 8 * CAP], BF16) for i in range(2)]
    o_sb = sb4("o_sb", [128, 8 * CAP], F32)
    w1b = [sb4("w1b%d" % i, [128, 8 * 256], BF16) for i in range(4)]
    w2b = [sb4("w2b%d" % i, [128, 16 * 128], BF16) for i in range(4)]
    gbuf = [sb4("gbuf%d" % i, [128, C + 8], BF16) for i in range(NG)]
    scatb = [sb4("scatb%d" % i, [128, C], F32) for i in range(NG)]
    idxf_s = [sb4("idxf_s%d" % i, [128, NG], F32) for i in range(2)]
    idxi_s = [sb4("idxi_s%d" % i, [128, NG], mybir.dt.int32) for i in range(2)]
    gv = [sb4("gv%d" % i, [128, NG], F32) for i in range(2)]
    pH = [ps4("pH%d" % i, [128, 1024], F32) for i in range(2)]
    pTin = [ps4("pTin%d" % i, [128, 512], BF16) for i in range(2)]
    pTout = [ps4("pTout%d" % i, [128, 512], F32) for i in range(2)]

    P = Prog()
    w1_view = d_w1
    w2_view = d_w2
    for _eng in ENGINES:
        P.wait(_eng, sC1, ld_consts)
        P.wait(_eng, sC2, wl_consts)
    for _eng in ("gpsimd", "sync"):
        P.wait(_eng, sSCT)
        P.wait(_eng, sGC)
        for s_ in dSPb.sems:
            P.wait(_eng, s_)
        for s_ in dSPp.sems:
            P.wait(_eng, s_)
        P.wait(_eng, dPF.sems[0])
    P.wait("gpsimd", sGC)
    for s_ in dSPb.sems:
        P.wait("gpsimd", s_)
    for _gb in gbuf:
        P.op("gpsimd", lambda Eo, b=_gb: Eo.memset(b[:], 0.0))
    P.op("gpsimd", lambda Eo: Eo.memset(gv[0][:], 0.0))
    P.op("gpsimd", lambda Eo: Eo.memset(gv[1][:], 0.0), inc=sPL)
    P.wait("gpsimd", sPL)
    P.wait("tensor", sPL)

    rW1 = Rot(4)
    rW2 = Rot(4)
    rGB = Rot(NG)
    rIX = Rot(2)
    rGV = Rot(2)
    rSCB = Rot(NG)
    rPH = Rot(2)
    rTIN = Rot(2)
    rTOUT = Rot(2)
    NB = ((0, 512), (512, CAP))

    in_done = {}

    idx_ready = {}

    def emit_meta(e):
        ep = e % 2
        rIX.acquire(P, "sync", e)
        dIX.load(P, "sync", e, lambda Eo, o=idxf_s[ep][:],
                 s=d_idxf[e, :].rearrange("(p g) -> p g", g=NG): Eo.dma_start(out=o, in_=s))
        dIX.wait(P, "vector", e)
        P.eop("vector", lambda Eo, o=idxi_s[ep][:], a=idxf_s[ep][:]: Eo.tensor_copy(o, a), sDV)
        rGV.acquire(P, "vector", e)
        idx_ready[e] = sDV.n

    def emit_gath1(e, g):
        ep = e % 2
        if g == 0:
            P.wait("gpsimd", sDV, idx_ready[e])
            for s_ in dW1.sems:
                P.wait("gpsimd", s_)
            for s_ in dW2.sems:
                P.wait("gpsimd", s_)
        rGB.acquire(P, "gpsimd", g)
        dGA.load(P, "gpsimd", e * NG + g, lambda Eo, g=g, ep=ep:
                 Eo.indirect_dma_start(
                     out=gbuf[g][:], out_offset=None,
                     in_=d_y1bf, in_offset=IndirectOffsetOnAxis(ap=idxi_s[ep][:, g:g + 1], axis=0),
                     bounds_check=regB_tok, oob_is_err=False))
        if g >= 1:
            dGA.wait(P, "gpsimd", e * NG + g - 1)

    def emit_gvx(e):
        ep = e % 2
        for g in range(NG):
            dGA.wait(P, "vector", e * NG + g)
            P.eop("vector", lambda Eo, g=g, ep=ep, e=e, o=gv[ep][:, g:g + 1]:
                  Eo.tensor_copy(o, gbuf[g][:, C + e:C + e + 1]), sDV)

    def emit_int(e):
        ep = e % 2
        for g in range(NG):
            dGA.wait(P, "tensor", e * NG + g)
            for half in range(2):
                seq = (e * NG + g) * 2 + half
                ti = seq % 2
                rTIN.acquire(P, "tensor", seq)
                for t in range(4):
                    kc = half * 4 + t
                    P.op("tensor", lambda Eo, t=t, kc=kc, g=g, ti=ti,
                         o=pTin[ti][:, t * 128:(t + 1) * 128],
                         a=gbuf[g][:, kc * 128:(kc + 1) * 128]:
                         Eo.transpose(o, a, idb[:]),
                         inc=(sPE if t == 3 else None))
                P.wait("vector", sPE)
                kc0 = half * 4
                P.op("vector", lambda Eo, kc0=kc0, g=g, ti=ti,
                     o=y1gT[ep][:].rearrange("p (kc c) -> p kc c", c=CAP)[:, kc0:kc0 + 4, g * 128:(g + 1) * 128],
                     pt=pTin[ti][:].rearrange("p (t c) -> p t c", c=128):
                     Eo.tensor_copy(o, pt), inc=sDV)
                rTIN.release(seq, sDV)
            rGB.release(g, sPE)
            rGB.release(g, sDV)
        in_done[e] = sDV.n

    PIPE_META = True
    prefetched = set()

    def load_w1(e, fp):
        wpar = (e * 16 + fp) % 4
        rW1.acquire(P, "gpsimd", e * 16 + fp)
        srcw = w1_view[e, :, fp * 256:(fp + 1) * 256].rearrange("(kc p) f -> p kc f", p=128)
        dW1.load(P, "gpsimd", e * 16 + fp,
                 lambda Eo, o=w1b[wpar][:].rearrange("p (kc f) -> p kc f", kc=8),
                 s=srcw: Eo.dma_start(out=o, in_=s))

    emit_meta(0)
    for _g in range(NG):
        emit_gath1(0, _g)
    dGA.wait(P, "gpsimd", NG - 1)
    emit_gvx(0)
    emit_int(0)
    for e in range(n_experts):
        ep = e % 2
        if e > 0:
            emit_int(e)
        P.wait("tensor", sDV, in_done[e])
        for fp in range(16):
            wpar = (e * 16 + fp) % 4
            if (e, fp) not in prefetched:
                load_w1(e, fp)
            dW1.wait(P, "tensor", e * 16 + fp)
            for fc in range(2):
                fcg = fp * 2 + fc
                hpar = fcg % 2
                rPH.acquire(P, "tensor", fcg)
                for kc in range(8):
                    for nbi, (lo, hi) in enumerate(NB):
                        P.op("tensor", lambda Eo, p=pH[hpar][:, lo:hi],
                             w=w1b[wpar][:, kc * 256 + fc * 128:kc * 256 + (fc + 1) * 128],
                             x=y1gT[ep][:, kc * CAP + lo:kc * CAP + hi],
                             st=(kc == 0), sp=(kc == 7):
                             Eo.matmul(p, w, x, start=st, stop=sp),
                             inc=(sPE if (nbi == 1 and kc == 7) else None))
                P.wait("scalar", sPE)
                hsl = H[:, fcg * CAP:(fcg + 1) * CAP]
                P.op("scalar", lambda Eo, o=hsl, p=pH[hpar][:, 0:CAP],
                     b=b1t[:, e * 32 + fcg:e * 32 + fcg + 1]:
                     Eo.activation(o, p, AF.Gelu_apprx_tanh, bias=b, scale=f32(1.0)),
                     inc=sAC)
                rPH.release(fcg, sAC)
            rW1.release(e * 16 + fp, sPE)
        h_done = sAC.n
        if PIPE_META and e + 1 < n_experts:
            emit_meta(e + 1)
        P.wait("tensor", sAC, h_done)
        for dc in range(8):
            opar = dc % 2
            if dc == 4 and e + 1 < n_experts:
                for g in range(NG):
                    emit_gath1(e + 1, g)
                dGA.wait(P, "gpsimd", (e + 1) * NG + NG - 1)
                emit_gvx(e + 1)
                for fpp in range(4):
                    load_w1(e + 1, fpp)
                    prefetched.add((e + 1, fpp))
            for fh in range(2):
                w2i = (e * 16 + dc * 2 + fh) % 4
                rW2.acquire(P, "gpsimd", e * 16 + dc * 2 + fh)
                srcw = w2_view[e, fh * 2048:(fh + 1) * 2048,
                               dc * 128:(dc + 1) * 128].rearrange("(fc p) d -> p fc d", p=128)
                dW2.load(P, "gpsimd", e * 16 + dc * 2 + fh,
                         lambda Eo, o=w2b[w2i][:].rearrange("p (fc d) -> p fc d", fc=16),
                         s=srcw: Eo.dma_start(out=o, in_=s))
                dW2.wait(P, "tensor", e * 16 + dc * 2 + fh)
                if fh == 0:
                    rPH.acquire(P, "tensor", dc)
                for fcL in range(16):
                    fcg = fh * 16 + fcL
                    for nbi, (lo, hi) in enumerate(NB):
                        P.op("tensor", lambda Eo, p=pH[opar][:, lo:hi],
                             w=w2b[w2i][:, fcL * 128:(fcL + 1) * 128],
                             x=H[:, fcg * CAP + lo:fcg * CAP + hi],
                             st=(fh == 0 and fcL == 0), sp=(fh == 1 and fcL == 15):
                             Eo.matmul(p, w, x, start=st, stop=sp),
                             inc=(sPE if (nbi == 1 and fcL == 15) else None))
                rW2.release(e * 16 + dc * 2 + fh, sPE)
            P.wait("vector", sPE)
            P.op("vector", lambda Eo, o=o_sb[:, dc * CAP:(dc + 1) * CAP],
                 p=pH[opar][:, 0:CAP], b=b2ts[:, dc * 8 + e:dc * 8 + e + 1]:
                 Eo.tensor_scalar(o, p, b, None, op0=ALU.add), inc=sDV)
            rPH.release(dc, sDV)
        o_done = sDV.n
        P.wait("tensor", sDV, o_done)
        scab = sSCA.n
        for g in range(NG):
            sbi = g
            rSCB.acquire(P, "tensor", e * NG + g)
            for half in range(2):
                seq = (e * NG + g) * 2 + half
                ti = seq % 2
                rTOUT.acquire(P, "tensor", seq)
                for t in range(4):
                    dc = half * 4 + t
                    P.op("tensor", lambda Eo, t=t, dc=dc, g=g, ti=ti,
                         o=pTout[ti][:, t * 128:(t + 1) * 128],
                         a=o_sb[:, dc * CAP + g * 128:dc * CAP + (g + 1) * 128]:
                         Eo.transpose(o, a, idf[:]),
                         inc=(sPE if t == 3 else None))
                P.wait("vector", sPE)
                dc0 = half * 4
                P.op("vector", lambda Eo, dc0=dc0, g=g, sbi=sbi, ti=ti,
                     o=scatb[sbi][:, dc0 * 128:(dc0 + 4) * 128],
                     p=pTout[ti][:], gvc=gv[ep][:, g:g + 1]:
                     Eo.tensor_scalar(o, p, gvc, None, op0=ALU.mult),
                     inc=sDV)
                rTOUT.release(seq, sDV)
            P.wait("gpsimd", sDV)
            if g == 0:
                P.wait("gpsimd", sSCA)
            elif g >= 2:
                P.wait("gpsimd", sSCA, scab + 16 * (g - 1))
            P.op("gpsimd", lambda Eo, g=g, ep=ep, sbi=sbi:
                 Eo.indirect_dma_start(
                     out=d_y1sp, out_offset=IndirectOffsetOnAxis(ap=idxi_s[ep][:, g:g + 1], axis=0),
                     in_=scatb[sbi][:], in_offset=None,
                     bounds_check=regB_tok, oob_is_err=False,
                     compute_op=ALU.add),
                 inc=sSCA, inc_n=16)
            rSCB.release(e * NG + g, sSCA)
        P.wait("gpsimd", sSCA)
        rIX.release(e, sSCA)
        rGV.release(e, sSCA)

    if "O_acc" in taps:
        P.wait("sync", sSCA)
        P.op("sync", lambda Eo: Eo.dma_start(out=taps["O_acc"], in_=o_sb[:]),
             inc=sTAP, inc_n=16)
    P.wait("sync", sTAP)
    barrier(P)
    with nc.Block("b4") as blk:
        P.run(blk)
    es4.close()

    es5 = ExitStack()
    sb5 = lambda name, shape, dt: es5.enter_context(nc.sbuf_tensor(name, shape, dt))
    z2ch = [sb5("z2ch%d" % i, [128, C], F32) for i in range(NCH)]
    sq2 = [sb5("sq2%d" % i, [128, C], F32) for i in range(3)]
    s1b = sb5("s1b", [128, NCH], F32)
    s2b = sb5("s2b", [128, NCH], F32)
    mu2a = sb5("mu2a", [128, NCH], F32)
    musq2 = sb5("musq2", [128, NCH], F32)
    var2a = sb5("var2a", [128, NCH], F32)
    lnv2a = sb5("lnv2a", [128, NCH], F32)
    rstd2a = sb5("rstd2a", [128, NCH], F32)

    P = Prog()
    for _eng in ENGINES:
        P.wait(_eng, sC1, ld_consts)
        P.wait(_eng, sC2, wl_consts)
    P.wait("sync", sSCA)
    rSQ2 = Rot(3)
    sq2_done = {}

    def emit_red2b(tc):
        P.wait("vector", sAC, sq2_done[tc])
        P.op("vector", lambda Eo, q=sq2[tc % 3][:], o=s2b[:, tc:tc + 1]:
             Eo.reduce_sum(o, q, axis=AX.X), inc=sDV)
        rSQ2.rel[tc % 3] = [(sDV, sDV.n)]

    for tc in range(NCH):
        z2 = z2ch[tc]
        dYC.load(P, "sync", tc, lambda Eo, o=z2[:], s=d_y1sp[tc * 128:(tc + 1) * 128, :]:
                 Eo.dma_start(out=o, in_=s))
        dYC.wait(P, "vector", tc)
        P.op("vector", lambda Eo, m=s1b[:, tc:tc + 1], z_=z2[:]:
             Eo.reduce_sum(m, z_, axis=AX.X), inc=sDV)
        dYC.wait(P, "scalar", tc)
        rSQ2.acquire(P, "scalar", tc)
        P.op("scalar", lambda Eo, q=sq2[tc % 3][:], z_=z2[:]:
             Eo.activation(q, z_, AF.Square), inc=sAC)
        sq2_done[tc] = sAC.n
        if tc >= 1:
            emit_red2b(tc - 1)
    emit_red2b(NCH - 1)
    P.wait("vector", sDV)
    P.eop("vector", lambda Eo: Eo.tensor_scalar(mu2a[:], s1b[:], c_inv1024, None, op0=ALU.mult), sDV)
    P.eop("vector", lambda Eo: Eo.tensor_mul(musq2[:], mu2a[:], mu2a[:]), sDV)
    P.eop("vector", lambda Eo: Eo.scalar_tensor_tensor(var2a[:], s2b[:], c_inv1024, musq2[:],
                                                       op0=ALU.mult, op1=ALU.subtract), sDV)
    P.wait("scalar", sDV)
    P.eop("scalar", lambda Eo: Eo.activation(lnv2a[:], var2a[:], AF.Ln, bias=f32(EPS)), sAC)
    P.op("scalar", lambda Eo: Eo.activation(rstd2a[:], lnv2a[:], AF.Exp, scale=f32(-0.5)), inc=sAC)
    P.wait("vector", sAC)
    for tc in range(NCH):
        z2 = z2ch[tc]
        P.op("vector", lambda Eo, z_=z2[:], m=mu2a[:, tc:tc + 1], r=rstd2a[:, tc:tc + 1]:
             Eo.tensor_scalar(z_, z_, m, r, op0=ALU.subtract, op1=ALU.mult), inc=sDV)
        P.wait("sync", sDV)
        dSTp.load(P, "sync", tc, lambda Eo, o=d_y2[tc * 128:(tc + 1) * 128, :], z_=z2[:]:
                  Eo.dma_start(out=o, in_=z_))
    for s_ in dSTp.sems:
        P.wait("sync", s_)
    P.wait("sync", sTAP)
    with nc.Block("b5") as blk:
        P.run(blk)
    es5.close()
    esP.close()
    esS.close()
    return nc


_CACHE = {}


def kernel(L0_attn_x, L0_attn_Wq, L0_attn_bq, L0_attn_Wk, L0_attn_bk,
           L0_attn_Wv, L0_attn_bv, L0_ffn_router_w, L0_ffn_w1, L0_ffn_b1,
           L0_ffn_w2, L0_ffn_b2):
    x = np.ascontiguousarray(np.asarray(L0_attn_x, dtype=np.float32))
    Wq = np.asarray(L0_attn_Wq, np.float32)
    Wk = np.asarray(L0_attn_Wk, np.float32)
    Wv = np.asarray(L0_attn_Wv, np.float32)
    bq = np.asarray(L0_attn_bq, np.float32)
    bk = np.asarray(L0_attn_bk, np.float32)
    bv = np.asarray(L0_attn_bv, np.float32)
    rw_in = np.asarray(L0_ffn_router_w, np.float32)
    w1 = np.ascontiguousarray(np.asarray(L0_ffn_w1, np.float32))
    b1 = np.asarray(L0_ffn_b1, np.float32)
    w2 = np.ascontiguousarray(np.asarray(L0_ffn_w2, np.float32))
    b2 = np.asarray(L0_ffn_b2, np.float32)

    wqkvT = np.concatenate([Wq.T, Wk.T, Wv.T], axis=1)
    wqkvT = np.ascontiguousarray(wqkvT.reshape(8, 128, 192).transpose(1, 0, 2))
    qkb = np.zeros((128, 2), np.float32)
    qkb[0:64, 0] = bq
    qkb[0:64, 1] = bk
    bvrow = bv.reshape(1, HS)
    b1t = np.ascontiguousarray(
        b1.reshape(E, 32, 128).transpose(2, 0, 1))
    rw_pre = np.ascontiguousarray(rw_in.reshape(8, 128, E).transpose(1, 0, 2))
    identb = np.eye(128, dtype=bf16)
    identf = np.eye(128, dtype=np.float32)
    qpiota = (np.arange(128, dtype=np.float32)[:, None]
              - np.arange(512, dtype=np.float32)[None, :])
    onesb = np.ones((1, 128), dtype=bf16)
    consts = np.zeros((128, 12), np.float32)
    consts[:, C_INV1024] = 1.0 / 1024.0
    consts[:, C_NEG132] = -1.0 / 32.0
    consts[:, C_ONE] = 1.0
    consts[:, C_NEG1E30] = -1e30
    consts[:, C_NEG1] = -1.0
    selb = np.zeros((E, E * 128), dtype=bf16)
    for e in range(E):
        selb[e, e * 128:(e + 1) * 128] = 1
    tokid = (np.arange(NCH, dtype=np.float32)[None, :] * 128.0
             + np.arange(128, dtype=np.float32)[:, None])
    evals = np.tile(np.arange(E, dtype=np.float32)[None, :], (128, 1))
    consts[:, C_BIG] = 1.0e6
    consts[:, C_EIGHT] = 8.0
    consts[:, C_CAP] = float(CAP)

    nc = build_nc()
    in_maps = []
    for c in range(NCORES):
        b = c // 4
        ph = c % 4
        chunks = [4 * j + ph for j in range(NCH)]
        x_b = x[b]
        x_own = np.ascontiguousarray(
            x_b.reshape(64, 128, C)[chunks].reshape(TOK, C))
        qoffb = np.full((128, 1), 128.0 * ph, np.float32)
        mth = np.zeros((128, 4), np.float32)
        for t in range(4):
            mth[:, t] = (t - ph) * 128.0
        in_maps.append(dict(
            x_own=x_own, x_batch=x_b, wqkvT=wqkvT, qkb=qkb, bvrow=bvrow,
            b1t=b1t, b2w=b2, rw=rw_pre, w1=w1, w2=w2, identb=identb,
            identf=identf, qpiota=qpiota, onesb=onesb, qoffb=qoffb,
            consts=consts, selb=selb, mth=mth, tokid=tokid, evals=evals,
            b2t=np.ascontiguousarray(b2.T.reshape(8, 128, E).transpose(1, 0, 2))))
    res = run_bass_kernel_spmd(nc, in_maps, core_ids=list(range(NCORES)))
    out = np.zeros((B, T, C), np.float32)
    for c in range(NCORES):
        b = c // 4
        ph = c % 4
        y2 = np.asarray(res.results[c]["y2"]).reshape(NCH, 128, C)
        for j in range(NCH):
            out[b, (4 * j + ph) * 128:(4 * j + ph + 1) * 128, :] = y2[j]
    return out
